# version 24
# baseline (speedup 1.0000x reference)
import numpy as np
import concourse.bass as bass
import concourse.mybir as mybir

F32 = mybir.dt.float32
BF16 = mybir.dt.bfloat16
I32 = mybir.dt.int32
AF = mybir.ActivationFunctionType
ALU = mybir.AluOpType
AX = mybir.AxisListType.X

D = 1024
T = 4096
NW = 8
WIN = 512
NTAU = 4
CT = 128
NDC = 8


class Op:
    __slots__ = ('stream', 'fn', 'deps', 'sem', 'inc', 'idx', 'label', 'phase')

    def __init__(self, stream, fn, sem, inc, idx, label):
        self.stream = stream
        self.fn = fn
        self.deps = []
        self.sem = sem
        self.inc = inc
        self.idx = idx
        self.label = label


class TK:
    STREAMS = ('pe', 'act', 'dve', 'pool', 'sp')
    ENGSEM = {'pe': 'pe', 'act': 'act', 'dve': 'dve', 'pool': 'pool', 'sp': 'spe'}

    NQSEM = 4

    def __init__(self, nc):
        self.nc = nc
        self.ops = []
        self.writer = {}
        self.readers = {}
        self.dma_ring = {}
        self.barrier_ops = []
        self.phase = 'init'

    def op(self, stream, fn, reads=(), writes=(), label='', is_dma=False):
        assert stream in self.STREAMS
        extra_dep = None
        if is_dma:
            ring = self.dma_ring.setdefault(stream, [])
            slot = len(ring) % self.NQSEM
            sem = f'q_{stream}_{slot}'
            inc = 16
            if len(ring) >= self.NQSEM:
                extra_dep = ring[-self.NQSEM]
            ring.append(None)
        else:
            sem = None
            inc = 1
        o = Op(stream, fn, sem, inc, len(self.ops), label)
        o.phase = self.phase
        if is_dma:
            self.dma_ring[stream][-1] = o
        deps = {}
        if extra_dep is not None:
            deps[extra_dep.idx] = extra_dep
        for k in reads:
            w = self.writer.get(k)
            if w is not None:
                if w.stream == 'pe' and stream == 'pe' and w.inc != 16:
                    continue
                deps[w.idx] = w
        for k in writes:
            w = self.writer.get(k)
            if w is not None and not (w.stream == stream and w.inc != 16):
                deps[w.idx] = w
            for r in self.readers.get(k, ()):
                if r.stream == stream and r.inc != 16:
                    continue
                deps[r.idx] = r
        o.deps = list(deps.values())
        for d in o.deps:
            if d.sem is None:
                d.sem = self.ENGSEM[d.stream]
        for k in writes:
            self.writer[k] = o
            self.readers[k] = []
        for k in reads:
            self.readers.setdefault(k, []).append(o)
        self.ops.append(o)
        return o

    def dma(self, queue, out, in_, reads=(), writes=(), label='', **kw):
        return self.op(queue, lambda e: e.dma_start(out=out, in_=in_, **kw),
                       reads=reads, writes=writes, label=label, is_dma=True)

    def mm(self, out, lhsT, rhs, start, stop, reads, writes, label=''):
        fn = (lambda out, lhsT, rhs, start, stop:
              lambda e: e.matmul(out, lhsT=lhsT, rhs=rhs, start=start, stop=stop,
                                 skip_group_check=True))(out, lhsT, rhs, start, stop)
        return self.op('pe', fn, reads=reads, writes=writes, label=label)

    def tr(self, out, in_, iden, reads, writes, label=''):
        fn = (lambda out, in_, iden:
              lambda e: e.transpose(out, in_=in_, identity=iden))(out, in_, iden)
        return self.op('pe', fn, reads=reads, writes=writes, label=label)

    def act(self, out, in_, func, reads, writes, bias=0.0, scale=1.0, eng='act', label=''):
        fn = (lambda out, in_, func, bias, scale:
              lambda e: e.activation(out, in_, func, bias=bias, scale=scale))(
                  out, in_, func, bias, scale)
        return self.op(eng, fn, reads=reads, writes=writes, label=label)

    def tt(self, out, a, b, opk, reads, writes, eng='dve', label=''):
        fn = (lambda out, a, b, opk:
              lambda e: e.tensor_tensor(out, a, b, op=opk))(out, a, b, opk)
        return self.op(eng, fn, reads=reads, writes=writes, label=label)

    def ts(self, out, a, s1, s2, op0, reads, writes, op1=None, accum=None, eng='dve', label=''):
        def mk(out, a, s1, s2, op0, op1, accum):
            def f(e):
                kw = {}
                if op1 is not None:
                    kw['op1'] = op1
                if accum is not None:
                    kw['accum_out'] = accum
                return e.tensor_scalar(out, a, s1, s2, op0=op0, **kw)
            return f
        return self.op(eng, mk(out, a, s1, s2, op0, op1, accum),
                       reads=reads, writes=writes, label=label)

    def stt(self, out, in0, scalar, in1, op0, op1, reads, writes, eng='dve', label=''):
        fn = (lambda out, in0, scalar, in1, op0, op1:
              lambda e: e.scalar_tensor_tensor(out, in0, scalar, in1, op0=op0, op1=op1))(
                  out, in0, scalar, in1, op0, op1)
        return self.op(eng, fn, reads=reads, writes=writes, label=label)

    def red(self, out, in_, opk, reads, writes, negate=False, eng='dve', label=''):
        fn = (lambda out, in_, opk, negate:
              lambda e: e.tensor_reduce(out, in_, AX, opk, negate=negate))(
                  out, in_, opk, negate)
        return self.op(eng, fn, reads=reads, writes=writes, label=label)

    def copy(self, out, in_, reads, writes, eng='dve', label=''):
        if eng == 'act':
            fn = (lambda out, in_: lambda e: e.copy(out, in_))(out, in_)
        else:
            fn = (lambda out, in_: lambda e: e.tensor_copy(out, in_))(out, in_)
        return self.op(eng, fn, reads=reads, writes=writes, label=label)

    def memset(self, ap, val, writes, eng='dve', label=''):
        fn = (lambda ap, val: lambda e: e.memset(ap, val))(ap, val)
        return self.op(eng, fn, reads=(), writes=writes, label=label)

    def recip(self, out, in_, reads, writes, label=''):
        fn = (lambda out, in_: lambda e: e.reciprocal(out, in_))(out, in_)
        return self.op('dve', fn, reads=reads, writes=writes, label=label)

    def cpred(self, out, mask, data, reads, writes, label=''):
        fn = (lambda out, mask, data:
              lambda e: e.copy_predicated(out, mask, data))(out, mask, data)
        return self.op('dve', fn, reads=reads, writes=writes, label=label)

    def barrier(self):
        deps = {}
        for w in self.writer.values():
            deps[w.idx] = w
        for rl in self.readers.values():
            for r in rl:
                deps[r.idx] = r
        dep_list = list(deps.values())
        for d in dep_list:
            if d.sem is None:
                d.sem = self.ENGSEM[d.stream]
        newbar = []
        for s in self.STREAMS:
            o = Op(s, lambda e: e.nop(), None, 1, len(self.ops), 'barrier')
            o.phase = self.phase
            o.deps = [d for d in dep_list if not (d.stream == s and d.inc != 16)] \
                + [b for b in self.barrier_ops]
            self.ops.append(o)
            newbar.append(o)
        self.writer = {}
        self.readers = {}
        self.barrier_ops = newbar
        for b in newbar:
            for d in b.deps:
                if d.sem is None:
                    d.sem = self.ENGSEM[d.stream]

    def flush(self, block):
        nc = self.nc
        semval = {}
        opval = {}
        for o in self.ops:
            if o.sem is not None:
                semval[o.sem] = semval.get(o.sem, 0) + o.inc
                opval[o.idx] = (o.sem, semval[o.sem])
        sems = {name: nc.alloc_semaphore(name) for name in semval}
        finals = dict(semval)
        by_stream = {s: [] for s in self.STREAMS}
        for o in self.ops:
            by_stream[o.stream].append(o)
        engmap = {'pe': block.tensor, 'act': block.scalar, 'dve': block.vector,
                  'pool': block.gpsimd, 'sp': block.sync}
        nceng = {'pe': nc.tensor, 'act': nc.scalar, 'dve': nc.vector,
                 'pool': nc.gpsimd, 'sp': nc.sync}
        for s in self.STREAMS:
            guar = {}

            def emit_stream(eng, s=s, guar=guar):
                cur = [None]
                curscope = [None]
                for o in by_stream[s]:
                    if o.phase != cur[0]:
                        if curscope[0] is not None:
                            curscope[0].__exit__(None, None, None)
                        cm = nc.named_scope(o.phase)
                        cm.__enter__()
                        curscope[0] = cm
                        cur[0] = o.phase
                    for d in o.deps:
                        if d.idx not in opval:
                            raise RuntimeError(f'dep without sem: {d.label}')
                        nm, v = opval[d.idx]
                        if guar.get(nm, 0) < v:
                            eng.wait_ge(sems[nm], v)
                            guar[nm] = v
                    inst = o.fn(eng)
                    if o.sem is not None:
                        if inst is None:
                            raise RuntimeError(f'op {o.label} returned no inst')
                        inst.then_inc(sems[o.sem], o.inc)
                if curscope[0] is not None:
                    curscope[0].__exit__(None, None, None)

            engmap[s](emit_stream)
        def final_wait(eng):
            for nm, v in finals.items():
                eng.wait_ge(sems[nm], v)
        block.sync(final_wait)
        return finals


CS = float(1.0 / np.sqrt(4.0 * np.sqrt(2.0)))
LIN = float(0.5 / CS)
SQB = float(np.sqrt(0.5))

def pair_list():
    pairs = [(a, b) for a in range(16) for b in range(16)]
    return pairs

def pack_consts():
    c = {}
    iden = np.eye(128, dtype=np.float32)
    c['iden_bf'] = iden.astype(np.float32)
    c['iden_f32'] = iden.copy()
    c['ones_col'] = np.ones((128, 1), np.float32)
    c['ones_row'] = np.ones((1, 128), np.float32)
    tri = (np.arange(128)[None, :] >= np.arange(128)[:, None]).astype(np.float32)
    c['mtri'] = tri
    pairs = pair_list()
    selA = np.zeros((64, 273), np.float32)
    selB = np.zeros((64, 273), np.float32)
    def setsel(base):
        selA[base + 16, 0] = 1.0; selB[base + 16, 0] = 1.0
        for a in range(16):
            selA[base + a, 1 + a] = 1.0; selB[base + 16, 1 + a] = 1.0
        for m in range(111):
            a, b = pairs[m]
            selA[base + a, 17 + m] = 1.0; selB[base + b, 17 + m] = 1.0
        for m in range(128):
            a, b = pairs[111 + m]
            selA[base + a, 128 + m] = 1.0; selB[base + b, 128 + m] = 1.0
        for m in range(17):
            a, b = pairs[239 + m]
            selA[base + a, 256 + m] = 1.0; selB[base + b, 256 + m] = 1.0
    setsel(0); setsel(32)
    c['selA'] = selA; c['selB'] = selB
    R = np.zeros((8, 128), np.float32)
    for e in range(8):
        R[e, e * 16:(e + 1) * 16] = 1.0
    c['rmat'] = R
    c['rmatT'] = np.ascontiguousarray(R.T)
    idx = (np.arange(128)[:, None] % 16) * 2048 + np.arange(2048)[None, :]
    c['idxmat'] = idx.astype(np.float32)
    c['irev'] = np.broadcast_to((7 - np.arange(8))[None, :].astype(np.float32),
                                (128, 8)).copy()
    c['ones8k'] = np.ones(8192, np.float32)
    return c

def chunk_lhsT(w, kchunk=128):
    K, M = w.shape
    nk = K // kchunk
    return np.ascontiguousarray(w.reshape(nk, kchunk, M).transpose(1, 0, 2))

def pack_weights(d):
    w = {}
    def packqk(W):
        Wt = W.T * CS
        out = np.zeros((1024, 8, 64), np.float32)
        for p in range(8):
            out[:, p, 0:16] = Wt[:, (2 * p) * 16:(2 * p) * 16 + 16]
            out[:, p, 32:48] = Wt[:, (2 * p + 1) * 16:(2 * p + 1) * 16 + 16]
        return np.ascontiguousarray(
            out.reshape(8, 128, 8, 64).transpose(1, 0, 2, 3))
    w['l0_wq'] = packqk(d['L0_attn_Wq'])
    w['l0_wk'] = packqk(d['L0_attn_Wk'])
    w['l0_wke'] = packqk(d['L0_attn_Wk_enc'])
    w['l0_wkT'] = chunk_lhsT(d['L0_attn_Wk'].T * CS)
    w['l0_wkeT'] = chunk_lhsT(d['L0_attn_Wk_enc'].T * CS)
    w['l0_wvT'] = chunk_lhsT(d['L0_attn_Wv'].T)
    w['l0_wveT'] = chunk_lhsT(d['L0_attn_Wv_enc'].T)
    w['l0_wout'] = chunk_lhsT(d['L0_attn_Wout'].T)
    w['routerT9'] = np.ascontiguousarray(np.concatenate(
        [chunk_lhsT(d['L0_ffn_router_w'].T),
         np.ones((128, 8, 1), np.float32)], axis=2))
    w['w1'] = np.ascontiguousarray(
        d['L0_ffn_W1'].reshape(8, 8, 128, 4096).transpose(0, 2, 1, 3))
    w['w2'] = np.ascontiguousarray(
        d['L0_ffn_W2'].reshape(8, 32, 128, 1024).transpose(0, 2, 1, 3))
    w['b1'] = np.ascontiguousarray(
        d['L0_ffn_b1'].reshape(8, 32, 128).transpose(0, 2, 1))
    w['b2'] = np.ascontiguousarray(
        d['L0_ffn_b2'].reshape(8, 8, 128).transpose(0, 2, 1))
    def packqk1(W, b):
        out = np.zeros((1024, 8, 64), np.float32)
        ob = np.zeros((32, 16), np.float32)
        for p in range(8):
            out[:, p, 0:32] = W[:, (2 * p) * 32:(2 * p) * 32 + 32]
            out[:, p, 32:64] = W[:, (2 * p + 1) * 32:(2 * p + 1) * 32 + 32]
        for h in range(16):
            ob[:, h] = b[h * 32:(h + 1) * 32]
        return (np.ascontiguousarray(out.reshape(8, 128, 8, 64).transpose(1, 0, 2, 3)),
                ob)
    w['l1_wq'], w['l1_bq'] = packqk1(d['L1_attn_wq'], d['L1_attn_bq'])
    w['l1_wk'], w['l1_bk'] = packqk1(d['L1_attn_wk'], d['L1_attn_bk'])
    w['l1_wkT'] = chunk_lhsT(d['L1_attn_wk'])
    w['l1_wvT'] = chunk_lhsT(d['L1_attn_wv'])
    w['l1_bv'] = d['L1_attn_bv'].reshape(1, 1024).copy()
    w['l1_bk_row'] = d['L1_attn_bk'].reshape(1, 512).copy()
    w['l1_wo'] = chunk_lhsT(d['L1_attn_wo'])
    w['l1_bo'] = np.ascontiguousarray(d['L1_attn_bo'].reshape(8, 128).T)
    w['wgT'] = chunk_lhsT(d['L1_ffn_w_gate'].T)
    w['wuT'] = chunk_lhsT(d['L1_ffn_w_up'].T)
    w['wdT'] = chunk_lhsT(d['L1_ffn_w_down'].T)
    return w


def declare_hbm(nc, v2=False):
    H = {}
    def dt_(name, shape, dtype, kind):
        H[name] = nc.dram_tensor(name, shape, dtype, kind=kind).ap()
    return H


def setup_consts(nc, tk):
    E = {}
    ci = {}
    for name, shape, dtype in [
        ('c_iden', [128, 128], F32), ('c_ones_col', [128, 1], F32),
        ('c_ones_row', [1, 128], F32), ('c_mtri', [128, 128], F32),
        ('c_selA', [64, 273], F32), ('c_selB', [64, 273], F32),
        ('c_rmat', [8, 128], F32), ('c_rmatT', [128, 8], F32),
        ('c_idxmat', [128, 2048], F32), ('c_irev', [128, 8], F32),
        ('c_ones8k', [8192], F32),
    ]:
        ci[name] = nc.dram_tensor(name, shape, dtype, kind="ExternalInput").ap()
    E['iden_f32'] = nc.alloc_sbuf_tensor('iden_f32', [128, 128], F32).ap()
    E['iden_bf'] = nc.alloc_sbuf_tensor('iden_bf', [128, 128], BF16).ap()
    E['ones_col'] = nc.alloc_sbuf_tensor('ones_col', [128, 1], F32).ap()
    E['ones_row'] = nc.alloc_sbuf_tensor('ones_row', [1, 128], F32).ap()
    E['ones_row_bf'] = nc.alloc_sbuf_tensor('ones_row_bf', [1, 128], BF16).ap()
    E['mtri_bf'] = nc.alloc_sbuf_tensor('mtri_bf', [128, 128], BF16).ap()
    E['mtri_f32'] = nc.alloc_sbuf_tensor('mtri_f32', [128, 128], F32).ap()
    E['selA'] = nc.alloc_sbuf_tensor('selA', [64, 273], BF16).ap()
    E['selB'] = nc.alloc_sbuf_tensor('selB', [64, 273], BF16).ap()
    E['rmat'] = nc.alloc_sbuf_tensor('rmat', [8, 128], F32).ap()
    E['rmatT'] = nc.alloc_sbuf_tensor('rmatT', [128, 8], F32).ap()
    E['idxmat'] = nc.alloc_sbuf_tensor('idxmat', [128, 2048], F32).ap()
    E['irev'] = nc.alloc_sbuf_tensor('irev', [128, 8], F32).ap()
    E['cst'] = nc.alloc_sbuf_tensor('cst', [128, 8], F32).ap()
    E['ones8k_h'] = ci['c_ones8k']
    tk.dma('sp', E['iden_f32'][:], ci['c_iden'], writes=['iden_f32'])
    tk.dma('pool', E['iden_bf'][:], ci['c_iden'], writes=['iden_bf'])
    tk.dma('sp', E['ones_col'][:], ci['c_ones_col'], writes=['ones_col'])
    tk.dma('sp', E['ones_row'][:], ci['c_ones_row'], writes=['ones_row'])
    tk.dma('pool', E['ones_row_bf'][:], ci['c_ones_row'], writes=['ones_row_bf'])
    tk.dma('pool', E['mtri_bf'][:], ci['c_mtri'], writes=['mtri_bf'])
    tk.dma('sp', E['mtri_f32'][:], ci['c_mtri'], writes=['mtri_f32'])
    tk.dma('pool', E['selA'][:], ci['c_selA'], writes=['selA'])
    tk.dma('pool', E['selB'][:], ci['c_selB'], writes=['selB'])
    tk.dma('sp', E['rmat'][:], ci['c_rmat'], writes=['rmat'])
    tk.dma('sp', E['rmatT'][:], ci['c_rmatT'], writes=['rmatT'])
    tk.dma('sp', E['idxmat'][:], ci['c_idxmat'], writes=['idxmat'])
    tk.dma('sp', E['irev'][:], ci['c_irev'], writes=['irev'])
    tk.op('dve', lambda e: e.memset(E['cst'][:, 0:1], 1e-5), writes=['cst'])
    tk.op('dve', lambda e: e.memset(E['cst'][:, 1:2], 0.70710678), writes=['cst'])
    tk.op('dve', lambda e: e.memset(E['cst'][:, 2:8], 0.0), writes=['cst'])
    return E


def alloc_psum_banks(nc, n=8):
    return [nc.alloc_psum_tensor(f'pb{i}', [128, 512], F32).ap() for i in range(n)]



def _mk_alloc(nc, scope, prefix):
    def A(n, sh, dt):
        if scope is None:
            return nc.alloc_sbuf_tensor(prefix + n, sh, dt).ap()
        h = scope.enter_context(nc.sbuf_tensor(prefix + n, sh, dt))
        return h.ap()
    return A

def emit_ln(tk, E, nc, xsrc, xn_h, pe_h=None, routerT=None, tag='ln0', scope=None):
    pb = E['pb']
    iden = E['iden_f32']; ones_col = E['ones_col']; ones_row = E['ones_row']
    cst = E['cst']; irev = E['irev']
    g = tag + '_'
    A = _mk_alloc(nc, scope, g)
    xw = [A(f'xw{i}', [128, NDC * WIN], F32) for i in range(2)]
    xnw = [A(f'xnw{i}', [128, NDC * WIN], BF16) for i in range(2)]
    x2b = [A(f'x2b{i}', [128, 128], F32) for i in range(2)]
    stats = A('stats', [128, 8], F32)
    rows3 = A('rows3', [1, 384], F32)
    xnt = [A(f'xnt{i}', [128, 128], F32) for i in range(2)]
    rt = pe_h is not None
    if rt:
        routerT_sb = A('routerT_sb', [128, 64], F32)
        negSr = A('negSr', [1, 8], F32)
        srcol = A('srcol', [8, 1], F32)
        lgt = A('lgt', [128, 8], F32)
        elg = A('elg', [128, 8], F32)
        eqt = A('eqt', [128, 8], F32)
        eqt2 = A('eqt2', [128, 8], F32)
        lgtx = A('lgtx', [128, 4], F32)
        pecols = A('pecols', [128, 2], F32)
        peacc = A('peacc', [2, 4096], F32)
        tk.dma('sp', routerT_sb[:], routerT.rearrange('p a b -> p (a b)'),
               writes=['routerT_sb'])
        for dc in range(NDC):
            tk.mm(pb[5][0:8, 0:1], routerT_sb[:, dc * 8:(dc + 1) * 8], ones_col[:],
                  dc == 0, dc == 7, reads=['routerT_sb', 'ones_col'], writes=['pb5'])
        tk.ts(srcol[:], pb[5][0:8, 0:1], -1.0, None, ALU.mult,
              reads=['pb5'], writes=['srcol'])
        tk.tr(pb[6][0:1, 0:8], srcol[:], iden[0:8, 0:8],
              reads=['srcol', 'iden_f32'], writes=['pb6'])
        tk.copy(negSr[:], pb[6][0:1, 0:8], reads=['pb6'], writes=['negSr'])

    for w in range(NW):
        p = w % 2
        for dc in range(NDC):
            tk.dma('sp', xw[p][:, dc * WIN:(dc + 1) * WIN],
                   xsrc[dc * 128:(dc + 1) * 128, w * WIN:(w + 1) * WIN],
                   writes=[f'xw{p}_{dc}'], label=f'ldx_w{w}_d{dc}')
        for tau in range(NTAU):
            i = w * NTAU + tau
            xt_ap = [xw[p][:, dc * WIN + tau * 128: dc * WIN + (tau + 1) * 128]
                     for dc in range(NDC)]
            for dc in range(NDC):
                xq = x2b[dc % 2]
                tk.act(xq[:], xt_ap[dc], AF.Square,
                       reads=[f'xw{p}_{dc}'], writes=[f'x2b{dc % 2}'])
                tk.mm(pb[0][:, 0:1], xt_ap[dc], ones_col[:], dc == 0, dc == 7,
                      reads=[f'xw{p}_{dc}', 'ones_col'], writes=['pb0'])
                tk.mm(pb[4][:, 0:1], xq[:], ones_col[:], dc == 0, dc == 7,
                      reads=[f'x2b{dc % 2}', 'ones_col'], writes=['pb1'])
            tk.ts(stats[:, 0:1], pb[0][:, 0:1], 1.0 / D, None, ALU.mult,
                  reads=['pb0'], writes=['stats'])
            tk.ts(stats[:, 1:2], pb[4][:, 0:1], 1.0 / D, None, ALU.mult,
                  reads=['pb1'], writes=['stats'])
            tk.tt(stats[:, 3:4], stats[:, 0:1], stats[:, 0:1], ALU.mult,
                  reads=['stats'], writes=['stats'])
            tk.tt(stats[:, 2:3], stats[:, 1:2], stats[:, 3:4], ALU.subtract,
                  reads=['stats'], writes=['stats'])
            tk.act(stats[:, 4:5], stats[:, 2:3], AF.Ln, bias=cst[:, 0:1],
                   reads=['stats', 'cst'], writes=['stats'])
            tk.act(stats[:, 5:6], stats[:, 4:5], AF.Exp, scale=-0.5,
                   reads=['stats'], writes=['stats'])
            tk.stt(stats[:, 6:7], stats[:, 0:1], -1.0, stats[:, 5:6],
                   ALU.mult, ALU.mult, reads=['stats'], writes=['stats'])
            tk.tr(pb[2][0:1, 0:128], stats[:, 5:6], iden[:],
                  reads=['stats', 'iden_f32'], writes=['pb2'])
            tk.tr(pb[2][0:1, 128:256], stats[:, 6:7], iden[:],
                  reads=['stats', 'iden_f32'], writes=['pb2'])
            tk.tr(pb[2][0:1, 256:384], stats[:, 0:1], iden[:],
                  reads=['stats', 'iden_f32'], writes=['pb2'])
            tk.copy(rows3[:], pb[2][0:1, 0:384], reads=['pb2'], writes=['rows3'])
            tk.mm(pb[3][:, 0:128], ones_row[:], rows3[0:1, 0:128], True, True,
                  reads=['ones_row', 'rows3'], writes=['pb3'])
            tk.mm(pb[3][:, 128:256], ones_row[:], rows3[0:1, 128:256], True, True,
                  reads=['ones_row', 'rows3'], writes=['pb3'])
            for dc in range(NDC):
                xt = xnt[dc % 2]
                tk.tt(xt[:], xt_ap[dc], pb[3][:, 0:128], ALU.mult,
                      reads=[f'xw{p}_{dc}', 'pb3'], writes=[f'xnt{dc % 2}'])
                tk.tt(xnw[p][:, dc * WIN + tau * 128: dc * WIN + (tau + 1) * 128],
                      xt[:], pb[3][:, 128:256], ALU.add,
                      reads=[f'xnt{dc % 2}', 'pb3'], writes=[f'xnw{p}_{dc}_{tau}'])
            if rt:
                for dc in range(NDC):
                    tk.mm(pb[4][:, 0:8], xt_ap[dc], routerT_sb[:, dc * 8:(dc + 1) * 8],
                          dc == 0, False, reads=[f'xw{p}_{dc}', 'routerT_sb'],
                          writes=['pb4'])
                tk.mm(pb[4][:, 0:8], rows3[0:1, 256:384], negSr[:], False, True,
                      reads=['rows3', 'negSr'], writes=['pb4'])
                tk.ts(lgt[:], pb[4][:, 0:8], stats[:, 5:6], None, ALU.mult,
                      reads=['pb4', 'stats'], writes=['lgt'])
                tk.red(lgtx[:, 0:1], lgt[:], ALU.max, negate=True,
                       reads=['lgt'], writes=['lgtx'])
                tk.act(elg[:], lgt[:], AF.Exp, bias=lgtx[:, 0:1],
                       reads=['lgt', 'lgtx'], writes=['elg'])
                tk.red(lgtx[:, 1:2], elg[:], ALU.add, reads=['elg'], writes=['lgtx'])
                tk.recip(pecols[:, 0:1], lgtx[:, 1:2], reads=['lgtx'], writes=['pecols'])
                tk.ts(eqt[:], lgt[:], lgtx[:, 0:1], 0.0, ALU.add, op1=ALU.is_ge,
                      reads=['lgt', 'lgtx'], writes=['eqt'])
                tk.tt(eqt2[:], eqt[:], irev[:], ALU.mult,
                      reads=['eqt', 'irev'], writes=['eqt2'])
                tk.red(lgtx[:, 2:3], eqt2[:], ALU.max, reads=['eqt2'], writes=['lgtx'])
                tk.ts(pecols[:, 1:2], lgtx[:, 2:3], -1.0, 7.0, ALU.mult, op1=ALU.add,
                      reads=['lgtx'], writes=['pecols'])
                tk.tr(pb[6][0:2, 128:256], pecols[:], iden[:],
                      reads=['pecols', 'iden_f32'], writes=['pb6'])
                tk.copy(peacc[:, i * 128:(i + 1) * 128], pb[6][0:2, 128:256],
                        reads=['pb6'], writes=[f'peacc_{i}'])
        for dc in range(NDC):
            tk.dma('sp', xn_h[dc * 128:(dc + 1) * 128, w * WIN:(w + 1) * WIN],
                   xnw[p][:, dc * WIN:(dc + 1) * WIN],
                   reads=[f'xnw{p}_{dc}_{t}' for t in range(NTAU)],
                   writes=[f'Hxn_{tag}_{w}_{dc}'],
                   label=f'stxn_w{w}_d{dc}')
        yield
    if rt:
        tk.dma('sp', pe_h, peacc[:],
               reads=[f'peacc_{i}' for i in range(NW * NTAU)])


CAP = 5120

def emit_routing(tk, E, nc, allpe, csel, cselE, kmask_h, scope=None):
    pb = E['pb']
    rmat, rmatT, idxmat = E['rmat'], E['rmatT'], E['idxmat']
    g = 'rt_'
    A = _mk_alloc(nc, scope, g)
    P16 = A('P16', [16, 2048], F32)
    E16 = A('E16', [16, 2048], F32)
    Mbig = A('M', [128, 2048], F32)
    eqm = A('eqm', [128, 2048], F32)
    big1 = A('big1', [128, 2048], F32)
    big2 = A('big2', [128, 2048], F32)
    Kbig = A('K', [128, 2048], F32)
    tmp16 = A('tmp16', [16, 2048], F32)
    tmp16b = A('tmp16b', [16, 2048], F32)
    part = A('part', [128, 1], F32)
    sc8 = A('sc8', [8, 12], F32)
    colb = A('colb', [128, 4], F32)
    cselb = A('csel', [128, 2], F32)
    cselEb = A('cselE', [128, 16], F32)
    k0t = A('k0t', [2, 2048], F32)
    f0 = A('f0', [2, 2048], F32)

    tk.dma('sp', cselb[:], csel, writes=['cselb'])
    tk.dma('sp', cselEb[:], cselE, writes=['cselEb'])
    for i in range(8):
        tk.dma('sp', P16[2 * i:2 * i + 1, :], allpe[2 * i:2 * i + 1, 0:2048],
               writes=['P16'])
        tk.dma('sp', P16[2 * i + 1:2 * i + 2, :], allpe[2 * i:2 * i + 1, 2048:4096],
               writes=['P16'])
        tk.dma('sp', E16[2 * i:2 * i + 1, :], allpe[2 * i + 1:2 * i + 2, 0:2048],
               writes=['E16'])
        tk.dma('sp', E16[2 * i + 1:2 * i + 2, :], allpe[2 * i + 1:2 * i + 2, 2048:4096],
               writes=['E16'])
    for e in range(8):
        tk.ts(tmp16[:], E16[:], float(e), None, ALU.is_equal,
              reads=['E16'], writes=['tmp16'])
        tk.stt(tmp16b[:], P16[:], 1.0, tmp16[:], ALU.add, ALU.mult,
               reads=['P16', 'tmp16'], writes=['tmp16b'])
        tk.ts(tmp16b[:], tmp16b[:], -1.0, None, ALU.add,
              reads=['tmp16b'], writes=['tmp16b'])
        tk.dma('sp', Mbig[e * 16:(e + 1) * 16, :], tmp16b[:],
               reads=['tmp16b'], writes=['Mbig'], label=f'mcp{e}')
    def collapse(dst, rkey):
        tk.mm(pb[5][0:8, 0:1], rmatT[:], part[:], True, True,
              reads=['rmatT', 'part'], writes=['pb5'])
        tk.copy(dst, pb[5][0:8, 0:1], reads=['pb5'], writes=['sc8'])
    def bcast8(src, dstcol):
        tk.mm(pb[6][:, 0:1], rmat[:], src, True, True,
              reads=['rmat', 'sc8'], writes=['pb6'])
        tk.copy(dstcol, pb[6][:, 0:1], reads=['pb6'], writes=['colb'])
    tk.ts(big1[:], Mbig[:], -0.5, None, ALU.is_ge, op1=ALU.add, accum=part[:],
          reads=['Mbig'], writes=['big1', 'part'])
    collapse(sc8[:, 6:7], 'tot')
    tk.memset(sc8[:, 0:1], 0.0, writes=['sc8'])
    tk.memset(sc8[:, 1:2], 1.5, writes=['sc8'])
    for it in range(40):
        tk.tt(sc8[:, 5:6], sc8[:, 0:1], sc8[:, 1:2], ALU.add,
              reads=['sc8'], writes=['sc8'])
        tk.ts(sc8[:, 2:3], sc8[:, 5:6], 0.5, None, ALU.mult,
              reads=['sc8'], writes=['sc8'])
        bcast8(sc8[:, 2:3], colb[:, 0:1])
        tk.ts(big1[:], Mbig[:], colb[:, 0:1], None, ALU.is_ge, op1=ALU.add,
              accum=part[:], reads=['Mbig', 'colb'], writes=['big1', 'part'])
        collapse(sc8[:, 3:4], 'cnt')
        tk.ts(sc8[:, 4:5], sc8[:, 3:4], -float(CAP), 0.0, ALU.add, op1=ALU.is_ge,
              reads=['sc8'], writes=['sc8'])
        tk.tt(sc8[:, 5:6], sc8[:, 2:3], sc8[:, 0:1], ALU.subtract,
              reads=['sc8'], writes=['sc8'])
        tk.tt(sc8[:, 5:6], sc8[:, 5:6], sc8[:, 4:5], ALU.mult,
              reads=['sc8'], writes=['sc8'])
        tk.tt(sc8[:, 0:1], sc8[:, 0:1], sc8[:, 5:6], ALU.add,
              reads=['sc8'], writes=['sc8'])
        tk.tt(sc8[:, 5:6], sc8[:, 1:2], sc8[:, 2:3], ALU.subtract,
              reads=['sc8'], writes=['sc8'])
        tk.tt(sc8[:, 5:6], sc8[:, 5:6], sc8[:, 4:5], ALU.mult,
              reads=['sc8'], writes=['sc8'])
        tk.tt(sc8[:, 1:2], sc8[:, 2:3], sc8[:, 5:6], ALU.add,
              reads=['sc8'], writes=['sc8'])
    bcast8(sc8[:, 0:1], colb[:, 0:1])
    tk.ts(big1[:], Mbig[:], colb[:, 0:1], None, ALU.is_gt, op1=ALU.add,
          accum=part[:], reads=['Mbig', 'colb'], writes=['big1', 'part'])
    collapse(sc8[:, 7:8], 'ngt')
    tk.ts(sc8[:, 8:9], sc8[:, 7:8], -1.0, float(CAP), ALU.mult, op1=ALU.add,
          reads=['sc8'], writes=['sc8'])
    tk.ts(eqm[:], Mbig[:], colb[:, 0:1], None, ALU.is_equal,
          reads=['Mbig', 'colb'], writes=['eqm'])
    tk.memset(sc8[:, 9:10], -1.0, writes=['sc8'])
    tk.memset(sc8[:, 10:11], 32768.0, writes=['sc8'])
    for it in range(18):
        tk.tt(sc8[:, 5:6], sc8[:, 9:10], sc8[:, 10:11], ALU.add,
              reads=['sc8'], writes=['sc8'])
        tk.ts(sc8[:, 2:3], sc8[:, 5:6], 0.5, None, ALU.mult,
              reads=['sc8'], writes=['sc8'])
        bcast8(sc8[:, 2:3], colb[:, 1:2])
        tk.ts(big1[:], idxmat[:], colb[:, 1:2], None, ALU.is_le,
              reads=['idxmat', 'colb'], writes=['big1'])
        tk.tt(big2[:], big1[:], eqm[:], ALU.mult,
              reads=['big1', 'eqm'], writes=['big2'])
        tk.ts(big2[:], big2[:], 1.0, None, ALU.mult, op1=ALU.add, accum=part[:],
              reads=['big2'], writes=['big2', 'part'])
        collapse(sc8[:, 3:4], 'icnt')
        tk.ts(sc8[:, 4:5], sc8[:, 3:4], sc8[:, 8:9], 0.0, ALU.subtract, op1=ALU.is_le,
              reads=['sc8'], writes=['sc8'])
        tk.tt(sc8[:, 5:6], sc8[:, 2:3], sc8[:, 9:10], ALU.subtract,
              reads=['sc8'], writes=['sc8'])
        tk.tt(sc8[:, 5:6], sc8[:, 5:6], sc8[:, 4:5], ALU.mult,
              reads=['sc8'], writes=['sc8'])
        tk.tt(sc8[:, 9:10], sc8[:, 9:10], sc8[:, 5:6], ALU.add,
              reads=['sc8'], writes=['sc8'])
        tk.tt(sc8[:, 5:6], sc8[:, 10:11], sc8[:, 2:3], ALU.subtract,
              reads=['sc8'], writes=['sc8'])
        tk.tt(sc8[:, 5:6], sc8[:, 5:6], sc8[:, 4:5], ALU.mult,
              reads=['sc8'], writes=['sc8'])
        tk.tt(sc8[:, 10:11], sc8[:, 2:3], sc8[:, 5:6], ALU.add,
              reads=['sc8'], writes=['sc8'])
    bcast8(sc8[:, 9:10], colb[:, 1:2])
    tk.ts(Kbig[:], Mbig[:], colb[:, 0:1], None, ALU.is_gt,
          reads=['Mbig', 'colb'], writes=['Kbig'])
    tk.ts(big1[:], idxmat[:], colb[:, 1:2], None, ALU.is_le,
          reads=['idxmat', 'colb'], writes=['big1'])
    tk.tt(big1[:], big1[:], eqm[:], ALU.mult,
          reads=['big1', 'eqm'], writes=['big1'])
    tk.tt(Kbig[:], Kbig[:], big1[:], ALU.add,
          reads=['Kbig', 'big1'], writes=['Kbig'])
    for q in range(4):
        tk.mm(pb[7][0:2, 0:512], cselb[:], Kbig[:, q * 512:(q + 1) * 512],
              True, True, reads=['cselb', 'Kbig'], writes=['pb7'])
        tk.ts(f0[:, q * 512:(q + 1) * 512], pb[7][0:2, 0:512], -1.0, 1.0,
              ALU.mult, op1=ALU.add, reads=['pb7'], writes=['f0'])
    for e in range(8):
        for q in range(4):
            tk.mm(pb[7][0:2, 0:512], cselEb[:, 2 * e:2 * e + 2],
                  Kbig[:, q * 512:(q + 1) * 512], True, True,
                  reads=['cselEb', 'Kbig'], writes=['pb7'])
            tk.copy(k0t[:, q * 512:(q + 1) * 512], pb[7][0:2, 0:512],
                    reads=['pb7'], writes=['k0t'], eng='act')
        if e == 0:
            tk.tt(k0t[:], k0t[:], f0[:], ALU.add, reads=['k0t', 'f0'],
                  writes=['k0t'])
        tk.dma('sp', kmask_h[e:e + 1, :], k0t[:], reads=['k0t'], label=f'stk{e}')


SQRT2 = float(np.sqrt(2.0))

def emit_attn0(tk, E, nc, xn_h, s_h, W, scope=None):
    pb = E['pb']
    iden_bf = E['iden_bf']; cst = E['cst']; mtri = E['mtri_bf']
    selA, selB = E['selA'], E['selB']
    g = 'a0_'
    A = _mk_alloc(nc, scope, g)
    wq = A('wq', [128, 4096], BF16); wk = A('wk', [128, 4096], BF16)
    wkT = A('wkT', [128, 2048], BF16)
    wvT = A('wvT', [128, 8192], BF16)
    wout = A('wout', [128, 8192], BF16)

    def loadw(sb, nm, key):
        tk.dma('pool', sb[:], W[nm].rearrange('p a b -> p (a b)') if W[nm].ndim == 3
               else W[nm].rearrange('p a b c -> p (a b c)'), writes=[key])
    loadw(wkT, 'wkeT', 'wkT')
    loadw(wvT, 'wveT', 'wvT')
    loadw(wq, 'wq', 'wq')
    loadw(wout, 'wout', 'wout')
    SAf = A('SAf', [128, 1040], F32); SBf = A('SBf', [128, 1040], F32)
    SCf = A('SCf', [17, 1040], F32)
    SAb = A('SAb', [128, 1040], BF16); SBb = A('SBb', [128, 1040], BF16)
    SCb = A('SCb', [17, 1040], BF16)
    for tname, t in [('SAf', SAf), ('SBf', SBf), ('SCf', SCf)]:
        tk.memset(t[:], 0.0, writes=[f'{tname}_{h}' for h in range(16)])
    xnw = [A(f'xnw{i}', [128, NDC * WIN], BF16) for i in range(2)]
    qsw = [A(f'qsw{i}', [64, 8 * WIN], BF16) for i in range(2)]
    ksw = [A(f'ksw{i}', [64, 8 * WIN], BF16) for i in range(2)]
    kstm = [A(f'kstm{i}', [128, 4 * 256], BF16) for i in range(2)]
    v1sb = [A(f'v1sb{i}', [128, 4 * 1040], BF16) for i in range(2)]
    kftm0 = A('kftm', [128, 4368], BF16)
    kftm = [kftm0, kftm0]
    qfA = [A(f'qfA{i}', [128, 128], BF16) for i in range(2)]
    qfB = [A(f'qfB{i}', [128, 128], BF16) for i in range(2)]
    qfC = [A(f'qfC{i}', [17, 128], BF16) for i in range(2)]
    p1sb = [A(f'p1sb{i}', [128, 128], BF16) for i in range(2)]
    asq = [A(f'asq{i}', [128, 128], F32) for i in range(2)]
    atb = [A(f'atb{i}', [128, 128], BF16) for i in range(2)]
    zc = A('zc', [128, 2], F32)
    otm = [A(f'otm{i}', [128, 1024], BF16) for i in range(2)]
    off = A('off', [128, 8 * WIN], BF16)
    sfw0 = A('sfw', [128, NDC * WIN], BF16)
    sfw = [sfw0, sfw0]
    ones_h = E['ones8k_h']
    for i in range(2):
        for r in (16, 48):
            tk.dma('pool', qsw[i][r:r + 1, :], ones_h[0:8 * WIN],
                   writes=[f'qsw{i}_ones'])
        tk.memset(v1sb[i].rearrange('p (t h c) -> p t h c', t=4, h=16)[:, :, :, 64:65],
                  1.0, writes=[f'v1sb{i}_ones'], eng='pool')
    tk.memset(kftm0.rearrange('p (h c) -> p h c', h=16)[:, :, 0:1],
              1.0, writes=['kftm_ones'], eng='pool')

    def proj_qk(wsb, wkey, dst, dkey, xbuf, xkeys):
        for p8 in range(8):
            for dc in range(NDC):
                tk.mm(pb[0][0:64, 0:512],
                      wsb[:, (dc * 8 + p8) * 64:(dc * 8 + p8) * 64 + 64],
                      xbuf[:, dc * WIN:(dc + 1) * WIN], dc == 0, dc == 7,
                      reads=[wkey, xkeys[dc]], writes=['pb0'])
            for rb in (0, 32):
                tk.copy(dst[rb:rb + 16, p8 * WIN:(p8 + 1) * WIN],
                        pb[0][rb:rb + 16, 0:512], reads=['pb0'],
                        writes=[f'{dkey}_{p8}'], eng='act')

    def proj_ktm2(wsb, wkey, dstbuf, dkey, xbuf, xkeys):
        for tau in range(NTAU):
            for dc in range(NDC):
                tk.mm(pb[0][:, 0:256],
                      xbuf[:, dc * WIN + tau * 128:dc * WIN + (tau + 1) * 128],
                      wsb[:, dc * 256:(dc + 1) * 256],
                      dc == 0, dc == 7, reads=[wkey, xkeys[dc]], writes=['pb0'])
            tk.copy(dstbuf[:, tau * 256:(tau + 1) * 256], pb[0][:, 0:256],
                    reads=['pb0'], writes=[f'{dkey}_{tau}'], eng='act')

    def proj_vtm(wsb, wkey, dstbuf, dkey, xbuf, xkeys):
        for tau in range(NTAU):
            for half in range(2):
                bank = pb[2] if half == 0 else pb[3]
                bk = 'pb2' if half == 0 else 'pb3'
                for dc in range(NDC):
                    tk.mm(bank[:, 0:512],
                          xbuf[:, dc * WIN + tau * 128:dc * WIN + (tau + 1) * 128],
                          wsb[:, dc * 1024 + half * 512:dc * 1024 + half * 512 + 512],
                          dc == 0, dc == 7, reads=[wkey, xkeys[dc]], writes=[bk])
                dst = dstbuf.rearrange('p (t h c) -> p t h c', t=4, h=16)[
                    :, tau, half * 8:(half + 1) * 8, 0:64]
                tk.copy(dst, bank[:, 0:512].rearrange('p (h c) -> p h c', h=8),
                        reads=[bk], writes=[f'{dkey}_{tau}'], eng='act')

    def build_kftm(buf, bkey, ksbuf, kskey, tau):
        kv = ksbuf[:, tau * 256:(tau + 1) * 256].rearrange('p (h f) -> p h f', h=16)
        ov = buf.rearrange('p (h c) -> p h c', h=16)
        tk.ts(ov[:, :, 1:17], kv, SQRT2, None, ALU.mult,
              reads=[kskey], writes=[bkey], eng='pool')
        for a in range(16):
            in1 = kv[:, :, a:a + 1].to_broadcast([128, 16, 16])
            tk.tt(ov[:, :, 17 + a * 16:17 + (a + 1) * 16], kv, in1, ALU.mult,
                  reads=[kskey], writes=[bkey], eng='pool')

    def state_update(kbuf, kkey, vbuf, vkey, tau, tag):
        for h in range(16):
            kc = kbuf[:, h * 273:(h + 1) * 273]
            v1 = vbuf.rearrange('p (t h c) -> p t h c', t=4, h=16)[
                :, tau, h, :].rearrange('p c -> p c')
            tk.mm(pb[5][:, 65:130], kc[:, 0:128], v1, True, True,
                  reads=[kkey, kkey + '_ones', vkey, vkey.rsplit('_', 1)[0] + '_ones'],
                  writes=['pb6'])
            tk.tt(SAf[:, h * 65:(h + 1) * 65], SAf[:, h * 65:(h + 1) * 65],
                  pb[5][:, 65:130], ALU.add, reads=['pb6', f'SAf_{h}'],
                  writes=[f'SAf_{h}'])
            tk.copy(SAb[:, h * 65:(h + 1) * 65], SAf[:, h * 65:(h + 1) * 65],
                    reads=[f'SAf_{h}'], writes=[f'SAb_{h}'], eng='pool')
            tk.mm(pb[5][:, 130:195], kc[:, 128:256], v1, True, True,
                  reads=[kkey, vkey], writes=['pb6'])
            tk.tt(SBf[:, h * 65:(h + 1) * 65], SBf[:, h * 65:(h + 1) * 65],
                  pb[5][:, 130:195], ALU.add, reads=['pb6', f'SBf_{h}'],
                  writes=[f'SBf_{h}'])
            tk.copy(SBb[:, h * 65:(h + 1) * 65], SBf[:, h * 65:(h + 1) * 65],
                    reads=[f'SBf_{h}'], writes=[f'SBb_{h}'], eng='pool')
            tk.mm(pb[5][0:17, 195:260], kc[:, 256:273], v1, True, True,
                  reads=[kkey, vkey], writes=['pb6'])
            tk.tt(SCf[:, h * 65:(h + 1) * 65], SCf[:, h * 65:(h + 1) * 65],
                  pb[5][0:17, 195:260], ALU.add, reads=['pb6', f'SCf_{h}'],
                  writes=[f'SCf_{h}'])
            tk.copy(SCb[:, h * 65:(h + 1) * 65], SCf[:, h * 65:(h + 1) * 65],
                    reads=[f'SCf_{h}'], writes=[f'SCb_{h}'], eng='pool')

    def load_xn(p, w):
        for dc in range(NDC):
            tk.dma('sp', xnw[p][:, dc * WIN:(dc + 1) * WIN],
                   xn_h[dc * 128:(dc + 1) * 128, w * WIN:(w + 1) * WIN],
                   reads=[f'Hxn_ln0_{w}_{dc}'],
                   writes=[f'xnw{p}_{dc}'], label=f'ldxn_w{w}_d{dc}')

    load_xn(0, 0)
    xkeys0 = [f'xnw0_{dc}' for dc in range(NDC)]
    proj_ktm2(wkT, 'wkT', kstm[1], 'kstm1', xnw[0], xkeys0)
    yield
    proj_vtm(wvT, 'wvT', v1sb[1], 'v1sb1', xnw[0], xkeys0)
    yield
    for tau in range(NTAU):
        build_kftm(kftm[1], 'kftm', kstm[1], f'kstm1_{tau}', tau)
        state_update(kftm[1], 'kftm', v1sb[1], f'v1sb1_{tau}', tau, 'enc')
        yield

    loadw(wk, 'wk', 'wk')
    loadw(wkT, 'wkT', 'wkT')
    loadw(wvT, 'wvT', 'wvT')
    for w in range(NW):
        p = w % 2
        if w > 0:
            load_xn(p, w)
        xkeys = [f'xnw{p}_{dc}' for dc in range(NDC)]
        proj_qk(wq, 'wq', qsw[p], f'qsw{p}', xnw[p], xkeys)
        yield
        proj_qk(wk, 'wk', ksw[p], f'ksw{p}', xnw[p], xkeys)
        yield
        proj_ktm2(wkT, 'wkT', kstm[p], f'kstm{p}', xnw[p], xkeys)
        proj_vtm(wvT, 'wvT', v1sb[p], f'v1sb{p}', xnw[p], xkeys)
        yield
        for tau in range(NTAU):
            build_kftm(kftm[p], 'kftm', kstm[p], f'kstm{p}_{tau}', tau)
            for h in range(16):
                p8, rb = h // 2, 32 * (h % 2)
                hp = h % 2
                qsl = qsw[p][rb:rb + 16, p8 * WIN + tau * 128:p8 * WIN + (tau + 1) * 128]
                ksl = ksw[p][rb:rb + 16, p8 * WIN + tau * 128:p8 * WIN + (tau + 1) * 128]
                q17 = qsw[p][rb:rb + 17, p8 * WIN + tau * 128:p8 * WIN + (tau + 1) * 128]
                tk.mm(pb[4][:, 0:128], ksl, qsl, True, True,
                      reads=[f'ksw{p}_{p8}', f'qsw{p}_{p8}'], writes=['pb4'])
                tk.act(asq[hp][:], pb[4][:, 0:128], AF.Square, bias=cst[:, 1:2],
                       reads=['pb4', 'cst'], writes=[f'asq{hp}'])
                tk.stt(atb[hp][:], asq[hp][:], 0.5, mtri[:], ALU.add, ALU.mult,
                       reads=[f'asq{hp}', 'mtri_bf'], writes=[f'atb{hp}'])
                for (cn, qf, c0, c1, m) in [('A', qfA[hp], 0, 128, 128),
                                            ('B', qfB[hp], 128, 256, 128),
                                            ('C', qfC[hp], 256, 273, 17)]:
                    tk.mm(pb[2][0:m, 0:128], selA[rb:rb + 17, c0:c1], q17,
                          True, True, reads=['selA', f'qsw{p}_{p8}', f'qsw{p}_ones'],
                          writes=['pb2'])
                    tk.mm(pb[3][0:m, 0:128], selB[rb:rb + 17, c0:c1], q17,
                          True, True, reads=['selB', f'qsw{p}_{p8}', f'qsw{p}_ones'],
                          writes=['pb3'])
                    tk.copy(p1sb[hp][0:m, :], pb[2][0:m, 0:128],
                            reads=['pb2'], writes=[f'p1sb{hp}'], eng='act')
                    tk.tt(qf[0:m, :], pb[3][0:m, 0:128], p1sb[hp][0:m, :],
                          ALU.mult, reads=['pb3', f'p1sb{hp}'], writes=[f'qf{cn}{hp}'])
                v1 = v1sb[p].rearrange('p (t h c) -> p t h c', t=4, h=16)[:, tau, h, :]
                tk.mm(pb[5][:, 0:65], qfA[hp][:], SAb[:, h * 65:(h + 1) * 65],
                      True, False, reads=[f'qfA{hp}', f'SAb_{h}'], writes=['pb5'])
                tk.mm(pb[5][:, 0:65], qfB[hp][:], SBb[:, h * 65:(h + 1) * 65],
                      False, False, reads=[f'qfB{hp}', f'SBb_{h}'], writes=['pb5'])
                tk.mm(pb[5][:, 0:65], qfC[hp][:], SCb[:, h * 65:(h + 1) * 65],
                      False, False, reads=[f'qfC{hp}', f'SCb_{h}'], writes=['pb5'])
                tk.mm(pb[5][:, 0:65], atb[hp][:], v1, False, True,
                      reads=[f'atb{hp}', f'v1sb{p}_{tau}', f'v1sb{p}_ones'],
                      writes=['pb5'])
                tk.ts(zc[:, 0:1], pb[5][:, 64:65], 1e-12, None, ALU.add,
                      reads=['pb5'], writes=['zc'])
                tk.recip(zc[:, 1:2], zc[:, 0:1], reads=['zc'], writes=['zc'])
                tk.ts(otm[tau % 2][:, h * 64:(h + 1) * 64], pb[5][:, 0:64],
                      zc[:, 1:2], None, ALU.mult, reads=['pb5', 'zc'],
                      writes=[f'otm{tau % 2}_{h}'])
                kc = kftm[p][:, h * 273:(h + 1) * 273]
                tk.mm(pb[5][:, 65:130], kc[:, 0:128], v1, True, True,
                      reads=['kftm', 'kftm_ones', f'v1sb{p}_{tau}',
                             f'v1sb{p}_ones'], writes=['pb6'])
                tk.tt(SAf[:, h * 65:(h + 1) * 65], SAf[:, h * 65:(h + 1) * 65],
                      pb[5][:, 65:130], ALU.add, reads=['pb6', f'SAf_{h}'],
                      writes=[f'SAf_{h}'])
                tk.copy(SAb[:, h * 65:(h + 1) * 65], SAf[:, h * 65:(h + 1) * 65],
                        reads=[f'SAf_{h}'], writes=[f'SAb_{h}'], eng='pool')
                tk.mm(pb[5][:, 130:195], kc[:, 128:256], v1, True, True,
                      reads=['kftm', f'v1sb{p}_{tau}'], writes=['pb6'])
                tk.tt(SBf[:, h * 65:(h + 1) * 65], SBf[:, h * 65:(h + 1) * 65],
                      pb[5][:, 130:195], ALU.add, reads=['pb6', f'SBf_{h}'],
                      writes=[f'SBf_{h}'])
                tk.copy(SBb[:, h * 65:(h + 1) * 65], SBf[:, h * 65:(h + 1) * 65],
                        reads=[f'SBf_{h}'], writes=[f'SBb_{h}'], eng='pool')
                tk.mm(pb[5][0:17, 195:260], kc[:, 256:273], v1, True, True,
                      reads=['kftm', f'v1sb{p}_{tau}'], writes=['pb6'])
                tk.tt(SCf[:, h * 65:(h + 1) * 65], SCf[:, h * 65:(h + 1) * 65],
                      pb[5][0:17, 195:260], ALU.add, reads=['pb6', f'SCf_{h}'],
                      writes=[f'SCf_{h}'])
                tk.copy(SCb[:, h * 65:(h + 1) * 65], SCf[:, h * 65:(h + 1) * 65],
                        reads=[f'SCf_{h}'], writes=[f'SCb_{h}'], eng='pool')
            for hc in range(8):
                tk.tr(pb[7].bitcast(BF16)[:, 0:128],
                      otm[tau % 2][:, hc * 128:(hc + 1) * 128], iden_bf[:],
                      reads=[f'otm{tau % 2}_{hh}' for hh in range(2 * hc, 2 * hc + 2)]
                      + ['iden_bf'], writes=['pb7'])
                tk.copy(off[:, hc * WIN + tau * 128:hc * WIN + (tau + 1) * 128],
                        pb[7].bitcast(BF16)[:, 0:128], reads=['pb7'],
                        writes=[f'off_{hc}'], eng='act')
            yield
        for dm in range(NDC):
            for hc in range(NDC):
                tk.mm(pb[0][:, 0:512], wout[:, hc * 1024 + dm * 128:hc * 1024 + dm * 128 + 128],
                      off[:, hc * WIN:(hc + 1) * WIN], hc == 0, hc == 7,
                      reads=['wout', f'off_{hc}'], writes=['pb0'])
            tk.copy(sfw[p][:, dm * WIN:(dm + 1) * WIN], pb[0][:, 0:512],
                    reads=['pb0'], writes=[f'sfw_{dm}'], eng='act')
        for dm in range(NDC):
            tk.dma('sp', s_h[dm * 128:(dm + 1) * 128, w * WIN:(w + 1) * WIN],
                   sfw[p][:, dm * WIN:(dm + 1) * WIN], reads=[f'sfw_{dm}'],
                   label=f'sts_w{w}_d{dm}')
        yield


def emit_moe(tk, E, nc, xn_h, kmask_h, ge_h, W, scope=None):
    pb = E['pb']
    ones_row_bf = E['ones_row_bf']
    g = 'mo_'
    A = _mk_alloc(nc, scope, g)
    w1sb = A('w1sb', [128, 8 * 4096], BF16)
    w2sb = A('w2sb', [128, 32 * 1024], BF16)
    b1sb = A('b1sb', [128, 32], F32)
    b2sb = A('b2sb', [128, 8], F32)
    xnw = [A(f'xnw{i}', [128, NDC * WIN], BF16) for i in range(2)]
    hsb = A('hsb', [128, 32 * WIN], BF16)
    ysb = [A(f'ysb{i}', [128, WIN], F32) for i in range(2)]
    ymb = [A(f'ymb{i}', [128, WIN], BF16) for i in range(4)]
    mrow = A('mrow', [1, WIN], BF16)
    mbc = A('mbc', [128, WIN], BF16)
    HB = [0, 1, 6, 7]
    YB = [2, 3, 4, 5]

    def load_weights(e):
        for dc in range(NDC):
            tk.dma('pool', w1sb[:, dc * 4096:(dc + 1) * 4096],
                   W['w1'][e, :, dc, :], writes=[f'w1_{dc}'], label=f'ldw1_{e}_{dc}')
    def load_weights2(e):
        for q in range(8):
            tk.dma('pool', w2sb[:, q * 4096:(q + 1) * 4096],
                   W['w2'][e, :, 4 * q:4 * (q + 1), :].rearrange('p a b -> p (a b)'),
                   writes=[f'w2_{q}'], label=f'ldw2_{e}_{q}')
        tk.dma('sp', b1sb[:], W['b1'][e], writes=['b1sb'])
        tk.dma('sp', b2sb[:], W['b2'][e], writes=['b2sb'])

    load_weights(0)
    load_weights2(0)
    for e in range(8):
        for w in range(NW):
            p = w % 2
            for dc in range(NDC):
                tk.dma('sp', xnw[p][:, dc * WIN:(dc + 1) * WIN],
                       xn_h[dc * 128:(dc + 1) * 128, w * WIN:(w + 1) * WIN],
                       writes=[f'xnw{p}_{dc}'], label=f'mxn_{e}_{w}_{dc}')
            tk.dma('pool', mrow[:], kmask_h[e:e + 1, w * WIN:(w + 1) * WIN],
                   writes=['mrow'])
            tk.mm(pb[6][:, 0:512], ones_row_bf[:], mrow[:], True, True,
                  reads=['ones_row_bf', 'mrow'], writes=['pb6'])
            tk.copy(mbc[:], pb[6][:, 0:512], reads=['pb6'],
                    writes=['mbc'], eng='act')
            for fc in range(32):
                bi = HB[fc % 4]
                bank, bk = pb[bi], f'pb{bi}'
                for dc in range(NDC):
                    tk.mm(bank[:, 0:512],
                          w1sb[:, dc * 4096 + fc * 128:dc * 4096 + (fc + 1) * 128],
                          xnw[p][:, dc * WIN:(dc + 1) * WIN], dc == 0, dc == 7,
                          reads=[f'w1_{dc}', f'xnw{p}_{dc}'], writes=[bk])
                tk.act(hsb[:, fc * WIN:(fc + 1) * WIN], bank[:, 0:512], AF.Relu,
                       bias=b1sb[:, fc:fc + 1], reads=[bk, 'b1sb'],
                       writes=[f'h_{fc}'])
            if w == NW - 1 and e < 7:
                load_weights(e + 1)
            for dc in range(NDC):
                bi = YB[dc % 4]
                bank, bk = pb[bi], f'pb{bi}'
                for fc in range(32):
                    tk.mm(bank[:, 0:512],
                          w2sb[:, fc * 1024 + dc * 128:fc * 1024 + (dc + 1) * 128],
                          hsb[:, fc * WIN:(fc + 1) * WIN], fc == 0, fc == 31,
                          reads=[f'w2_{fc // 4}', f'h_{fc}'], writes=[bk])
                yp = dc % 2
                tk.ts(ysb[yp][:], bank[:, 0:512], b2sb[:, dc:dc + 1], None, ALU.add,
                      reads=[bk, 'b2sb'], writes=[f'ysb{yp}'])
                y4 = dc % 4
                tk.tt(ymb[y4][:], ysb[yp][:], mbc[:], ALU.mult,
                      reads=[f'ysb{yp}', 'mbc'], writes=[f'ymb{y4}'])
                tk.dma('sp', ge_h[e, dc * 128:(dc + 1) * 128, w * WIN:(w + 1) * WIN],
                       ymb[y4][:], reads=[f'ymb{y4}'], label=f'stg_{e}_{w}_{dc}')
        if e < 7:
            load_weights2(e + 1)


def emit_combine(tk, E, nc, xsrc, s_h, g_h, xdst, scope=None, tag='cb', ge=False):
    g = tag + '_'
    A = _mk_alloc(nc, scope, g)
    xw = [A(f'xw{i}', [128, NDC * WIN], F32) for i in range(2)]
    sw = [A(f'sw{i}', [128, NDC * WIN], BF16) for i in range(2)]
    NG = 8 if ge else 1
    gw = [A(f'gw{i}', [128, NG * WIN], BF16) for i in range(8)]
    gs = [A(f'gs{i}', [128, WIN], F32) for i in range(8)]
    sg = [A(f'sg{i}', [128, WIN], F32) for i in range(2)]
    gsl = [A(f'gsl{i}', [128, WIN], F32) for i in range(2)]
    xo = [A(f'xo{i}', [128, NDC * WIN], F32) for i in range(2)]
    for w in range(NW):
        p = w % 2
        for dc in range(NDC):
            sl = (slice(dc * 128, (dc + 1) * 128), slice(w * WIN, (w + 1) * WIN))
            tk.dma('act', xw[p][:, dc * WIN:(dc + 1) * WIN], xsrc[sl[0], sl[1]],
                   writes=[f'xw{p}_{dc}'])
            tk.dma('act', sw[p][:, dc * WIN:(dc + 1) * WIN], s_h[sl[0], sl[1]],
                   writes=[f'sw{p}_{dc}'])
            q4 = dc
            if ge:
                src = g_h[:, sl[0], sl[1]].transpose([1, 0, 2])
                tk.dma('sp', gw[q4].rearrange('p (e j) -> p e j', e=8), src,
                       writes=[f'gw{q4}_0'])
            else:
                tk.dma('sp', gw[q4][:, 0:WIN], g_h[sl[0], sl[1]],
                       writes=[f'gw{q4}_0'])
        for dc in range(NDC):
            q = dc % 2
            q4 = dc
            if ge:
                v = gw[q4].rearrange('p (e j) -> p j e', e=8)
                tk.red(gs[q4][:].unsqueeze(2), v, ALU.add,
                       reads=[f'gw{q4}_0'], writes=[f'gs{q4}'])
                gsrc, gk = gs[q4][:], f'gs{q4}'
            else:
                gsrc, gk = gw[q4][:, 0:WIN], f'gw{q4}_0'
            tk.act(gsl[q][:], gsrc, AF.Silu, reads=[gk], writes=[f'gsl{q}'])
            tk.tt(sg[q][:], sw[p][:, dc * WIN:(dc + 1) * WIN], gsl[q][:], ALU.mult,
                  reads=[f'sw{p}_{dc}', f'gsl{q}'], writes=[f'sg{q}'])
            tk.tt(xo[p][:, dc * WIN:(dc + 1) * WIN],
                  xw[p][:, dc * WIN:(dc + 1) * WIN], sg[q][:], ALU.add,
                  reads=[f'xw{p}_{dc}', f'sg{q}'], writes=[f'xo{p}_{dc}'])
            tk.dma('sp', xdst[dc * 128:(dc + 1) * 128, w * WIN:(w + 1) * WIN],
                   xo[p][:, dc * WIN:(dc + 1) * WIN], reads=[f'xo{p}_{dc}'],
                   label=f'stxo_{w}_{dc}')


def emit_attn1(tk, E, nc, xn_h, s_h, W, scope=None):
    pb = E['pb']
    iden_bf = E['iden_bf']; mtri = E['mtri_bf']
    g = 'a1_'
    A = _mk_alloc(nc, scope, g)
    wq = A('wq', [128, 4096], BF16); wk = A('wk', [128, 4096], BF16)
    wkT = A('wkT', [128, 4096], BF16)
    wvT = A('wvT', [128, 8192], BF16)
    wo = A('wo', [128, 8192], BF16)
    bq = A('bq', [32, 16], F32); bk = A('bk', [32, 16], F32)
    bo = A('bo', [128, 8], F32)
    bvr = A('bvr', [1, 1024], BF16); bkr = A('bkr', [1, 512], BF16)
    bvb = A('bvb', [128, 1024], BF16); bkb = A('bkb', [128, 512], BF16)
    for nm, sb in [('wq1', wq), ('wk1', wk), ('wkT1', wkT), ('wvT1', wvT),
                   ('wo1', wo)]:
        tk.dma('pool', sb[:], W[nm].rearrange('p a b -> p (a b)') if W[nm].ndim == 3
               else W[nm].rearrange('p a b c -> p (a b c)'), writes=[nm])
    tk.dma('sp', bq[:], W['bq1'], writes=['bq'])
    tk.dma('sp', bk[:], W['bk1'], writes=['bk'])
    tk.dma('sp', bo[:], W['bo1'], writes=['bo'])
    tk.dma('pool', bvr[:], W['bv1'], writes=['bvr'])
    tk.dma('pool', bkr[:], W['bk_row'], writes=['bkr'])
    for half in range(2):
        tk.mm(pb[6][:, 0:512], E['ones_row_bf'][:], bvr[0:1, half * 512:(half + 1) * 512],
              True, True, reads=['ones_row_bf', 'bvr'], writes=['pb6'])
        tk.copy(bvb[:, half * 512:(half + 1) * 512], pb[6][:, 0:512],
                reads=['pb6'], writes=['bvb'], eng='act')
    tk.mm(pb[6][:, 0:512], E['ones_row_bf'][:], bkr[0:1, :], True, True,
          reads=['ones_row_bf', 'bkr'], writes=['pb6'])
    tk.copy(bkb[:], pb[6][:, 0:512], reads=['pb6'], writes=['bkb'], eng='act')
    S1f = A('S1f', [32, 1040], F32); S1b = A('S1b', [32, 1040], BF16)
    tk.memset(S1f[:], 0.0, writes=[f'S1f_{h}' for h in range(16)])
    tk.memset(S1b[:], 0.0, writes=[f'S1b_{h}' for h in range(16)])
    xnw = [A(f'xnw{i}', [128, NDC * WIN], BF16) for i in range(2)]
    qsw = [A(f'qsw{i}', [32, 16 * WIN], BF16) for i in range(2)]
    ksw = [A(f'ksw{i}', [32, 16 * WIN], BF16) for i in range(2)]
    ktp0 = A('ktp0', [128, 4 * 512], BF16)
    ktp = [ktp0, ktp0]
    v1sb = [A(f'v1sb{i}', [128, 4 * 1040], BF16) for i in range(2)]
    tmf = [A(f'tmf{i}', [128, 512], F32) for i in range(3)]
    atb = [A(f'atb{i}', [128, 128], BF16) for i in range(2)]
    zc = A('zc', [128, 2], F32)
    otm = [A(f'otm{i}', [128, 1024], BF16) for i in range(2)]
    off = A('off', [128, 8 * WIN], BF16)
    sfw0 = A('sfw0', [128, NDC * WIN], BF16)
    sfw = [sfw0, sfw0]
    for i in range(2):
        tk.memset(v1sb[i].rearrange('p (t h c) -> p t h c', t=4, h=16)[:, :, :, 64:65],
                  1.0, writes=[f'v1sb{i}_ones'], eng='pool')

    def phi_fm(psum_ap, pkey, bcol, bkey, dst, dkey):
        tk.ts(tmf[0][0:32, :], psum_ap, bcol, 0.0, ALU.add, op1=ALU.min,
              reads=[pkey, bkey], writes=['tmf0'])
        tk.ts(tmf[1][0:32, :], psum_ap, bcol, 0.0, ALU.add, op1=ALU.max,
              reads=[pkey, bkey], writes=['tmf1'])
        tk.act(tmf[2][0:32, :], tmf[0][0:32, :], AF.Exp,
               reads=['tmf0'], writes=['tmf2'])
        tk.tt(dst, tmf[2][0:32, :], tmf[1][0:32, :], ALU.add,
              reads=['tmf2', 'tmf1'], writes=[dkey])

    def load_xn(p, w):
        for dc in range(NDC):
            tk.dma('sp', xnw[p][:, dc * WIN:(dc + 1) * WIN],
                   xn_h[dc * 128:(dc + 1) * 128, w * WIN:(w + 1) * WIN],
                   writes=[f'xnw{p}_{dc}'], label=f'l1xn_w{w}_d{dc}')

    for w in range(NW):
        p = w % 2
        load_xn(p, w)
        xkeys = [f'xnw{p}_{dc}' for dc in range(NDC)]
        for (wsb, wkey, bb, bbk, dst, dkey) in [(wq, 'wq1', bq, 'bq', qsw[p], f'qsw{p}'),
                                                (wk, 'wk1', bk, 'bk', ksw[p], f'ksw{p}')]:
            for h in range(16):
                woff = (h // 2) * 64 + (h % 2) * 32
                for dc in range(NDC):
                    tk.mm(pb[0][0:32, 0:512],
                          wsb[:, dc * 512 + woff:dc * 512 + woff + 32],
                          xnw[p][:, dc * WIN:(dc + 1) * WIN], dc == 0, dc == 7,
                          reads=[wkey, xkeys[dc]], writes=['pb0'])
                phi_fm(pb[0][0:32, 0:512], 'pb0', bb[:, h:h + 1], bbk,
                       dst[0:32, h * WIN:(h + 1) * WIN], f'{dkey}_{h}')
        for tau in range(NTAU):
            for dc in range(NDC):
                tk.mm(pb[1][:, 0:512],
                      xnw[p][:, dc * WIN + tau * 128:dc * WIN + (tau + 1) * 128],
                      wkT[:, dc * 512:(dc + 1) * 512], dc == 0, dc == 7,
                      reads=['wkT1', xkeys[dc]], writes=['pb1'])
            tk.tt(tmf[0][:], pb[1][:, 0:512], bkb[:], ALU.add,
                  reads=['pb1', 'bkb'], writes=['tmf0'])
            tk.ts(tmf[1][:], tmf[0][:], 0.0, None, ALU.max,
                  reads=['tmf0'], writes=['tmf1'])
            tk.ts(tmf[0][:], tmf[0][:], 0.0, None, ALU.min,
                  reads=['tmf0'], writes=['tmf0'])
            tk.act(tmf[2][:], tmf[0][:], AF.Exp, reads=['tmf0'], writes=['tmf2'])
            tk.tt(ktp[p][:, tau * 512:(tau + 1) * 512], tmf[2][:], tmf[1][:],
                  ALU.add, reads=['tmf2', 'tmf1'], writes=[f'ktp_{tau}'])
        for tau in range(NTAU):
            for half in range(2):
                bank, bk2 = (pb[2], 'pb2') if half == 0 else (pb[3], 'pb3')
                for dc in range(NDC):
                    tk.mm(bank[:, 0:512],
                          xnw[p][:, dc * WIN + tau * 128:dc * WIN + (tau + 1) * 128],
                          wvT[:, dc * 1024 + half * 512:dc * 1024 + half * 512 + 512],
                          dc == 0, dc == 7, reads=['wvT1', xkeys[dc]], writes=[bk2])
                dst = v1sb[p].rearrange('p (t h c) -> p t h c', t=4, h=16)[
                    :, tau, half * 8:(half + 1) * 8, 0:64]
                tk.tt(dst, bank[:, 0:512].rearrange('p (h c) -> p h c', h=8),
                      bvb[:, half * 512:(half + 1) * 512].rearrange('p (h c) -> p h c', h=8),
                      ALU.add, reads=[bk2, 'bvb'], writes=[f'v1sb{p}_{tau}'])
        for tau in range(NTAU):
            for h in range(16):
                hp = h % 2
                qsl = qsw[p][0:32, h * WIN + tau * 128:h * WIN + (tau + 1) * 128]
                ksl = ksw[p][0:32, h * WIN + tau * 128:h * WIN + (tau + 1) * 128]
                tk.mm(pb[4][:, 0:128], ksl, qsl, True, True,
                      reads=[f'ksw{p}_{h}', f'qsw{p}_{h}'], writes=['pb4'])
                tk.tt(atb[hp][:], pb[4][:, 0:128], mtri[:], ALU.mult,
                      reads=['pb4', 'mtri_bf'], writes=[f'atb{hp}'])
                v1 = v1sb[p].rearrange('p (t h c) -> p t h c', t=4, h=16)[:, tau, h, :]
                tk.mm(pb[5][:, 0:65], qsl, S1b[0:32, h * 65:(h + 1) * 65],
                      True, False, reads=[f'qsw{p}_{h}', f'S1b_{h}'], writes=['pb5'])
                tk.mm(pb[5][:, 0:65], atb[hp][:], v1, False, True,
                      reads=[f'atb{hp}', f'v1sb{p}_{tau}', f'v1sb{p}_ones'],
                      writes=['pb5'])
                tk.ts(zc[:, 0:1], pb[5][:, 64:65], 1e-6, None, ALU.add,
                      reads=['pb5'], writes=['zc'])
                tk.recip(zc[:, 1:2], zc[:, 0:1], reads=['zc'], writes=['zc'])
                tk.ts(otm[tau % 2][:, h * 64:(h + 1) * 64], pb[5][:, 0:64],
                      zc[:, 1:2], None, ALU.mult, reads=['pb5', 'zc'],
                      writes=[f'otm{tau % 2}_{h}'])
                tk.mm(pb[6][0:32, 0:65], ktp[p][:, tau * 512 + h * 32:tau * 512 + (h + 1) * 32],
                      v1, True, True,
                      reads=[f'ktp_{tau}', f'v1sb{p}_{tau}', f'v1sb{p}_ones'],
                      writes=['pb6'])
                tk.tt(S1f[:, h * 65:(h + 1) * 65], S1f[:, h * 65:(h + 1) * 65],
                      pb[6][0:32, 0:65], ALU.add, reads=['pb6', f'S1f_{h}'],
                      writes=[f'S1f_{h}'])
                tk.copy(S1b[:, h * 65:(h + 1) * 65], S1f[:, h * 65:(h + 1) * 65],
                        reads=[f'S1f_{h}'], writes=[f'S1b_{h}'], eng='pool')
            for hc in range(8):
                tk.tr(pb[7].bitcast(BF16)[:, 0:128],
                      otm[tau % 2][:, hc * 128:(hc + 1) * 128], iden_bf[:],
                      reads=[f'otm{tau % 2}_{hh}' for hh in range(2 * hc, 2 * hc + 2)]
                      + ['iden_bf'], writes=['pb7'])
                tk.copy(off[:, hc * WIN + tau * 128:hc * WIN + (tau + 1) * 128],
                        pb[7].bitcast(BF16)[:, 0:128], reads=['pb7'],
                        writes=[f'off_{hc}'], eng='act')
        for dm in range(NDC):
            for hc in range(NDC):
                tk.mm(pb[0][:, 0:512], wo[:, hc * 1024 + dm * 128:hc * 1024 + dm * 128 + 128],
                      off[:, hc * WIN:(hc + 1) * WIN], hc == 0, hc == 7,
                      reads=['wo1', f'off_{hc}'], writes=['pb0'])
            tk.ts(sfw[p][:, dm * WIN:(dm + 1) * WIN], pb[0][:, 0:512],
                  bo[:, dm:dm + 1], None, ALU.add, reads=['pb0', 'bo'],
                  writes=[f'sfw_{dm}'])
        for dm in range(NDC):
            tk.dma('sp', s_h[dm * 128:(dm + 1) * 128, w * WIN:(w + 1) * WIN],
                   sfw[p][:, dm * WIN:(dm + 1) * WIN], reads=[f'sfw_{dm}'],
                   label=f'sts1_w{w}_d{dm}')


def emit_ffn1(tk, E, nc, xn_h, g_h, W, scope=None):
    pb = E['pb']
    SC = 1.0 / 127.0
    g = 'f1_'
    A = _mk_alloc(nc, scope, g)
    wg = A('wg', [128, 8 * 3072], BF16)
    wu = A('wu', [128, 8 * 3072], BF16)
    wd = A('wd', [128, 24 * 1024], BF16)
    for nm, sb in [('wgT', wg), ('wuT', wu), ('wdT', wd)]:
        tk.dma('pool', sb[:], W[nm].rearrange('p a b -> p (a b)'), writes=[nm])
    xnw = [A(f'xnw{i}', [128, NDC * WIN], BF16) for i in range(2)]
    hsb = A('hsb', [128, 24 * WIN], BF16)
    gsl = [A(f'gsl{i}', [128, WIN], BF16) for i in range(2)]
    usl = [A(f'usl{i}', [128, WIN], BF16) for i in range(2)]
    gw = [A(f'gw{i}', [128, WIN], BF16) for i in range(2)]
    for w in range(NW):
        p = w % 2
        for dc in range(NDC):
            tk.dma('sp', xnw[p][:, dc * WIN:(dc + 1) * WIN],
                   xn_h[dc * 128:(dc + 1) * 128, w * WIN:(w + 1) * WIN],
                   writes=[f'xnw{p}_{dc}'], label=f'f1xn_w{w}_d{dc}')
        for m3 in range(24):
            q = m3 % 2
            bank, bk = (pb[0], 'pb0') if q == 0 else (pb[1], 'pb1')
            for dc in range(NDC):
                tk.mm(bank[:, 0:512],
                      wg[:, dc * 3072 + m3 * 128:dc * 3072 + (m3 + 1) * 128],
                      xnw[p][:, dc * WIN:(dc + 1) * WIN], dc == 0, dc == 7,
                      reads=['wgT', f'xnw{p}_{dc}'], writes=[bk])
            tk.act(gsl[q][:], bank[:, 0:512], AF.Silu, scale=SC,
                   reads=[bk], writes=[f'gsl{q}'])
            bank2, bk2 = (pb[2], 'pb2') if q == 0 else (pb[3], 'pb3')
            for dc in range(NDC):
                tk.mm(bank2[:, 0:512],
                      wu[:, dc * 3072 + m3 * 128:dc * 3072 + (m3 + 1) * 128],
                      xnw[p][:, dc * WIN:(dc + 1) * WIN], dc == 0, dc == 7,
                      reads=['wuT', f'xnw{p}_{dc}'], writes=[bk2])
            tk.act(usl[q][:], bank2[:, 0:512], AF.Copy, scale=SC,
                   reads=[bk2], writes=[f'usl{q}'])
            tk.tt(hsb[:, m3 * WIN:(m3 + 1) * WIN], gsl[q][:], usl[q][:], ALU.mult,
                  reads=[f'gsl{q}', f'usl{q}'], writes=[f'h_{m3}'])
        for dc in range(NDC):
            q = dc % 2
            bank, bk = (pb[4], 'pb4') if q == 0 else (pb[5], 'pb5')
            for m3 in range(24):
                tk.mm(bank[:, 0:512],
                      wd[:, m3 * 1024 + dc * 128:m3 * 1024 + (dc + 1) * 128],
                      hsb[:, m3 * WIN:(m3 + 1) * WIN], m3 == 0, m3 == 23,
                      reads=['wdT', f'h_{m3}'], writes=[bk])
            tk.act(gw[q][:], bank[:, 0:512], AF.Copy, scale=SC,
                   reads=[bk], writes=[f'gw{q}'])
            tk.dma('sp', g_h[dc * 128:(dc + 1) * 128, w * WIN:(w + 1) * WIN],
                   gw[q][:], reads=[f'gw{q}'], label=f'stg1_{w}_{dc}')


def emit_pe_all(tk, E, nc, x_all, routerT, allpe, scope=None):
    pb = E['pb']
    iden = E['iden_f32']; ones_col = E['ones_col']; ones_row = E['ones_row']
    cst = E['cst']; irev = E['irev']
    g = 'pa_'
    A = _mk_alloc(nc, scope, g)
    xw = [A(f'xw{i}', [128, NDC * WIN], F32) for i in range(2)]
    x2w = [A(f'x2w{i}', [128, NDC * WIN], F32) for i in range(2)]
    stats = [A(f'stats{q}', [128, 24], F32) for q in range(2)]
    routerT_sb = A('routerT_sb', [128, 64], F32)
    negSr = A('negSr', [1, 8], F32)
    srb = A('srb', [128, 8], F32)
    srcol = A('srcol', [8, 1], F32)
    lgt = [A(f'lgt{q}', [128, 32], F32) for q in range(2)]
    elg = [A(f'elg{q}', [128, 32], F32) for q in range(2)]
    eqt = [A(f'eqt{q}', [128, 32], F32) for q in range(2)]
    t2b = [A(f't2b{q}', [128, 32], F32) for q in range(2)]
    mx4 = [A(f'mx4{q}', [128, 16], F32) for q in range(2)]
    pecols = [A(f'pecols{q}', [128, 8], F32) for q in range(2)]
    peacc = A('peacc', [8, 1024], F32)
    irev4 = A('irev4', [128, 32], F32)
    tk.dma('sp', routerT_sb[:], routerT.rearrange('p a b -> p (a b)'),
           writes=['routerT_sb'])
    for rr in range(4):
        tk.copy(irev4[:, rr * 8:(rr + 1) * 8], irev[:], reads=['irev'],
                writes=['irev4'], eng='pool')
    for dc in range(NDC):
        tk.mm(pb[5][0:8, 0:1], routerT_sb[:, dc * 8:(dc + 1) * 8], ones_col[:],
              dc == 0, dc == 7, reads=['routerT_sb', 'ones_col'], writes=['pb5'])
    tk.ts(srcol[:], pb[5][0:8, 0:1], -1.0, None, ALU.mult,
          reads=['pb5'], writes=['srcol'])
    tk.tr(pb[6][0:1, 0:8], srcol[:], iden[0:8, 0:8],
          reads=['srcol', 'iden_f32'], writes=['pb6'])
    tk.copy(negSr[:], pb[6][0:1, 0:8], reads=['pb6'], writes=['negSr'])
    tk.mm(pb[6][:, 8:16], ones_row[:], negSr[:], True, True,
          reads=['ones_row', 'negSr'], writes=['pb6'])
    tk.copy(srb[:], pb[6][:, 8:16], reads=['pb6'], writes=['srb'])
    srows = A('srows', [1, 1024], F32)
    for r in range(8):
        for w in range(NW):
            p = w % 2
            q = w % 2
            bs, bq2, br = (pb[0], pb[1], pb[2]) if q == 0 else (pb[3], pb[4], pb[5])
            ks, kq, kr = (f'pb{j}' for j in ((0, 1, 2) if q == 0 else (3, 4, 5)))
            bt, kt = (pb[6], 'pb6') if q == 0 else (pb[7], 'pb7')
            st = stats[q]; sk = f'st{q}'
            for dc in range(NDC):
                tk.dma('sp', xw[p][:, dc * WIN:(dc + 1) * WIN],
                       x_all[r, dc * 128:(dc + 1) * 128, w * WIN:(w + 1) * WIN],
                       writes=[f'xw{p}_{dc}'], label=f'pa_{r}_{w}_{dc}')
            for dc in range(NDC):
                tk.act(x2w[p][:, dc * WIN:(dc + 1) * WIN],
                       xw[p][:, dc * WIN:(dc + 1) * WIN], AF.Square,
                       reads=[f'xw{p}_{dc}'], writes=[f'x2w{p}_{dc}'])
            for dc in range(NDC):
                tk.mm(bs[0:1, 0:512], ones_col[:], xw[p][:, dc * WIN:(dc + 1) * WIN],
                      dc == 0, dc == 7, reads=[f'xw{p}_{dc}', 'ones_col'], writes=[ks])
                tk.mm(bq2[0:1, 0:512], ones_col[:], x2w[p][:, dc * WIN:(dc + 1) * WIN],
                      dc == 0, dc == 7, reads=[f'x2w{p}_{dc}', 'ones_col'], writes=[kq])
            tk.copy(srows[0:1, 0:512], bs[0:1, 0:512], reads=[ks],
                    writes=['srows'], eng='act')
            tk.copy(srows[0:1, 512:1024], bq2[0:1, 0:512], reads=[kq],
                    writes=['srows'], eng='act')
            for tau in range(NTAU):
                for dc in range(NDC):
                    xt = xw[p][:, dc * WIN + tau * 128: dc * WIN + (tau + 1) * 128]
                    tk.mm(br[:, 8 * tau:8 * tau + 8], xt,
                          routerT_sb[:, dc * 8:(dc + 1) * 8], dc == 0, dc == 7,
                          reads=[f'xw{p}_{dc}', 'routerT_sb'], writes=[kr])
            for tau in range(NTAU):
                tk.tr(bt[:, tau:tau + 1], srows[0:1, tau * 128:(tau + 1) * 128],
                      iden[0:1, 0:1], reads=['srows', 'iden_f32'], writes=[kt])
                tk.tr(bt[:, 4 + tau:5 + tau],
                      srows[0:1, 512 + tau * 128:512 + (tau + 1) * 128],
                      iden[0:1, 0:1], reads=['srows', 'iden_f32'], writes=[kt])
            tk.ts(st[:, 0:4], bt[:, 0:4], 1.0 / D, None, ALU.mult,
                  reads=[kt], writes=[sk])
            tk.act(st[:, 4:8], st[:, 0:4], AF.Square, reads=[sk], writes=[sk])
            tk.stt(st[:, 8:12], bt[:, 4:8], 1.0 / D, st[:, 4:8],
                   ALU.mult, ALU.subtract, reads=[kt, sk], writes=[sk])
            tk.act(st[:, 12:16], st[:, 8:12], AF.Ln, bias=cst[:, 0:1],
                   reads=[sk, 'cst'], writes=[sk])
            tk.act(st[:, 16:20], st[:, 12:16], AF.Exp, scale=-0.5,
                   reads=[sk], writes=[sk])
            tk.stt(st[:, 20:24], st[:, 0:4], -1.0, st[:, 16:20],
                   ALU.mult, ALU.mult, reads=[sk], writes=[sk])
            v_t2 = t2b[q].rearrange('p (t e) -> p t e', t=4)
            tk.tt(v_t2, st[:, 20:24].unsqueeze(2).to_broadcast([128, 4, 8]),
                  srb[:].unsqueeze(1).to_broadcast([128, 4, 8]),
                  ALU.mult, reads=[sk, 'srb'], writes=[f't2b{q}'])
            v_lgt = lgt[q].rearrange('p (t e) -> p t e', t=4)
            tk.tt(v_lgt, br[:, 0:32].rearrange('p (t e) -> p t e', t=4),
                  st[:, 16:20].unsqueeze(2).to_broadcast([128, 4, 8]),
                  ALU.mult, reads=[kr, sk], writes=[f'lgt{q}'])
            tk.tt(lgt[q][:], lgt[q][:], t2b[q][:], ALU.add,
                  reads=[f'lgt{q}', f't2b{q}'], writes=[f'lgt{q}'])
            tk.red(mx4[q][:, 0:4].unsqueeze(2), v_lgt, ALU.max,
                   negate=True, reads=[f'lgt{q}'], writes=[f'mx4{q}'])
            v_el = elg[q].rearrange('p (t e) -> p t e', t=4)
            tk.tt(v_el, v_lgt,
                  mx4[q][:, 0:4].unsqueeze(2).to_broadcast([128, 4, 8]),
                  ALU.add, reads=[f'lgt{q}', f'mx4{q}'], writes=[f'elg{q}'])
            tk.act(elg[q][:], elg[q][:], AF.Exp, reads=[f'elg{q}'], writes=[f'elg{q}'])
            tk.red(mx4[q][:, 4:8].unsqueeze(2), v_el, ALU.add,
                   reads=[f'elg{q}'], writes=[f'mx4{q}'])
            tk.recip(mx4[q][:, 8:12], mx4[q][:, 4:8],
                     reads=[f'mx4{q}'], writes=[f'mx4{q}'])
            tk.ts(eqt[q][:], elg[q][:], 1.0, None, ALU.is_ge,
                  reads=[f'elg{q}'], writes=[f'eqt{q}'])
            tk.tt(eqt[q][:], eqt[q][:], irev4[:], ALU.mult,
                  reads=[f'eqt{q}', 'irev4'], writes=[f'eqt{q}'])
            tk.red(mx4[q][:, 12:16].unsqueeze(2),
                   eqt[q].rearrange('p (t e) -> p t e', t=4), ALU.max,
                   reads=[f'eqt{q}'], writes=[f'mx4{q}'])
            vp = pecols[q].rearrange('p (t c) -> p t c', t=4)
            tk.copy(vp[:, :, 0:1], mx4[q][:, 8:12].unsqueeze(2),
                    reads=[f'mx4{q}'], writes=[f'pe{q}'])
            tk.ts(vp[:, :, 1:2], mx4[q][:, 12:16].unsqueeze(2),
                  -1.0, 7.0, ALU.mult, op1=ALU.add,
                  reads=[f'mx4{q}'], writes=[f'pe{q}'])
            bt, kt = (pb[6], 'pb6') if q == 0 else (pb[7], 'pb7')
            tk.tr(bt[0:8, 128:256], pecols[q][:], iden[:],
                  reads=[f'pe{q}', 'iden_f32'], writes=[kt])
            tk.copy(peacc[:, w * 128:(w + 1) * 128], bt[0:8, 128:256],
                    reads=[kt], writes=[f'peacc_{w}'], eng='act')
        Fv = allpe.rearrange('a (w t j) -> a t w j', w=8, j=128)
        for cc in range(2):
            tk.dma('sp', Fv[2 * r + cc],
                   peacc[cc:8:2, :].rearrange('p (w j) -> p w j', j=128),
                   reads=[f'peacc_{w}' for w in range(NW)],
                   writes=[f'allpe_{r}'], label=f'stpe_{r}_{cc}')


def emit_pe_all2(tk, E, nc, x_all, routerT9, allpe, scope=None):
    pb = E['pb']
    iden = E['iden_f32']; ones_col = E['ones_col']; ones_row = E['ones_row']
    cst = E['cst']; irev = E['irev']
    HW = 256
    NHW = T // HW
    FINE_YIELD = getattr(emit_pe_all2, 'FINE_YIELD', False)
    g = 'pa_'
    A = _mk_alloc(nc, scope, g)
    xw = [A(f'xw{i}', [128, NDC * HW], F32) for i in range(2)]
    x2t0 = A('x2t', [128, HW], F32)
    x2t = [x2t0, x2t0]
    row9 = [A(f'row9_{i}', [9, HW], F32) for i in range(2)]
    x2r = [A(f'x2r{i}', [1, HW], F32) for i in range(2)]
    rT9 = A('rT9', [128, 72], F32)
    srcol = A('srcol', [8, 1], F32)
    negSr = A('negSr', [1, 8], F32)
    srb = A('srb', [128, 8], F32)
    st = [A(f'st{q}', [128, 12], F32) for q in range(2)]
    lgt = [A(f'lgt{q}', [128, 16], F32) for q in range(2)]
    elg = [A(f'elg{q}', [128, 16], F32) for q in range(2)]
    eqt = [A(f'eqt{q}', [128, 16], F32) for q in range(2)]
    t2b = [A(f't2b{q}', [128, 16], F32) for q in range(2)]
    mx = [A(f'mx{q}', [128, 8], F32) for q in range(2)]
    pecols = [A(f'pecols{q}', [128, 4], F32) for q in range(2)]
    peacc = A('peacc2', [4, 512], F32)
    tk.dma('sp', rT9[:], routerT9.rearrange('p a b -> p (a b)'),
           writes=['pa_rT9'])
    for dc in range(NDC):
        tk.mm(pb[6][0:8, 0:1], rT9[:, dc * 9:dc * 9 + 8], ones_col[:],
              dc == 0, dc == 7, reads=['pa_rT9', 'ones_col'], writes=['pa_psB'])
    tk.ts(srcol[:], pb[6][0:8, 0:1], 1.0, None, ALU.mult,
          reads=['pa_psB'], writes=['pa_srcol'])
    tk.tr(pb[1][0:1, 0:8], srcol[:], iden[0:8, 0:8],
          reads=['pa_srcol', 'iden_f32'], writes=['pa_psD0'])
    tk.copy(negSr[:], pb[1][0:1, 0:8], reads=['pa_psD0'], writes=['pa_negSr'])
    tk.mm(pb[6][:, 0:8], ones_row[:], negSr[:], True, True,
          reads=['ones_row', 'pa_negSr'], writes=['pa_psB'])
    tk.copy(srb[:], pb[6][:, 0:8], reads=['pa_psB'], writes=['pa_srb'])
    for r in range(8):
        for hw in range(NHW):
            p = hw % 2
            off = hw * HW
            for dc in range(NDC):
                tk.dma('sp', xw[p][:, dc * HW:(dc + 1) * HW],
                       x_all[r, dc * 128:(dc + 1) * 128, off:off + HW],
                       writes=[f'pa_xw{p}_{dc}'], label=f'pa_{r}_{hw}_{dc}')
            if FINE_YIELD:
                yield 'ld'
            for dc in range(NDC):
                tk.mm(pb[6][0:9, 0:HW], rT9[:, dc * 9:(dc + 1) * 9],
                      xw[p][:, dc * HW:(dc + 1) * HW], dc == 0, dc == 7,
                      reads=['pa_rT9', f'pa_xw{p}_{dc}'], writes=['pa_psB'])
            if FINE_YIELD:
                yield 'mm'
            for dc in range(NDC):
                tk.act(x2t0[:], xw[p][:, dc * HW:(dc + 1) * HW], AF.Square,
                       reads=[f'pa_xw{p}_{dc}'], writes=['pa_x2t'])
                tk.mm(pb[6][0:1, 256:512], ones_col[:], x2t0[:],
                      dc == 0, dc == 7,
                      reads=['pa_x2t', 'ones_col'], writes=['pa_psC'])
            if FINE_YIELD:
                yield 'x2'
            tk.copy(row9[p][:], pb[6][0:9, 0:HW], reads=['pa_psB'],
                    writes=[f'pa_row9_{p}'], eng='act')
            tk.copy(x2r[p][:], pb[6][0:1, 256:512], reads=['pa_psC'],
                    writes=[f'pa_x2r{p}'], eng='act')
            if FINE_YIELD:
                yield 'cp'
            c0 = p * 20
            for t2 in range(2):
                tk.tr(pb[1][:, c0 + t2 * 10:c0 + t2 * 10 + 9],
                      row9[p][0:9, t2 * 128:(t2 + 1) * 128], iden[0:9, 0:9],
                      reads=[f'pa_row9_{p}', 'iden_f32'], writes=[f'pa_psD{p}'])
                tk.tr(pb[1][:, c0 + t2 * 10 + 9:c0 + t2 * 10 + 10],
                      x2r[p][0:1, t2 * 128:(t2 + 1) * 128], iden[0:1, 0:1],
                      reads=[f'pa_x2r{p}', 'iden_f32'], writes=[f'pa_psD{p}'])
            DV = pb[1][:, c0:c0 + 20].rearrange('p (t c) -> p t c', t=2)
            sq = st[p]; kk = f'pa_st{p}'; kD = f'pa_psD{p}'
            tk.ts(sq[:, 0:2].unsqueeze(2), DV[:, :, 8:9], 1.0 / D, None,
                  ALU.mult, reads=[kD], writes=[kk])
            tk.ts(sq[:, 2:4].unsqueeze(2), DV[:, :, 9:10], 1.0 / D, None,
                  ALU.mult, reads=[kD], writes=[kk])
            tk.act(sq[:, 4:6], sq[:, 0:2], AF.Square, reads=[kk], writes=[kk])
            tk.tt(sq[:, 4:6], sq[:, 2:4], sq[:, 4:6], ALU.subtract,
                  reads=[kk], writes=[kk])
            tk.act(sq[:, 6:8], sq[:, 4:6], AF.Ln, bias=cst[:, 0:1],
                   reads=[kk, 'cst'], writes=[kk])
            tk.act(sq[:, 8:10], sq[:, 6:8], AF.Exp, scale=-0.5,
                   reads=[kk], writes=[kk])
            tk.stt(sq[:, 10:12], sq[:, 0:2], -1.0, sq[:, 8:10],
                   ALU.mult, ALU.mult, reads=[kk], writes=[kk])
            v_t2 = t2b[p].rearrange('p (t e) -> p t e', t=2)
            tk.tt(v_t2, sq[:, 10:12].unsqueeze(2).to_broadcast([128, 2, 8]),
                  srb[:].unsqueeze(1).to_broadcast([128, 2, 8]),
                  ALU.mult, reads=[kk, 'pa_srb'], writes=[f'pa_t2b{p}'])
            v_lgt = lgt[p].rearrange('p (t e) -> p t e', t=2)
            tk.tt(v_lgt, DV[:, :, 0:8],
                  sq[:, 8:10].unsqueeze(2).to_broadcast([128, 2, 8]),
                  ALU.mult, reads=[kD, kk], writes=[f'pa_lgt{p}'])
            tk.tt(lgt[p][:], lgt[p][:], t2b[p][:], ALU.add,
                  reads=[f'pa_lgt{p}', f'pa_t2b{p}'], writes=[f'pa_lgt{p}'])
            tk.red(mx[p][:, 0:2].unsqueeze(2), v_lgt, ALU.max, negate=True,
                   reads=[f'pa_lgt{p}'], writes=[f'pa_mx{p}'])
            v_el = elg[p].rearrange('p (t e) -> p t e', t=2)
            tk.tt(v_el, v_lgt,
                  mx[p][:, 0:2].unsqueeze(2).to_broadcast([128, 2, 8]),
                  ALU.add, reads=[f'pa_lgt{p}', f'pa_mx{p}'], writes=[f'pa_elg{p}'])
            tk.act(elg[p][:], elg[p][:], AF.Exp, reads=[f'pa_elg{p}'],
                   writes=[f'pa_elg{p}'])
            tk.red(mx[p][:, 2:4].unsqueeze(2), v_el, ALU.add,
                   reads=[f'pa_elg{p}'], writes=[f'pa_mx{p}'])
            tk.recip(mx[p][:, 4:6], mx[p][:, 2:4],
                     reads=[f'pa_mx{p}'], writes=[f'pa_mx{p}'])
            tk.ts(eqt[p][:], elg[p][:], 1.0, None, ALU.is_ge,
                  reads=[f'pa_elg{p}'], writes=[f'pa_eqt{p}'])
            tk.tt(eqt[p].rearrange('p (t e) -> p t e', t=2),
                  eqt[p].rearrange('p (t e) -> p t e', t=2),
                  irev[:].unsqueeze(1).to_broadcast([128, 2, 8]),
                  ALU.mult, reads=[f'pa_eqt{p}', 'irev'], writes=[f'pa_eqt{p}'])
            tk.red(mx[p][:, 6:8].unsqueeze(2),
                   eqt[p].rearrange('p (t e) -> p t e', t=2), ALU.max,
                   reads=[f'pa_eqt{p}'], writes=[f'pa_mx{p}'])
            vp = pecols[p].rearrange('p (c t) -> p c t', c=2)
            tk.copy(vp[:, 0:1, :], mx[p][:, 4:6].unsqueeze(1),
                    reads=[f'pa_mx{p}'], writes=[f'pa_pe{p}'])
            tk.ts(vp[:, 1:2, :], mx[p][:, 6:8].unsqueeze(1),
                  -1.0, 7.0, ALU.mult, op1=ALU.add,
                  reads=[f'pa_mx{p}'], writes=[f'pa_pe{p}'])
            ce = 40
            tk.tr(pb[1][0:4, ce:ce + 128], pecols[p][:], iden[:],
                  reads=[f'pa_pe{p}', 'iden_f32'], writes=['pa_psE'])
            hws = hw % 4
            tk.copy(peacc[:, hws * 128:(hws + 1) * 128], pb[1][0:4, ce:ce + 128],
                    reads=['pa_psE'], writes=[f'pa_peacc_{hws}'], eng='act')
            if hws == 3:
                s = hw // 4
                Fv = allpe.rearrange('a (s hw t j) -> a s t hw j',
                                     s=4, t=2, j=128)
                for cc in range(2):
                    tk.dma('sp', Fv[2 * r + cc, s],
                           peacc[2 * cc:2 * cc + 2, :].rearrange(
                               'p (hw j) -> p hw j', j=128),
                           reads=[f'pa_peacc_{i}' for i in range(4)],
                           writes=[f'allpe_{r}_{s}'], label=f'stpe_{r}_{s}_{cc}')
            yield


from contextlib import ExitStack
from concourse.bass_utils import run_bass_kernel_spmd

_CONST_KEYS = ['iden', 'ones_col', 'ones_row', 'mtri', 'selA', 'selB', 'rmat',
               'rmatT', 'idxmat', 'irev', 'ones8k']


def build_fused(nc):
    F, B, I = F32, BF16, I32
    ci = {}

    def ein(name, shape, dtype):
        ci[name] = nc.dram_tensor(name, shape, dtype, kind="ExternalInput").ap()
        return ci[name]

    x0 = ein('x0', [1024, 4096], F)
    x_all = ein('x_all', [8, 1024, 4096], F)
    routerT9 = ein('routerT9', [128, 8, 9], F)
    csel = ein('csel', [128, 2], F)
    cselE = ein('cselE', [128, 16], F)
    W0 = {nm: ein(nm, sh, F) for nm, sh in [
        ('wq', [128, 8, 8, 64]), ('wk', [128, 8, 8, 64]), ('wkT', [128, 8, 256]),
        ('wkeT', [128, 8, 256]), ('wvT', [128, 8, 1024]), ('wveT', [128, 8, 1024]),
        ('wout', [128, 8, 1024])]}
    WM = {nm: ein(nm, sh, F) for nm, sh in [
        ('w1', [8, 128, 8, 4096]), ('w2', [8, 128, 32, 1024]),
        ('b1', [8, 128, 32]), ('b2', [8, 128, 8])]}
    W1 = {}
    for nm, sh, dt in [('wq1', [128, 8, 8, 64], F), ('wk1', [128, 8, 8, 64], F),
                       ('bq1', [32, 16], F), ('bk1', [32, 16], F),
                       ('wkT1', [128, 8, 512], F), ('wvT1', [128, 8, 1024], F),
                       ('bv1', [1, 1024], F), ('bk_row', [1, 512], F),
                       ('wo1', [128, 8, 1024], F), ('bo1', [128, 8], F),
                       ('wgT', [128, 8, 3072], I), ('wuT', [128, 8, 3072], I),
                       ('wdT', [128, 24, 1024], I)]:
        W1[nm] = ein(nm, sh, dt)
    out_h = nc.dram_tensor('out_h', [1024, 4096], F, kind="ExternalOutput").ap()
    def intl(name, shape, dtype):
        return nc.dram_tensor(name, shape, dtype, kind="Internal").ap()
    allpe = intl('allpe', [16, 4096], F)
    xn0 = intl('xn0', [1024, 4096], B)
    kmask = intl('kmask', [8, 4096], F)
    s0 = intl('s0', [1024, 4096], B)
    g0 = intl('g0', [8, 1024, 4096], B)
    x1 = intl('x1', [1024, 4096], F)
    xn1 = intl('xn1', [1024, 4096], B)
    s1 = intl('s1', [1024, 4096], B)
    g1 = intl('g1', [1024, 4096], B)

    def pump(gen, n):
        try:
            for _ in range(n):
                next(gen)
            return True
        except StopIteration:
            return False

    tk = TK(nc)
    with nc.Block() as block:
        E = setup_consts(nc, tk)
        E['pb'] = alloc_psum_banks(nc)
        INTERLEAVE = False
        INTERLEAVE_LN = True
        tk.phase = 'ln0'
        sc_pa = ExitStack()
        gp = emit_pe_all2(tk, E, nc, x_all, routerT9, allpe, scope=sc_pa)
        pump(gp, 1)
        sc_ln = ExitStack()
        gl = emit_ln(tk, E, nc, x0, xn0, tag='ln0', scope=sc_ln)
        for _ in gl:
            if INTERLEAVE_LN:
                pump(gp, 5 if getattr(emit_pe_all2, 'FINE_YIELD', False) else 1)
        sc_ln.close()
        tk.barrier()
        tk.phase = 'attn0'
        sc_a0 = ExitStack()
        ga = emit_attn0(tk, E, nc, xn0, s0, W0, scope=sc_a0)
        for yi, _ in enumerate(ga):
            if INTERLEAVE:
                pump(gp, 10 if getattr(emit_pe_all2, 'FINE_YIELD', False) else 2)
        tk.barrier()
        tk.phase = 'peall'
        while pump(gp, 1):
            pass
        tk.barrier()
        sc_a0.close()
        sc_pa.close()
        tk.phase = 'routing'
        with ExitStack() as sc:
            emit_routing(tk, E, nc, allpe, csel, cselE, kmask, scope=sc)
        tk.barrier()
        tk.phase = 'moe'
        with ExitStack() as sc:
            emit_moe(tk, E, nc, xn0, kmask, g0, WM, scope=sc)
        tk.barrier()
        tk.phase = 'cb0'
        with ExitStack() as sc:
            emit_combine(tk, E, nc, x0, s0, g0, x1, scope=sc, tag='cb0', ge=True)
        tk.barrier()
        tk.phase = 'ln1'
        with ExitStack() as sc:
            for _ in emit_ln(tk, E, nc, x1, xn1, tag='ln1', scope=sc):
                pass
        tk.barrier()
        tk.phase = 'attn1'
        with ExitStack() as sc:
            emit_attn1(tk, E, nc, xn1, s1, W1, scope=sc)
        tk.barrier()
        tk.phase = 'ffn1'
        with ExitStack() as sc:
            emit_ffn1(tk, E, nc, xn1, g1, W1, scope=sc)
        tk.barrier()
        tk.phase = 'cb1'
        with ExitStack() as sc:
            emit_combine(tk, E, nc, x1, s1, g1, out_h, scope=sc, tag='cb1')
        tk.flush(block)
    return nc


emit_pe_all2.FINE_YIELD = True
build_fused.NSUB = 1


def kernel(**inputs):
    c = pack_consts()
    w = pack_weights(inputs)
    x = inputs['L0_attn_x']
    xallT = np.ascontiguousarray(np.transpose(x, (0, 2, 1))).astype(np.float32)
    nc = build_fused(bass.Bass("TRN2", target_bir_lowering=False,
                               detect_race_conditions=False))
    cim = {}
    for k2 in _CONST_KEYS:
        key = {'iden': 'iden_f32'}.get(k2, k2)
        cim['c_' + k2] = c[key]
    wm = {'wq': 'l0_wq', 'wk': 'l0_wk', 'wkT': 'l0_wkT', 'wkeT': 'l0_wkeT',
          'wvT': 'l0_wvT', 'wveT': 'l0_wveT', 'wout': 'l0_wout',
          'w1': 'w1', 'w2': 'w2', 'b1': 'b1', 'b2': 'b2',
          'wq1': 'l1_wq', 'wk1': 'l1_wk', 'bq1': 'l1_bq', 'bk1': 'l1_bk',
          'wkT1': 'l1_wkT', 'wvT1': 'l1_wvT', 'bv1': 'l1_bv',
          'bk_row': 'l1_bk_row', 'wo1': 'l1_wo', 'bo1': 'l1_bo',
          'wgT': 'wgT', 'wuT': 'wuT', 'wdT': 'wdT', 'routerT9': 'routerT9'}
    in_maps = []
    for core in range(8):
        im = dict(cim)
        im['x0'] = np.ascontiguousarray(x[core].T).astype(np.float32)
        im['x_all'] = xallT
        for k2, v in wm.items():
            im[k2] = w[v]
        cs = np.zeros((128, 2), np.float32)
        cse = np.zeros((128, 16), np.float32)
        for gg in range(2):
            for e in range(8):
                cs[e * 16 + 2 * core + gg, gg] = 1.0
                cse[e * 16 + 2 * core + gg, 2 * e + gg] = 1.0
        im['csel'] = cs
        im['cselE'] = cse
        in_maps.append(im)
    res = run_bass_kernel_spmd(nc, in_maps, core_ids=list(range(8)))
    out = np.stack([np.ascontiguousarray(r['out_h'].T) for r in res.results],
                   axis=0)
    return out.astype(np.float32)


# version 25
# speedup vs baseline: 1.0386x; 1.0386x over previous
import numpy as np
import concourse.bass as bass
import concourse.mybir as mybir

F32 = mybir.dt.float32
BF16 = mybir.dt.bfloat16
I32 = mybir.dt.int32
AF = mybir.ActivationFunctionType
ALU = mybir.AluOpType
AX = mybir.AxisListType.X

D = 1024
T = 4096
NW = 8
WIN = 512
NTAU = 4
CT = 128
NDC = 8


class Op:
    __slots__ = ('stream', 'fn', 'deps', 'sem', 'inc', 'idx', 'label', 'phase')

    def __init__(self, stream, fn, sem, inc, idx, label):
        self.stream = stream
        self.fn = fn
        self.deps = []
        self.sem = sem
        self.inc = inc
        self.idx = idx
        self.label = label


class TK:
    STREAMS = ('pe', 'act', 'dve', 'pool', 'sp')
    ENGSEM = {'pe': 'pe', 'act': 'act', 'dve': 'dve', 'pool': 'pool', 'sp': 'spe'}

    NQSEM = 4

    def __init__(self, nc):
        self.nc = nc
        self.ops = []
        self.writer = {}
        self.readers = {}
        self.dma_ring = {}
        self.barrier_ops = []
        self.phase = 'init'

    def op(self, stream, fn, reads=(), writes=(), label='', is_dma=False):
        assert stream in self.STREAMS
        extra_dep = None
        if is_dma:
            ring = self.dma_ring.setdefault(stream, [])
            slot = len(ring) % self.NQSEM
            sem = f'q_{stream}_{slot}'
            inc = 16
            if len(ring) >= self.NQSEM:
                extra_dep = ring[-self.NQSEM]
            ring.append(None)
        else:
            sem = None
            inc = 1
        o = Op(stream, fn, sem, inc, len(self.ops), label)
        o.phase = self.phase
        if is_dma:
            self.dma_ring[stream][-1] = o
        deps = {}
        if extra_dep is not None:
            deps[extra_dep.idx] = extra_dep
        for k in reads:
            w = self.writer.get(k)
            if w is not None:
                if w.stream == 'pe' and stream == 'pe' and w.inc != 16:
                    continue
                deps[w.idx] = w
        for k in writes:
            w = self.writer.get(k)
            if w is not None and not (w.stream == stream and w.inc != 16):
                deps[w.idx] = w
            for r in self.readers.get(k, ()):
                if r.stream == stream and r.inc != 16:
                    continue
                deps[r.idx] = r
        o.deps = list(deps.values())
        for d in o.deps:
            if d.sem is None:
                d.sem = self.ENGSEM[d.stream]
        for k in writes:
            self.writer[k] = o
            self.readers[k] = []
        for k in reads:
            self.readers.setdefault(k, []).append(o)
        self.ops.append(o)
        return o

    def dma(self, queue, out, in_, reads=(), writes=(), label='', **kw):
        return self.op(queue, lambda e: e.dma_start(out=out, in_=in_, **kw),
                       reads=reads, writes=writes, label=label, is_dma=True)

    def mm(self, out, lhsT, rhs, start, stop, reads, writes, label=''):
        fn = (lambda out, lhsT, rhs, start, stop:
              lambda e: e.matmul(out, lhsT=lhsT, rhs=rhs, start=start, stop=stop,
                                 skip_group_check=True))(out, lhsT, rhs, start, stop)
        return self.op('pe', fn, reads=reads, writes=writes, label=label)

    def tr(self, out, in_, iden, reads, writes, label=''):
        fn = (lambda out, in_, iden:
              lambda e: e.transpose(out, in_=in_, identity=iden))(out, in_, iden)
        return self.op('pe', fn, reads=reads, writes=writes, label=label)

    def act(self, out, in_, func, reads, writes, bias=0.0, scale=1.0, eng='act', label=''):
        fn = (lambda out, in_, func, bias, scale:
              lambda e: e.activation(out, in_, func, bias=bias, scale=scale))(
                  out, in_, func, bias, scale)
        return self.op(eng, fn, reads=reads, writes=writes, label=label)

    def tt(self, out, a, b, opk, reads, writes, eng='dve', label=''):
        fn = (lambda out, a, b, opk:
              lambda e: e.tensor_tensor(out, a, b, op=opk))(out, a, b, opk)
        return self.op(eng, fn, reads=reads, writes=writes, label=label)

    def ts(self, out, a, s1, s2, op0, reads, writes, op1=None, accum=None, eng='dve', label=''):
        def mk(out, a, s1, s2, op0, op1, accum):
            def f(e):
                kw = {}
                if op1 is not None:
                    kw['op1'] = op1
                if accum is not None:
                    kw['accum_out'] = accum
                return e.tensor_scalar(out, a, s1, s2, op0=op0, **kw)
            return f
        return self.op(eng, mk(out, a, s1, s2, op0, op1, accum),
                       reads=reads, writes=writes, label=label)

    def stt(self, out, in0, scalar, in1, op0, op1, reads, writes, eng='dve', label=''):
        fn = (lambda out, in0, scalar, in1, op0, op1:
              lambda e: e.scalar_tensor_tensor(out, in0, scalar, in1, op0=op0, op1=op1))(
                  out, in0, scalar, in1, op0, op1)
        return self.op(eng, fn, reads=reads, writes=writes, label=label)

    def red(self, out, in_, opk, reads, writes, negate=False, eng='dve', label=''):
        fn = (lambda out, in_, opk, negate:
              lambda e: e.tensor_reduce(out, in_, AX, opk, negate=negate))(
                  out, in_, opk, negate)
        return self.op(eng, fn, reads=reads, writes=writes, label=label)

    def copy(self, out, in_, reads, writes, eng='dve', label=''):
        if eng == 'act':
            fn = (lambda out, in_: lambda e: e.copy(out, in_))(out, in_)
        else:
            fn = (lambda out, in_: lambda e: e.tensor_copy(out, in_))(out, in_)
        return self.op(eng, fn, reads=reads, writes=writes, label=label)

    def memset(self, ap, val, writes, eng='dve', label=''):
        fn = (lambda ap, val: lambda e: e.memset(ap, val))(ap, val)
        return self.op(eng, fn, reads=(), writes=writes, label=label)

    def recip(self, out, in_, reads, writes, label=''):
        fn = (lambda out, in_: lambda e: e.reciprocal(out, in_))(out, in_)
        return self.op('dve', fn, reads=reads, writes=writes, label=label)

    def cpred(self, out, mask, data, reads, writes, label=''):
        fn = (lambda out, mask, data:
              lambda e: e.copy_predicated(out, mask, data))(out, mask, data)
        return self.op('dve', fn, reads=reads, writes=writes, label=label)

    def barrier(self):
        deps = {}
        for w in self.writer.values():
            deps[w.idx] = w
        for rl in self.readers.values():
            for r in rl:
                deps[r.idx] = r
        dep_list = list(deps.values())
        for d in dep_list:
            if d.sem is None:
                d.sem = self.ENGSEM[d.stream]
        newbar = []
        for s in self.STREAMS:
            o = Op(s, lambda e: e.nop(), None, 1, len(self.ops), 'barrier')
            o.phase = self.phase
            o.deps = [d for d in dep_list if not (d.stream == s and d.inc != 16)] \
                + [b for b in self.barrier_ops]
            self.ops.append(o)
            newbar.append(o)
        self.writer = {}
        self.readers = {}
        self.barrier_ops = newbar
        for b in newbar:
            for d in b.deps:
                if d.sem is None:
                    d.sem = self.ENGSEM[d.stream]

    def flush(self, block):
        nc = self.nc
        semval = {}
        opval = {}
        for o in self.ops:
            if o.sem is not None:
                semval[o.sem] = semval.get(o.sem, 0) + o.inc
                opval[o.idx] = (o.sem, semval[o.sem])
        sems = {name: nc.alloc_semaphore(name) for name in semval}
        finals = dict(semval)
        by_stream = {s: [] for s in self.STREAMS}
        for o in self.ops:
            by_stream[o.stream].append(o)
        engmap = {'pe': block.tensor, 'act': block.scalar, 'dve': block.vector,
                  'pool': block.gpsimd, 'sp': block.sync}
        nceng = {'pe': nc.tensor, 'act': nc.scalar, 'dve': nc.vector,
                 'pool': nc.gpsimd, 'sp': nc.sync}
        for s in self.STREAMS:
            guar = {}

            def emit_stream(eng, s=s, guar=guar):
                cur = [None]
                curscope = [None]
                for o in by_stream[s]:
                    if o.phase != cur[0]:
                        if curscope[0] is not None:
                            curscope[0].__exit__(None, None, None)
                        cm = nc.named_scope(o.phase)
                        cm.__enter__()
                        curscope[0] = cm
                        cur[0] = o.phase
                    for d in o.deps:
                        if d.idx not in opval:
                            raise RuntimeError(f'dep without sem: {d.label}')
                        nm, v = opval[d.idx]
                        if guar.get(nm, 0) < v:
                            eng.wait_ge(sems[nm], v)
                            guar[nm] = v
                    inst = o.fn(eng)
                    if o.sem is not None:
                        if inst is None:
                            raise RuntimeError(f'op {o.label} returned no inst')
                        inst.then_inc(sems[o.sem], o.inc)
                if curscope[0] is not None:
                    curscope[0].__exit__(None, None, None)

            engmap[s](emit_stream)
        def final_wait(eng):
            for nm, v in finals.items():
                eng.wait_ge(sems[nm], v)
        block.sync(final_wait)
        return finals


CS = float(1.0 / np.sqrt(4.0 * np.sqrt(2.0)))
LIN = float(0.5 / CS)
SQB = float(np.sqrt(0.5))

def pair_list():
    pairs = [(a, b) for a in range(16) for b in range(16)]
    return pairs

def pack_consts():
    c = {}
    iden = np.eye(128, dtype=np.float32)
    c['iden_bf'] = iden.astype(np.float32)
    c['iden_f32'] = iden.copy()
    c['ones_col'] = np.ones((128, 1), np.float32)
    c['ones_row'] = np.ones((1, 128), np.float32)
    tri = (np.arange(128)[None, :] >= np.arange(128)[:, None]).astype(np.float32)
    c['mtri'] = tri
    pairs = pair_list()
    selA = np.zeros((64, 273), np.float32)
    selB = np.zeros((64, 273), np.float32)
    def setsel(base):
        selA[base + 16, 0] = 1.0; selB[base + 16, 0] = 1.0
        for a in range(16):
            selA[base + a, 1 + a] = 1.0; selB[base + 16, 1 + a] = 1.0
        for m in range(111):
            a, b = pairs[m]
            selA[base + a, 17 + m] = 1.0; selB[base + b, 17 + m] = 1.0
        for m in range(128):
            a, b = pairs[111 + m]
            selA[base + a, 128 + m] = 1.0; selB[base + b, 128 + m] = 1.0
        for m in range(17):
            a, b = pairs[239 + m]
            selA[base + a, 256 + m] = 1.0; selB[base + b, 256 + m] = 1.0
    setsel(0); setsel(32)
    c['selA'] = selA; c['selB'] = selB
    R = np.zeros((8, 128), np.float32)
    for e in range(8):
        R[e, e * 16:(e + 1) * 16] = 1.0
    c['rmat'] = R
    c['rmatT'] = np.ascontiguousarray(R.T)
    idx = (np.arange(128)[:, None] % 16) * 2048 + np.arange(2048)[None, :]
    c['idxmat'] = idx.astype(np.float32)
    c['irev'] = np.broadcast_to((7 - np.arange(8))[None, :].astype(np.float32),
                                (128, 8)).copy()
    c['ones8k'] = np.ones(8192, np.float32)
    return c

def chunk_lhsT(w, kchunk=128):
    K, M = w.shape
    nk = K // kchunk
    return np.ascontiguousarray(w.reshape(nk, kchunk, M).transpose(1, 0, 2))

def pack_weights(d):
    w = {}
    def packqk(W):
        Wt = W.T * CS
        out = np.zeros((1024, 8, 64), np.float32)
        for p in range(8):
            out[:, p, 0:16] = Wt[:, (2 * p) * 16:(2 * p) * 16 + 16]
            out[:, p, 32:48] = Wt[:, (2 * p + 1) * 16:(2 * p + 1) * 16 + 16]
        return np.ascontiguousarray(
            out.reshape(8, 128, 8, 64).transpose(1, 0, 2, 3))
    w['l0_wq'] = packqk(d['L0_attn_Wq'])
    w['l0_wk'] = packqk(d['L0_attn_Wk'])
    w['l0_wke'] = packqk(d['L0_attn_Wk_enc'])
    w['l0_wkT'] = chunk_lhsT(d['L0_attn_Wk'].T * CS)
    w['l0_wkeT'] = chunk_lhsT(d['L0_attn_Wk_enc'].T * CS)
    w['l0_wvT'] = chunk_lhsT(d['L0_attn_Wv'].T)
    w['l0_wveT'] = chunk_lhsT(d['L0_attn_Wv_enc'].T)
    w['l0_wout'] = chunk_lhsT(d['L0_attn_Wout'].T)
    w['routerT9'] = np.ascontiguousarray(np.concatenate(
        [chunk_lhsT(d['L0_ffn_router_w'].T),
         np.ones((128, 8, 1), np.float32)], axis=2))
    w['w1'] = np.ascontiguousarray(
        d['L0_ffn_W1'].reshape(8, 8, 128, 4096).transpose(0, 2, 1, 3))
    w['w2'] = np.ascontiguousarray(
        d['L0_ffn_W2'].reshape(8, 32, 128, 1024).transpose(0, 2, 1, 3))
    w['b1'] = np.ascontiguousarray(
        d['L0_ffn_b1'].reshape(8, 32, 128).transpose(0, 2, 1))
    w['b2'] = np.ascontiguousarray(
        d['L0_ffn_b2'].reshape(8, 8, 128).transpose(0, 2, 1))
    def packqk1(W, b):
        out = np.zeros((1024, 8, 64), np.float32)
        ob = np.zeros((32, 16), np.float32)
        for p in range(8):
            out[:, p, 0:32] = W[:, (2 * p) * 32:(2 * p) * 32 + 32]
            out[:, p, 32:64] = W[:, (2 * p + 1) * 32:(2 * p + 1) * 32 + 32]
        for h in range(16):
            ob[:, h] = b[h * 32:(h + 1) * 32]
        return (np.ascontiguousarray(out.reshape(8, 128, 8, 64).transpose(1, 0, 2, 3)),
                ob)
    w['l1_wq'], w['l1_bq'] = packqk1(d['L1_attn_wq'], d['L1_attn_bq'])
    w['l1_wk'], w['l1_bk'] = packqk1(d['L1_attn_wk'], d['L1_attn_bk'])
    w['l1_wkT'] = chunk_lhsT(d['L1_attn_wk'])
    w['l1_wvT'] = chunk_lhsT(d['L1_attn_wv'])
    w['l1_bv'] = d['L1_attn_bv'].reshape(1, 1024).copy()
    w['l1_bk_row'] = d['L1_attn_bk'].reshape(1, 512).copy()
    w['l1_wo'] = chunk_lhsT(d['L1_attn_wo'])
    w['l1_bo'] = np.ascontiguousarray(d['L1_attn_bo'].reshape(8, 128).T)
    w['wgT'] = chunk_lhsT(d['L1_ffn_w_gate'].T)
    w['wuT'] = chunk_lhsT(d['L1_ffn_w_up'].T)
    w['wdT'] = chunk_lhsT(d['L1_ffn_w_down'].T)
    return w


def declare_hbm(nc, v2=False):
    H = {}
    def dt_(name, shape, dtype, kind):
        H[name] = nc.dram_tensor(name, shape, dtype, kind=kind).ap()
    return H


def setup_consts(nc, tk):
    E = {}
    ci = {}
    for name, shape, dtype in [
        ('c_iden', [128, 128], F32), ('c_ones_col', [128, 1], F32),
        ('c_ones_row', [1, 128], F32), ('c_mtri', [128, 128], F32),
        ('c_selA', [64, 273], F32), ('c_selB', [64, 273], F32),
        ('c_rmat', [8, 128], F32), ('c_rmatT', [128, 8], F32),
        ('c_idxmat', [128, 2048], F32), ('c_irev', [128, 8], F32),
        ('c_ones8k', [8192], F32),
    ]:
        ci[name] = nc.dram_tensor(name, shape, dtype, kind="ExternalInput").ap()
    E['iden_f32'] = nc.alloc_sbuf_tensor('iden_f32', [128, 128], F32).ap()
    E['iden_bf'] = nc.alloc_sbuf_tensor('iden_bf', [128, 128], BF16).ap()
    E['ones_col'] = nc.alloc_sbuf_tensor('ones_col', [128, 1], F32).ap()
    E['ones_row'] = nc.alloc_sbuf_tensor('ones_row', [1, 128], F32).ap()
    E['ones_row_bf'] = nc.alloc_sbuf_tensor('ones_row_bf', [1, 128], BF16).ap()
    E['mtri_bf'] = nc.alloc_sbuf_tensor('mtri_bf', [128, 128], BF16).ap()
    E['mtri_f32'] = nc.alloc_sbuf_tensor('mtri_f32', [128, 128], F32).ap()
    E['selA'] = nc.alloc_sbuf_tensor('selA', [64, 273], BF16).ap()
    E['selB'] = nc.alloc_sbuf_tensor('selB', [64, 273], BF16).ap()
    E['rmat'] = nc.alloc_sbuf_tensor('rmat', [8, 128], F32).ap()
    E['rmatT'] = nc.alloc_sbuf_tensor('rmatT', [128, 8], F32).ap()
    E['idxmat'] = nc.alloc_sbuf_tensor('idxmat', [128, 2048], F32).ap()
    E['irev'] = nc.alloc_sbuf_tensor('irev', [128, 8], F32).ap()
    E['cst'] = nc.alloc_sbuf_tensor('cst', [128, 8], F32).ap()
    E['ones8k_h'] = ci['c_ones8k']
    tk.dma('sp', E['iden_f32'][:], ci['c_iden'], writes=['iden_f32'])
    tk.dma('pool', E['iden_bf'][:], ci['c_iden'], writes=['iden_bf'])
    tk.dma('sp', E['ones_col'][:], ci['c_ones_col'], writes=['ones_col'])
    tk.dma('sp', E['ones_row'][:], ci['c_ones_row'], writes=['ones_row'])
    tk.dma('pool', E['ones_row_bf'][:], ci['c_ones_row'], writes=['ones_row_bf'])
    tk.dma('pool', E['mtri_bf'][:], ci['c_mtri'], writes=['mtri_bf'])
    tk.dma('sp', E['mtri_f32'][:], ci['c_mtri'], writes=['mtri_f32'])
    tk.dma('pool', E['selA'][:], ci['c_selA'], writes=['selA'])
    tk.dma('pool', E['selB'][:], ci['c_selB'], writes=['selB'])
    tk.dma('sp', E['rmat'][:], ci['c_rmat'], writes=['rmat'])
    tk.dma('sp', E['rmatT'][:], ci['c_rmatT'], writes=['rmatT'])
    tk.dma('sp', E['idxmat'][:], ci['c_idxmat'], writes=['idxmat'])
    tk.dma('sp', E['irev'][:], ci['c_irev'], writes=['irev'])
    tk.op('dve', lambda e: e.memset(E['cst'][:, 0:1], 1e-5), writes=['cst'])
    tk.op('dve', lambda e: e.memset(E['cst'][:, 1:2], 0.70710678), writes=['cst'])
    tk.op('dve', lambda e: e.memset(E['cst'][:, 2:8], 0.0), writes=['cst'])
    return E


def alloc_psum_banks(nc, n=8):
    return [nc.alloc_psum_tensor(f'pb{i}', [128, 512], F32).ap() for i in range(n)]



def _mk_alloc(nc, scope, prefix):
    def A(n, sh, dt):
        if scope is None:
            return nc.alloc_sbuf_tensor(prefix + n, sh, dt).ap()
        h = scope.enter_context(nc.sbuf_tensor(prefix + n, sh, dt))
        return h.ap()
    return A

def emit_ln(tk, E, nc, xsrc, xn_h, pe_h=None, routerT=None, tag='ln0', scope=None):
    pb = E['pb']
    iden = E['iden_f32']; ones_col = E['ones_col']; ones_row = E['ones_row']
    cst = E['cst']; irev = E['irev']
    g = tag + '_'
    A = _mk_alloc(nc, scope, g)
    xw = [A(f'xw{i}', [128, NDC * WIN], F32) for i in range(2)]
    xnw = [A(f'xnw{i}', [128, NDC * WIN], BF16) for i in range(2)]
    x2b = [A(f'x2b{i}', [128, 128], F32) for i in range(2)]
    stats = A('stats', [128, 8], F32)
    rows3 = A('rows3', [1, 384], F32)
    xnt = [A(f'xnt{i}', [128, 128], F32) for i in range(2)]
    rt = pe_h is not None
    if rt:
        routerT_sb = A('routerT_sb', [128, 64], F32)
        negSr = A('negSr', [1, 8], F32)
        srcol = A('srcol', [8, 1], F32)
        lgt = A('lgt', [128, 8], F32)
        elg = A('elg', [128, 8], F32)
        eqt = A('eqt', [128, 8], F32)
        eqt2 = A('eqt2', [128, 8], F32)
        lgtx = A('lgtx', [128, 4], F32)
        pecols = A('pecols', [128, 2], F32)
        peacc = A('peacc', [2, 4096], F32)
        tk.dma('sp', routerT_sb[:], routerT.rearrange('p a b -> p (a b)'),
               writes=['routerT_sb'])
        for dc in range(NDC):
            tk.mm(pb[5][0:8, 0:1], routerT_sb[:, dc * 8:(dc + 1) * 8], ones_col[:],
                  dc == 0, dc == 7, reads=['routerT_sb', 'ones_col'], writes=['pb5'])
        tk.ts(srcol[:], pb[5][0:8, 0:1], -1.0, None, ALU.mult,
              reads=['pb5'], writes=['srcol'])
        tk.tr(pb[6][0:1, 0:8], srcol[:], iden[0:8, 0:8],
              reads=['srcol', 'iden_f32'], writes=['pb6'])
        tk.copy(negSr[:], pb[6][0:1, 0:8], reads=['pb6'], writes=['negSr'])

    for w in range(NW):
        p = w % 2
        for dc in range(NDC):
            tk.dma('sp', xw[p][:, dc * WIN:(dc + 1) * WIN],
                   xsrc[dc * 128:(dc + 1) * 128, w * WIN:(w + 1) * WIN],
                   writes=[f'xw{p}_{dc}'], label=f'ldx_w{w}_d{dc}')
        for tau in range(NTAU):
            i = w * NTAU + tau
            xt_ap = [xw[p][:, dc * WIN + tau * 128: dc * WIN + (tau + 1) * 128]
                     for dc in range(NDC)]
            for dc in range(NDC):
                xq = x2b[dc % 2]
                tk.act(xq[:], xt_ap[dc], AF.Square,
                       reads=[f'xw{p}_{dc}'], writes=[f'x2b{dc % 2}'])
                tk.mm(pb[0][:, 0:1], xt_ap[dc], ones_col[:], dc == 0, dc == 7,
                      reads=[f'xw{p}_{dc}', 'ones_col'], writes=['pb0'])
                tk.mm(pb[4][:, 0:1], xq[:], ones_col[:], dc == 0, dc == 7,
                      reads=[f'x2b{dc % 2}', 'ones_col'], writes=['pb1'])
            tk.ts(stats[:, 0:1], pb[0][:, 0:1], 1.0 / D, None, ALU.mult,
                  reads=['pb0'], writes=['stats'])
            tk.ts(stats[:, 1:2], pb[4][:, 0:1], 1.0 / D, None, ALU.mult,
                  reads=['pb1'], writes=['stats'])
            tk.tt(stats[:, 3:4], stats[:, 0:1], stats[:, 0:1], ALU.mult,
                  reads=['stats'], writes=['stats'])
            tk.tt(stats[:, 2:3], stats[:, 1:2], stats[:, 3:4], ALU.subtract,
                  reads=['stats'], writes=['stats'])
            tk.act(stats[:, 4:5], stats[:, 2:3], AF.Ln, bias=cst[:, 0:1],
                   reads=['stats', 'cst'], writes=['stats'])
            tk.act(stats[:, 5:6], stats[:, 4:5], AF.Exp, scale=-0.5,
                   reads=['stats'], writes=['stats'])
            tk.stt(stats[:, 6:7], stats[:, 0:1], -1.0, stats[:, 5:6],
                   ALU.mult, ALU.mult, reads=['stats'], writes=['stats'])
            tk.tr(pb[2][0:1, 0:128], stats[:, 5:6], iden[:],
                  reads=['stats', 'iden_f32'], writes=['pb2'])
            tk.tr(pb[2][0:1, 128:256], stats[:, 6:7], iden[:],
                  reads=['stats', 'iden_f32'], writes=['pb2'])
            tk.tr(pb[2][0:1, 256:384], stats[:, 0:1], iden[:],
                  reads=['stats', 'iden_f32'], writes=['pb2'])
            tk.copy(rows3[:], pb[2][0:1, 0:384], reads=['pb2'], writes=['rows3'])
            tk.mm(pb[3][:, 0:128], ones_row[:], rows3[0:1, 0:128], True, True,
                  reads=['ones_row', 'rows3'], writes=['pb3'])
            tk.mm(pb[3][:, 128:256], ones_row[:], rows3[0:1, 128:256], True, True,
                  reads=['ones_row', 'rows3'], writes=['pb3'])
            for dc in range(NDC):
                xt = xnt[dc % 2]
                tk.tt(xt[:], xt_ap[dc], pb[3][:, 0:128], ALU.mult,
                      reads=[f'xw{p}_{dc}', 'pb3'], writes=[f'xnt{dc % 2}'])
                tk.tt(xnw[p][:, dc * WIN + tau * 128: dc * WIN + (tau + 1) * 128],
                      xt[:], pb[3][:, 128:256], ALU.add,
                      reads=[f'xnt{dc % 2}', 'pb3'], writes=[f'xnw{p}_{dc}_{tau}'])
            if rt:
                for dc in range(NDC):
                    tk.mm(pb[4][:, 0:8], xt_ap[dc], routerT_sb[:, dc * 8:(dc + 1) * 8],
                          dc == 0, False, reads=[f'xw{p}_{dc}', 'routerT_sb'],
                          writes=['pb4'])
                tk.mm(pb[4][:, 0:8], rows3[0:1, 256:384], negSr[:], False, True,
                      reads=['rows3', 'negSr'], writes=['pb4'])
                tk.ts(lgt[:], pb[4][:, 0:8], stats[:, 5:6], None, ALU.mult,
                      reads=['pb4', 'stats'], writes=['lgt'])
                tk.red(lgtx[:, 0:1], lgt[:], ALU.max, negate=True,
                       reads=['lgt'], writes=['lgtx'])
                tk.act(elg[:], lgt[:], AF.Exp, bias=lgtx[:, 0:1],
                       reads=['lgt', 'lgtx'], writes=['elg'])
                tk.red(lgtx[:, 1:2], elg[:], ALU.add, reads=['elg'], writes=['lgtx'])
                tk.recip(pecols[:, 0:1], lgtx[:, 1:2], reads=['lgtx'], writes=['pecols'])
                tk.ts(eqt[:], lgt[:], lgtx[:, 0:1], 0.0, ALU.add, op1=ALU.is_ge,
                      reads=['lgt', 'lgtx'], writes=['eqt'])
                tk.tt(eqt2[:], eqt[:], irev[:], ALU.mult,
                      reads=['eqt', 'irev'], writes=['eqt2'])
                tk.red(lgtx[:, 2:3], eqt2[:], ALU.max, reads=['eqt2'], writes=['lgtx'])
                tk.ts(pecols[:, 1:2], lgtx[:, 2:3], -1.0, 7.0, ALU.mult, op1=ALU.add,
                      reads=['lgtx'], writes=['pecols'])
                tk.tr(pb[6][0:2, 128:256], pecols[:], iden[:],
                      reads=['pecols', 'iden_f32'], writes=['pb6'])
                tk.copy(peacc[:, i * 128:(i + 1) * 128], pb[6][0:2, 128:256],
                        reads=['pb6'], writes=[f'peacc_{i}'])
        for dc in range(NDC):
            tk.dma('sp', xn_h[dc * 128:(dc + 1) * 128, w * WIN:(w + 1) * WIN],
                   xnw[p][:, dc * WIN:(dc + 1) * WIN],
                   reads=[f'xnw{p}_{dc}_{t}' for t in range(NTAU)],
                   writes=[f'Hxn_{tag}_{w}_{dc}'],
                   label=f'stxn_w{w}_d{dc}')
        yield
    if rt:
        tk.dma('sp', pe_h, peacc[:],
               reads=[f'peacc_{i}' for i in range(NW * NTAU)])


CAP = 5120

def emit_routing(tk, E, nc, allpe, csel, cselE, kmask_h, scope=None):
    pb = E['pb']
    rmat, rmatT, idxmat = E['rmat'], E['rmatT'], E['idxmat']
    g = 'rt_'
    A = _mk_alloc(nc, scope, g)
    P16 = A('P16', [16, 2048], F32)
    E16 = A('E16', [16, 2048], F32)
    Mbig = A('M', [128, 2048], F32)
    eqm = A('eqm', [128, 2048], F32)
    big1 = A('big1', [128, 2048], F32)
    big2 = A('big2', [128, 2048], F32)
    Kbig = A('K', [128, 2048], F32)
    tmp16 = A('tmp16', [16, 2048], F32)
    tmp16b = A('tmp16b', [16, 2048], F32)
    part = A('part', [128, 1], F32)
    sc8 = A('sc8', [8, 12], F32)
    colb = A('colb', [128, 4], F32)
    cselb = A('csel', [128, 2], F32)
    cselEb = A('cselE', [128, 16], F32)
    k0t = A('k0t', [2, 2048], F32)
    f0 = A('f0', [2, 2048], F32)

    tk.dma('sp', cselb[:], csel, writes=['cselb'])
    tk.dma('sp', cselEb[:], cselE, writes=['cselEb'])
    for i in range(8):
        tk.dma('sp', P16[2 * i:2 * i + 1, :], allpe[2 * i:2 * i + 1, 0:2048],
               writes=['P16'])
        tk.dma('sp', P16[2 * i + 1:2 * i + 2, :], allpe[2 * i:2 * i + 1, 2048:4096],
               writes=['P16'])
        tk.dma('sp', E16[2 * i:2 * i + 1, :], allpe[2 * i + 1:2 * i + 2, 0:2048],
               writes=['E16'])
        tk.dma('sp', E16[2 * i + 1:2 * i + 2, :], allpe[2 * i + 1:2 * i + 2, 2048:4096],
               writes=['E16'])
    for e in range(8):
        tk.ts(tmp16[:], E16[:], float(e), None, ALU.is_equal,
              reads=['E16'], writes=['tmp16'])
        tk.stt(tmp16b[:], P16[:], 1.0, tmp16[:], ALU.add, ALU.mult,
               reads=['P16', 'tmp16'], writes=['tmp16b'])
        tk.ts(tmp16b[:], tmp16b[:], -1.0, None, ALU.add,
              reads=['tmp16b'], writes=['tmp16b'])
        tk.dma('sp', Mbig[e * 16:(e + 1) * 16, :], tmp16b[:],
               reads=['tmp16b'], writes=['Mbig'], label=f'mcp{e}')
    def collapse(dst, rkey):
        tk.mm(pb[5][0:8, 0:1], rmatT[:], part[:], True, True,
              reads=['rmatT', 'part'], writes=['pb5'])
        tk.copy(dst, pb[5][0:8, 0:1], reads=['pb5'], writes=['sc8'])
    def bcast8(src, dstcol):
        tk.mm(pb[6][:, 0:1], rmat[:], src, True, True,
              reads=['rmat', 'sc8'], writes=['pb6'])
        tk.copy(dstcol, pb[6][:, 0:1], reads=['pb6'], writes=['colb'])
    tk.ts(big1[:], Mbig[:], -0.5, None, ALU.is_ge, op1=ALU.add, accum=part[:],
          reads=['Mbig'], writes=['big1', 'part'])
    collapse(sc8[:, 6:7], 'tot')
    tk.memset(sc8[:, 0:1], 0.0, writes=['sc8'])
    tk.memset(sc8[:, 1:2], 1.5, writes=['sc8'])
    for it in range(40):
        tk.tt(sc8[:, 5:6], sc8[:, 0:1], sc8[:, 1:2], ALU.add,
              reads=['sc8'], writes=['sc8'])
        tk.ts(sc8[:, 2:3], sc8[:, 5:6], 0.5, None, ALU.mult,
              reads=['sc8'], writes=['sc8'])
        bcast8(sc8[:, 2:3], colb[:, 0:1])
        tk.ts(big1[:], Mbig[:], colb[:, 0:1], None, ALU.is_ge, op1=ALU.add,
              accum=part[:], reads=['Mbig', 'colb'], writes=['big1', 'part'])
        collapse(sc8[:, 3:4], 'cnt')
        tk.ts(sc8[:, 4:5], sc8[:, 3:4], -float(CAP), 0.0, ALU.add, op1=ALU.is_ge,
              reads=['sc8'], writes=['sc8'])
        tk.tt(sc8[:, 5:6], sc8[:, 2:3], sc8[:, 0:1], ALU.subtract,
              reads=['sc8'], writes=['sc8'])
        tk.tt(sc8[:, 5:6], sc8[:, 5:6], sc8[:, 4:5], ALU.mult,
              reads=['sc8'], writes=['sc8'])
        tk.tt(sc8[:, 0:1], sc8[:, 0:1], sc8[:, 5:6], ALU.add,
              reads=['sc8'], writes=['sc8'])
        tk.tt(sc8[:, 5:6], sc8[:, 1:2], sc8[:, 2:3], ALU.subtract,
              reads=['sc8'], writes=['sc8'])
        tk.tt(sc8[:, 5:6], sc8[:, 5:6], sc8[:, 4:5], ALU.mult,
              reads=['sc8'], writes=['sc8'])
        tk.tt(sc8[:, 1:2], sc8[:, 2:3], sc8[:, 5:6], ALU.add,
              reads=['sc8'], writes=['sc8'])
    bcast8(sc8[:, 0:1], colb[:, 0:1])
    tk.ts(big1[:], Mbig[:], colb[:, 0:1], None, ALU.is_gt, op1=ALU.add,
          accum=part[:], reads=['Mbig', 'colb'], writes=['big1', 'part'])
    collapse(sc8[:, 7:8], 'ngt')
    tk.ts(sc8[:, 8:9], sc8[:, 7:8], -1.0, float(CAP), ALU.mult, op1=ALU.add,
          reads=['sc8'], writes=['sc8'])
    tk.ts(eqm[:], Mbig[:], colb[:, 0:1], None, ALU.is_equal,
          reads=['Mbig', 'colb'], writes=['eqm'])
    tk.memset(sc8[:, 9:10], -1.0, writes=['sc8'])
    tk.memset(sc8[:, 10:11], 32768.0, writes=['sc8'])
    for it in range(18):
        tk.tt(sc8[:, 5:6], sc8[:, 9:10], sc8[:, 10:11], ALU.add,
              reads=['sc8'], writes=['sc8'])
        tk.ts(sc8[:, 2:3], sc8[:, 5:6], 0.5, None, ALU.mult,
              reads=['sc8'], writes=['sc8'])
        bcast8(sc8[:, 2:3], colb[:, 1:2])
        tk.ts(big1[:], idxmat[:], colb[:, 1:2], None, ALU.is_le,
              reads=['idxmat', 'colb'], writes=['big1'])
        tk.tt(big2[:], big1[:], eqm[:], ALU.mult,
              reads=['big1', 'eqm'], writes=['big2'])
        tk.ts(big2[:], big2[:], 1.0, None, ALU.mult, op1=ALU.add, accum=part[:],
              reads=['big2'], writes=['big2', 'part'])
        collapse(sc8[:, 3:4], 'icnt')
        tk.ts(sc8[:, 4:5], sc8[:, 3:4], sc8[:, 8:9], 0.0, ALU.subtract, op1=ALU.is_le,
              reads=['sc8'], writes=['sc8'])
        tk.tt(sc8[:, 5:6], sc8[:, 2:3], sc8[:, 9:10], ALU.subtract,
              reads=['sc8'], writes=['sc8'])
        tk.tt(sc8[:, 5:6], sc8[:, 5:6], sc8[:, 4:5], ALU.mult,
              reads=['sc8'], writes=['sc8'])
        tk.tt(sc8[:, 9:10], sc8[:, 9:10], sc8[:, 5:6], ALU.add,
              reads=['sc8'], writes=['sc8'])
        tk.tt(sc8[:, 5:6], sc8[:, 10:11], sc8[:, 2:3], ALU.subtract,
              reads=['sc8'], writes=['sc8'])
        tk.tt(sc8[:, 5:6], sc8[:, 5:6], sc8[:, 4:5], ALU.mult,
              reads=['sc8'], writes=['sc8'])
        tk.tt(sc8[:, 10:11], sc8[:, 2:3], sc8[:, 5:6], ALU.add,
              reads=['sc8'], writes=['sc8'])
    bcast8(sc8[:, 9:10], colb[:, 1:2])
    tk.ts(Kbig[:], Mbig[:], colb[:, 0:1], None, ALU.is_gt,
          reads=['Mbig', 'colb'], writes=['Kbig'])
    tk.ts(big1[:], idxmat[:], colb[:, 1:2], None, ALU.is_le,
          reads=['idxmat', 'colb'], writes=['big1'])
    tk.tt(big1[:], big1[:], eqm[:], ALU.mult,
          reads=['big1', 'eqm'], writes=['big1'])
    tk.tt(Kbig[:], Kbig[:], big1[:], ALU.add,
          reads=['Kbig', 'big1'], writes=['Kbig'])
    for q in range(4):
        tk.mm(pb[7][0:2, 0:512], cselb[:], Kbig[:, q * 512:(q + 1) * 512],
              True, True, reads=['cselb', 'Kbig'], writes=['pb7'])
        tk.ts(f0[:, q * 512:(q + 1) * 512], pb[7][0:2, 0:512], -1.0, 1.0,
              ALU.mult, op1=ALU.add, reads=['pb7'], writes=['f0'])
    for e in range(8):
        for q in range(4):
            tk.mm(pb[7][0:2, 0:512], cselEb[:, 2 * e:2 * e + 2],
                  Kbig[:, q * 512:(q + 1) * 512], True, True,
                  reads=['cselEb', 'Kbig'], writes=['pb7'])
            tk.copy(k0t[:, q * 512:(q + 1) * 512], pb[7][0:2, 0:512],
                    reads=['pb7'], writes=['k0t'], eng='act')
        if e == 0:
            tk.tt(k0t[:], k0t[:], f0[:], ALU.add, reads=['k0t', 'f0'],
                  writes=['k0t'])
        tk.dma('sp', kmask_h[e:e + 1, :], k0t[:], reads=['k0t'], label=f'stk{e}')


SQRT2 = float(np.sqrt(2.0))

def emit_attn0(tk, E, nc, xn_h, s_h, W, scope=None):
    pb = E['pb']
    iden_bf = E['iden_bf']; cst = E['cst']; mtri = E['mtri_bf']
    selA, selB = E['selA'], E['selB']
    g = 'a0_'
    A = _mk_alloc(nc, scope, g)
    wq = A('wq', [128, 4096], BF16); wk = A('wk', [128, 4096], BF16)
    wkT = A('wkT', [128, 2048], BF16)
    wvT = A('wvT', [128, 8192], BF16)
    wout = A('wout', [128, 8192], BF16)

    def loadw(sb, nm, key):
        tk.dma('pool', sb[:], W[nm].rearrange('p a b -> p (a b)') if W[nm].ndim == 3
               else W[nm].rearrange('p a b c -> p (a b c)'), writes=[key])
    loadw(wkT, 'wkeT', 'wkT')
    loadw(wvT, 'wveT', 'wvT')
    loadw(wq, 'wq', 'wq')
    loadw(wout, 'wout', 'wout')
    SAf = A('SAf', [128, 1040], F32); SBf = A('SBf', [128, 1040], F32)
    SCf = A('SCf', [17, 1040], F32)
    SAb = A('SAb', [128, 1040], BF16); SBb = A('SBb', [128, 1040], BF16)
    SCb = A('SCb', [17, 1040], BF16)
    for tname, t in [('SAf', SAf), ('SBf', SBf), ('SCf', SCf)]:
        tk.memset(t[:], 0.0, writes=[f'{tname}_{h}' for h in range(16)])
    xnw = [A(f'xnw{i}', [128, NDC * WIN], BF16) for i in range(2)]
    qsw = [A(f'qsw{i}', [64, 8 * WIN], BF16) for i in range(2)]
    ksw = [A(f'ksw{i}', [64, 8 * WIN], BF16) for i in range(2)]
    kstm = [A(f'kstm{i}', [128, 4 * 256], BF16) for i in range(2)]
    v1sb = [A(f'v1sb{i}', [128, 4 * 1040], BF16) for i in range(2)]
    kftm0 = A('kftm', [128, 4368], BF16)
    kftm = [kftm0, kftm0]
    qfA = [A(f'qfA{i}', [128, 128], BF16) for i in range(2)]
    qfB = [A(f'qfB{i}', [128, 128], BF16) for i in range(2)]
    qfC = [A(f'qfC{i}', [17, 128], BF16) for i in range(2)]
    p1sb = [A(f'p1sb{i}', [128, 128], BF16) for i in range(2)]
    asq = [A(f'asq{i}', [128, 128], F32) for i in range(2)]
    atb = [A(f'atb{i}', [128, 128], BF16) for i in range(2)]
    zc = A('zc', [128, 2], F32)
    otm = [A(f'otm{i}', [128, 1024], BF16) for i in range(2)]
    off = A('off', [128, 8 * WIN], BF16)
    sfw0 = A('sfw', [128, NDC * WIN], BF16)
    sfw = [sfw0, sfw0]
    ones_h = E['ones8k_h']
    for i in range(2):
        for r in (16, 48):
            tk.dma('pool', qsw[i][r:r + 1, :], ones_h[0:8 * WIN],
                   writes=[f'qsw{i}_ones'])
        tk.memset(v1sb[i].rearrange('p (t h c) -> p t h c', t=4, h=16)[:, :, :, 64:65],
                  1.0, writes=[f'v1sb{i}_ones'], eng='pool')
    tk.memset(kftm0.rearrange('p (h c) -> p h c', h=16)[:, :, 0:1],
              1.0, writes=['kftm_ones'], eng='pool')

    def proj_qk(wsb, wkey, dst, dkey, xbuf, xkeys):
        for p8 in range(8):
            for dc in range(NDC):
                tk.mm(pb[0][0:64, 0:512],
                      wsb[:, (dc * 8 + p8) * 64:(dc * 8 + p8) * 64 + 64],
                      xbuf[:, dc * WIN:(dc + 1) * WIN], dc == 0, dc == 7,
                      reads=[wkey, xkeys[dc]], writes=['pb0'])
            for rb in (0, 32):
                tk.copy(dst[rb:rb + 16, p8 * WIN:(p8 + 1) * WIN],
                        pb[0][rb:rb + 16, 0:512], reads=['pb0'],
                        writes=[f'{dkey}_{p8}'], eng='act')

    def proj_ktm2(wsb, wkey, dstbuf, dkey, xbuf, xkeys):
        for tau in range(NTAU):
            for dc in range(NDC):
                tk.mm(pb[0][:, 0:256],
                      xbuf[:, dc * WIN + tau * 128:dc * WIN + (tau + 1) * 128],
                      wsb[:, dc * 256:(dc + 1) * 256],
                      dc == 0, dc == 7, reads=[wkey, xkeys[dc]], writes=['pb0'])
            tk.copy(dstbuf[:, tau * 256:(tau + 1) * 256], pb[0][:, 0:256],
                    reads=['pb0'], writes=[f'{dkey}_{tau}'], eng='act')

    def proj_vtm(wsb, wkey, dstbuf, dkey, xbuf, xkeys):
        for tau in range(NTAU):
            for half in range(2):
                bank = pb[2] if half == 0 else pb[3]
                bk = 'pb2' if half == 0 else 'pb3'
                for dc in range(NDC):
                    tk.mm(bank[:, 0:512],
                          xbuf[:, dc * WIN + tau * 128:dc * WIN + (tau + 1) * 128],
                          wsb[:, dc * 1024 + half * 512:dc * 1024 + half * 512 + 512],
                          dc == 0, dc == 7, reads=[wkey, xkeys[dc]], writes=[bk])
                dst = dstbuf.rearrange('p (t h c) -> p t h c', t=4, h=16)[
                    :, tau, half * 8:(half + 1) * 8, 0:64]
                tk.copy(dst, bank[:, 0:512].rearrange('p (h c) -> p h c', h=8),
                        reads=[bk], writes=[f'{dkey}_{tau}'], eng='act')

    def build_kftm(buf, bkey, ksbuf, kskey, tau):
        kv = ksbuf[:, tau * 256:(tau + 1) * 256].rearrange('p (h f) -> p h f', h=16)
        ov = buf.rearrange('p (h c) -> p h c', h=16)
        tk.ts(ov[:, :, 1:17], kv, SQRT2, None, ALU.mult,
              reads=[kskey], writes=[bkey], eng='pool')
        for a in range(16):
            in1 = kv[:, :, a:a + 1].to_broadcast([128, 16, 16])
            tk.tt(ov[:, :, 17 + a * 16:17 + (a + 1) * 16], kv, in1, ALU.mult,
                  reads=[kskey], writes=[bkey], eng='pool')

    def state_update(kbuf, kkey, vbuf, vkey, tau, tag):
        for h in range(16):
            kc = kbuf[:, h * 273:(h + 1) * 273]
            v1 = vbuf.rearrange('p (t h c) -> p t h c', t=4, h=16)[
                :, tau, h, :].rearrange('p c -> p c')
            tk.mm(pb[5][:, 65:130], kc[:, 0:128], v1, True, True,
                  reads=[kkey, kkey + '_ones', vkey, vkey.rsplit('_', 1)[0] + '_ones'],
                  writes=['pb6'])
            tk.tt(SAf[:, h * 65:(h + 1) * 65], SAf[:, h * 65:(h + 1) * 65],
                  pb[5][:, 65:130], ALU.add, reads=['pb6', f'SAf_{h}'],
                  writes=[f'SAf_{h}'])
            tk.copy(SAb[:, h * 65:(h + 1) * 65], SAf[:, h * 65:(h + 1) * 65],
                    reads=[f'SAf_{h}'], writes=[f'SAb_{h}'], eng='pool')
            tk.mm(pb[5][:, 130:195], kc[:, 128:256], v1, True, True,
                  reads=[kkey, vkey], writes=['pb6'])
            tk.tt(SBf[:, h * 65:(h + 1) * 65], SBf[:, h * 65:(h + 1) * 65],
                  pb[5][:, 130:195], ALU.add, reads=['pb6', f'SBf_{h}'],
                  writes=[f'SBf_{h}'])
            tk.copy(SBb[:, h * 65:(h + 1) * 65], SBf[:, h * 65:(h + 1) * 65],
                    reads=[f'SBf_{h}'], writes=[f'SBb_{h}'], eng='pool')
            tk.mm(pb[5][0:17, 195:260], kc[:, 256:273], v1, True, True,
                  reads=[kkey, vkey], writes=['pb6'])
            tk.tt(SCf[:, h * 65:(h + 1) * 65], SCf[:, h * 65:(h + 1) * 65],
                  pb[5][0:17, 195:260], ALU.add, reads=['pb6', f'SCf_{h}'],
                  writes=[f'SCf_{h}'])
            tk.copy(SCb[:, h * 65:(h + 1) * 65], SCf[:, h * 65:(h + 1) * 65],
                    reads=[f'SCf_{h}'], writes=[f'SCb_{h}'], eng='pool')

    def load_xn(p, w):
        for dc in range(NDC):
            tk.dma('sp', xnw[p][:, dc * WIN:(dc + 1) * WIN],
                   xn_h[dc * 128:(dc + 1) * 128, w * WIN:(w + 1) * WIN],
                   reads=[f'Hxn_ln0_{w}_{dc}'],
                   writes=[f'xnw{p}_{dc}'], label=f'ldxn_w{w}_d{dc}')

    load_xn(0, 0)
    xkeys0 = [f'xnw0_{dc}' for dc in range(NDC)]
    proj_ktm2(wkT, 'wkT', kstm[1], 'kstm1', xnw[0], xkeys0)
    yield
    proj_vtm(wvT, 'wvT', v1sb[1], 'v1sb1', xnw[0], xkeys0)
    yield
    for tau in range(NTAU):
        build_kftm(kftm[1], 'kftm', kstm[1], f'kstm1_{tau}', tau)
        state_update(kftm[1], 'kftm', v1sb[1], f'v1sb1_{tau}', tau, 'enc')
        yield

    loadw(wk, 'wk', 'wk')
    loadw(wkT, 'wkT', 'wkT')
    loadw(wvT, 'wvT', 'wvT')
    for w in range(NW):
        p = w % 2
        if w > 0:
            load_xn(p, w)
        xkeys = [f'xnw{p}_{dc}' for dc in range(NDC)]
        proj_qk(wq, 'wq', qsw[p], f'qsw{p}', xnw[p], xkeys)
        yield
        proj_qk(wk, 'wk', ksw[p], f'ksw{p}', xnw[p], xkeys)
        yield
        proj_ktm2(wkT, 'wkT', kstm[p], f'kstm{p}', xnw[p], xkeys)
        proj_vtm(wvT, 'wvT', v1sb[p], f'v1sb{p}', xnw[p], xkeys)
        yield
        for tau in range(NTAU):
            build_kftm(kftm[p], 'kftm', kstm[p], f'kstm{p}_{tau}', tau)
            for h in range(16):
                p8, rb = h // 2, 32 * (h % 2)
                hp = h % 2
                qsl = qsw[p][rb:rb + 16, p8 * WIN + tau * 128:p8 * WIN + (tau + 1) * 128]
                ksl = ksw[p][rb:rb + 16, p8 * WIN + tau * 128:p8 * WIN + (tau + 1) * 128]
                q17 = qsw[p][rb:rb + 17, p8 * WIN + tau * 128:p8 * WIN + (tau + 1) * 128]
                tk.mm(pb[4][:, 0:128], ksl, qsl, True, True,
                      reads=[f'ksw{p}_{p8}', f'qsw{p}_{p8}'], writes=['pb4'])
                tk.act(asq[hp][:], pb[4][:, 0:128], AF.Square, bias=cst[:, 1:2],
                       reads=['pb4', 'cst'], writes=[f'asq{hp}'])
                tk.stt(atb[hp][:], asq[hp][:], 0.5, mtri[:], ALU.add, ALU.mult,
                       reads=[f'asq{hp}', 'mtri_bf'], writes=[f'atb{hp}'])
                for (cn, qf, c0, c1, m) in [('A', qfA[hp], 0, 128, 128),
                                            ('B', qfB[hp], 128, 256, 128),
                                            ('C', qfC[hp], 256, 273, 17)]:
                    tk.mm(pb[2][0:m, 0:128], selA[rb:rb + 17, c0:c1], q17,
                          True, True, reads=['selA', f'qsw{p}_{p8}', f'qsw{p}_ones'],
                          writes=['pb2'])
                    tk.mm(pb[3][0:m, 0:128], selB[rb:rb + 17, c0:c1], q17,
                          True, True, reads=['selB', f'qsw{p}_{p8}', f'qsw{p}_ones'],
                          writes=['pb3'])
                    tk.copy(p1sb[hp][0:m, :], pb[2][0:m, 0:128],
                            reads=['pb2'], writes=[f'p1sb{hp}'], eng='act')
                    tk.tt(qf[0:m, :], pb[3][0:m, 0:128], p1sb[hp][0:m, :],
                          ALU.mult, reads=['pb3', f'p1sb{hp}'], writes=[f'qf{cn}{hp}'])
                v1 = v1sb[p].rearrange('p (t h c) -> p t h c', t=4, h=16)[:, tau, h, :]
                tk.mm(pb[5][:, 0:65], qfA[hp][:], SAb[:, h * 65:(h + 1) * 65],
                      True, False, reads=[f'qfA{hp}', f'SAb_{h}'], writes=['pb5'])
                tk.mm(pb[5][:, 0:65], qfB[hp][:], SBb[:, h * 65:(h + 1) * 65],
                      False, False, reads=[f'qfB{hp}', f'SBb_{h}'], writes=['pb5'])
                tk.mm(pb[5][:, 0:65], qfC[hp][:], SCb[:, h * 65:(h + 1) * 65],
                      False, False, reads=[f'qfC{hp}', f'SCb_{h}'], writes=['pb5'])
                tk.mm(pb[5][:, 0:65], atb[hp][:], v1, False, True,
                      reads=[f'atb{hp}', f'v1sb{p}_{tau}', f'v1sb{p}_ones'],
                      writes=['pb5'])
                tk.ts(zc[:, 0:1], pb[5][:, 64:65], 1e-12, None, ALU.add,
                      reads=['pb5'], writes=['zc'])
                tk.recip(zc[:, 1:2], zc[:, 0:1], reads=['zc'], writes=['zc'])
                tk.ts(otm[tau % 2][:, h * 64:(h + 1) * 64], pb[5][:, 0:64],
                      zc[:, 1:2], None, ALU.mult, reads=['pb5', 'zc'],
                      writes=[f'otm{tau % 2}_{h}'])
                kc = kftm[p][:, h * 273:(h + 1) * 273]
                tk.mm(pb[5][:, 65:130], kc[:, 0:128], v1, True, True,
                      reads=['kftm', 'kftm_ones', f'v1sb{p}_{tau}',
                             f'v1sb{p}_ones'], writes=['pb6'])
                tk.tt(SAf[:, h * 65:(h + 1) * 65], SAf[:, h * 65:(h + 1) * 65],
                      pb[5][:, 65:130], ALU.add, reads=['pb6', f'SAf_{h}'],
                      writes=[f'SAf_{h}'])
                tk.copy(SAb[:, h * 65:(h + 1) * 65], SAf[:, h * 65:(h + 1) * 65],
                        reads=[f'SAf_{h}'], writes=[f'SAb_{h}'], eng='pool')
                tk.mm(pb[5][:, 130:195], kc[:, 128:256], v1, True, True,
                      reads=['kftm', f'v1sb{p}_{tau}'], writes=['pb6'])
                tk.tt(SBf[:, h * 65:(h + 1) * 65], SBf[:, h * 65:(h + 1) * 65],
                      pb[5][:, 130:195], ALU.add, reads=['pb6', f'SBf_{h}'],
                      writes=[f'SBf_{h}'])
                tk.copy(SBb[:, h * 65:(h + 1) * 65], SBf[:, h * 65:(h + 1) * 65],
                        reads=[f'SBf_{h}'], writes=[f'SBb_{h}'], eng='pool')
                tk.mm(pb[5][0:17, 195:260], kc[:, 256:273], v1, True, True,
                      reads=['kftm', f'v1sb{p}_{tau}'], writes=['pb6'])
                tk.tt(SCf[:, h * 65:(h + 1) * 65], SCf[:, h * 65:(h + 1) * 65],
                      pb[5][0:17, 195:260], ALU.add, reads=['pb6', f'SCf_{h}'],
                      writes=[f'SCf_{h}'])
                tk.copy(SCb[:, h * 65:(h + 1) * 65], SCf[:, h * 65:(h + 1) * 65],
                        reads=[f'SCf_{h}'], writes=[f'SCb_{h}'], eng='pool')
            for hc in range(8):
                tk.tr(pb[7].bitcast(BF16)[:, 0:128],
                      otm[tau % 2][:, hc * 128:(hc + 1) * 128], iden_bf[:],
                      reads=[f'otm{tau % 2}_{hh}' for hh in range(2 * hc, 2 * hc + 2)]
                      + ['iden_bf'], writes=['pb7'])
                tk.copy(off[:, hc * WIN + tau * 128:hc * WIN + (tau + 1) * 128],
                        pb[7].bitcast(BF16)[:, 0:128], reads=['pb7'],
                        writes=[f'off_{hc}'], eng='act')
            yield
        for dm in range(NDC):
            for hc in range(NDC):
                tk.mm(pb[0][:, 0:512], wout[:, hc * 1024 + dm * 128:hc * 1024 + dm * 128 + 128],
                      off[:, hc * WIN:(hc + 1) * WIN], hc == 0, hc == 7,
                      reads=['wout', f'off_{hc}'], writes=['pb0'])
            tk.copy(sfw[p][:, dm * WIN:(dm + 1) * WIN], pb[0][:, 0:512],
                    reads=['pb0'], writes=[f'sfw_{dm}'], eng='act')
        for dm in range(NDC):
            tk.dma('sp', s_h[dm * 128:(dm + 1) * 128, w * WIN:(w + 1) * WIN],
                   sfw[p][:, dm * WIN:(dm + 1) * WIN], reads=[f'sfw_{dm}'],
                   label=f'sts_w{w}_d{dm}')
        yield


def emit_moe(tk, E, nc, xn_h, kmask_h, ge_h, W, scope=None):
    pb = E['pb']
    ones_row_bf = E['ones_row_bf']
    g = 'mo_'
    A = _mk_alloc(nc, scope, g)
    w1sb = A('w1sb', [128, 8 * 4096], BF16)
    w2sb = A('w2sb', [128, 32 * 1024], BF16)
    b1sb = A('b1sb', [128, 32], F32)
    b2sb = A('b2sb', [128, 8], F32)
    xnw = [A(f'xnw{i}', [128, NDC * WIN], BF16) for i in range(2)]
    hsb = A('hsb', [128, 32 * WIN], BF16)
    ysb = [A(f'ysb{i}', [128, WIN], F32) for i in range(2)]
    ymb = [A(f'ymb{i}', [128, WIN], BF16) for i in range(4)]
    mrow = A('mrow', [1, WIN], BF16)
    mbc = A('mbc', [128, WIN], BF16)
    HB = [0, 1, 6, 7]
    YB = [2, 3, 4, 5]

    def load_weights(e):
        for dc in range(NDC):
            tk.dma('pool', w1sb[:, dc * 4096:(dc + 1) * 4096],
                   W['w1'][e, :, dc, :], writes=[f'w1_{dc}'], label=f'ldw1_{e}_{dc}')
    def load_weights2(e):
        for q in range(8):
            tk.dma('pool', w2sb[:, q * 4096:(q + 1) * 4096],
                   W['w2'][e, :, 4 * q:4 * (q + 1), :].rearrange('p a b -> p (a b)'),
                   writes=[f'w2_{q}'], label=f'ldw2_{e}_{q}')
        tk.dma('sp', b1sb[:], W['b1'][e], writes=['b1sb'])
        tk.dma('sp', b2sb[:], W['b2'][e], writes=['b2sb'])

    load_weights(0)
    load_weights2(0)
    for e in range(8):
        for w in range(NW):
            p = w % 2
            for dc in range(NDC):
                tk.dma('sp', xnw[p][:, dc * WIN:(dc + 1) * WIN],
                       xn_h[dc * 128:(dc + 1) * 128, w * WIN:(w + 1) * WIN],
                       writes=[f'xnw{p}_{dc}'], label=f'mxn_{e}_{w}_{dc}')
            tk.dma('pool', mrow[:], kmask_h[e:e + 1, w * WIN:(w + 1) * WIN],
                   writes=['mrow'])
            tk.mm(pb[6][:, 0:512], ones_row_bf[:], mrow[:], True, True,
                  reads=['ones_row_bf', 'mrow'], writes=['pb6'])
            tk.copy(mbc[:], pb[6][:, 0:512], reads=['pb6'],
                    writes=['mbc'], eng='act')
            for fc in range(32):
                bi = HB[fc % 4]
                bank, bk = pb[bi], f'pb{bi}'
                for dc in range(NDC):
                    tk.mm(bank[:, 0:512],
                          w1sb[:, dc * 4096 + fc * 128:dc * 4096 + (fc + 1) * 128],
                          xnw[p][:, dc * WIN:(dc + 1) * WIN], dc == 0, dc == 7,
                          reads=[f'w1_{dc}', f'xnw{p}_{dc}'], writes=[bk])
                tk.act(hsb[:, fc * WIN:(fc + 1) * WIN], bank[:, 0:512], AF.Relu,
                       bias=b1sb[:, fc:fc + 1], reads=[bk, 'b1sb'],
                       writes=[f'h_{fc}'])
            if w == NW - 1 and e < 7:
                load_weights(e + 1)
            for dc in range(NDC):
                bi = YB[dc % 4]
                bank, bk = pb[bi], f'pb{bi}'
                for fc in range(32):
                    tk.mm(bank[:, 0:512],
                          w2sb[:, fc * 1024 + dc * 128:fc * 1024 + (dc + 1) * 128],
                          hsb[:, fc * WIN:(fc + 1) * WIN], fc == 0, fc == 31,
                          reads=[f'w2_{fc // 4}', f'h_{fc}'], writes=[bk])
                yp = dc % 2
                tk.ts(ysb[yp][:], bank[:, 0:512], b2sb[:, dc:dc + 1], None, ALU.add,
                      reads=[bk, 'b2sb'], writes=[f'ysb{yp}'])
                y4 = dc % 4
                tk.tt(ymb[y4][:], ysb[yp][:], mbc[:], ALU.mult,
                      reads=[f'ysb{yp}', 'mbc'], writes=[f'ymb{y4}'])
                tk.dma('sp', ge_h[e, dc * 128:(dc + 1) * 128, w * WIN:(w + 1) * WIN],
                       ymb[y4][:], reads=[f'ymb{y4}'], label=f'stg_{e}_{w}_{dc}')
        if e < 7:
            load_weights2(e + 1)


def emit_combine(tk, E, nc, xsrc, s_h, g_h, xdst, scope=None, tag='cb', ge=False):
    g = tag + '_'
    A = _mk_alloc(nc, scope, g)
    xw = [A(f'xw{i}', [128, NDC * WIN], F32) for i in range(2)]
    sw = [A(f'sw{i}', [128, NDC * WIN], BF16) for i in range(2)]
    NG = 8 if ge else 1
    gw = [A(f'gw{i}', [128, NG * WIN], BF16) for i in range(8)]
    gs = [A(f'gs{i}', [128, WIN], F32) for i in range(8)]
    sg = [A(f'sg{i}', [128, WIN], F32) for i in range(2)]
    gsl = [A(f'gsl{i}', [128, WIN], F32) for i in range(2)]
    xo = [A(f'xo{i}', [128, NDC * WIN], F32) for i in range(2)]
    for w in range(NW):
        p = w % 2
        for dc in range(NDC):
            sl = (slice(dc * 128, (dc + 1) * 128), slice(w * WIN, (w + 1) * WIN))
            tk.dma('act', xw[p][:, dc * WIN:(dc + 1) * WIN], xsrc[sl[0], sl[1]],
                   writes=[f'xw{p}_{dc}'])
            tk.dma('act', sw[p][:, dc * WIN:(dc + 1) * WIN], s_h[sl[0], sl[1]],
                   writes=[f'sw{p}_{dc}'])
            q4 = dc
            if ge:
                src = g_h[:, sl[0], sl[1]].transpose([1, 0, 2])
                tk.dma('sp', gw[q4].rearrange('p (e j) -> p e j', e=8), src,
                       writes=[f'gw{q4}_0'])
            else:
                tk.dma('sp', gw[q4][:, 0:WIN], g_h[sl[0], sl[1]],
                       writes=[f'gw{q4}_0'])
        for dc in range(NDC):
            q = dc % 2
            q4 = dc
            if ge:
                v = gw[q4].rearrange('p (e j) -> p j e', e=8)
                tk.red(gs[q4][:].unsqueeze(2), v, ALU.add,
                       reads=[f'gw{q4}_0'], writes=[f'gs{q4}'])
                gsrc, gk = gs[q4][:], f'gs{q4}'
            else:
                gsrc, gk = gw[q4][:, 0:WIN], f'gw{q4}_0'
            tk.act(gsl[q][:], gsrc, AF.Silu, reads=[gk], writes=[f'gsl{q}'])
            tk.tt(sg[q][:], sw[p][:, dc * WIN:(dc + 1) * WIN], gsl[q][:], ALU.mult,
                  reads=[f'sw{p}_{dc}', f'gsl{q}'], writes=[f'sg{q}'])
            tk.tt(xo[p][:, dc * WIN:(dc + 1) * WIN],
                  xw[p][:, dc * WIN:(dc + 1) * WIN], sg[q][:], ALU.add,
                  reads=[f'xw{p}_{dc}', f'sg{q}'], writes=[f'xo{p}_{dc}'])
            tk.dma('sp', xdst[dc * 128:(dc + 1) * 128, w * WIN:(w + 1) * WIN],
                   xo[p][:, dc * WIN:(dc + 1) * WIN], reads=[f'xo{p}_{dc}'],
                   label=f'stxo_{w}_{dc}')


def emit_attn1(tk, E, nc, xn_h, s_h, W, scope=None):
    pb = E['pb']
    iden_bf = E['iden_bf']; mtri = E['mtri_bf']
    g = 'a1_'
    A = _mk_alloc(nc, scope, g)
    wq = A('wq', [128, 4096], BF16); wk = A('wk', [128, 4096], BF16)
    wkT = A('wkT', [128, 4096], BF16)
    wvT = A('wvT', [128, 8192], BF16)
    wo = A('wo', [128, 8192], BF16)
    bq = A('bq', [32, 16], F32); bk = A('bk', [32, 16], F32)
    bo = A('bo', [128, 8], F32)
    bvr = A('bvr', [1, 1024], BF16); bkr = A('bkr', [1, 512], BF16)
    bvb = A('bvb', [128, 1024], BF16); bkb = A('bkb', [128, 512], BF16)
    for nm, sb in [('wq1', wq), ('wk1', wk), ('wkT1', wkT), ('wvT1', wvT),
                   ('wo1', wo)]:
        tk.dma('pool', sb[:], W[nm].rearrange('p a b -> p (a b)') if W[nm].ndim == 3
               else W[nm].rearrange('p a b c -> p (a b c)'), writes=[nm])
    tk.dma('sp', bq[:], W['bq1'], writes=['bq'])
    tk.dma('sp', bk[:], W['bk1'], writes=['bk'])
    tk.dma('sp', bo[:], W['bo1'], writes=['bo'])
    tk.dma('pool', bvr[:], W['bv1'], writes=['bvr'])
    tk.dma('pool', bkr[:], W['bk_row'], writes=['bkr'])
    for half in range(2):
        tk.mm(pb[6][:, 0:512], E['ones_row_bf'][:], bvr[0:1, half * 512:(half + 1) * 512],
              True, True, reads=['ones_row_bf', 'bvr'], writes=['pb6'])
        tk.copy(bvb[:, half * 512:(half + 1) * 512], pb[6][:, 0:512],
                reads=['pb6'], writes=['bvb'], eng='act')
    tk.mm(pb[6][:, 0:512], E['ones_row_bf'][:], bkr[0:1, :], True, True,
          reads=['ones_row_bf', 'bkr'], writes=['pb6'])
    tk.copy(bkb[:], pb[6][:, 0:512], reads=['pb6'], writes=['bkb'], eng='act')
    S1f = A('S1f', [32, 1040], F32); S1b = A('S1b', [32, 1040], BF16)
    tk.memset(S1f[:], 0.0, writes=[f'S1f_{h}' for h in range(16)])
    tk.memset(S1b[:], 0.0, writes=[f'S1b_{h}' for h in range(16)])
    xnw = [A(f'xnw{i}', [128, NDC * WIN], BF16) for i in range(2)]
    qsw = [A(f'qsw{i}', [32, 16 * WIN], BF16) for i in range(2)]
    ksw = [A(f'ksw{i}', [32, 16 * WIN], BF16) for i in range(2)]
    ktp0 = A('ktp0', [128, 4 * 512], BF16)
    ktp = [ktp0, ktp0]
    v1sb = [A(f'v1sb{i}', [128, 4 * 1040], BF16) for i in range(2)]
    tmf = [A(f'tmf{i}', [128, 512], F32) for i in range(3)]
    atb = [A(f'atb{i}', [128, 128], BF16) for i in range(2)]
    zc = A('zc', [128, 2], F32)
    otm = [A(f'otm{i}', [128, 1024], BF16) for i in range(2)]
    off = A('off', [128, 8 * WIN], BF16)
    sfw0 = A('sfw0', [128, NDC * WIN], BF16)
    sfw = [sfw0, sfw0]
    for i in range(2):
        tk.memset(v1sb[i].rearrange('p (t h c) -> p t h c', t=4, h=16)[:, :, :, 64:65],
                  1.0, writes=[f'v1sb{i}_ones'], eng='pool')

    def phi_fm(psum_ap, pkey, bcol, bkey, dst, dkey):
        tk.ts(tmf[0][0:32, :], psum_ap, bcol, 0.0, ALU.add, op1=ALU.min,
              reads=[pkey, bkey], writes=['tmf0'])
        tk.ts(tmf[1][0:32, :], psum_ap, bcol, 0.0, ALU.add, op1=ALU.max,
              reads=[pkey, bkey], writes=['tmf1'])
        tk.act(tmf[2][0:32, :], tmf[0][0:32, :], AF.Exp,
               reads=['tmf0'], writes=['tmf2'])
        tk.tt(dst, tmf[2][0:32, :], tmf[1][0:32, :], ALU.add,
              reads=['tmf2', 'tmf1'], writes=[dkey])

    def load_xn(p, w):
        for dc in range(NDC):
            tk.dma('sp', xnw[p][:, dc * WIN:(dc + 1) * WIN],
                   xn_h[dc * 128:(dc + 1) * 128, w * WIN:(w + 1) * WIN],
                   writes=[f'xnw{p}_{dc}'], label=f'l1xn_w{w}_d{dc}')

    for w in range(NW):
        p = w % 2
        load_xn(p, w)
        xkeys = [f'xnw{p}_{dc}' for dc in range(NDC)]
        for (wsb, wkey, bb, bbk, dst, dkey) in [(wq, 'wq1', bq, 'bq', qsw[p], f'qsw{p}'),
                                                (wk, 'wk1', bk, 'bk', ksw[p], f'ksw{p}')]:
            for h in range(16):
                woff = (h // 2) * 64 + (h % 2) * 32
                for dc in range(NDC):
                    tk.mm(pb[0][0:32, 0:512],
                          wsb[:, dc * 512 + woff:dc * 512 + woff + 32],
                          xnw[p][:, dc * WIN:(dc + 1) * WIN], dc == 0, dc == 7,
                          reads=[wkey, xkeys[dc]], writes=['pb0'])
                phi_fm(pb[0][0:32, 0:512], 'pb0', bb[:, h:h + 1], bbk,
                       dst[0:32, h * WIN:(h + 1) * WIN], f'{dkey}_{h}')
        for tau in range(NTAU):
            for dc in range(NDC):
                tk.mm(pb[1][:, 0:512],
                      xnw[p][:, dc * WIN + tau * 128:dc * WIN + (tau + 1) * 128],
                      wkT[:, dc * 512:(dc + 1) * 512], dc == 0, dc == 7,
                      reads=['wkT1', xkeys[dc]], writes=['pb1'])
            tk.tt(tmf[0][:], pb[1][:, 0:512], bkb[:], ALU.add,
                  reads=['pb1', 'bkb'], writes=['tmf0'])
            tk.ts(tmf[1][:], tmf[0][:], 0.0, None, ALU.max,
                  reads=['tmf0'], writes=['tmf1'])
            tk.ts(tmf[0][:], tmf[0][:], 0.0, None, ALU.min,
                  reads=['tmf0'], writes=['tmf0'])
            tk.act(tmf[2][:], tmf[0][:], AF.Exp, reads=['tmf0'], writes=['tmf2'])
            tk.tt(ktp[p][:, tau * 512:(tau + 1) * 512], tmf[2][:], tmf[1][:],
                  ALU.add, reads=['tmf2', 'tmf1'], writes=[f'ktp_{tau}'])
        for tau in range(NTAU):
            for half in range(2):
                bank, bk2 = (pb[2], 'pb2') if half == 0 else (pb[3], 'pb3')
                for dc in range(NDC):
                    tk.mm(bank[:, 0:512],
                          xnw[p][:, dc * WIN + tau * 128:dc * WIN + (tau + 1) * 128],
                          wvT[:, dc * 1024 + half * 512:dc * 1024 + half * 512 + 512],
                          dc == 0, dc == 7, reads=['wvT1', xkeys[dc]], writes=[bk2])
                dst = v1sb[p].rearrange('p (t h c) -> p t h c', t=4, h=16)[
                    :, tau, half * 8:(half + 1) * 8, 0:64]
                tk.tt(dst, bank[:, 0:512].rearrange('p (h c) -> p h c', h=8),
                      bvb[:, half * 512:(half + 1) * 512].rearrange('p (h c) -> p h c', h=8),
                      ALU.add, reads=[bk2, 'bvb'], writes=[f'v1sb{p}_{tau}'])
        for tau in range(NTAU):
            for h in range(16):
                hp = h % 2
                qsl = qsw[p][0:32, h * WIN + tau * 128:h * WIN + (tau + 1) * 128]
                ksl = ksw[p][0:32, h * WIN + tau * 128:h * WIN + (tau + 1) * 128]
                tk.mm(pb[4][:, 0:128], ksl, qsl, True, True,
                      reads=[f'ksw{p}_{h}', f'qsw{p}_{h}'], writes=['pb4'])
                tk.tt(atb[hp][:], pb[4][:, 0:128], mtri[:], ALU.mult,
                      reads=['pb4', 'mtri_bf'], writes=[f'atb{hp}'])
                v1 = v1sb[p].rearrange('p (t h c) -> p t h c', t=4, h=16)[:, tau, h, :]
                tk.mm(pb[5][:, 0:65], qsl, S1b[0:32, h * 65:(h + 1) * 65],
                      True, False, reads=[f'qsw{p}_{h}', f'S1b_{h}'], writes=['pb5'])
                tk.mm(pb[5][:, 0:65], atb[hp][:], v1, False, True,
                      reads=[f'atb{hp}', f'v1sb{p}_{tau}', f'v1sb{p}_ones'],
                      writes=['pb5'])
                tk.ts(zc[:, 0:1], pb[5][:, 64:65], 1e-6, None, ALU.add,
                      reads=['pb5'], writes=['zc'])
                tk.recip(zc[:, 1:2], zc[:, 0:1], reads=['zc'], writes=['zc'])
                tk.ts(otm[tau % 2][:, h * 64:(h + 1) * 64], pb[5][:, 0:64],
                      zc[:, 1:2], None, ALU.mult, reads=['pb5', 'zc'],
                      writes=[f'otm{tau % 2}_{h}'])
                tk.mm(pb[6][0:32, 0:65], ktp[p][:, tau * 512 + h * 32:tau * 512 + (h + 1) * 32],
                      v1, True, True,
                      reads=[f'ktp_{tau}', f'v1sb{p}_{tau}', f'v1sb{p}_ones'],
                      writes=['pb6'])
                tk.tt(S1f[:, h * 65:(h + 1) * 65], S1f[:, h * 65:(h + 1) * 65],
                      pb[6][0:32, 0:65], ALU.add, reads=['pb6', f'S1f_{h}'],
                      writes=[f'S1f_{h}'])
                tk.copy(S1b[:, h * 65:(h + 1) * 65], S1f[:, h * 65:(h + 1) * 65],
                        reads=[f'S1f_{h}'], writes=[f'S1b_{h}'], eng='pool')
            for hc in range(8):
                tk.tr(pb[7].bitcast(BF16)[:, 0:128],
                      otm[tau % 2][:, hc * 128:(hc + 1) * 128], iden_bf[:],
                      reads=[f'otm{tau % 2}_{hh}' for hh in range(2 * hc, 2 * hc + 2)]
                      + ['iden_bf'], writes=['pb7'])
                tk.copy(off[:, hc * WIN + tau * 128:hc * WIN + (tau + 1) * 128],
                        pb[7].bitcast(BF16)[:, 0:128], reads=['pb7'],
                        writes=[f'off_{hc}'], eng='act')
        for dm in range(NDC):
            for hc in range(NDC):
                tk.mm(pb[0][:, 0:512], wo[:, hc * 1024 + dm * 128:hc * 1024 + dm * 128 + 128],
                      off[:, hc * WIN:(hc + 1) * WIN], hc == 0, hc == 7,
                      reads=['wo1', f'off_{hc}'], writes=['pb0'])
            tk.ts(sfw[p][:, dm * WIN:(dm + 1) * WIN], pb[0][:, 0:512],
                  bo[:, dm:dm + 1], None, ALU.add, reads=['pb0', 'bo'],
                  writes=[f'sfw_{dm}'])
        for dm in range(NDC):
            tk.dma('sp', s_h[dm * 128:(dm + 1) * 128, w * WIN:(w + 1) * WIN],
                   sfw[p][:, dm * WIN:(dm + 1) * WIN], reads=[f'sfw_{dm}'],
                   label=f'sts1_w{w}_d{dm}')


def emit_ffn1(tk, E, nc, xn_h, g_h, W, scope=None):
    pb = E['pb']
    SC = 1.0 / 127.0
    g = 'f1_'
    A = _mk_alloc(nc, scope, g)
    wg = A('wg', [128, 8 * 3072], BF16)
    wu = A('wu', [128, 8 * 3072], BF16)
    wd = A('wd', [128, 24 * 1024], BF16)
    for nm, sb in [('wgT', wg), ('wuT', wu), ('wdT', wd)]:
        tk.dma('pool', sb[:], W[nm].rearrange('p a b -> p (a b)'), writes=[nm])
    xnw = [A(f'xnw{i}', [128, NDC * WIN], BF16) for i in range(2)]
    hsb = A('hsb', [128, 24 * WIN], BF16)
    gsl = [A(f'gsl{i}', [128, WIN], BF16) for i in range(2)]
    usl = [A(f'usl{i}', [128, WIN], BF16) for i in range(2)]
    gw = [A(f'gw{i}', [128, WIN], BF16) for i in range(2)]
    for w in range(NW):
        p = w % 2
        for dc in range(NDC):
            tk.dma('sp', xnw[p][:, dc * WIN:(dc + 1) * WIN],
                   xn_h[dc * 128:(dc + 1) * 128, w * WIN:(w + 1) * WIN],
                   writes=[f'xnw{p}_{dc}'], label=f'f1xn_w{w}_d{dc}')
        for m3 in range(24):
            q = m3 % 2
            bank, bk = (pb[0], 'pb0') if q == 0 else (pb[1], 'pb1')
            for dc in range(NDC):
                tk.mm(bank[:, 0:512],
                      wg[:, dc * 3072 + m3 * 128:dc * 3072 + (m3 + 1) * 128],
                      xnw[p][:, dc * WIN:(dc + 1) * WIN], dc == 0, dc == 7,
                      reads=['wgT', f'xnw{p}_{dc}'], writes=[bk])
            tk.act(gsl[q][:], bank[:, 0:512], AF.Silu, scale=SC,
                   reads=[bk], writes=[f'gsl{q}'])
            bank2, bk2 = (pb[2], 'pb2') if q == 0 else (pb[3], 'pb3')
            for dc in range(NDC):
                tk.mm(bank2[:, 0:512],
                      wu[:, dc * 3072 + m3 * 128:dc * 3072 + (m3 + 1) * 128],
                      xnw[p][:, dc * WIN:(dc + 1) * WIN], dc == 0, dc == 7,
                      reads=['wuT', f'xnw{p}_{dc}'], writes=[bk2])
            tk.act(usl[q][:], bank2[:, 0:512], AF.Copy, scale=SC,
                   reads=[bk2], writes=[f'usl{q}'])
            tk.tt(hsb[:, m3 * WIN:(m3 + 1) * WIN], gsl[q][:], usl[q][:], ALU.mult,
                  reads=[f'gsl{q}', f'usl{q}'], writes=[f'h_{m3}'])
        for dc in range(NDC):
            q = dc % 2
            bank, bk = (pb[4], 'pb4') if q == 0 else (pb[5], 'pb5')
            for m3 in range(24):
                tk.mm(bank[:, 0:512],
                      wd[:, m3 * 1024 + dc * 128:m3 * 1024 + (dc + 1) * 128],
                      hsb[:, m3 * WIN:(m3 + 1) * WIN], m3 == 0, m3 == 23,
                      reads=['wdT', f'h_{m3}'], writes=[bk])
            tk.act(gw[q][:], bank[:, 0:512], AF.Copy, scale=SC,
                   reads=[bk], writes=[f'gw{q}'])
            tk.dma('sp', g_h[dc * 128:(dc + 1) * 128, w * WIN:(w + 1) * WIN],
                   gw[q][:], reads=[f'gw{q}'], label=f'stg1_{w}_{dc}')


def emit_pe_all(tk, E, nc, x_all, routerT, allpe, scope=None):
    pb = E['pb']
    iden = E['iden_f32']; ones_col = E['ones_col']; ones_row = E['ones_row']
    cst = E['cst']; irev = E['irev']
    g = 'pa_'
    A = _mk_alloc(nc, scope, g)
    xw = [A(f'xw{i}', [128, NDC * WIN], F32) for i in range(2)]
    x2w = [A(f'x2w{i}', [128, NDC * WIN], F32) for i in range(2)]
    stats = [A(f'stats{q}', [128, 24], F32) for q in range(2)]
    routerT_sb = A('routerT_sb', [128, 64], F32)
    negSr = A('negSr', [1, 8], F32)
    srb = A('srb', [128, 8], F32)
    srcol = A('srcol', [8, 1], F32)
    lgt = [A(f'lgt{q}', [128, 32], F32) for q in range(2)]
    elg = [A(f'elg{q}', [128, 32], F32) for q in range(2)]
    eqt = [A(f'eqt{q}', [128, 32], F32) for q in range(2)]
    t2b = [A(f't2b{q}', [128, 32], F32) for q in range(2)]
    mx4 = [A(f'mx4{q}', [128, 16], F32) for q in range(2)]
    pecols = [A(f'pecols{q}', [128, 8], F32) for q in range(2)]
    peacc = A('peacc', [8, 1024], F32)
    irev4 = A('irev4', [128, 32], F32)
    tk.dma('sp', routerT_sb[:], routerT.rearrange('p a b -> p (a b)'),
           writes=['routerT_sb'])
    for rr in range(4):
        tk.copy(irev4[:, rr * 8:(rr + 1) * 8], irev[:], reads=['irev'],
                writes=['irev4'], eng='pool')
    for dc in range(NDC):
        tk.mm(pb[5][0:8, 0:1], routerT_sb[:, dc * 8:(dc + 1) * 8], ones_col[:],
              dc == 0, dc == 7, reads=['routerT_sb', 'ones_col'], writes=['pb5'])
    tk.ts(srcol[:], pb[5][0:8, 0:1], -1.0, None, ALU.mult,
          reads=['pb5'], writes=['srcol'])
    tk.tr(pb[6][0:1, 0:8], srcol[:], iden[0:8, 0:8],
          reads=['srcol', 'iden_f32'], writes=['pb6'])
    tk.copy(negSr[:], pb[6][0:1, 0:8], reads=['pb6'], writes=['negSr'])
    tk.mm(pb[6][:, 8:16], ones_row[:], negSr[:], True, True,
          reads=['ones_row', 'negSr'], writes=['pb6'])
    tk.copy(srb[:], pb[6][:, 8:16], reads=['pb6'], writes=['srb'])
    srows = A('srows', [1, 1024], F32)
    for r in range(8):
        for w in range(NW):
            p = w % 2
            q = w % 2
            bs, bq2, br = (pb[0], pb[1], pb[2]) if q == 0 else (pb[3], pb[4], pb[5])
            ks, kq, kr = (f'pb{j}' for j in ((0, 1, 2) if q == 0 else (3, 4, 5)))
            bt, kt = (pb[6], 'pb6') if q == 0 else (pb[7], 'pb7')
            st = stats[q]; sk = f'st{q}'
            for dc in range(NDC):
                tk.dma('sp', xw[p][:, dc * WIN:(dc + 1) * WIN],
                       x_all[r, dc * 128:(dc + 1) * 128, w * WIN:(w + 1) * WIN],
                       writes=[f'xw{p}_{dc}'], label=f'pa_{r}_{w}_{dc}')
            for dc in range(NDC):
                tk.act(x2w[p][:, dc * WIN:(dc + 1) * WIN],
                       xw[p][:, dc * WIN:(dc + 1) * WIN], AF.Square,
                       reads=[f'xw{p}_{dc}'], writes=[f'x2w{p}_{dc}'])
            for dc in range(NDC):
                tk.mm(bs[0:1, 0:512], ones_col[:], xw[p][:, dc * WIN:(dc + 1) * WIN],
                      dc == 0, dc == 7, reads=[f'xw{p}_{dc}', 'ones_col'], writes=[ks])
                tk.mm(bq2[0:1, 0:512], ones_col[:], x2w[p][:, dc * WIN:(dc + 1) * WIN],
                      dc == 0, dc == 7, reads=[f'x2w{p}_{dc}', 'ones_col'], writes=[kq])
            tk.copy(srows[0:1, 0:512], bs[0:1, 0:512], reads=[ks],
                    writes=['srows'], eng='act')
            tk.copy(srows[0:1, 512:1024], bq2[0:1, 0:512], reads=[kq],
                    writes=['srows'], eng='act')
            for tau in range(NTAU):
                for dc in range(NDC):
                    xt = xw[p][:, dc * WIN + tau * 128: dc * WIN + (tau + 1) * 128]
                    tk.mm(br[:, 8 * tau:8 * tau + 8], xt,
                          routerT_sb[:, dc * 8:(dc + 1) * 8], dc == 0, dc == 7,
                          reads=[f'xw{p}_{dc}', 'routerT_sb'], writes=[kr])
            for tau in range(NTAU):
                tk.tr(bt[:, tau:tau + 1], srows[0:1, tau * 128:(tau + 1) * 128],
                      iden[0:1, 0:1], reads=['srows', 'iden_f32'], writes=[kt])
                tk.tr(bt[:, 4 + tau:5 + tau],
                      srows[0:1, 512 + tau * 128:512 + (tau + 1) * 128],
                      iden[0:1, 0:1], reads=['srows', 'iden_f32'], writes=[kt])
            tk.ts(st[:, 0:4], bt[:, 0:4], 1.0 / D, None, ALU.mult,
                  reads=[kt], writes=[sk])
            tk.act(st[:, 4:8], st[:, 0:4], AF.Square, reads=[sk], writes=[sk])
            tk.stt(st[:, 8:12], bt[:, 4:8], 1.0 / D, st[:, 4:8],
                   ALU.mult, ALU.subtract, reads=[kt, sk], writes=[sk])
            tk.act(st[:, 12:16], st[:, 8:12], AF.Ln, bias=cst[:, 0:1],
                   reads=[sk, 'cst'], writes=[sk])
            tk.act(st[:, 16:20], st[:, 12:16], AF.Exp, scale=-0.5,
                   reads=[sk], writes=[sk])
            tk.stt(st[:, 20:24], st[:, 0:4], -1.0, st[:, 16:20],
                   ALU.mult, ALU.mult, reads=[sk], writes=[sk])
            v_t2 = t2b[q].rearrange('p (t e) -> p t e', t=4)
            tk.tt(v_t2, st[:, 20:24].unsqueeze(2).to_broadcast([128, 4, 8]),
                  srb[:].unsqueeze(1).to_broadcast([128, 4, 8]),
                  ALU.mult, reads=[sk, 'srb'], writes=[f't2b{q}'])
            v_lgt = lgt[q].rearrange('p (t e) -> p t e', t=4)
            tk.tt(v_lgt, br[:, 0:32].rearrange('p (t e) -> p t e', t=4),
                  st[:, 16:20].unsqueeze(2).to_broadcast([128, 4, 8]),
                  ALU.mult, reads=[kr, sk], writes=[f'lgt{q}'])
            tk.tt(lgt[q][:], lgt[q][:], t2b[q][:], ALU.add,
                  reads=[f'lgt{q}', f't2b{q}'], writes=[f'lgt{q}'])
            tk.red(mx4[q][:, 0:4].unsqueeze(2), v_lgt, ALU.max,
                   negate=True, reads=[f'lgt{q}'], writes=[f'mx4{q}'])
            v_el = elg[q].rearrange('p (t e) -> p t e', t=4)
            tk.tt(v_el, v_lgt,
                  mx4[q][:, 0:4].unsqueeze(2).to_broadcast([128, 4, 8]),
                  ALU.add, reads=[f'lgt{q}', f'mx4{q}'], writes=[f'elg{q}'])
            tk.act(elg[q][:], elg[q][:], AF.Exp, reads=[f'elg{q}'], writes=[f'elg{q}'])
            tk.red(mx4[q][:, 4:8].unsqueeze(2), v_el, ALU.add,
                   reads=[f'elg{q}'], writes=[f'mx4{q}'])
            tk.recip(mx4[q][:, 8:12], mx4[q][:, 4:8],
                     reads=[f'mx4{q}'], writes=[f'mx4{q}'])
            tk.ts(eqt[q][:], elg[q][:], 1.0, None, ALU.is_ge,
                  reads=[f'elg{q}'], writes=[f'eqt{q}'])
            tk.tt(eqt[q][:], eqt[q][:], irev4[:], ALU.mult,
                  reads=[f'eqt{q}', 'irev4'], writes=[f'eqt{q}'])
            tk.red(mx4[q][:, 12:16].unsqueeze(2),
                   eqt[q].rearrange('p (t e) -> p t e', t=4), ALU.max,
                   reads=[f'eqt{q}'], writes=[f'mx4{q}'])
            vp = pecols[q].rearrange('p (t c) -> p t c', t=4)
            tk.copy(vp[:, :, 0:1], mx4[q][:, 8:12].unsqueeze(2),
                    reads=[f'mx4{q}'], writes=[f'pe{q}'])
            tk.ts(vp[:, :, 1:2], mx4[q][:, 12:16].unsqueeze(2),
                  -1.0, 7.0, ALU.mult, op1=ALU.add,
                  reads=[f'mx4{q}'], writes=[f'pe{q}'])
            bt, kt = (pb[6], 'pb6') if q == 0 else (pb[7], 'pb7')
            tk.tr(bt[0:8, 128:256], pecols[q][:], iden[:],
                  reads=[f'pe{q}', 'iden_f32'], writes=[kt])
            tk.copy(peacc[:, w * 128:(w + 1) * 128], bt[0:8, 128:256],
                    reads=[kt], writes=[f'peacc_{w}'], eng='act')
        Fv = allpe.rearrange('a (w t j) -> a t w j', w=8, j=128)
        for cc in range(2):
            tk.dma('sp', Fv[2 * r + cc],
                   peacc[cc:8:2, :].rearrange('p (w j) -> p w j', j=128),
                   reads=[f'peacc_{w}' for w in range(NW)],
                   writes=[f'allpe_{r}'], label=f'stpe_{r}_{cc}')


def emit_pe_all2(tk, E, nc, x_all, routerT9, allpe, scope=None):
    pb = E['pb']
    iden = E['iden_f32']; ones_col = E['ones_col']; ones_row = E['ones_row']
    cst = E['cst']; irev = E['irev']
    HW = 256
    NHW = T // HW
    FINE_YIELD = getattr(emit_pe_all2, 'FINE_YIELD', False)
    g = 'pa_'
    A = _mk_alloc(nc, scope, g)
    xw = [A(f'xw{i}', [128, NDC * HW], F32) for i in range(2)]
    x2t0 = A('x2t', [128, HW], F32)
    x2t = [x2t0, x2t0]
    row9 = [A(f'row9_{i}', [9, HW], F32) for i in range(2)]
    x2r = [A(f'x2r{i}', [1, HW], F32) for i in range(2)]
    rT9 = A('rT9', [128, 72], F32)
    srcol = A('srcol', [8, 1], F32)
    negSr = A('negSr', [1, 8], F32)
    srb = A('srb', [128, 8], F32)
    st = [A(f'st{q}', [128, 12], F32) for q in range(2)]
    lgt = [A(f'lgt{q}', [128, 16], F32) for q in range(2)]
    elg = [A(f'elg{q}', [128, 16], F32) for q in range(2)]
    eqt = [A(f'eqt{q}', [128, 16], F32) for q in range(2)]
    t2b = [A(f't2b{q}', [128, 16], F32) for q in range(2)]
    mx = [A(f'mx{q}', [128, 8], F32) for q in range(2)]
    pecols = [A(f'pecols{q}', [128, 4], F32) for q in range(2)]
    peacc = A('peacc2', [4, 512], F32)
    tk.dma('sp', rT9[:], routerT9.rearrange('p a b -> p (a b)'),
           writes=['pa_rT9'])
    for dc in range(NDC):
        tk.mm(pb[6][0:8, 0:1], rT9[:, dc * 9:dc * 9 + 8], ones_col[:],
              dc == 0, dc == 7, reads=['pa_rT9', 'ones_col'], writes=['pa_psB'])
    tk.ts(srcol[:], pb[6][0:8, 0:1], 1.0, None, ALU.mult,
          reads=['pa_psB'], writes=['pa_srcol'])
    tk.tr(pb[1][0:1, 0:8], srcol[:], iden[0:8, 0:8],
          reads=['pa_srcol', 'iden_f32'], writes=['pa_psD0'])
    tk.copy(negSr[:], pb[1][0:1, 0:8], reads=['pa_psD0'], writes=['pa_negSr'])
    tk.mm(pb[6][:, 0:8], ones_row[:], negSr[:], True, True,
          reads=['ones_row', 'pa_negSr'], writes=['pa_psB'])
    tk.copy(srb[:], pb[6][:, 0:8], reads=['pa_psB'], writes=['pa_srb'])
    for r in range(8):
        for hw in range(NHW):
            p = hw % 2
            off = hw * HW
            for dc in range(NDC):
                tk.dma('sp', xw[p][:, dc * HW:(dc + 1) * HW],
                       x_all[r, dc * 128:(dc + 1) * 128, off:off + HW],
                       writes=[f'pa_xw{p}_{dc}'], label=f'pa_{r}_{hw}_{dc}')
            if FINE_YIELD:
                yield 'ld'
            for dc in range(NDC):
                tk.mm(pb[6][0:9, 0:HW], rT9[:, dc * 9:(dc + 1) * 9],
                      xw[p][:, dc * HW:(dc + 1) * HW], dc == 0, dc == 7,
                      reads=['pa_rT9', f'pa_xw{p}_{dc}'], writes=['pa_psB'])
            if FINE_YIELD:
                yield 'mm'
            for dc in range(NDC):
                tk.act(x2t0[:], xw[p][:, dc * HW:(dc + 1) * HW], AF.Square,
                       reads=[f'pa_xw{p}_{dc}'], writes=['pa_x2t'])
                tk.mm(pb[6][0:1, 256:512], ones_col[:], x2t0[:],
                      dc == 0, dc == 7,
                      reads=['pa_x2t', 'ones_col'], writes=['pa_psC'])
            if FINE_YIELD:
                yield 'x2'
            tk.copy(row9[p][:], pb[6][0:9, 0:HW], reads=['pa_psB'],
                    writes=[f'pa_row9_{p}'], eng='act')
            tk.copy(x2r[p][:], pb[6][0:1, 256:512], reads=['pa_psC'],
                    writes=[f'pa_x2r{p}'], eng='act')
            if FINE_YIELD:
                yield 'cp'
            c0 = p * 20
            for t2 in range(2):
                tk.tr(pb[1][:, c0 + t2 * 10:c0 + t2 * 10 + 9],
                      row9[p][0:9, t2 * 128:(t2 + 1) * 128], iden[0:9, 0:9],
                      reads=[f'pa_row9_{p}', 'iden_f32'], writes=[f'pa_psD{p}'])
                tk.tr(pb[1][:, c0 + t2 * 10 + 9:c0 + t2 * 10 + 10],
                      x2r[p][0:1, t2 * 128:(t2 + 1) * 128], iden[0:1, 0:1],
                      reads=[f'pa_x2r{p}', 'iden_f32'], writes=[f'pa_psD{p}'])
            DV = pb[1][:, c0:c0 + 20].rearrange('p (t c) -> p t c', t=2)
            sq = st[p]; kk = f'pa_st{p}'; kD = f'pa_psD{p}'
            tk.ts(sq[:, 0:2].unsqueeze(2), DV[:, :, 8:9], 1.0 / D, None,
                  ALU.mult, reads=[kD], writes=[kk])
            tk.ts(sq[:, 2:4].unsqueeze(2), DV[:, :, 9:10], 1.0 / D, None,
                  ALU.mult, reads=[kD], writes=[kk])
            tk.act(sq[:, 4:6], sq[:, 0:2], AF.Square, reads=[kk], writes=[kk])
            tk.tt(sq[:, 4:6], sq[:, 2:4], sq[:, 4:6], ALU.subtract,
                  reads=[kk], writes=[kk])
            tk.act(sq[:, 6:8], sq[:, 4:6], AF.Ln, bias=cst[:, 0:1],
                   reads=[kk, 'cst'], writes=[kk])
            tk.act(sq[:, 8:10], sq[:, 6:8], AF.Exp, scale=-0.5,
                   reads=[kk], writes=[kk])
            tk.stt(sq[:, 10:12], sq[:, 0:2], -1.0, sq[:, 8:10],
                   ALU.mult, ALU.mult, reads=[kk], writes=[kk])
            v_t2 = t2b[p].rearrange('p (t e) -> p t e', t=2)
            tk.tt(v_t2, sq[:, 10:12].unsqueeze(2).to_broadcast([128, 2, 8]),
                  srb[:].unsqueeze(1).to_broadcast([128, 2, 8]),
                  ALU.mult, reads=[kk, 'pa_srb'], writes=[f'pa_t2b{p}'])
            v_lgt = lgt[p].rearrange('p (t e) -> p t e', t=2)
            tk.tt(v_lgt, DV[:, :, 0:8],
                  sq[:, 8:10].unsqueeze(2).to_broadcast([128, 2, 8]),
                  ALU.mult, reads=[kD, kk], writes=[f'pa_lgt{p}'])
            tk.tt(lgt[p][:], lgt[p][:], t2b[p][:], ALU.add,
                  reads=[f'pa_lgt{p}', f'pa_t2b{p}'], writes=[f'pa_lgt{p}'])
            tk.red(mx[p][:, 0:2].unsqueeze(2), v_lgt, ALU.max, negate=True,
                   reads=[f'pa_lgt{p}'], writes=[f'pa_mx{p}'])
            v_el = elg[p].rearrange('p (t e) -> p t e', t=2)
            tk.tt(v_el, v_lgt,
                  mx[p][:, 0:2].unsqueeze(2).to_broadcast([128, 2, 8]),
                  ALU.add, reads=[f'pa_lgt{p}', f'pa_mx{p}'], writes=[f'pa_elg{p}'])
            tk.act(elg[p][:], elg[p][:], AF.Exp, reads=[f'pa_elg{p}'],
                   writes=[f'pa_elg{p}'])
            tk.red(mx[p][:, 2:4].unsqueeze(2), v_el, ALU.add,
                   reads=[f'pa_elg{p}'], writes=[f'pa_mx{p}'])
            tk.recip(mx[p][:, 4:6], mx[p][:, 2:4],
                     reads=[f'pa_mx{p}'], writes=[f'pa_mx{p}'])
            tk.ts(eqt[p][:], elg[p][:], 1.0, None, ALU.is_ge,
                  reads=[f'pa_elg{p}'], writes=[f'pa_eqt{p}'])
            tk.tt(eqt[p].rearrange('p (t e) -> p t e', t=2),
                  eqt[p].rearrange('p (t e) -> p t e', t=2),
                  irev[:].unsqueeze(1).to_broadcast([128, 2, 8]),
                  ALU.mult, reads=[f'pa_eqt{p}', 'irev'], writes=[f'pa_eqt{p}'])
            tk.red(mx[p][:, 6:8].unsqueeze(2),
                   eqt[p].rearrange('p (t e) -> p t e', t=2), ALU.max,
                   reads=[f'pa_eqt{p}'], writes=[f'pa_mx{p}'])
            vp = pecols[p].rearrange('p (c t) -> p c t', c=2)
            tk.copy(vp[:, 0:1, :], mx[p][:, 4:6].unsqueeze(1),
                    reads=[f'pa_mx{p}'], writes=[f'pa_pe{p}'])
            tk.ts(vp[:, 1:2, :], mx[p][:, 6:8].unsqueeze(1),
                  -1.0, 7.0, ALU.mult, op1=ALU.add,
                  reads=[f'pa_mx{p}'], writes=[f'pa_pe{p}'])
            ce = 40
            tk.tr(pb[1][0:4, ce:ce + 128], pecols[p][:], iden[:],
                  reads=[f'pa_pe{p}', 'iden_f32'], writes=['pa_psE'])
            hws = hw % 4
            tk.copy(peacc[:, hws * 128:(hws + 1) * 128], pb[1][0:4, ce:ce + 128],
                    reads=['pa_psE'], writes=[f'pa_peacc_{hws}'], eng='act')
            if hws == 3:
                s = hw // 4
                Fv = allpe.rearrange('a (s hw t j) -> a s t hw j',
                                     s=4, t=2, j=128)
                for cc in range(2):
                    tk.dma('sp', Fv[2 * r + cc, s],
                           peacc[2 * cc:2 * cc + 2, :].rearrange(
                               'p (hw j) -> p hw j', j=128),
                           reads=[f'pa_peacc_{i}' for i in range(4)],
                           writes=[f'allpe_{r}_{s}'], label=f'stpe_{r}_{s}_{cc}')
            yield


from contextlib import ExitStack
from concourse.bass_utils import run_bass_kernel_spmd

_CONST_KEYS = ['iden', 'ones_col', 'ones_row', 'mtri', 'selA', 'selB', 'rmat',
               'rmatT', 'idxmat', 'irev', 'ones8k']


def build_fused(nc):
    F, B, I = F32, BF16, I32
    ci = {}

    def ein(name, shape, dtype):
        ci[name] = nc.dram_tensor(name, shape, dtype, kind="ExternalInput").ap()
        return ci[name]

    x0 = ein('x0', [1024, 4096], F)
    x_all = ein('x_all', [8, 1024, 4096], F)
    routerT9 = ein('routerT9', [128, 8, 9], F)
    csel = ein('csel', [128, 2], F)
    cselE = ein('cselE', [128, 16], F)
    W0 = {nm: ein(nm, sh, F) for nm, sh in [
        ('wq', [128, 8, 8, 64]), ('wk', [128, 8, 8, 64]), ('wkT', [128, 8, 256]),
        ('wkeT', [128, 8, 256]), ('wvT', [128, 8, 1024]), ('wveT', [128, 8, 1024]),
        ('wout', [128, 8, 1024])]}
    WM = {nm: ein(nm, sh, F) for nm, sh in [
        ('w1', [8, 128, 8, 4096]), ('w2', [8, 128, 32, 1024]),
        ('b1', [8, 128, 32]), ('b2', [8, 128, 8])]}
    W1 = {}
    for nm, sh, dt in [('wq1', [128, 8, 8, 64], F), ('wk1', [128, 8, 8, 64], F),
                       ('bq1', [32, 16], F), ('bk1', [32, 16], F),
                       ('wkT1', [128, 8, 512], F), ('wvT1', [128, 8, 1024], F),
                       ('bv1', [1, 1024], F), ('bk_row', [1, 512], F),
                       ('wo1', [128, 8, 1024], F), ('bo1', [128, 8], F),
                       ('wgT', [128, 8, 3072], I), ('wuT', [128, 8, 3072], I),
                       ('wdT', [128, 24, 1024], I)]:
        W1[nm] = ein(nm, sh, dt)
    out_h = nc.dram_tensor('out_h', [1024, 4096], F, kind="ExternalOutput").ap()
    def intl(name, shape, dtype):
        return nc.dram_tensor(name, shape, dtype, kind="Internal").ap()
    allpe = intl('allpe', [16, 4096], F)
    xn0 = intl('xn0', [1024, 4096], B)
    kmask = intl('kmask', [8, 4096], F)
    s0 = intl('s0', [1024, 4096], B)
    g0 = intl('g0', [8, 1024, 4096], B)
    x1 = intl('x1', [1024, 4096], F)
    xn1 = intl('xn1', [1024, 4096], B)
    s1 = intl('s1', [1024, 4096], B)
    g1 = intl('g1', [1024, 4096], B)

    def pump(gen, n):
        try:
            for _ in range(n):
                next(gen)
            return True
        except StopIteration:
            return False

    tk = TK(nc)
    with nc.Block() as block:
        E = setup_consts(nc, tk)
        E['pb'] = alloc_psum_banks(nc)
        INTERLEAVE = True
        INTERLEAVE_LN = True
        tk.phase = 'ln0'
        sc_pa = ExitStack()
        gp = emit_pe_all2(tk, E, nc, x_all, routerT9, allpe, scope=sc_pa)
        pump(gp, 1)
        sc_ln = ExitStack()
        gl = emit_ln(tk, E, nc, x0, xn0, tag='ln0', scope=sc_ln)
        for _ in gl:
            if INTERLEAVE_LN:
                pump(gp, 5 if getattr(emit_pe_all2, 'FINE_YIELD', False) else 1)
        sc_ln.close()
        tk.barrier()
        tk.phase = 'attn0'
        sc_a0 = ExitStack()
        ga = emit_attn0(tk, E, nc, xn0, s0, W0, scope=sc_a0)
        for yi, _ in enumerate(ga):
            if INTERLEAVE and yi == 0:
                pump(gp, getattr(build_fused, 'NSUB', 1))
        tk.barrier()
        tk.phase = 'peall'
        while pump(gp, 1):
            pass
        tk.barrier()
        sc_a0.close()
        sc_pa.close()
        tk.phase = 'routing'
        with ExitStack() as sc:
            emit_routing(tk, E, nc, allpe, csel, cselE, kmask, scope=sc)
        tk.barrier()
        tk.phase = 'moe'
        with ExitStack() as sc:
            emit_moe(tk, E, nc, xn0, kmask, g0, WM, scope=sc)
        tk.barrier()
        tk.phase = 'cb0'
        with ExitStack() as sc:
            emit_combine(tk, E, nc, x0, s0, g0, x1, scope=sc, tag='cb0', ge=True)
        tk.barrier()
        tk.phase = 'ln1'
        with ExitStack() as sc:
            for _ in emit_ln(tk, E, nc, x1, xn1, tag='ln1', scope=sc):
                pass
        tk.barrier()
        tk.phase = 'attn1'
        with ExitStack() as sc:
            emit_attn1(tk, E, nc, xn1, s1, W1, scope=sc)
        tk.barrier()
        tk.phase = 'ffn1'
        with ExitStack() as sc:
            emit_ffn1(tk, E, nc, xn1, g1, W1, scope=sc)
        tk.barrier()
        tk.phase = 'cb1'
        with ExitStack() as sc:
            emit_combine(tk, E, nc, x1, s1, g1, out_h, scope=sc, tag='cb1')
        tk.flush(block)
    return nc


emit_pe_all2.FINE_YIELD = True
build_fused.NSUB = 4


def kernel(**inputs):
    c = pack_consts()
    w = pack_weights(inputs)
    x = inputs['L0_attn_x']
    xallT = np.ascontiguousarray(np.transpose(x, (0, 2, 1))).astype(np.float32)
    nc = build_fused(bass.Bass("TRN2", target_bir_lowering=False,
                               detect_race_conditions=False))
    cim = {}
    for k2 in _CONST_KEYS:
        key = {'iden': 'iden_f32'}.get(k2, k2)
        cim['c_' + k2] = c[key]
    wm = {'wq': 'l0_wq', 'wk': 'l0_wk', 'wkT': 'l0_wkT', 'wkeT': 'l0_wkeT',
          'wvT': 'l0_wvT', 'wveT': 'l0_wveT', 'wout': 'l0_wout',
          'w1': 'w1', 'w2': 'w2', 'b1': 'b1', 'b2': 'b2',
          'wq1': 'l1_wq', 'wk1': 'l1_wk', 'bq1': 'l1_bq', 'bk1': 'l1_bk',
          'wkT1': 'l1_wkT', 'wvT1': 'l1_wvT', 'bv1': 'l1_bv',
          'bk_row': 'l1_bk_row', 'wo1': 'l1_wo', 'bo1': 'l1_bo',
          'wgT': 'wgT', 'wuT': 'wuT', 'wdT': 'wdT', 'routerT9': 'routerT9'}
    in_maps = []
    for core in range(8):
        im = dict(cim)
        im['x0'] = np.ascontiguousarray(x[core].T).astype(np.float32)
        im['x_all'] = xallT
        for k2, v in wm.items():
            im[k2] = w[v]
        cs = np.zeros((128, 2), np.float32)
        cse = np.zeros((128, 16), np.float32)
        for gg in range(2):
            for e in range(8):
                cs[e * 16 + 2 * core + gg, gg] = 1.0
                cse[e * 16 + 2 * core + gg, 2 * e + gg] = 1.0
        im['csel'] = cs
        im['cselE'] = cse
        in_maps.append(im)
    res = run_bass_kernel_spmd(nc, in_maps, core_ids=list(range(8)))
    out = np.stack([np.ascontiguousarray(r['out_h'].T) for r in res.results],
                   axis=0)
    return out.astype(np.float32)


# version 30
# speedup vs baseline: 1.0700x; 1.0302x over previous
import numpy as np
import concourse.bass as bass
import concourse.mybir as mybir

F32 = mybir.dt.float32
BF16 = mybir.dt.bfloat16
I32 = mybir.dt.int32
AF = mybir.ActivationFunctionType
ALU = mybir.AluOpType
AX = mybir.AxisListType.X

D = 1024
T = 4096
NW = 8
WIN = 512
NTAU = 4
CT = 128
NDC = 8


class Op:
    __slots__ = ('stream', 'fn', 'deps', 'sem', 'inc', 'idx', 'label', 'phase')

    def __init__(self, stream, fn, sem, inc, idx, label):
        self.stream = stream
        self.fn = fn
        self.deps = []
        self.sem = sem
        self.inc = inc
        self.idx = idx
        self.label = label


class TK:
    STREAMS = ('pe', 'act', 'dve', 'pool', 'sp')
    ENGSEM = {'pe': 'pe', 'act': 'act', 'dve': 'dve', 'pool': 'pool', 'sp': 'spe'}

    NQSEM = 4

    def __init__(self, nc):
        self.nc = nc
        self.ops = []
        self.writer = {}
        self.readers = {}
        self.dma_ring = {}
        self.barrier_ops = []
        self.phase = 'init'

    def op(self, stream, fn, reads=(), writes=(), label='', is_dma=False):
        assert stream in self.STREAMS
        extra_dep = None
        if is_dma:
            ring = self.dma_ring.setdefault(stream, [])
            slot = len(ring) % self.NQSEM
            sem = f'q_{stream}_{slot}'
            inc = 16
            if len(ring) >= self.NQSEM:
                extra_dep = ring[-self.NQSEM]
            ring.append(None)
        else:
            sem = None
            inc = 1
        o = Op(stream, fn, sem, inc, len(self.ops), label)
        o.phase = self.phase
        if is_dma:
            self.dma_ring[stream][-1] = o
        deps = {}
        if extra_dep is not None:
            deps[extra_dep.idx] = extra_dep
        for k in reads:
            w = self.writer.get(k)
            if w is not None:
                if w.stream == 'pe' and stream == 'pe' and w.inc != 16:
                    continue
                deps[w.idx] = w
        for k in writes:
            w = self.writer.get(k)
            if w is not None and not (w.stream == stream and w.inc != 16):
                deps[w.idx] = w
            for r in self.readers.get(k, ()):
                if r.stream == stream and r.inc != 16:
                    continue
                deps[r.idx] = r
        o.deps = list(deps.values())
        for d in o.deps:
            if d.sem is None:
                d.sem = self.ENGSEM[d.stream]
        for k in writes:
            self.writer[k] = o
            self.readers[k] = []
        for k in reads:
            self.readers.setdefault(k, []).append(o)
        self.ops.append(o)
        return o

    def dma(self, queue, out, in_, reads=(), writes=(), label='', **kw):
        return self.op(queue, lambda e: e.dma_start(out=out, in_=in_, **kw),
                       reads=reads, writes=writes, label=label, is_dma=True)

    def mm(self, out, lhsT, rhs, start, stop, reads, writes, label=''):
        fn = (lambda out, lhsT, rhs, start, stop:
              lambda e: e.matmul(out, lhsT=lhsT, rhs=rhs, start=start, stop=stop,
                                 skip_group_check=True))(out, lhsT, rhs, start, stop)
        return self.op('pe', fn, reads=reads, writes=writes, label=label)

    def tr(self, out, in_, iden, reads, writes, label=''):
        fn = (lambda out, in_, iden:
              lambda e: e.transpose(out, in_=in_, identity=iden))(out, in_, iden)
        return self.op('pe', fn, reads=reads, writes=writes, label=label)

    def act(self, out, in_, func, reads, writes, bias=0.0, scale=1.0, eng='act', label=''):
        fn = (lambda out, in_, func, bias, scale:
              lambda e: e.activation(out, in_, func, bias=bias, scale=scale))(
                  out, in_, func, bias, scale)
        return self.op(eng, fn, reads=reads, writes=writes, label=label)

    def tt(self, out, a, b, opk, reads, writes, eng='dve', label=''):
        fn = (lambda out, a, b, opk:
              lambda e: e.tensor_tensor(out, a, b, op=opk))(out, a, b, opk)
        return self.op(eng, fn, reads=reads, writes=writes, label=label)

    def ts(self, out, a, s1, s2, op0, reads, writes, op1=None, accum=None, eng='dve', label=''):
        def mk(out, a, s1, s2, op0, op1, accum):
            def f(e):
                kw = {}
                if op1 is not None:
                    kw['op1'] = op1
                if accum is not None:
                    kw['accum_out'] = accum
                return e.tensor_scalar(out, a, s1, s2, op0=op0, **kw)
            return f
        return self.op(eng, mk(out, a, s1, s2, op0, op1, accum),
                       reads=reads, writes=writes, label=label)

    def stt(self, out, in0, scalar, in1, op0, op1, reads, writes, eng='dve', label=''):
        fn = (lambda out, in0, scalar, in1, op0, op1:
              lambda e: e.scalar_tensor_tensor(out, in0, scalar, in1, op0=op0, op1=op1))(
                  out, in0, scalar, in1, op0, op1)
        return self.op(eng, fn, reads=reads, writes=writes, label=label)

    def red(self, out, in_, opk, reads, writes, negate=False, eng='dve', label=''):
        fn = (lambda out, in_, opk, negate:
              lambda e: e.tensor_reduce(out, in_, AX, opk, negate=negate))(
                  out, in_, opk, negate)
        return self.op(eng, fn, reads=reads, writes=writes, label=label)

    def copy(self, out, in_, reads, writes, eng='dve', label=''):
        if eng == 'act':
            fn = (lambda out, in_: lambda e: e.copy(out, in_))(out, in_)
        else:
            fn = (lambda out, in_: lambda e: e.tensor_copy(out, in_))(out, in_)
        return self.op(eng, fn, reads=reads, writes=writes, label=label)

    def memset(self, ap, val, writes, eng='dve', label=''):
        fn = (lambda ap, val: lambda e: e.memset(ap, val))(ap, val)
        return self.op(eng, fn, reads=(), writes=writes, label=label)

    def recip(self, out, in_, reads, writes, label=''):
        fn = (lambda out, in_: lambda e: e.reciprocal(out, in_))(out, in_)
        return self.op('dve', fn, reads=reads, writes=writes, label=label)

    def cpred(self, out, mask, data, reads, writes, label=''):
        fn = (lambda out, mask, data:
              lambda e: e.copy_predicated(out, mask, data))(out, mask, data)
        return self.op('dve', fn, reads=reads, writes=writes, label=label)

    def barrier(self):
        deps = {}
        for w in self.writer.values():
            deps[w.idx] = w
        for rl in self.readers.values():
            for r in rl:
                deps[r.idx] = r
        dep_list = list(deps.values())
        for d in dep_list:
            if d.sem is None:
                d.sem = self.ENGSEM[d.stream]
        newbar = []
        for s in self.STREAMS:
            o = Op(s, lambda e: e.nop(), None, 1, len(self.ops), 'barrier')
            o.phase = self.phase
            o.deps = [d for d in dep_list if not (d.stream == s and d.inc != 16)] \
                + [b for b in self.barrier_ops]
            self.ops.append(o)
            newbar.append(o)
        self.writer = {}
        self.readers = {}
        self.barrier_ops = newbar
        for b in newbar:
            for d in b.deps:
                if d.sem is None:
                    d.sem = self.ENGSEM[d.stream]

    def flush(self, block):
        nc = self.nc
        semval = {}
        opval = {}
        for o in self.ops:
            if o.sem is not None:
                semval[o.sem] = semval.get(o.sem, 0) + o.inc
                opval[o.idx] = (o.sem, semval[o.sem])
        sems = {name: nc.alloc_semaphore(name) for name in semval}
        finals = dict(semval)
        by_stream = {s: [] for s in self.STREAMS}
        for o in self.ops:
            by_stream[o.stream].append(o)
        engmap = {'pe': block.tensor, 'act': block.scalar, 'dve': block.vector,
                  'pool': block.gpsimd, 'sp': block.sync}
        nceng = {'pe': nc.tensor, 'act': nc.scalar, 'dve': nc.vector,
                 'pool': nc.gpsimd, 'sp': nc.sync}
        for s in self.STREAMS:
            guar = {}

            def emit_stream(eng, s=s, guar=guar):
                cur = [None]
                curscope = [None]
                for o in by_stream[s]:
                    if o.phase != cur[0]:
                        if curscope[0] is not None:
                            curscope[0].__exit__(None, None, None)
                        cm = nc.named_scope(o.phase)
                        cm.__enter__()
                        curscope[0] = cm
                        cur[0] = o.phase
                    for d in o.deps:
                        if d.idx not in opval:
                            raise RuntimeError(f'dep without sem: {d.label}')
                        nm, v = opval[d.idx]
                        if guar.get(nm, 0) < v:
                            eng.wait_ge(sems[nm], v)
                            guar[nm] = v
                    inst = o.fn(eng)
                    if o.sem is not None:
                        if inst is None:
                            raise RuntimeError(f'op {o.label} returned no inst')
                        inst.then_inc(sems[o.sem], o.inc)
                if curscope[0] is not None:
                    curscope[0].__exit__(None, None, None)

            engmap[s](emit_stream)
        def final_wait(eng):
            for nm, v in finals.items():
                eng.wait_ge(sems[nm], v)
        block.sync(final_wait)
        return finals


CS = float(1.0 / np.sqrt(4.0 * np.sqrt(2.0)))
LIN = float(0.5 / CS)
SQB = float(np.sqrt(0.5))

def pair_list():
    pairs = [(a, b) for a in range(16) for b in range(16)]
    return pairs

def pack_consts():
    c = {}
    iden = np.eye(128, dtype=np.float32)
    c['iden_bf'] = iden.astype(np.float32)
    c['iden_f32'] = iden.copy()
    c['ones_col'] = np.ones((128, 1), np.float32)
    c['ones_row'] = np.ones((1, 128), np.float32)
    tri = (np.arange(128)[None, :] >= np.arange(128)[:, None]).astype(np.float32)
    c['mtri'] = tri
    pairs = pair_list()
    selA = np.zeros((64, 273), np.float32)
    selB = np.zeros((64, 273), np.float32)
    def setsel(base):
        selA[base + 16, 0] = 1.0; selB[base + 16, 0] = 1.0
        for a in range(16):
            selA[base + a, 1 + a] = 1.0; selB[base + 16, 1 + a] = 1.0
        for m in range(111):
            a, b = pairs[m]
            selA[base + a, 17 + m] = 1.0; selB[base + b, 17 + m] = 1.0
        for m in range(128):
            a, b = pairs[111 + m]
            selA[base + a, 128 + m] = 1.0; selB[base + b, 128 + m] = 1.0
        for m in range(17):
            a, b = pairs[239 + m]
            selA[base + a, 256 + m] = 1.0; selB[base + b, 256 + m] = 1.0
    setsel(0); setsel(32)
    c['selA'] = selA; c['selB'] = selB
    R = np.zeros((8, 128), np.float32)
    for e in range(8):
        R[e, e * 16:(e + 1) * 16] = 1.0
    c['rmat'] = R
    c['rmatT'] = np.ascontiguousarray(R.T)
    idx = (np.arange(128)[:, None] % 16) * 2048 + np.arange(2048)[None, :]
    c['idxmat'] = idx.astype(np.float32)
    c['irev'] = np.broadcast_to((7 - np.arange(8))[None, :].astype(np.float32),
                                (128, 8)).copy()
    c['ones8k'] = np.ones(8192, np.float32)
    return c

def chunk_lhsT(w, kchunk=128):
    K, M = w.shape
    nk = K // kchunk
    return np.ascontiguousarray(w.reshape(nk, kchunk, M).transpose(1, 0, 2))

def pack_weights(d):
    w = {}
    def packqk(W):
        Wt = W.T * CS
        out = np.zeros((1024, 8, 64), np.float32)
        for p in range(8):
            out[:, p, 0:16] = Wt[:, (2 * p) * 16:(2 * p) * 16 + 16]
            out[:, p, 32:48] = Wt[:, (2 * p + 1) * 16:(2 * p + 1) * 16 + 16]
        return np.ascontiguousarray(
            out.reshape(8, 128, 8, 64).transpose(1, 0, 2, 3))
    w['l0_wq'] = packqk(d['L0_attn_Wq'])
    w['l0_wk'] = packqk(d['L0_attn_Wk'])
    w['l0_wke'] = packqk(d['L0_attn_Wk_enc'])
    w['l0_wkT'] = chunk_lhsT(d['L0_attn_Wk'].T * CS)
    w['l0_wkeT'] = chunk_lhsT(d['L0_attn_Wk_enc'].T * CS)
    w['l0_wvT'] = chunk_lhsT(d['L0_attn_Wv'].T)
    w['l0_wveT'] = chunk_lhsT(d['L0_attn_Wv_enc'].T)
    w['l0_wout'] = chunk_lhsT(d['L0_attn_Wout'].T)
    w['routerT'] = chunk_lhsT(d['L0_ffn_router_w'].T)
    w['w1'] = np.ascontiguousarray(
        d['L0_ffn_W1'].reshape(8, 8, 128, 32, 128).transpose(0, 2, 3, 1, 4)
        .reshape(8, 128, 32, 1024))
    w['w2'] = np.ascontiguousarray(
        d['L0_ffn_W2'].reshape(8, 32, 128, 1024).transpose(0, 2, 1, 3))
    w['b1'] = np.ascontiguousarray(
        d['L0_ffn_b1'].reshape(8, 32, 128).transpose(0, 2, 1))
    w['b2'] = np.ascontiguousarray(
        d['L0_ffn_b2'].reshape(8, 8, 128).transpose(0, 2, 1))
    def packqk1(W, b):
        out = np.zeros((1024, 8, 64), np.float32)
        ob = np.zeros((32, 16), np.float32)
        for p in range(8):
            out[:, p, 0:32] = W[:, (2 * p) * 32:(2 * p) * 32 + 32]
            out[:, p, 32:64] = W[:, (2 * p + 1) * 32:(2 * p + 1) * 32 + 32]
        for h in range(16):
            ob[:, h] = b[h * 32:(h + 1) * 32]
        return (np.ascontiguousarray(out.reshape(8, 128, 8, 64).transpose(1, 0, 2, 3)),
                ob)
    w['l1_wq'], w['l1_bq'] = packqk1(d['L1_attn_wq'], d['L1_attn_bq'])
    w['l1_wk'], w['l1_bk'] = packqk1(d['L1_attn_wk'], d['L1_attn_bk'])
    w['l1_wkT'] = chunk_lhsT(d['L1_attn_wk'])
    w['l1_wvT'] = chunk_lhsT(d['L1_attn_wv'])
    w['l1_bv'] = d['L1_attn_bv'].reshape(1, 1024).copy()
    w['l1_bk_row'] = d['L1_attn_bk'].reshape(1, 512).copy()
    w['l1_wo'] = chunk_lhsT(d['L1_attn_wo'])
    w['l1_bo'] = np.ascontiguousarray(d['L1_attn_bo'].reshape(8, 128).T)
    w['wgT'] = chunk_lhsT(d['L1_ffn_w_gate'].T)
    w['wuT'] = chunk_lhsT(d['L1_ffn_w_up'].T)
    w['wdT'] = chunk_lhsT(d['L1_ffn_w_down'].T)
    return w


def declare_hbm(nc, v2=False):
    H = {}
    def dt_(name, shape, dtype, kind):
        H[name] = nc.dram_tensor(name, shape, dtype, kind=kind).ap()
    return H


def setup_consts(nc, tk):
    E = {}
    ci = {}
    for name, shape, dtype in [
        ('c_iden', [128, 128], F32), ('c_ones_col', [128, 1], F32),
        ('c_ones_row', [1, 128], F32), ('c_mtri', [128, 128], F32),
        ('c_selA', [64, 273], F32), ('c_selB', [64, 273], F32),
        ('c_rmat', [8, 128], F32), ('c_rmatT', [128, 8], F32),
        ('c_idxmat', [128, 2048], F32), ('c_irev', [128, 8], F32),
        ('c_ones8k', [8192], F32),
    ]:
        ci[name] = nc.dram_tensor(name, shape, dtype, kind="ExternalInput").ap()
    E['iden_f32'] = nc.alloc_sbuf_tensor('iden_f32', [128, 128], F32).ap()
    E['iden_bf'] = nc.alloc_sbuf_tensor('iden_bf', [128, 128], BF16).ap()
    E['ones_col'] = nc.alloc_sbuf_tensor('ones_col', [128, 1], F32).ap()
    E['ones_row'] = nc.alloc_sbuf_tensor('ones_row', [1, 128], F32).ap()
    E['ones_row_bf'] = nc.alloc_sbuf_tensor('ones_row_bf', [1, 128], BF16).ap()
    E['mtri_bf'] = nc.alloc_sbuf_tensor('mtri_bf', [128, 128], BF16).ap()
    E['mtri_f32'] = nc.alloc_sbuf_tensor('mtri_f32', [128, 128], F32).ap()
    E['selA'] = nc.alloc_sbuf_tensor('selA', [64, 273], BF16).ap()
    E['selB'] = nc.alloc_sbuf_tensor('selB', [64, 273], BF16).ap()
    E['rmat'] = nc.alloc_sbuf_tensor('rmat', [8, 128], F32).ap()
    E['rmatT'] = nc.alloc_sbuf_tensor('rmatT', [128, 8], F32).ap()
    E['idxmat'] = nc.alloc_sbuf_tensor('idxmat', [128, 2048], F32).ap()
    E['irev'] = nc.alloc_sbuf_tensor('irev', [128, 8], F32).ap()
    E['cst'] = nc.alloc_sbuf_tensor('cst', [128, 8], F32).ap()
    E['ones8k_h'] = ci['c_ones8k']
    tk.dma('sp', E['iden_f32'][:], ci['c_iden'], writes=['iden_f32'])
    tk.dma('pool', E['iden_bf'][:], ci['c_iden'], writes=['iden_bf'])
    tk.dma('sp', E['ones_col'][:], ci['c_ones_col'], writes=['ones_col'])
    tk.dma('sp', E['ones_row'][:], ci['c_ones_row'], writes=['ones_row'])
    tk.dma('pool', E['ones_row_bf'][:], ci['c_ones_row'], writes=['ones_row_bf'])
    tk.dma('pool', E['mtri_bf'][:], ci['c_mtri'], writes=['mtri_bf'])
    tk.dma('sp', E['mtri_f32'][:], ci['c_mtri'], writes=['mtri_f32'])
    tk.dma('pool', E['selA'][:], ci['c_selA'], writes=['selA'])
    tk.dma('pool', E['selB'][:], ci['c_selB'], writes=['selB'])
    tk.dma('sp', E['rmat'][:], ci['c_rmat'], writes=['rmat'])
    tk.dma('sp', E['rmatT'][:], ci['c_rmatT'], writes=['rmatT'])
    tk.dma('sp', E['idxmat'][:], ci['c_idxmat'], writes=['idxmat'])
    tk.dma('sp', E['irev'][:], ci['c_irev'], writes=['irev'])
    tk.op('dve', lambda e: e.memset(E['cst'][:, 0:1], 1e-5), writes=['cst'])
    tk.op('dve', lambda e: e.memset(E['cst'][:, 1:2], 0.70710678), writes=['cst'])
    tk.op('dve', lambda e: e.memset(E['cst'][:, 2:8], 0.0), writes=['cst'])
    return E


def alloc_psum_banks(nc, n=8):
    return [nc.alloc_psum_tensor(f'pb{i}', [128, 512], F32).ap() for i in range(n)]



def _mk_alloc(nc, scope, prefix):
    def A(n, sh, dt):
        if scope is None:
            return nc.alloc_sbuf_tensor(prefix + n, sh, dt).ap()
        h = scope.enter_context(nc.sbuf_tensor(prefix + n, sh, dt))
        return h.ap()
    return A

def emit_ln(tk, E, nc, xsrc, xn_h, pe_h=None, routerT=None, tag='ln0', scope=None):
    pb = E['pb']
    iden = E['iden_f32']; ones_col = E['ones_col']; ones_row = E['ones_row']
    cst = E['cst']; irev = E['irev']
    g = tag + '_'
    A = _mk_alloc(nc, scope, g)
    xw = [A(f'xw{i}', [128, NDC * WIN], F32) for i in range(2)]
    xnw = [A(f'xnw{i}', [128, NDC * WIN], BF16) for i in range(2)]
    x2b = [A(f'x2b{i}', [128, 128], F32) for i in range(2)]
    stats = A('stats', [128, 8], F32)
    rows3 = A('rows3', [1, 384], F32)
    xnt = [A(f'xnt{i}', [128, 128], F32) for i in range(2)]
    rt = pe_h is not None
    if rt:
        routerT_sb = A('routerT_sb', [128, 64], F32)
        negSr = A('negSr', [1, 8], F32)
        srcol = A('srcol', [8, 1], F32)
        lgt = A('lgt', [128, 8], F32)
        elg = A('elg', [128, 8], F32)
        eqt = A('eqt', [128, 8], F32)
        eqt2 = A('eqt2', [128, 8], F32)
        lgtx = A('lgtx', [128, 4], F32)
        pecols = A('pecols', [128, 2], F32)
        peacc = A('peacc', [2, 4096], F32)
        tk.dma('sp', routerT_sb[:], routerT.rearrange('p a b -> p (a b)'),
               writes=['routerT_sb'])
        for dc in range(NDC):
            tk.mm(pb[5][0:8, 0:1], routerT_sb[:, dc * 8:(dc + 1) * 8], ones_col[:],
                  dc == 0, dc == 7, reads=['routerT_sb', 'ones_col'], writes=['pb5'])
        tk.ts(srcol[:], pb[5][0:8, 0:1], -1.0, None, ALU.mult,
              reads=['pb5'], writes=['srcol'])
        tk.tr(pb[6][0:1, 0:8], srcol[:], iden[0:8, 0:8],
              reads=['srcol', 'iden_f32'], writes=['pb6'])
        tk.copy(negSr[:], pb[6][0:1, 0:8], reads=['pb6'], writes=['negSr'])

    for w in range(NW):
        p = w % 2
        for dc in range(NDC):
            tk.dma('sp', xw[p][:, dc * WIN:(dc + 1) * WIN],
                   xsrc[dc * 128:(dc + 1) * 128, w * WIN:(w + 1) * WIN],
                   writes=[f'xw{p}_{dc}'], label=f'ldx_w{w}_d{dc}')
        for tau in range(NTAU):
            i = w * NTAU + tau
            xt_ap = [xw[p][:, dc * WIN + tau * 128: dc * WIN + (tau + 1) * 128]
                     for dc in range(NDC)]
            for dc in range(NDC):
                xq = x2b[dc % 2]
                tk.act(xq[:], xt_ap[dc], AF.Square,
                       reads=[f'xw{p}_{dc}'], writes=[f'x2b{dc % 2}'])
                tk.mm(pb[0][:, 0:1], xt_ap[dc], ones_col[:], dc == 0, dc == 7,
                      reads=[f'xw{p}_{dc}', 'ones_col'], writes=['pb0'])
                tk.mm(pb[1][:, 0:1], xq[:], ones_col[:], dc == 0, dc == 7,
                      reads=[f'x2b{dc % 2}', 'ones_col'], writes=['pb1'])
            tk.ts(stats[:, 0:1], pb[0][:, 0:1], 1.0 / D, None, ALU.mult,
                  reads=['pb0'], writes=['stats'])
            tk.ts(stats[:, 1:2], pb[1][:, 0:1], 1.0 / D, None, ALU.mult,
                  reads=['pb1'], writes=['stats'])
            tk.tt(stats[:, 3:4], stats[:, 0:1], stats[:, 0:1], ALU.mult,
                  reads=['stats'], writes=['stats'])
            tk.tt(stats[:, 2:3], stats[:, 1:2], stats[:, 3:4], ALU.subtract,
                  reads=['stats'], writes=['stats'])
            tk.act(stats[:, 4:5], stats[:, 2:3], AF.Ln, bias=cst[:, 0:1],
                   reads=['stats', 'cst'], writes=['stats'])
            tk.act(stats[:, 5:6], stats[:, 4:5], AF.Exp, scale=-0.5,
                   reads=['stats'], writes=['stats'])
            tk.stt(stats[:, 6:7], stats[:, 0:1], -1.0, stats[:, 5:6],
                   ALU.mult, ALU.mult, reads=['stats'], writes=['stats'])
            tk.tr(pb[2][0:1, 0:128], stats[:, 5:6], iden[:],
                  reads=['stats', 'iden_f32'], writes=['pb2'])
            tk.tr(pb[2][0:1, 128:256], stats[:, 6:7], iden[:],
                  reads=['stats', 'iden_f32'], writes=['pb2'])
            tk.tr(pb[2][0:1, 256:384], stats[:, 0:1], iden[:],
                  reads=['stats', 'iden_f32'], writes=['pb2'])
            tk.copy(rows3[:], pb[2][0:1, 0:384], reads=['pb2'], writes=['rows3'])
            tk.mm(pb[3][:, 0:128], ones_row[:], rows3[0:1, 0:128], True, True,
                  reads=['ones_row', 'rows3'], writes=['pb3'])
            tk.mm(pb[3][:, 128:256], ones_row[:], rows3[0:1, 128:256], True, True,
                  reads=['ones_row', 'rows3'], writes=['pb3'])
            for dc in range(NDC):
                xt = xnt[dc % 2]
                tk.tt(xt[:], xt_ap[dc], pb[3][:, 0:128], ALU.mult,
                      reads=[f'xw{p}_{dc}', 'pb3'], writes=[f'xnt{dc % 2}'])
                tk.tt(xnw[p][:, dc * WIN + tau * 128: dc * WIN + (tau + 1) * 128],
                      xt[:], pb[3][:, 128:256], ALU.add,
                      reads=[f'xnt{dc % 2}', 'pb3'], writes=[f'xnw{p}_{dc}_{tau}'])
            if rt:
                for dc in range(NDC):
                    tk.mm(pb[4][:, 0:8], xt_ap[dc], routerT_sb[:, dc * 8:(dc + 1) * 8],
                          dc == 0, False, reads=[f'xw{p}_{dc}', 'routerT_sb'],
                          writes=['pb4'])
                tk.mm(pb[4][:, 0:8], rows3[0:1, 256:384], negSr[:], False, True,
                      reads=['rows3', 'negSr'], writes=['pb4'])
                tk.ts(lgt[:], pb[4][:, 0:8], stats[:, 5:6], None, ALU.mult,
                      reads=['pb4', 'stats'], writes=['lgt'])
                tk.red(lgtx[:, 0:1], lgt[:], ALU.max, negate=True,
                       reads=['lgt'], writes=['lgtx'])
                tk.act(elg[:], lgt[:], AF.Exp, bias=lgtx[:, 0:1],
                       reads=['lgt', 'lgtx'], writes=['elg'])
                tk.red(lgtx[:, 1:2], elg[:], ALU.add, reads=['elg'], writes=['lgtx'])
                tk.recip(pecols[:, 0:1], lgtx[:, 1:2], reads=['lgtx'], writes=['pecols'])
                tk.ts(eqt[:], lgt[:], lgtx[:, 0:1], 0.0, ALU.add, op1=ALU.is_ge,
                      reads=['lgt', 'lgtx'], writes=['eqt'])
                tk.tt(eqt2[:], eqt[:], irev[:], ALU.mult,
                      reads=['eqt', 'irev'], writes=['eqt2'])
                tk.red(lgtx[:, 2:3], eqt2[:], ALU.max, reads=['eqt2'], writes=['lgtx'])
                tk.ts(pecols[:, 1:2], lgtx[:, 2:3], -1.0, 7.0, ALU.mult, op1=ALU.add,
                      reads=['lgtx'], writes=['pecols'])
                tk.tr(pb[6][0:2, 128:256], pecols[:], iden[:],
                      reads=['pecols', 'iden_f32'], writes=['pb6'])
                tk.copy(peacc[:, i * 128:(i + 1) * 128], pb[6][0:2, 128:256],
                        reads=['pb6'], writes=[f'peacc_{i}'])
        for dc in range(NDC):
            tk.dma('sp', xn_h[dc * 128:(dc + 1) * 128, w * WIN:(w + 1) * WIN],
                   xnw[p][:, dc * WIN:(dc + 1) * WIN],
                   reads=[f'xnw{p}_{dc}_{t}' for t in range(NTAU)],
                   label=f'stxn_w{w}_d{dc}')
    if rt:
        tk.dma('sp', pe_h, peacc[:],
               reads=[f'peacc_{i}' for i in range(NW * NTAU)])


CAP = 5120

def emit_routing(tk, E, nc, allpe, csel, cselE, kmask_h, scope=None):
    pb = E['pb']
    rmat, rmatT, idxmat = E['rmat'], E['rmatT'], E['idxmat']
    g = 'rt_'
    A = _mk_alloc(nc, scope, g)
    P16 = A('P16', [16, 2048], F32)
    E16 = A('E16', [16, 2048], F32)
    Mbig = A('M', [128, 2048], F32)
    eqm = A('eqm', [128, 2048], F32)
    big1 = A('big1', [128, 2048], F32)
    big2 = A('big2', [128, 2048], F32)
    Kbig = A('K', [128, 2048], F32)
    tmp16 = A('tmp16', [16, 2048], F32)
    tmp16b = A('tmp16b', [16, 2048], F32)
    part = A('part', [128, 1], F32)
    sc8 = A('sc8', [8, 12], F32)
    colb = A('colb', [128, 4], F32)
    cselb = A('csel', [128, 2], F32)
    cselEb = A('cselE', [128, 16], F32)
    k0t = A('k0t', [2, 2048], F32)
    f0 = A('f0', [2, 2048], F32)

    tk.dma('sp', cselb[:], csel, writes=['cselb'])
    tk.dma('sp', cselEb[:], cselE, writes=['cselEb'])
    tk.dma('sp', P16[0:16:2, :], allpe[0:16:2, 0:2048], writes=['P16'])
    tk.dma('sp', P16[1:16:2, :], allpe[0:16:2, 2048:4096], writes=['P16'])
    tk.dma('sp', E16[0:16:2, :], allpe[1:16:2, 0:2048], writes=['E16'])
    tk.dma('sp', E16[1:16:2, :], allpe[1:16:2, 2048:4096], writes=['E16'])
    for e in range(8):
        tk.ts(tmp16[:], E16[:], float(e), None, ALU.is_equal,
              reads=['E16'], writes=['tmp16'])
        tk.stt(tmp16b[:], P16[:], 1.0, tmp16[:], ALU.add, ALU.mult,
               reads=['P16', 'tmp16'], writes=['tmp16b'])
        tk.ts(tmp16b[:], tmp16b[:], -1.0, None, ALU.add,
              reads=['tmp16b'], writes=['tmp16b'])
        tk.dma('sp', Mbig[e * 16:(e + 1) * 16, :], tmp16b[:],
               reads=['tmp16b'], writes=['Mbig'], label=f'mcp{e}')
    def collapse(dst, rkey):
        tk.mm(pb[5][0:8, 0:1], rmatT[:], part[:], True, True,
              reads=['rmatT', 'part'], writes=['pb5'])
        tk.copy(dst, pb[5][0:8, 0:1], reads=['pb5'], writes=['sc8'])
    def bcast8(src, dstcol):
        tk.mm(pb[6][:, 0:1], rmat[:], src, True, True,
              reads=['rmat', 'sc8'], writes=['pb6'])
        tk.copy(dstcol, pb[6][:, 0:1], reads=['pb6'], writes=['colb'])
    tk.ts(big1[:], Mbig[:], -0.5, None, ALU.is_ge, op1=ALU.add, accum=part[:],
          reads=['Mbig'], writes=['big1', 'part'])
    collapse(sc8[:, 6:7], 'tot')
    tk.memset(sc8[:, 0:1], 0.0, writes=['sc8'])
    tk.memset(sc8[:, 1:2], 1.5, writes=['sc8'])
    for it in range(40):
        tk.tt(sc8[:, 5:6], sc8[:, 0:1], sc8[:, 1:2], ALU.add,
              reads=['sc8'], writes=['sc8'])
        tk.ts(sc8[:, 2:3], sc8[:, 5:6], 0.5, None, ALU.mult,
              reads=['sc8'], writes=['sc8'])
        bcast8(sc8[:, 2:3], colb[:, 0:1])
        tk.ts(big1[:], Mbig[:], colb[:, 0:1], None, ALU.is_ge, op1=ALU.add,
              accum=part[:], reads=['Mbig', 'colb'], writes=['big1', 'part'])
        collapse(sc8[:, 3:4], 'cnt')
        tk.ts(sc8[:, 4:5], sc8[:, 3:4], -float(CAP), 0.0, ALU.add, op1=ALU.is_ge,
              reads=['sc8'], writes=['sc8'])
        tk.tt(sc8[:, 5:6], sc8[:, 2:3], sc8[:, 0:1], ALU.subtract,
              reads=['sc8'], writes=['sc8'])
        tk.tt(sc8[:, 5:6], sc8[:, 5:6], sc8[:, 4:5], ALU.mult,
              reads=['sc8'], writes=['sc8'])
        tk.tt(sc8[:, 0:1], sc8[:, 0:1], sc8[:, 5:6], ALU.add,
              reads=['sc8'], writes=['sc8'])
        tk.tt(sc8[:, 5:6], sc8[:, 1:2], sc8[:, 2:3], ALU.subtract,
              reads=['sc8'], writes=['sc8'])
        tk.tt(sc8[:, 5:6], sc8[:, 5:6], sc8[:, 4:5], ALU.mult,
              reads=['sc8'], writes=['sc8'])
        tk.tt(sc8[:, 1:2], sc8[:, 2:3], sc8[:, 5:6], ALU.add,
              reads=['sc8'], writes=['sc8'])
    bcast8(sc8[:, 0:1], colb[:, 0:1])
    tk.ts(big1[:], Mbig[:], colb[:, 0:1], None, ALU.is_gt, op1=ALU.add,
          accum=part[:], reads=['Mbig', 'colb'], writes=['big1', 'part'])
    collapse(sc8[:, 7:8], 'ngt')
    tk.ts(sc8[:, 8:9], sc8[:, 7:8], -1.0, float(CAP), ALU.mult, op1=ALU.add,
          reads=['sc8'], writes=['sc8'])
    tk.ts(eqm[:], Mbig[:], colb[:, 0:1], None, ALU.is_equal,
          reads=['Mbig', 'colb'], writes=['eqm'])
    tk.memset(sc8[:, 9:10], -1.0, writes=['sc8'])
    tk.memset(sc8[:, 10:11], 32768.0, writes=['sc8'])
    for it in range(18):
        tk.tt(sc8[:, 5:6], sc8[:, 9:10], sc8[:, 10:11], ALU.add,
              reads=['sc8'], writes=['sc8'])
        tk.ts(sc8[:, 2:3], sc8[:, 5:6], 0.5, None, ALU.mult,
              reads=['sc8'], writes=['sc8'])
        bcast8(sc8[:, 2:3], colb[:, 1:2])
        tk.ts(big1[:], idxmat[:], colb[:, 1:2], None, ALU.is_le,
              reads=['idxmat', 'colb'], writes=['big1'])
        tk.tt(big2[:], big1[:], eqm[:], ALU.mult,
              reads=['big1', 'eqm'], writes=['big2'])
        tk.ts(big2[:], big2[:], 1.0, None, ALU.mult, op1=ALU.add, accum=part[:],
              reads=['big2'], writes=['big2', 'part'])
        collapse(sc8[:, 3:4], 'icnt')
        tk.ts(sc8[:, 4:5], sc8[:, 3:4], sc8[:, 8:9], 0.0, ALU.subtract, op1=ALU.is_le,
              reads=['sc8'], writes=['sc8'])
        tk.tt(sc8[:, 5:6], sc8[:, 2:3], sc8[:, 9:10], ALU.subtract,
              reads=['sc8'], writes=['sc8'])
        tk.tt(sc8[:, 5:6], sc8[:, 5:6], sc8[:, 4:5], ALU.mult,
              reads=['sc8'], writes=['sc8'])
        tk.tt(sc8[:, 9:10], sc8[:, 9:10], sc8[:, 5:6], ALU.add,
              reads=['sc8'], writes=['sc8'])
        tk.tt(sc8[:, 5:6], sc8[:, 10:11], sc8[:, 2:3], ALU.subtract,
              reads=['sc8'], writes=['sc8'])
        tk.tt(sc8[:, 5:6], sc8[:, 5:6], sc8[:, 4:5], ALU.mult,
              reads=['sc8'], writes=['sc8'])
        tk.tt(sc8[:, 10:11], sc8[:, 2:3], sc8[:, 5:6], ALU.add,
              reads=['sc8'], writes=['sc8'])
    bcast8(sc8[:, 9:10], colb[:, 1:2])
    tk.ts(Kbig[:], Mbig[:], colb[:, 0:1], None, ALU.is_gt,
          reads=['Mbig', 'colb'], writes=['Kbig'])
    tk.ts(big1[:], idxmat[:], colb[:, 1:2], None, ALU.is_le,
          reads=['idxmat', 'colb'], writes=['big1'])
    tk.tt(big1[:], big1[:], eqm[:], ALU.mult,
          reads=['big1', 'eqm'], writes=['big1'])
    tk.tt(Kbig[:], Kbig[:], big1[:], ALU.add,
          reads=['Kbig', 'big1'], writes=['Kbig'])
    for q in range(4):
        tk.mm(pb[7][0:2, 0:512], cselb[:], Kbig[:, q * 512:(q + 1) * 512],
              True, True, reads=['cselb', 'Kbig'], writes=['pb7'])
        tk.ts(f0[:, q * 512:(q + 1) * 512], pb[7][0:2, 0:512], -1.0, 1.0,
              ALU.mult, op1=ALU.add, reads=['pb7'], writes=['f0'])
    for e in range(8):
        for q in range(4):
            tk.mm(pb[7][0:2, 0:512], cselEb[:, 2 * e:2 * e + 2],
                  Kbig[:, q * 512:(q + 1) * 512], True, True,
                  reads=['cselEb', 'Kbig'], writes=['pb7'])
            tk.copy(k0t[:, q * 512:(q + 1) * 512], pb[7][0:2, 0:512],
                    reads=['pb7'], writes=['k0t'], eng='act')
        if e == 0:
            tk.tt(k0t[:], k0t[:], f0[:], ALU.add, reads=['k0t', 'f0'],
                  writes=['k0t'])
        tk.dma('sp', kmask_h[e:e + 1, :], k0t[:], reads=['k0t'], label=f'stk{e}')


SQRT2 = float(np.sqrt(2.0))

def emit_attn0(tk, E, nc, xn_h, s_h, W, scope=None):
    pb = E['pb']
    iden_bf = E['iden_bf']; cst = E['cst']; mtri = E['mtri_bf']
    selA, selB = E['selA'], E['selB']
    g = 'a0_'
    A = _mk_alloc(nc, scope, g)
    wq = A('wq', [128, 4096], BF16); wk = A('wk', [128, 4096], BF16)
    wkT = A('wkT', [128, 2048], BF16)
    wvT = A('wvT', [128, 8192], BF16)
    wout = A('wout', [128, 8192], BF16)

    def loadw(sb, nm, key):
        tk.dma('pool', sb[:], W[nm].rearrange('p a b -> p (a b)') if W[nm].ndim == 3
               else W[nm].rearrange('p a b c -> p (a b c)'), writes=[key])
    loadw(wkT, 'wkeT', 'wkT')
    loadw(wvT, 'wveT', 'wvT')
    loadw(wq, 'wq', 'wq')
    loadw(wout, 'wout', 'wout')
    SAf = A('SAf', [128, 1040], F32); SBf = A('SBf', [128, 1040], F32)
    SCf = A('SCf', [17, 1040], F32)
    SAb = A('SAb', [128, 1040], BF16); SBb = A('SBb', [128, 1040], BF16)
    SCb = A('SCb', [17, 1040], BF16)
    for tname, t in [('SAf', SAf), ('SBf', SBf), ('SCf', SCf)]:
        tk.memset(t[:], 0.0, writes=[f'{tname}_{h}' for h in range(16)])
    xnw = [A(f'xnw{i}', [128, NDC * WIN], BF16) for i in range(2)]
    qsw = [A(f'qsw{i}', [64, 8 * WIN], BF16) for i in range(2)]
    ksw = [A(f'ksw{i}', [64, 8 * WIN], BF16) for i in range(2)]
    kstm = [A(f'kstm{i}', [128, 4 * 256], BF16) for i in range(2)]
    v1sb = [A(f'v1sb{i}', [128, 4 * 1040], BF16) for i in range(2)]
    kftm = [A(f'kftm{i}', [128, 4368], BF16) for i in range(2)]
    qfA = [A(f'qfA{i}', [128, 128], BF16) for i in range(2)]
    qfB = [A(f'qfB{i}', [128, 128], BF16) for i in range(2)]
    qfC = [A(f'qfC{i}', [17, 128], BF16) for i in range(2)]
    p1sb = [A(f'p1sb{i}', [128, 128], BF16) for i in range(2)]
    asq = [A(f'asq{i}', [128, 128], F32) for i in range(2)]
    atb = [A(f'atb{i}', [128, 128], BF16) for i in range(2)]
    zc = A('zc', [128, 2], F32)
    otm = [A(f'otm{i}', [128, 1024], BF16) for i in range(2)]
    off = A('off', [128, 8 * WIN], BF16)
    sfw = [A(f'sfw{i}', [128, NDC * WIN], BF16) for i in range(2)]
    ones_h = E['ones8k_h']
    for i in range(2):
        for r in (16, 48):
            tk.dma('pool', qsw[i][r:r + 1, :], ones_h[0:8 * WIN],
                   writes=[f'qsw{i}_ones'])
        tk.memset(v1sb[i].rearrange('p (t h c) -> p t h c', t=4, h=16)[:, :, :, 64:65],
                  1.0, writes=[f'v1sb{i}_ones'], eng='pool')
        tk.memset(kftm[i].rearrange('p (h c) -> p h c', h=16)[:, :, 0:1],
                  1.0, writes=[f'kftm{i}_ones'], eng='pool')

    def proj_qk(wsb, wkey, dst, dkey, xbuf, xkeys):
        for p8 in range(8):
            for dc in range(NDC):
                tk.mm(pb[0][0:64, 0:512],
                      wsb[:, (dc * 8 + p8) * 64:(dc * 8 + p8) * 64 + 64],
                      xbuf[:, dc * WIN:(dc + 1) * WIN], dc == 0, dc == 7,
                      reads=[wkey, xkeys[dc]], writes=['pb0'])
            for rb in (0, 32):
                tk.copy(dst[rb:rb + 16, p8 * WIN:(p8 + 1) * WIN],
                        pb[0][rb:rb + 16, 0:512], reads=['pb0'],
                        writes=[f'{dkey}_{p8}'], eng='act')

    def proj_ktm2(wsb, wkey, dstbuf, dkey, xbuf, xkeys):
        for tau in range(NTAU):
            for dc in range(NDC):
                tk.mm(pb[1][:, 0:256],
                      xbuf[:, dc * WIN + tau * 128:dc * WIN + (tau + 1) * 128],
                      wsb[:, dc * 256:(dc + 1) * 256],
                      dc == 0, dc == 7, reads=[wkey, xkeys[dc]], writes=['pb1'])
            tk.copy(dstbuf[:, tau * 256:(tau + 1) * 256], pb[1][:, 0:256],
                    reads=['pb1'], writes=[f'{dkey}_{tau}'], eng='act')

    def proj_vtm(wsb, wkey, dstbuf, dkey, xbuf, xkeys):
        for tau in range(NTAU):
            for half in range(2):
                bank = pb[2] if half == 0 else pb[3]
                bk = 'pb2' if half == 0 else 'pb3'
                for dc in range(NDC):
                    tk.mm(bank[:, 0:512],
                          xbuf[:, dc * WIN + tau * 128:dc * WIN + (tau + 1) * 128],
                          wsb[:, dc * 1024 + half * 512:dc * 1024 + half * 512 + 512],
                          dc == 0, dc == 7, reads=[wkey, xkeys[dc]], writes=[bk])
                dst = dstbuf.rearrange('p (t h c) -> p t h c', t=4, h=16)[
                    :, tau, half * 8:(half + 1) * 8, 0:64]
                tk.copy(dst, bank[:, 0:512].rearrange('p (h c) -> p h c', h=8),
                        reads=[bk], writes=[f'{dkey}_{tau}'], eng='act')

    def build_kftm(buf, bkey, ksbuf, kskey, tau):
        kv = ksbuf[:, tau * 256:(tau + 1) * 256].rearrange('p (h f) -> p h f', h=16)
        ov = buf.rearrange('p (h c) -> p h c', h=16)
        tk.ts(ov[:, :, 1:17], kv, SQRT2, None, ALU.mult,
              reads=[kskey], writes=[bkey], eng='pool')
        for a in range(16):
            in1 = kv[:, :, a:a + 1].to_broadcast([128, 16, 16])
            tk.tt(ov[:, :, 17 + a * 16:17 + (a + 1) * 16], kv, in1, ALU.mult,
                  reads=[kskey], writes=[bkey], eng='pool')

    def state_update(kbuf, kkey, vbuf, vkey, tau, tag):
        for h in range(16):
            kc = kbuf[:, h * 273:(h + 1) * 273]
            v1 = vbuf.rearrange('p (t h c) -> p t h c', t=4, h=16)[
                :, tau, h, :].rearrange('p c -> p c')
            tk.mm(pb[6][:, 0:65], kc[:, 0:128], v1, True, True,
                  reads=[kkey, kkey + '_ones', vkey, vkey.rsplit('_', 1)[0] + '_ones'],
                  writes=['pb6'])
            tk.tt(SAf[:, h * 65:(h + 1) * 65], SAf[:, h * 65:(h + 1) * 65],
                  pb[6][:, 0:65], ALU.add, reads=['pb6', f'SAf_{h}'],
                  writes=[f'SAf_{h}'])
            tk.copy(SAb[:, h * 65:(h + 1) * 65], SAf[:, h * 65:(h + 1) * 65],
                    reads=[f'SAf_{h}'], writes=[f'SAb_{h}'], eng='pool')
            tk.mm(pb[6][:, 65:130], kc[:, 128:256], v1, True, True,
                  reads=[kkey, vkey], writes=['pb6'])
            tk.tt(SBf[:, h * 65:(h + 1) * 65], SBf[:, h * 65:(h + 1) * 65],
                  pb[6][:, 65:130], ALU.add, reads=['pb6', f'SBf_{h}'],
                  writes=[f'SBf_{h}'])
            tk.copy(SBb[:, h * 65:(h + 1) * 65], SBf[:, h * 65:(h + 1) * 65],
                    reads=[f'SBf_{h}'], writes=[f'SBb_{h}'], eng='pool')
            tk.mm(pb[6][0:17, 130:195], kc[:, 256:273], v1, True, True,
                  reads=[kkey, vkey], writes=['pb6'])
            tk.tt(SCf[:, h * 65:(h + 1) * 65], SCf[:, h * 65:(h + 1) * 65],
                  pb[6][0:17, 130:195], ALU.add, reads=['pb6', f'SCf_{h}'],
                  writes=[f'SCf_{h}'])
            tk.copy(SCb[:, h * 65:(h + 1) * 65], SCf[:, h * 65:(h + 1) * 65],
                    reads=[f'SCf_{h}'], writes=[f'SCb_{h}'], eng='pool')

    def load_xn(p, w):
        for dc in range(NDC):
            tk.dma('sp', xnw[p][:, dc * WIN:(dc + 1) * WIN],
                   xn_h[dc * 128:(dc + 1) * 128, w * WIN:(w + 1) * WIN],
                   writes=[f'xnw{p}_{dc}'], label=f'ldxn_w{w}_d{dc}')

    load_xn(0, 0)
    xkeys0 = [f'xnw0_{dc}' for dc in range(NDC)]
    proj_ktm2(wkT, 'wkT', kstm[1], 'kstm1', xnw[0], xkeys0)
    proj_vtm(wvT, 'wvT', v1sb[1], 'v1sb1', xnw[0], xkeys0)
    for tau in range(NTAU):
        build_kftm(kftm[1], 'kftm1', kstm[1], f'kstm1_{tau}', tau)
        state_update(kftm[1], 'kftm1', v1sb[1], f'v1sb1_{tau}', tau, 'enc')

    loadw(wk, 'wk', 'wk')
    loadw(wkT, 'wkT', 'wkT')
    loadw(wvT, 'wvT', 'wvT')
    for w in range(NW):
        p = w % 2
        if w > 0:
            load_xn(p, w)
        xkeys = [f'xnw{p}_{dc}' for dc in range(NDC)]
        proj_qk(wq, 'wq', qsw[p], f'qsw{p}', xnw[p], xkeys)
        proj_qk(wk, 'wk', ksw[p], f'ksw{p}', xnw[p], xkeys)
        proj_ktm2(wkT, 'wkT', kstm[p], f'kstm{p}', xnw[p], xkeys)
        proj_vtm(wvT, 'wvT', v1sb[p], f'v1sb{p}', xnw[p], xkeys)
        for tau in range(NTAU):
            build_kftm(kftm[p], f'kftm{p}', kstm[p], f'kstm{p}_{tau}', tau)
            for h in range(16):
                p8, rb = h // 2, 32 * (h % 2)
                hp = h % 2
                qsl = qsw[p][rb:rb + 16, p8 * WIN + tau * 128:p8 * WIN + (tau + 1) * 128]
                ksl = ksw[p][rb:rb + 16, p8 * WIN + tau * 128:p8 * WIN + (tau + 1) * 128]
                q17 = qsw[p][rb:rb + 17, p8 * WIN + tau * 128:p8 * WIN + (tau + 1) * 128]
                tk.mm(pb[4][:, 0:128], ksl, qsl, True, True,
                      reads=[f'ksw{p}_{p8}', f'qsw{p}_{p8}'], writes=['pb4'])
                tk.act(asq[hp][:], pb[4][:, 0:128], AF.Square, bias=cst[:, 1:2],
                       reads=['pb4', 'cst'], writes=[f'asq{hp}'])
                tk.stt(atb[hp][:], asq[hp][:], 0.5, mtri[:], ALU.add, ALU.mult,
                       reads=[f'asq{hp}', 'mtri_bf'], writes=[f'atb{hp}'])
                for (cn, qf, c0, c1, m) in [('A', qfA[hp], 0, 128, 128),
                                            ('B', qfB[hp], 128, 256, 128),
                                            ('C', qfC[hp], 256, 273, 17)]:
                    tk.mm(pb[2][0:m, 0:128], selA[rb:rb + 17, c0:c1], q17,
                          True, True, reads=['selA', f'qsw{p}_{p8}', f'qsw{p}_ones'],
                          writes=['pb2'])
                    tk.mm(pb[3][0:m, 0:128], selB[rb:rb + 17, c0:c1], q17,
                          True, True, reads=['selB', f'qsw{p}_{p8}', f'qsw{p}_ones'],
                          writes=['pb3'])
                    tk.copy(p1sb[hp][0:m, :], pb[2][0:m, 0:128],
                            reads=['pb2'], writes=[f'p1sb{hp}'], eng='act')
                    tk.tt(qf[0:m, :], pb[3][0:m, 0:128], p1sb[hp][0:m, :],
                          ALU.mult, reads=['pb3', f'p1sb{hp}'], writes=[f'qf{cn}{hp}'])
                v1 = v1sb[p].rearrange('p (t h c) -> p t h c', t=4, h=16)[:, tau, h, :]
                tk.mm(pb[5][:, 0:65], qfA[hp][:], SAb[:, h * 65:(h + 1) * 65],
                      True, False, reads=[f'qfA{hp}', f'SAb_{h}'], writes=['pb5'])
                tk.mm(pb[5][:, 0:65], qfB[hp][:], SBb[:, h * 65:(h + 1) * 65],
                      False, False, reads=[f'qfB{hp}', f'SBb_{h}'], writes=['pb5'])
                tk.mm(pb[5][:, 0:65], qfC[hp][:], SCb[:, h * 65:(h + 1) * 65],
                      False, False, reads=[f'qfC{hp}', f'SCb_{h}'], writes=['pb5'])
                tk.mm(pb[5][:, 0:65], atb[hp][:], v1, False, True,
                      reads=[f'atb{hp}', f'v1sb{p}_{tau}', f'v1sb{p}_ones'],
                      writes=['pb5'])
                tk.ts(zc[:, 0:1], pb[5][:, 64:65], 1e-12, None, ALU.add,
                      reads=['pb5'], writes=['zc'])
                tk.recip(zc[:, 1:2], zc[:, 0:1], reads=['zc'], writes=['zc'])
                tk.ts(otm[tau % 2][:, h * 64:(h + 1) * 64], pb[5][:, 0:64],
                      zc[:, 1:2], None, ALU.mult, reads=['pb5', 'zc'],
                      writes=[f'otm{tau % 2}_{h}'])
                kc = kftm[p][:, h * 273:(h + 1) * 273]
                tk.mm(pb[6][:, 0:65], kc[:, 0:128], v1, True, True,
                      reads=[f'kftm{p}', f'kftm{p}_ones', f'v1sb{p}_{tau}',
                             f'v1sb{p}_ones'], writes=['pb6'])
                tk.tt(SAf[:, h * 65:(h + 1) * 65], SAf[:, h * 65:(h + 1) * 65],
                      pb[6][:, 0:65], ALU.add, reads=['pb6', f'SAf_{h}'],
                      writes=[f'SAf_{h}'])
                tk.copy(SAb[:, h * 65:(h + 1) * 65], SAf[:, h * 65:(h + 1) * 65],
                        reads=[f'SAf_{h}'], writes=[f'SAb_{h}'], eng='pool')
                tk.mm(pb[6][:, 65:130], kc[:, 128:256], v1, True, True,
                      reads=[f'kftm{p}', f'v1sb{p}_{tau}'], writes=['pb6'])
                tk.tt(SBf[:, h * 65:(h + 1) * 65], SBf[:, h * 65:(h + 1) * 65],
                      pb[6][:, 65:130], ALU.add, reads=['pb6', f'SBf_{h}'],
                      writes=[f'SBf_{h}'])
                tk.copy(SBb[:, h * 65:(h + 1) * 65], SBf[:, h * 65:(h + 1) * 65],
                        reads=[f'SBf_{h}'], writes=[f'SBb_{h}'], eng='pool')
                tk.mm(pb[6][0:17, 130:195], kc[:, 256:273], v1, True, True,
                      reads=[f'kftm{p}', f'v1sb{p}_{tau}'], writes=['pb6'])
                tk.tt(SCf[:, h * 65:(h + 1) * 65], SCf[:, h * 65:(h + 1) * 65],
                      pb[6][0:17, 130:195], ALU.add, reads=['pb6', f'SCf_{h}'],
                      writes=[f'SCf_{h}'])
                tk.copy(SCb[:, h * 65:(h + 1) * 65], SCf[:, h * 65:(h + 1) * 65],
                        reads=[f'SCf_{h}'], writes=[f'SCb_{h}'], eng='pool')
            for hc in range(8):
                tk.tr(pb[7].bitcast(BF16)[:, 0:128],
                      otm[tau % 2][:, hc * 128:(hc + 1) * 128], iden_bf[:],
                      reads=[f'otm{tau % 2}_{hh}' for hh in range(2 * hc, 2 * hc + 2)]
                      + ['iden_bf'], writes=['pb7'])
                tk.copy(off[:, hc * WIN + tau * 128:hc * WIN + (tau + 1) * 128],
                        pb[7].bitcast(BF16)[:, 0:128], reads=['pb7'],
                        writes=[f'off_{hc}'], eng='act')
        for dm in range(NDC):
            for hc in range(NDC):
                tk.mm(pb[0][:, 0:512], wout[:, hc * 1024 + dm * 128:hc * 1024 + dm * 128 + 128],
                      off[:, hc * WIN:(hc + 1) * WIN], hc == 0, hc == 7,
                      reads=['wout', f'off_{hc}'], writes=['pb0'])
            tk.copy(sfw[p][:, dm * WIN:(dm + 1) * WIN], pb[0][:, 0:512],
                    reads=['pb0'], writes=[f'sfw{p}_{dm}'], eng='act')
        for dm in range(NDC):
            tk.dma('sp', s_h[dm * 128:(dm + 1) * 128, w * WIN:(w + 1) * WIN],
                   sfw[p][:, dm * WIN:(dm + 1) * WIN], reads=[f'sfw{p}_{dm}'],
                   label=f'sts_w{w}_d{dm}')


def emit_moe(tk, E, nc, xn_h, kmask_h, ge_h, W, scope=None):
    pb = E['pb']
    ones_row_bf = E['ones_row_bf']
    g = 'mo_'
    A = _mk_alloc(nc, scope, g)
    w1sb = A('w1sb', [128, 32 * 1024], BF16)
    w2sb = A('w2sb', [128, 32 * 1024], BF16)
    b1sb = A('b1sb', [128, 32], F32)
    b2sb = A('b2sb', [128, 8], F32)
    xnw = [A(f'xnw{i}', [128, NDC * WIN], BF16) for i in range(2)]
    hsb = A('hsb', [128, 32 * WIN], BF16)
    ysb = [A(f'ysb{i}', [128, WIN], F32) for i in range(2)]
    ymb = [A(f'ymb{i}', [128, WIN], BF16) for i in range(4)]
    mrow = A('mrow', [1, WIN], BF16)
    mbc = A('mbc', [128, WIN], BF16)
    HB = [0, 1, 6, 7]

    def load_w1_fc(e, fc):
        tk.dma('pool', w1sb[:, fc * 1024:(fc + 1) * 1024],
               W['w1'][e, :, fc, :], writes=[f'w1_{fc}'], label=f'ldw1_{e}_{fc}')
    def load_w2_q(e, q):
        tk.dma('pool', w2sb[:, q * 4096:(q + 1) * 4096],
               W['w2'][e, :, 4 * q:4 * (q + 1), :].rearrange('p a b -> p (a b)'),
               writes=[f'w2_{q}'], label=f'ldw2_{e}_{q}')

    for fc in range(32):
        load_w1_fc(0, fc)
    for q in range(8):
        load_w2_q(0, q)
    tk.dma('sp', b1sb[:], W['b1'][0], writes=['b1sb'])
    tk.dma('sp', b2sb[:], W['b2'][0], writes=['b2sb'])
    for e in range(8):
        for w in range(NW):
            p = w % 2
            last = (w == NW - 1 and e < 7)
            for dc in range(NDC):
                tk.dma('sp', xnw[p][:, dc * WIN:(dc + 1) * WIN],
                       xn_h[dc * 128:(dc + 1) * 128, w * WIN:(w + 1) * WIN],
                       writes=[f'xnw{p}_{dc}'], label=f'mxn_{e}_{w}_{dc}')
            tk.dma('pool', mrow[:], kmask_h[e:e + 1, w * WIN:(w + 1) * WIN],
                   writes=['mrow'])
            tk.mm(pb[6][:, 0:512], ones_row_bf[:], mrow[:], True, True,
                  reads=['ones_row_bf', 'mrow'], writes=['pb6'])
            tk.copy(mbc[:], pb[6][:, 0:512], reads=['pb6'],
                    writes=['mbc'], eng='act')
            for fc in range(32):
                bi = HB[fc % 4]
                bank, bk = pb[bi], f'pb{bi}'
                for dc in range(NDC):
                    tk.mm(bank[:, 0:512],
                          w1sb[:, fc * 1024 + dc * 128:fc * 1024 + (dc + 1) * 128],
                          xnw[p][:, dc * WIN:(dc + 1) * WIN], dc == 0, dc == 7,
                          reads=[f'w1_{fc}', f'xnw{p}_{dc}'], writes=[bk])
                tk.act(hsb[:, fc * WIN:(fc + 1) * WIN], bank[:, 0:512], AF.Relu,
                       bias=b1sb[:, fc:fc + 1], reads=[bk, 'b1sb'],
                       writes=[f'h_{fc}'])
                if last:
                    load_w1_fc(e + 1, fc)
            if last:
                tk.dma('sp', b1sb[:], W['b1'][e + 1], writes=['b1sb'])
            for fc in range(32):
                for dc in range(NDC):
                    tk.mm(pb[dc][:, 0:512],
                          w2sb[:, fc * 1024 + dc * 128:fc * 1024 + (dc + 1) * 128],
                          hsb[:, fc * WIN:(fc + 1) * WIN], fc == 0, fc == 31,
                          reads=[f'w2_{fc // 4}', f'h_{fc}'], writes=[f'pb{dc}'])
                if last and fc % 4 == 3:
                    load_w2_q(e + 1, fc // 4)
            for dc in range(NDC):
                yp = dc % 2
                tk.ts(ysb[yp][:], pb[dc][:, 0:512], b2sb[:, dc:dc + 1], None,
                      ALU.add, reads=[f'pb{dc}', 'b2sb'], writes=[f'ysb{yp}'])
                y4 = dc % 4
                tk.tt(ymb[y4][:], ysb[yp][:], mbc[:], ALU.mult,
                      reads=[f'ysb{yp}', 'mbc'], writes=[f'ymb{y4}'])
                tk.dma('sp', ge_h[e, dc * 128:(dc + 1) * 128, w * WIN:(w + 1) * WIN],
                       ymb[y4][:], reads=[f'ymb{y4}'], label=f'stg_{e}_{w}_{dc}')
            if last:
                tk.dma('sp', b2sb[:], W['b2'][e + 1], writes=['b2sb'])


def emit_combine(tk, E, nc, xsrc, s_h, g_h, xdst, scope=None, tag='cb', ge=False):
    g = tag + '_'
    A = _mk_alloc(nc, scope, g)
    xw = [A(f'xw{i}', [128, NDC * WIN], F32) for i in range(2)]
    sw = [A(f'sw{i}', [128, NDC * WIN], BF16) for i in range(2)]
    NG = 8 if ge else 1
    gw = [A(f'gw{i}', [128, NG * WIN], BF16) for i in range(8)]
    gs = [A(f'gs{i}', [128, WIN], F32) for i in range(8)]
    sg = [A(f'sg{i}', [128, WIN], F32) for i in range(2)]
    gsl = [A(f'gsl{i}', [128, WIN], F32) for i in range(2)]
    xo = [A(f'xo{i}', [128, NDC * WIN], F32) for i in range(2)]
    for w in range(NW):
        p = w % 2
        for dc in range(NDC):
            sl = (slice(dc * 128, (dc + 1) * 128), slice(w * WIN, (w + 1) * WIN))
            tk.dma('act', xw[p][:, dc * WIN:(dc + 1) * WIN], xsrc[sl[0], sl[1]],
                   writes=[f'xw{p}_{dc}'])
            tk.dma('act', sw[p][:, dc * WIN:(dc + 1) * WIN], s_h[sl[0], sl[1]],
                   writes=[f'sw{p}_{dc}'])
            q4 = dc
            if ge:
                src = g_h[:, sl[0], sl[1]].transpose([1, 0, 2])
                tk.dma('sp', gw[q4].rearrange('p (e j) -> p e j', e=8), src,
                       writes=[f'gw{q4}_0'])
            else:
                tk.dma('sp', gw[q4][:, 0:WIN], g_h[sl[0], sl[1]],
                       writes=[f'gw{q4}_0'])
        for dc in range(NDC):
            q = dc % 2
            q4 = dc
            if ge:
                v = gw[q4].rearrange('p (e j) -> p j e', e=8)
                tk.red(gs[q4][:].unsqueeze(2), v, ALU.add,
                       reads=[f'gw{q4}_0'], writes=[f'gs{q4}'])
                gsrc, gk = gs[q4][:], f'gs{q4}'
            else:
                gsrc, gk = gw[q4][:, 0:WIN], f'gw{q4}_0'
            tk.act(gsl[q][:], gsrc, AF.Silu, reads=[gk], writes=[f'gsl{q}'])
            tk.tt(sg[q][:], sw[p][:, dc * WIN:(dc + 1) * WIN], gsl[q][:], ALU.mult,
                  reads=[f'sw{p}_{dc}', f'gsl{q}'], writes=[f'sg{q}'])
            tk.tt(xo[p][:, dc * WIN:(dc + 1) * WIN],
                  xw[p][:, dc * WIN:(dc + 1) * WIN], sg[q][:], ALU.add,
                  reads=[f'xw{p}_{dc}', f'sg{q}'], writes=[f'xo{p}_{dc}'])
            tk.dma('sp', xdst[dc * 128:(dc + 1) * 128, w * WIN:(w + 1) * WIN],
                   xo[p][:, dc * WIN:(dc + 1) * WIN], reads=[f'xo{p}_{dc}'],
                   label=f'stxo_{w}_{dc}')


def emit_attn1(tk, E, nc, xn_h, s_h, W, scope=None):
    pb = E['pb']
    iden_bf = E['iden_bf']; mtri = E['mtri_bf']
    g = 'a1_'
    A = _mk_alloc(nc, scope, g)
    wq = A('wq', [128, 4096], BF16); wk = A('wk', [128, 4096], BF16)
    wkT = A('wkT', [128, 4096], BF16)
    wvT = A('wvT', [128, 8192], BF16)
    wo = A('wo', [128, 8192], BF16)
    bq = A('bq', [32, 16], F32); bk = A('bk', [32, 16], F32)
    bo = A('bo', [128, 8], F32)
    bvr = A('bvr', [1, 1024], BF16); bkr = A('bkr', [1, 512], BF16)
    bvb = A('bvb', [128, 1024], BF16); bkb = A('bkb', [128, 512], BF16)
    for nm, sb in [('wq1', wq), ('wk1', wk), ('wkT1', wkT), ('wvT1', wvT),
                   ('wo1', wo)]:
        tk.dma('pool', sb[:], W[nm].rearrange('p a b -> p (a b)') if W[nm].ndim == 3
               else W[nm].rearrange('p a b c -> p (a b c)'), writes=[nm])
    tk.dma('sp', bq[:], W['bq1'], writes=['bq'])
    tk.dma('sp', bk[:], W['bk1'], writes=['bk'])
    tk.dma('sp', bo[:], W['bo1'], writes=['bo'])
    tk.dma('pool', bvr[:], W['bv1'], writes=['bvr'])
    tk.dma('pool', bkr[:], W['bk_row'], writes=['bkr'])
    for half in range(2):
        tk.mm(pb[6][:, 0:512], E['ones_row_bf'][:], bvr[0:1, half * 512:(half + 1) * 512],
              True, True, reads=['ones_row_bf', 'bvr'], writes=['pb6'])
        tk.copy(bvb[:, half * 512:(half + 1) * 512], pb[6][:, 0:512],
                reads=['pb6'], writes=['bvb'], eng='act')
    tk.mm(pb[6][:, 0:512], E['ones_row_bf'][:], bkr[0:1, :], True, True,
          reads=['ones_row_bf', 'bkr'], writes=['pb6'])
    tk.copy(bkb[:], pb[6][:, 0:512], reads=['pb6'], writes=['bkb'], eng='act')
    S1f = A('S1f', [32, 1040], F32); S1b = A('S1b', [32, 1040], BF16)
    tk.memset(S1f[:], 0.0, writes=[f'S1f_{h}' for h in range(16)])
    tk.memset(S1b[:], 0.0, writes=[f'S1b_{h}' for h in range(16)])
    xnw = [A(f'xnw{i}', [128, NDC * WIN], BF16) for i in range(2)]
    qsw = [A(f'qsw{i}', [32, 16 * WIN], BF16) for i in range(2)]
    ksw = [A(f'ksw{i}', [32, 16 * WIN], BF16) for i in range(2)]
    ktp0 = A('ktp0', [128, 4 * 512], BF16)
    ktp = [ktp0, ktp0]
    v1sb = [A(f'v1sb{i}', [128, 4 * 1040], BF16) for i in range(2)]
    tmf = [A(f'tmf{i}', [128, 512], F32) for i in range(3)]
    atb = [A(f'atb{i}', [128, 128], BF16) for i in range(2)]
    zc = A('zc', [128, 2], F32)
    otm = [A(f'otm{i}', [128, 1024], BF16) for i in range(2)]
    off = A('off', [128, 8 * WIN], BF16)
    sfw0 = A('sfw0', [128, NDC * WIN], BF16)
    sfw = [sfw0, sfw0]
    for i in range(2):
        tk.memset(v1sb[i].rearrange('p (t h c) -> p t h c', t=4, h=16)[:, :, :, 64:65],
                  1.0, writes=[f'v1sb{i}_ones'], eng='pool')

    def phi_fm(psum_ap, pkey, bcol, bkey, dst, dkey):
        tk.ts(tmf[0][0:32, :], psum_ap, bcol, 0.0, ALU.add, op1=ALU.min,
              reads=[pkey, bkey], writes=['tmf0'])
        tk.ts(tmf[1][0:32, :], psum_ap, bcol, 0.0, ALU.add, op1=ALU.max,
              reads=[pkey, bkey], writes=['tmf1'])
        tk.act(tmf[2][0:32, :], tmf[0][0:32, :], AF.Exp,
               reads=['tmf0'], writes=['tmf2'])
        tk.tt(dst, tmf[2][0:32, :], tmf[1][0:32, :], ALU.add,
              reads=['tmf2', 'tmf1'], writes=[dkey])

    def load_xn(p, w):
        for dc in range(NDC):
            tk.dma('sp', xnw[p][:, dc * WIN:(dc + 1) * WIN],
                   xn_h[dc * 128:(dc + 1) * 128, w * WIN:(w + 1) * WIN],
                   writes=[f'xnw{p}_{dc}'], label=f'l1xn_w{w}_d{dc}')

    for w in range(NW):
        p = w % 2
        load_xn(p, w)
        xkeys = [f'xnw{p}_{dc}' for dc in range(NDC)]
        for (wsb, wkey, bb, bbk, dst, dkey) in [(wq, 'wq1', bq, 'bq', qsw[p], f'qsw{p}'),
                                                (wk, 'wk1', bk, 'bk', ksw[p], f'ksw{p}')]:
            for h in range(16):
                woff = (h // 2) * 64 + (h % 2) * 32
                for dc in range(NDC):
                    tk.mm(pb[0][0:32, 0:512],
                          wsb[:, dc * 512 + woff:dc * 512 + woff + 32],
                          xnw[p][:, dc * WIN:(dc + 1) * WIN], dc == 0, dc == 7,
                          reads=[wkey, xkeys[dc]], writes=['pb0'])
                phi_fm(pb[0][0:32, 0:512], 'pb0', bb[:, h:h + 1], bbk,
                       dst[0:32, h * WIN:(h + 1) * WIN], f'{dkey}_{h}')
        for tau in range(NTAU):
            for dc in range(NDC):
                tk.mm(pb[1][:, 0:512],
                      xnw[p][:, dc * WIN + tau * 128:dc * WIN + (tau + 1) * 128],
                      wkT[:, dc * 512:(dc + 1) * 512], dc == 0, dc == 7,
                      reads=['wkT1', xkeys[dc]], writes=['pb1'])
            tk.tt(tmf[0][:], pb[1][:, 0:512], bkb[:], ALU.add,
                  reads=['pb1', 'bkb'], writes=['tmf0'])
            tk.ts(tmf[1][:], tmf[0][:], 0.0, None, ALU.max,
                  reads=['tmf0'], writes=['tmf1'])
            tk.ts(tmf[0][:], tmf[0][:], 0.0, None, ALU.min,
                  reads=['tmf0'], writes=['tmf0'])
            tk.act(tmf[2][:], tmf[0][:], AF.Exp, reads=['tmf0'], writes=['tmf2'])
            tk.tt(ktp[p][:, tau * 512:(tau + 1) * 512], tmf[2][:], tmf[1][:],
                  ALU.add, reads=['tmf2', 'tmf1'], writes=[f'ktp_{tau}'])
        for tau in range(NTAU):
            for half in range(2):
                bank, bk2 = (pb[2], 'pb2') if half == 0 else (pb[3], 'pb3')
                for dc in range(NDC):
                    tk.mm(bank[:, 0:512],
                          xnw[p][:, dc * WIN + tau * 128:dc * WIN + (tau + 1) * 128],
                          wvT[:, dc * 1024 + half * 512:dc * 1024 + half * 512 + 512],
                          dc == 0, dc == 7, reads=['wvT1', xkeys[dc]], writes=[bk2])
                dst = v1sb[p].rearrange('p (t h c) -> p t h c', t=4, h=16)[
                    :, tau, half * 8:(half + 1) * 8, 0:64]
                tk.tt(dst, bank[:, 0:512].rearrange('p (h c) -> p h c', h=8),
                      bvb[:, half * 512:(half + 1) * 512].rearrange('p (h c) -> p h c', h=8),
                      ALU.add, reads=[bk2, 'bvb'], writes=[f'v1sb{p}_{tau}'])
        for tau in range(NTAU):
            for h in range(16):
                hp = h % 2
                qsl = qsw[p][0:32, h * WIN + tau * 128:h * WIN + (tau + 1) * 128]
                ksl = ksw[p][0:32, h * WIN + tau * 128:h * WIN + (tau + 1) * 128]
                tk.mm(pb[4][:, 0:128], ksl, qsl, True, True,
                      reads=[f'ksw{p}_{h}', f'qsw{p}_{h}'], writes=['pb4'])
                tk.tt(atb[hp][:], pb[4][:, 0:128], mtri[:], ALU.mult,
                      reads=['pb4', 'mtri_bf'], writes=[f'atb{hp}'])
                v1 = v1sb[p].rearrange('p (t h c) -> p t h c', t=4, h=16)[:, tau, h, :]
                tk.mm(pb[5][:, 0:65], qsl, S1b[0:32, h * 65:(h + 1) * 65],
                      True, False, reads=[f'qsw{p}_{h}', f'S1b_{h}'], writes=['pb5'])
                tk.mm(pb[5][:, 0:65], atb[hp][:], v1, False, True,
                      reads=[f'atb{hp}', f'v1sb{p}_{tau}', f'v1sb{p}_ones'],
                      writes=['pb5'])
                tk.ts(zc[:, 0:1], pb[5][:, 64:65], 1e-6, None, ALU.add,
                      reads=['pb5'], writes=['zc'])
                tk.recip(zc[:, 1:2], zc[:, 0:1], reads=['zc'], writes=['zc'])
                tk.ts(otm[tau % 2][:, h * 64:(h + 1) * 64], pb[5][:, 0:64],
                      zc[:, 1:2], None, ALU.mult, reads=['pb5', 'zc'],
                      writes=[f'otm{tau % 2}_{h}'])
                tk.mm(pb[6][0:32, 0:65], ktp[p][:, tau * 512 + h * 32:tau * 512 + (h + 1) * 32],
                      v1, True, True,
                      reads=[f'ktp_{tau}', f'v1sb{p}_{tau}', f'v1sb{p}_ones'],
                      writes=['pb6'])
                tk.tt(S1f[:, h * 65:(h + 1) * 65], S1f[:, h * 65:(h + 1) * 65],
                      pb[6][0:32, 0:65], ALU.add, reads=['pb6', f'S1f_{h}'],
                      writes=[f'S1f_{h}'])
                tk.copy(S1b[:, h * 65:(h + 1) * 65], S1f[:, h * 65:(h + 1) * 65],
                        reads=[f'S1f_{h}'], writes=[f'S1b_{h}'], eng='pool')
            for hc in range(8):
                tk.tr(pb[7].bitcast(BF16)[:, 0:128],
                      otm[tau % 2][:, hc * 128:(hc + 1) * 128], iden_bf[:],
                      reads=[f'otm{tau % 2}_{hh}' for hh in range(2 * hc, 2 * hc + 2)]
                      + ['iden_bf'], writes=['pb7'])
                tk.copy(off[:, hc * WIN + tau * 128:hc * WIN + (tau + 1) * 128],
                        pb[7].bitcast(BF16)[:, 0:128], reads=['pb7'],
                        writes=[f'off_{hc}'], eng='act')
        for dm in range(NDC):
            for hc in range(NDC):
                tk.mm(pb[0][:, 0:512], wo[:, hc * 1024 + dm * 128:hc * 1024 + dm * 128 + 128],
                      off[:, hc * WIN:(hc + 1) * WIN], hc == 0, hc == 7,
                      reads=['wo1', f'off_{hc}'], writes=['pb0'])
            tk.ts(sfw[p][:, dm * WIN:(dm + 1) * WIN], pb[0][:, 0:512],
                  bo[:, dm:dm + 1], None, ALU.add, reads=['pb0', 'bo'],
                  writes=[f'sfw_{dm}'])
        for dm in range(NDC):
            tk.dma('sp', s_h[dm * 128:(dm + 1) * 128, w * WIN:(w + 1) * WIN],
                   sfw[p][:, dm * WIN:(dm + 1) * WIN], reads=[f'sfw_{dm}'],
                   label=f'sts1_w{w}_d{dm}')


def emit_ffn1(tk, E, nc, xn_h, g_h, W, scope=None):
    pb = E['pb']
    SC = 1.0 / 127.0
    g = 'f1_'
    A = _mk_alloc(nc, scope, g)
    wg = A('wg', [128, 8 * 3072], BF16)
    wu = A('wu', [128, 8 * 3072], BF16)
    wd = A('wd', [128, 24 * 1024], BF16)
    for nm, sb in [('wgT', wg), ('wuT', wu), ('wdT', wd)]:
        tk.dma('pool', sb[:], W[nm].rearrange('p a b -> p (a b)'), writes=[nm])
    xnw = [A(f'xnw{i}', [128, NDC * WIN], BF16) for i in range(2)]
    hsb = A('hsb', [128, 24 * WIN], BF16)
    gsl = [A(f'gsl{i}', [128, WIN], BF16) for i in range(2)]
    usl = [A(f'usl{i}', [128, WIN], BF16) for i in range(2)]
    gw = [A(f'gw{i}', [128, WIN], BF16) for i in range(2)]
    for w in range(NW):
        p = w % 2
        for dc in range(NDC):
            tk.dma('sp', xnw[p][:, dc * WIN:(dc + 1) * WIN],
                   xn_h[dc * 128:(dc + 1) * 128, w * WIN:(w + 1) * WIN],
                   writes=[f'xnw{p}_{dc}'], label=f'f1xn_w{w}_d{dc}')
        for m3 in range(24):
            q = m3 % 2
            bank, bk = (pb[0], 'pb0') if q == 0 else (pb[1], 'pb1')
            for dc in range(NDC):
                tk.mm(bank[:, 0:512],
                      wg[:, dc * 3072 + m3 * 128:dc * 3072 + (m3 + 1) * 128],
                      xnw[p][:, dc * WIN:(dc + 1) * WIN], dc == 0, dc == 7,
                      reads=['wgT', f'xnw{p}_{dc}'], writes=[bk])
            tk.act(gsl[q][:], bank[:, 0:512], AF.Silu, scale=SC,
                   reads=[bk], writes=[f'gsl{q}'])
            bank2, bk2 = (pb[2], 'pb2') if q == 0 else (pb[3], 'pb3')
            for dc in range(NDC):
                tk.mm(bank2[:, 0:512],
                      wu[:, dc * 3072 + m3 * 128:dc * 3072 + (m3 + 1) * 128],
                      xnw[p][:, dc * WIN:(dc + 1) * WIN], dc == 0, dc == 7,
                      reads=['wuT', f'xnw{p}_{dc}'], writes=[bk2])
            tk.act(usl[q][:], bank2[:, 0:512], AF.Copy, scale=SC,
                   reads=[bk2], writes=[f'usl{q}'])
            tk.tt(hsb[:, m3 * WIN:(m3 + 1) * WIN], gsl[q][:], usl[q][:], ALU.mult,
                  reads=[f'gsl{q}', f'usl{q}'], writes=[f'h_{m3}'])
        for dc in range(NDC):
            q = dc % 2
            bank, bk = (pb[4], 'pb4') if q == 0 else (pb[5], 'pb5')
            for m3 in range(24):
                tk.mm(bank[:, 0:512],
                      wd[:, m3 * 1024 + dc * 128:m3 * 1024 + (dc + 1) * 128],
                      hsb[:, m3 * WIN:(m3 + 1) * WIN], m3 == 0, m3 == 23,
                      reads=['wdT', f'h_{m3}'], writes=[bk])
            tk.act(gw[q][:], bank[:, 0:512], AF.Copy, scale=SC,
                   reads=[bk], writes=[f'gw{q}'])
            tk.dma('sp', g_h[dc * 128:(dc + 1) * 128, w * WIN:(w + 1) * WIN],
                   gw[q][:], reads=[f'gw{q}'], label=f'stg1_{w}_{dc}')


def emit_pe_all(tk, E, nc, x_all, routerT, allpe, scope=None):
    pb = E['pb']
    iden = E['iden_f32']; ones_col = E['ones_col']; ones_row = E['ones_row']
    cst = E['cst']; irev = E['irev']
    g = 'pa_'
    A = _mk_alloc(nc, scope, g)
    xw = [A(f'xw{i}', [128, NDC * WIN], F32) for i in range(2)]
    x2w = [A(f'x2w{i}', [128, NDC * WIN], F32) for i in range(2)]
    stats = [A(f'stats{q}', [128, 24], F32) for q in range(2)]
    routerT_sb = A('routerT_sb', [128, 64], F32)
    negSr = A('negSr', [1, 8], F32)
    srb = A('srb', [128, 8], F32)
    srcol = A('srcol', [8, 1], F32)
    lgt = [A(f'lgt{q}', [128, 32], F32) for q in range(2)]
    elg = [A(f'elg{q}', [128, 32], F32) for q in range(2)]
    eqt = [A(f'eqt{q}', [128, 32], F32) for q in range(2)]
    t2b = [A(f't2b{q}', [128, 32], F32) for q in range(2)]
    mx4 = [A(f'mx4{q}', [128, 16], F32) for q in range(2)]
    pecols = [A(f'pecols{q}', [128, 8], F32) for q in range(2)]
    peacc = A('peacc', [8, 1024], F32)
    irev4 = A('irev4', [128, 32], F32)
    tk.dma('sp', routerT_sb[:], routerT.rearrange('p a b -> p (a b)'),
           writes=['routerT_sb'])
    for rr in range(4):
        tk.copy(irev4[:, rr * 8:(rr + 1) * 8], irev[:], reads=['irev'],
                writes=['irev4'], eng='pool')
    for dc in range(NDC):
        tk.mm(pb[5][0:8, 0:1], routerT_sb[:, dc * 8:(dc + 1) * 8], ones_col[:],
              dc == 0, dc == 7, reads=['routerT_sb', 'ones_col'], writes=['pb5'])
    tk.ts(srcol[:], pb[5][0:8, 0:1], -1.0, None, ALU.mult,
          reads=['pb5'], writes=['srcol'])
    tk.tr(pb[6][0:1, 0:8], srcol[:], iden[0:8, 0:8],
          reads=['srcol', 'iden_f32'], writes=['pb6'])
    tk.copy(negSr[:], pb[6][0:1, 0:8], reads=['pb6'], writes=['negSr'])
    tk.mm(pb[6][:, 8:16], ones_row[:], negSr[:], True, True,
          reads=['ones_row', 'negSr'], writes=['pb6'])
    tk.copy(srb[:], pb[6][:, 8:16], reads=['pb6'], writes=['srb'])
    srows = A('srows', [1, 1024], F32)
    for r in range(8):
        for w in range(NW):
            p = w % 2
            q = w % 2
            bs, bq2, br = (pb[0], pb[1], pb[2]) if q == 0 else (pb[3], pb[4], pb[5])
            ks, kq, kr = (f'pb{j}' for j in ((0, 1, 2) if q == 0 else (3, 4, 5)))
            bt, kt = (pb[6], 'pb6') if q == 0 else (pb[7], 'pb7')
            st = stats[q]; sk = f'st{q}'
            for dc in range(NDC):
                tk.dma('sp', xw[p][:, dc * WIN:(dc + 1) * WIN],
                       x_all[r, dc * 128:(dc + 1) * 128, w * WIN:(w + 1) * WIN],
                       writes=[f'xw{p}_{dc}'], label=f'pa_{r}_{w}_{dc}')
            for dc in range(NDC):
                tk.act(x2w[p][:, dc * WIN:(dc + 1) * WIN],
                       xw[p][:, dc * WIN:(dc + 1) * WIN], AF.Square,
                       reads=[f'xw{p}_{dc}'], writes=[f'x2w{p}_{dc}'])
            for dc in range(NDC):
                tk.mm(bs[0:1, 0:512], ones_col[:], xw[p][:, dc * WIN:(dc + 1) * WIN],
                      dc == 0, dc == 7, reads=[f'xw{p}_{dc}', 'ones_col'], writes=[ks])
                tk.mm(bq2[0:1, 0:512], ones_col[:], x2w[p][:, dc * WIN:(dc + 1) * WIN],
                      dc == 0, dc == 7, reads=[f'x2w{p}_{dc}', 'ones_col'], writes=[kq])
            tk.copy(srows[0:1, 0:512], bs[0:1, 0:512], reads=[ks],
                    writes=['srows'], eng='act')
            tk.copy(srows[0:1, 512:1024], bq2[0:1, 0:512], reads=[kq],
                    writes=['srows'], eng='act')
            for tau in range(NTAU):
                for dc in range(NDC):
                    xt = xw[p][:, dc * WIN + tau * 128: dc * WIN + (tau + 1) * 128]
                    tk.mm(br[:, 8 * tau:8 * tau + 8], xt,
                          routerT_sb[:, dc * 8:(dc + 1) * 8], dc == 0, dc == 7,
                          reads=[f'xw{p}_{dc}', 'routerT_sb'], writes=[kr])
            for tau in range(NTAU):
                tk.tr(bt[:, tau:tau + 1], srows[0:1, tau * 128:(tau + 1) * 128],
                      iden[0:1, 0:1], reads=['srows', 'iden_f32'], writes=[kt])
                tk.tr(bt[:, 4 + tau:5 + tau],
                      srows[0:1, 512 + tau * 128:512 + (tau + 1) * 128],
                      iden[0:1, 0:1], reads=['srows', 'iden_f32'], writes=[kt])
            tk.ts(st[:, 0:4], bt[:, 0:4], 1.0 / D, None, ALU.mult,
                  reads=[kt], writes=[sk])
            tk.act(st[:, 4:8], st[:, 0:4], AF.Square, reads=[sk], writes=[sk])
            tk.stt(st[:, 8:12], bt[:, 4:8], 1.0 / D, st[:, 4:8],
                   ALU.mult, ALU.subtract, reads=[kt, sk], writes=[sk])
            tk.act(st[:, 12:16], st[:, 8:12], AF.Ln, bias=cst[:, 0:1],
                   reads=[sk, 'cst'], writes=[sk])
            tk.act(st[:, 16:20], st[:, 12:16], AF.Exp, scale=-0.5,
                   reads=[sk], writes=[sk])
            tk.stt(st[:, 20:24], st[:, 0:4], -1.0, st[:, 16:20],
                   ALU.mult, ALU.mult, reads=[sk], writes=[sk])
            v_t2 = t2b[q].rearrange('p (t e) -> p t e', t=4)
            tk.tt(v_t2, st[:, 20:24].unsqueeze(2).to_broadcast([128, 4, 8]),
                  srb[:].unsqueeze(1).to_broadcast([128, 4, 8]),
                  ALU.mult, reads=[sk, 'srb'], writes=[f't2b{q}'])
            v_lgt = lgt[q].rearrange('p (t e) -> p t e', t=4)
            tk.tt(v_lgt, br[:, 0:32].rearrange('p (t e) -> p t e', t=4),
                  st[:, 16:20].unsqueeze(2).to_broadcast([128, 4, 8]),
                  ALU.mult, reads=[kr, sk], writes=[f'lgt{q}'])
            tk.tt(lgt[q][:], lgt[q][:], t2b[q][:], ALU.add,
                  reads=[f'lgt{q}', f't2b{q}'], writes=[f'lgt{q}'])
            tk.red(mx4[q][:, 0:4].unsqueeze(2), v_lgt, ALU.max,
                   negate=True, reads=[f'lgt{q}'], writes=[f'mx4{q}'])
            v_el = elg[q].rearrange('p (t e) -> p t e', t=4)
            tk.tt(v_el, v_lgt,
                  mx4[q][:, 0:4].unsqueeze(2).to_broadcast([128, 4, 8]),
                  ALU.add, reads=[f'lgt{q}', f'mx4{q}'], writes=[f'elg{q}'])
            tk.act(elg[q][:], elg[q][:], AF.Exp, reads=[f'elg{q}'], writes=[f'elg{q}'])
            tk.red(mx4[q][:, 4:8].unsqueeze(2), v_el, ALU.add,
                   reads=[f'elg{q}'], writes=[f'mx4{q}'])
            tk.recip(mx4[q][:, 8:12], mx4[q][:, 4:8],
                     reads=[f'mx4{q}'], writes=[f'mx4{q}'])
            tk.ts(eqt[q][:], elg[q][:], 1.0, None, ALU.is_ge,
                  reads=[f'elg{q}'], writes=[f'eqt{q}'])
            tk.tt(eqt[q][:], eqt[q][:], irev4[:], ALU.mult,
                  reads=[f'eqt{q}', 'irev4'], writes=[f'eqt{q}'])
            tk.red(mx4[q][:, 12:16].unsqueeze(2),
                   eqt[q].rearrange('p (t e) -> p t e', t=4), ALU.max,
                   reads=[f'eqt{q}'], writes=[f'mx4{q}'])
            vp = pecols[q].rearrange('p (t c) -> p t c', t=4)
            tk.copy(vp[:, :, 0:1], mx4[q][:, 8:12].unsqueeze(2),
                    reads=[f'mx4{q}'], writes=[f'pe{q}'])
            tk.ts(vp[:, :, 1:2], mx4[q][:, 12:16].unsqueeze(2),
                  -1.0, 7.0, ALU.mult, op1=ALU.add,
                  reads=[f'mx4{q}'], writes=[f'pe{q}'])
            bt, kt = (pb[6], 'pb6') if q == 0 else (pb[7], 'pb7')
            tk.tr(bt[0:8, 128:256], pecols[q][:], iden[:],
                  reads=[f'pe{q}', 'iden_f32'], writes=[kt])
            tk.copy(peacc[:, w * 128:(w + 1) * 128], bt[0:8, 128:256],
                    reads=[kt], writes=[f'peacc_{w}'], eng='act')
        Fv = allpe.rearrange('a (w t j) -> a t w j', w=8, j=128)
        for cc in range(2):
            tk.dma('sp', Fv[2 * r + cc],
                   peacc[cc:8:2, :].rearrange('p (w j) -> p w j', j=128),
                   reads=[f'peacc_{w}' for w in range(NW)],
                   writes=[f'allpe_{r}'], label=f'stpe_{r}_{cc}')


from contextlib import ExitStack
from concourse.bass_utils import run_bass_kernel_spmd

_CONST_KEYS = ['iden', 'ones_col', 'ones_row', 'mtri', 'selA', 'selB', 'rmat',
               'rmatT', 'idxmat', 'irev', 'ones8k']


def build_fused(nc):
    F, B, I = F32, BF16, I32
    ci = {}

    def ein(name, shape, dtype):
        ci[name] = nc.dram_tensor(name, shape, dtype, kind="ExternalInput").ap()
        return ci[name]

    x0 = ein('x0', [1024, 4096], F)
    x_all = ein('x_all', [8, 1024, 4096], F)
    routerT = ein('routerT', [128, 8, 8], F)
    csel = ein('csel', [128, 2], F)
    cselE = ein('cselE', [128, 16], F)
    W0 = {nm: ein(nm, sh, F) for nm, sh in [
        ('wq', [128, 8, 8, 64]), ('wk', [128, 8, 8, 64]), ('wkT', [128, 8, 256]),
        ('wkeT', [128, 8, 256]), ('wvT', [128, 8, 1024]), ('wveT', [128, 8, 1024]),
        ('wout', [128, 8, 1024])]}
    WM = {nm: ein(nm, sh, F) for nm, sh in [
        ('w1', [8, 128, 32, 1024]), ('w2', [8, 128, 32, 1024]),
        ('b1', [8, 128, 32]), ('b2', [8, 128, 8])]}
    W1 = {}
    for nm, sh, dt in [('wq1', [128, 8, 8, 64], F), ('wk1', [128, 8, 8, 64], F),
                       ('bq1', [32, 16], F), ('bk1', [32, 16], F),
                       ('wkT1', [128, 8, 512], F), ('wvT1', [128, 8, 1024], F),
                       ('bv1', [1, 1024], F), ('bk_row', [1, 512], F),
                       ('wo1', [128, 8, 1024], F), ('bo1', [128, 8], F),
                       ('wgT', [128, 8, 3072], I), ('wuT', [128, 8, 3072], I),
                       ('wdT', [128, 24, 1024], I)]:
        W1[nm] = ein(nm, sh, dt)
    out_h = nc.dram_tensor('out_h', [1024, 4096], F, kind="ExternalOutput").ap()
    def intl(name, shape, dtype):
        return nc.dram_tensor(name, shape, dtype, kind="Internal").ap()
    allpe = intl('allpe', [16, 4096], F)
    xn0 = intl('xn0', [1024, 4096], B)
    kmask = intl('kmask', [8, 4096], F)
    s0 = intl('s0', [1024, 4096], B)
    g0 = intl('g0', [8, 1024, 4096], B)
    x1 = intl('x1', [1024, 4096], F)
    xn1 = intl('xn1', [1024, 4096], B)
    s1 = intl('s1', [1024, 4096], B)
    g1 = intl('g1', [1024, 4096], B)

    tk = TK(nc)
    with nc.Block() as block:
        E = setup_consts(nc, tk)
        E['pb'] = alloc_psum_banks(nc)
        tk.phase = 'ln0'
        with ExitStack() as sc:
            emit_ln(tk, E, nc, x0, xn0, tag='ln0', scope=sc)
        tk.barrier()
        tk.phase = 'peall'
        with ExitStack() as sc:
            emit_pe_all(tk, E, nc, x_all, routerT, allpe, scope=sc)
        tk.barrier()
        tk.phase = 'routing'
        with ExitStack() as sc:
            emit_routing(tk, E, nc, allpe, csel, cselE, kmask, scope=sc)
        tk.barrier()
        tk.phase = 'attn0'
        with ExitStack() as sc:
            emit_attn0(tk, E, nc, xn0, s0, W0, scope=sc)
        tk.barrier()
        tk.phase = 'moe'
        with ExitStack() as sc:
            emit_moe(tk, E, nc, xn0, kmask, g0, WM, scope=sc)
        tk.barrier()
        tk.phase = 'cb0'
        with ExitStack() as sc:
            emit_combine(tk, E, nc, x0, s0, g0, x1, scope=sc, tag='cb0', ge=True)
        tk.barrier()
        tk.phase = 'ln1'
        with ExitStack() as sc:
            emit_ln(tk, E, nc, x1, xn1, tag='ln1', scope=sc)
        tk.barrier()
        tk.phase = 'attn1'
        with ExitStack() as sc:
            emit_attn1(tk, E, nc, xn1, s1, W1, scope=sc)
        tk.barrier()
        tk.phase = 'ffn1'
        with ExitStack() as sc:
            emit_ffn1(tk, E, nc, xn1, g1, W1, scope=sc)
        tk.barrier()
        tk.phase = 'cb1'
        with ExitStack() as sc:
            emit_combine(tk, E, nc, x1, s1, g1, out_h, scope=sc, tag='cb1')
        tk.flush(block)
    return nc


def kernel(**inputs):
    c = pack_consts()
    w = pack_weights(inputs)
    x = inputs['L0_attn_x']
    xallT = np.ascontiguousarray(np.transpose(x, (0, 2, 1))).astype(np.float32)
    nc = build_fused(bass.Bass("TRN2", target_bir_lowering=False,
                               detect_race_conditions=False))
    cim = {}
    for k2 in _CONST_KEYS:
        key = {'iden': 'iden_f32'}.get(k2, k2)
        cim['c_' + k2] = c[key]
    wm = {'wq': 'l0_wq', 'wk': 'l0_wk', 'wkT': 'l0_wkT', 'wkeT': 'l0_wkeT',
          'wvT': 'l0_wvT', 'wveT': 'l0_wveT', 'wout': 'l0_wout',
          'w1': 'w1', 'w2': 'w2', 'b1': 'b1', 'b2': 'b2',
          'wq1': 'l1_wq', 'wk1': 'l1_wk', 'bq1': 'l1_bq', 'bk1': 'l1_bk',
          'wkT1': 'l1_wkT', 'wvT1': 'l1_wvT', 'bv1': 'l1_bv',
          'bk_row': 'l1_bk_row', 'wo1': 'l1_wo', 'bo1': 'l1_bo',
          'wgT': 'wgT', 'wuT': 'wuT', 'wdT': 'wdT', 'routerT': 'routerT'}
    in_maps = []
    for core in range(8):
        im = dict(cim)
        im['x0'] = np.ascontiguousarray(x[core].T).astype(np.float32)
        im['x_all'] = xallT
        for k2, v in wm.items():
            im[k2] = w[v]
        cs = np.zeros((128, 2), np.float32)
        cse = np.zeros((128, 16), np.float32)
        for gg in range(2):
            for e in range(8):
                cs[e * 16 + 2 * core + gg, gg] = 1.0
                cse[e * 16 + 2 * core + gg, 2 * e + gg] = 1.0
        im['csel'] = cs
        im['cselE'] = cse
        in_maps.append(im)
    res = run_bass_kernel_spmd(nc, in_maps, core_ids=list(range(8)))
    out = np.stack([np.ascontiguousarray(r['out_h'].T) for r in res.results],
                   axis=0)
    return out.astype(np.float32)


# version 31
# speedup vs baseline: 1.0954x; 1.0237x over previous
import numpy as np
import concourse.bass as bass
import concourse.mybir as mybir

F32 = mybir.dt.float32
BF16 = mybir.dt.bfloat16
I32 = mybir.dt.int32
AF = mybir.ActivationFunctionType
ALU = mybir.AluOpType
AX = mybir.AxisListType.X

D = 1024
T = 4096
NW = 8
WIN = 512
NTAU = 4
CT = 128
NDC = 8


class Op:
    __slots__ = ('stream', 'fn', 'deps', 'sem', 'inc', 'idx', 'label', 'phase')

    def __init__(self, stream, fn, sem, inc, idx, label):
        self.stream = stream
        self.fn = fn
        self.deps = []
        self.sem = sem
        self.inc = inc
        self.idx = idx
        self.label = label


class TK:
    STREAMS = ('pe', 'act', 'dve', 'pool', 'sp')
    ENGSEM = {'pe': 'pe', 'act': 'act', 'dve': 'dve', 'pool': 'pool', 'sp': 'spe'}

    NQSEM = 4

    def __init__(self, nc):
        self.nc = nc
        self.ops = []
        self.writer = {}
        self.readers = {}
        self.dma_ring = {}
        self.barrier_ops = []
        self.phase = 'init'

    def op(self, stream, fn, reads=(), writes=(), label='', is_dma=False):
        assert stream in self.STREAMS
        extra_dep = None
        if is_dma:
            ring = self.dma_ring.setdefault(stream, [])
            slot = len(ring) % self.NQSEM
            sem = f'q_{stream}_{slot}'
            inc = 16
            if len(ring) >= self.NQSEM:
                extra_dep = ring[-self.NQSEM]
            ring.append(None)
        else:
            sem = None
            inc = 1
        o = Op(stream, fn, sem, inc, len(self.ops), label)
        o.phase = self.phase
        if is_dma:
            self.dma_ring[stream][-1] = o
        deps = {}
        if extra_dep is not None:
            deps[extra_dep.idx] = extra_dep
        for k in reads:
            w = self.writer.get(k)
            if w is not None:
                if w.stream == 'pe' and stream == 'pe' and w.inc != 16:
                    continue
                deps[w.idx] = w
        for k in writes:
            w = self.writer.get(k)
            if w is not None and not (w.stream == stream and w.inc != 16):
                deps[w.idx] = w
            for r in self.readers.get(k, ()):
                if r.stream == stream and r.inc != 16:
                    continue
                deps[r.idx] = r
        o.deps = list(deps.values())
        for d in o.deps:
            if d.sem is None:
                d.sem = self.ENGSEM[d.stream]
        for k in writes:
            self.writer[k] = o
            self.readers[k] = []
        for k in reads:
            self.readers.setdefault(k, []).append(o)
        self.ops.append(o)
        return o

    def dma(self, queue, out, in_, reads=(), writes=(), label='', **kw):
        return self.op(queue, lambda e: e.dma_start(out=out, in_=in_, **kw),
                       reads=reads, writes=writes, label=label, is_dma=True)

    def mm(self, out, lhsT, rhs, start, stop, reads, writes, label=''):
        fn = (lambda out, lhsT, rhs, start, stop:
              lambda e: e.matmul(out, lhsT=lhsT, rhs=rhs, start=start, stop=stop,
                                 skip_group_check=True))(out, lhsT, rhs, start, stop)
        return self.op('pe', fn, reads=reads, writes=writes, label=label)

    def tr(self, out, in_, iden, reads, writes, label=''):
        fn = (lambda out, in_, iden:
              lambda e: e.transpose(out, in_=in_, identity=iden))(out, in_, iden)
        return self.op('pe', fn, reads=reads, writes=writes, label=label)

    def act(self, out, in_, func, reads, writes, bias=0.0, scale=1.0, eng='act', label=''):
        fn = (lambda out, in_, func, bias, scale:
              lambda e: e.activation(out, in_, func, bias=bias, scale=scale))(
                  out, in_, func, bias, scale)
        return self.op(eng, fn, reads=reads, writes=writes, label=label)

    def tt(self, out, a, b, opk, reads, writes, eng='dve', label=''):
        fn = (lambda out, a, b, opk:
              lambda e: e.tensor_tensor(out, a, b, op=opk))(out, a, b, opk)
        return self.op(eng, fn, reads=reads, writes=writes, label=label)

    def ts(self, out, a, s1, s2, op0, reads, writes, op1=None, accum=None, eng='dve', label=''):
        def mk(out, a, s1, s2, op0, op1, accum):
            def f(e):
                kw = {}
                if op1 is not None:
                    kw['op1'] = op1
                if accum is not None:
                    kw['accum_out'] = accum
                return e.tensor_scalar(out, a, s1, s2, op0=op0, **kw)
            return f
        return self.op(eng, mk(out, a, s1, s2, op0, op1, accum),
                       reads=reads, writes=writes, label=label)

    def stt(self, out, in0, scalar, in1, op0, op1, reads, writes, eng='dve', label=''):
        fn = (lambda out, in0, scalar, in1, op0, op1:
              lambda e: e.scalar_tensor_tensor(out, in0, scalar, in1, op0=op0, op1=op1))(
                  out, in0, scalar, in1, op0, op1)
        return self.op(eng, fn, reads=reads, writes=writes, label=label)

    def red(self, out, in_, opk, reads, writes, negate=False, eng='dve', label=''):
        fn = (lambda out, in_, opk, negate:
              lambda e: e.tensor_reduce(out, in_, AX, opk, negate=negate))(
                  out, in_, opk, negate)
        return self.op(eng, fn, reads=reads, writes=writes, label=label)

    def copy(self, out, in_, reads, writes, eng='dve', label=''):
        if eng == 'act':
            fn = (lambda out, in_: lambda e: e.copy(out, in_))(out, in_)
        else:
            fn = (lambda out, in_: lambda e: e.tensor_copy(out, in_))(out, in_)
        return self.op(eng, fn, reads=reads, writes=writes, label=label)

    def memset(self, ap, val, writes, eng='dve', label=''):
        fn = (lambda ap, val: lambda e: e.memset(ap, val))(ap, val)
        return self.op(eng, fn, reads=(), writes=writes, label=label)

    def recip(self, out, in_, reads, writes, label=''):
        fn = (lambda out, in_: lambda e: e.reciprocal(out, in_))(out, in_)
        return self.op('dve', fn, reads=reads, writes=writes, label=label)

    def cpred(self, out, mask, data, reads, writes, label=''):
        fn = (lambda out, mask, data:
              lambda e: e.copy_predicated(out, mask, data))(out, mask, data)
        return self.op('dve', fn, reads=reads, writes=writes, label=label)

    def barrier(self):
        deps = {}
        for w in self.writer.values():
            deps[w.idx] = w
        for rl in self.readers.values():
            for r in rl:
                deps[r.idx] = r
        dep_list = list(deps.values())
        for d in dep_list:
            if d.sem is None:
                d.sem = self.ENGSEM[d.stream]
        newbar = []
        for s in self.STREAMS:
            o = Op(s, lambda e: e.nop(), None, 1, len(self.ops), 'barrier')
            o.phase = self.phase
            o.deps = [d for d in dep_list if not (d.stream == s and d.inc != 16)] \
                + [b for b in self.barrier_ops]
            self.ops.append(o)
            newbar.append(o)
        self.writer = {}
        self.readers = {}
        self.barrier_ops = newbar
        for b in newbar:
            for d in b.deps:
                if d.sem is None:
                    d.sem = self.ENGSEM[d.stream]

    def flush(self, block):
        nc = self.nc
        semval = {}
        opval = {}
        for o in self.ops:
            if o.sem is not None:
                semval[o.sem] = semval.get(o.sem, 0) + o.inc
                opval[o.idx] = (o.sem, semval[o.sem])
        sems = {name: nc.alloc_semaphore(name) for name in semval}
        finals = dict(semval)
        by_stream = {s: [] for s in self.STREAMS}
        for o in self.ops:
            by_stream[o.stream].append(o)
        engmap = {'pe': block.tensor, 'act': block.scalar, 'dve': block.vector,
                  'pool': block.gpsimd, 'sp': block.sync}
        nceng = {'pe': nc.tensor, 'act': nc.scalar, 'dve': nc.vector,
                 'pool': nc.gpsimd, 'sp': nc.sync}
        for s in self.STREAMS:
            guar = {}

            def emit_stream(eng, s=s, guar=guar):
                cur = [None]
                curscope = [None]
                for o in by_stream[s]:
                    if o.phase != cur[0]:
                        if curscope[0] is not None:
                            curscope[0].__exit__(None, None, None)
                        cm = nc.named_scope(o.phase)
                        cm.__enter__()
                        curscope[0] = cm
                        cur[0] = o.phase
                    for d in o.deps:
                        if d.idx not in opval:
                            raise RuntimeError(f'dep without sem: {d.label}')
                        nm, v = opval[d.idx]
                        if guar.get(nm, 0) < v:
                            eng.wait_ge(sems[nm], v)
                            guar[nm] = v
                    inst = o.fn(eng)
                    if o.sem is not None:
                        if inst is None:
                            raise RuntimeError(f'op {o.label} returned no inst')
                        inst.then_inc(sems[o.sem], o.inc)
                if curscope[0] is not None:
                    curscope[0].__exit__(None, None, None)

            engmap[s](emit_stream)
        def final_wait(eng):
            for nm, v in finals.items():
                eng.wait_ge(sems[nm], v)
        block.sync(final_wait)
        return finals


CS = float(1.0 / np.sqrt(4.0 * np.sqrt(2.0)))
LIN = float(0.5 / CS)
SQB = float(np.sqrt(0.5))

def pair_list():
    pairs = [(a, b) for a in range(16) for b in range(16)]
    return pairs

def pack_consts():
    c = {}
    iden = np.eye(128, dtype=np.float32)
    c['iden_bf'] = iden.astype(np.float32)
    c['iden_f32'] = iden.copy()
    c['ones_col'] = np.ones((128, 1), np.float32)
    c['ones_row'] = np.ones((1, 128), np.float32)
    tri = (np.arange(128)[None, :] >= np.arange(128)[:, None]).astype(np.float32)
    c['mtri'] = tri
    pairs = pair_list()
    selA = np.zeros((64, 273), np.float32)
    selB = np.zeros((64, 273), np.float32)
    def setsel(base):
        selA[base + 16, 0] = 1.0; selB[base + 16, 0] = 1.0
        for a in range(16):
            selA[base + a, 1 + a] = 1.0; selB[base + 16, 1 + a] = 1.0
        for m in range(111):
            a, b = pairs[m]
            selA[base + a, 17 + m] = 1.0; selB[base + b, 17 + m] = 1.0
        for m in range(128):
            a, b = pairs[111 + m]
            selA[base + a, 128 + m] = 1.0; selB[base + b, 128 + m] = 1.0
        for m in range(17):
            a, b = pairs[239 + m]
            selA[base + a, 256 + m] = 1.0; selB[base + b, 256 + m] = 1.0
    setsel(0); setsel(32)
    c['selA'] = selA; c['selB'] = selB
    R = np.zeros((8, 128), np.float32)
    for e in range(8):
        R[e, e * 16:(e + 1) * 16] = 1.0
    c['rmat'] = R
    c['rmatT'] = np.ascontiguousarray(R.T)
    idx = (np.arange(128)[:, None] % 16) * 2048 + np.arange(2048)[None, :]
    c['idxmat'] = idx.astype(np.float32)
    c['irev'] = np.broadcast_to((7 - np.arange(8))[None, :].astype(np.float32),
                                (128, 8)).copy()
    c['ones8k'] = np.ones(8192, np.float32)
    return c

def chunk_lhsT(w, kchunk=128):
    K, M = w.shape
    nk = K // kchunk
    return np.ascontiguousarray(w.reshape(nk, kchunk, M).transpose(1, 0, 2))

def pack_weights(d):
    w = {}
    def packqk(W):
        Wt = W.T * CS
        out = np.zeros((1024, 8, 64), np.float32)
        for p in range(8):
            out[:, p, 0:16] = Wt[:, (2 * p) * 16:(2 * p) * 16 + 16]
            out[:, p, 32:48] = Wt[:, (2 * p + 1) * 16:(2 * p + 1) * 16 + 16]
        return np.ascontiguousarray(
            out.reshape(8, 128, 8, 64).transpose(1, 0, 2, 3))
    w['l0_wq'] = packqk(d['L0_attn_Wq'])
    w['l0_wk'] = packqk(d['L0_attn_Wk'])
    w['l0_wke'] = packqk(d['L0_attn_Wk_enc'])
    w['l0_wkT'] = chunk_lhsT(d['L0_attn_Wk'].T * CS)
    w['l0_wkeT'] = chunk_lhsT(d['L0_attn_Wk_enc'].T * CS)
    w['l0_wvT'] = chunk_lhsT(d['L0_attn_Wv'].T)
    w['l0_wveT'] = chunk_lhsT(d['L0_attn_Wv_enc'].T)
    w['l0_wout'] = chunk_lhsT(d['L0_attn_Wout'].T)
    w['routerT'] = chunk_lhsT(d['L0_ffn_router_w'].T)
    w['w1'] = np.ascontiguousarray(
        d['L0_ffn_W1'].reshape(8, 8, 128, 32, 128).transpose(0, 2, 3, 1, 4)
        .reshape(8, 128, 32, 1024))
    w['w2'] = np.ascontiguousarray(
        d['L0_ffn_W2'].reshape(8, 32, 128, 1024).transpose(0, 2, 1, 3))
    w['b1'] = np.ascontiguousarray(
        d['L0_ffn_b1'].reshape(8, 32, 128).transpose(0, 2, 1))
    w['b2'] = np.ascontiguousarray(
        d['L0_ffn_b2'].reshape(8, 8, 128).transpose(0, 2, 1))
    def packqk1(W, b):
        out = np.zeros((1024, 8, 64), np.float32)
        ob = np.zeros((32, 16), np.float32)
        for p in range(8):
            out[:, p, 0:32] = W[:, (2 * p) * 32:(2 * p) * 32 + 32]
            out[:, p, 32:64] = W[:, (2 * p + 1) * 32:(2 * p + 1) * 32 + 32]
        for h in range(16):
            ob[:, h] = b[h * 32:(h + 1) * 32]
        return (np.ascontiguousarray(out.reshape(8, 128, 8, 64).transpose(1, 0, 2, 3)),
                ob)
    w['l1_wq'], w['l1_bq'] = packqk1(d['L1_attn_wq'], d['L1_attn_bq'])
    w['l1_wk'], w['l1_bk'] = packqk1(d['L1_attn_wk'], d['L1_attn_bk'])
    w['l1_wkT'] = chunk_lhsT(d['L1_attn_wk'])
    w['l1_wvT'] = chunk_lhsT(d['L1_attn_wv'])
    w['l1_bv'] = d['L1_attn_bv'].reshape(1, 1024).copy()
    w['l1_bk_row'] = d['L1_attn_bk'].reshape(1, 512).copy()
    w['l1_wo'] = chunk_lhsT(d['L1_attn_wo'])
    w['l1_bo'] = np.ascontiguousarray(d['L1_attn_bo'].reshape(8, 128).T)
    w['wgT'] = chunk_lhsT(d['L1_ffn_w_gate'].T)
    w['wuT'] = chunk_lhsT(d['L1_ffn_w_up'].T)
    w['wdT'] = chunk_lhsT(d['L1_ffn_w_down'].T)
    return w


def declare_hbm(nc, v2=False):
    H = {}
    def dt_(name, shape, dtype, kind):
        H[name] = nc.dram_tensor(name, shape, dtype, kind=kind).ap()
    return H


def setup_consts(nc, tk):
    E = {}
    ci = {}
    for name, shape, dtype in [
        ('c_iden', [128, 128], F32), ('c_ones_col', [128, 1], F32),
        ('c_ones_row', [1, 128], F32), ('c_mtri', [128, 128], F32),
        ('c_selA', [64, 273], F32), ('c_selB', [64, 273], F32),
        ('c_rmat', [8, 128], F32), ('c_rmatT', [128, 8], F32),
        ('c_idxmat', [128, 2048], F32), ('c_irev', [128, 8], F32),
        ('c_ones8k', [8192], F32),
    ]:
        ci[name] = nc.dram_tensor(name, shape, dtype, kind="ExternalInput").ap()
    E['iden_f32'] = nc.alloc_sbuf_tensor('iden_f32', [128, 128], F32).ap()
    E['iden_bf'] = nc.alloc_sbuf_tensor('iden_bf', [128, 128], BF16).ap()
    E['ones_col'] = nc.alloc_sbuf_tensor('ones_col', [128, 1], F32).ap()
    E['ones_row'] = nc.alloc_sbuf_tensor('ones_row', [1, 128], F32).ap()
    E['ones_row_bf'] = nc.alloc_sbuf_tensor('ones_row_bf', [1, 128], BF16).ap()
    E['mtri_bf'] = nc.alloc_sbuf_tensor('mtri_bf', [128, 128], BF16).ap()
    E['mtri_f32'] = nc.alloc_sbuf_tensor('mtri_f32', [128, 128], F32).ap()
    E['selA'] = nc.alloc_sbuf_tensor('selA', [64, 273], BF16).ap()
    E['selB'] = nc.alloc_sbuf_tensor('selB', [64, 273], BF16).ap()
    E['rmat'] = nc.alloc_sbuf_tensor('rmat', [8, 128], F32).ap()
    E['rmatT'] = nc.alloc_sbuf_tensor('rmatT', [128, 8], F32).ap()
    E['idxmat'] = nc.alloc_sbuf_tensor('idxmat', [128, 2048], F32).ap()
    E['irev'] = nc.alloc_sbuf_tensor('irev', [128, 8], F32).ap()
    E['cst'] = nc.alloc_sbuf_tensor('cst', [128, 8], F32).ap()
    E['ones8k_h'] = ci['c_ones8k']
    tk.dma('sp', E['iden_f32'][:], ci['c_iden'], writes=['iden_f32'])
    tk.dma('pool', E['iden_bf'][:], ci['c_iden'], writes=['iden_bf'])
    tk.dma('sp', E['ones_col'][:], ci['c_ones_col'], writes=['ones_col'])
    tk.dma('sp', E['ones_row'][:], ci['c_ones_row'], writes=['ones_row'])
    tk.dma('pool', E['ones_row_bf'][:], ci['c_ones_row'], writes=['ones_row_bf'])
    tk.dma('pool', E['mtri_bf'][:], ci['c_mtri'], writes=['mtri_bf'])
    tk.dma('sp', E['mtri_f32'][:], ci['c_mtri'], writes=['mtri_f32'])
    tk.dma('pool', E['selA'][:], ci['c_selA'], writes=['selA'])
    tk.dma('pool', E['selB'][:], ci['c_selB'], writes=['selB'])
    tk.dma('sp', E['rmat'][:], ci['c_rmat'], writes=['rmat'])
    tk.dma('sp', E['rmatT'][:], ci['c_rmatT'], writes=['rmatT'])
    tk.dma('sp', E['idxmat'][:], ci['c_idxmat'], writes=['idxmat'])
    tk.dma('sp', E['irev'][:], ci['c_irev'], writes=['irev'])
    tk.op('dve', lambda e: e.memset(E['cst'][:, 0:1], 1e-5), writes=['cst'])
    tk.op('dve', lambda e: e.memset(E['cst'][:, 1:2], 0.70710678), writes=['cst'])
    tk.op('dve', lambda e: e.memset(E['cst'][:, 2:8], 0.0), writes=['cst'])
    return E


def alloc_psum_banks(nc, n=8):
    return [nc.alloc_psum_tensor(f'pb{i}', [128, 512], F32).ap() for i in range(n)]



def _mk_alloc(nc, scope, prefix):
    def A(n, sh, dt):
        if scope is None:
            return nc.alloc_sbuf_tensor(prefix + n, sh, dt).ap()
        h = scope.enter_context(nc.sbuf_tensor(prefix + n, sh, dt))
        return h.ap()
    return A

def emit_ln(tk, E, nc, xsrc, xn_h, pe_h=None, routerT=None, tag='ln0', scope=None):
    pb = E['pb']
    iden = E['iden_f32']; ones_col = E['ones_col']; ones_row = E['ones_row']
    cst = E['cst']; irev = E['irev']
    g = tag + '_'
    A = _mk_alloc(nc, scope, g)
    xw = [A(f'xw{i}', [128, NDC * WIN], F32) for i in range(2)]
    xnw = [A(f'xnw{i}', [128, NDC * WIN], BF16) for i in range(2)]
    x2b = [A(f'x2b{i}', [128, 128], F32) for i in range(2)]
    stats = A('stats', [128, 8], F32)
    rows3 = A('rows3', [1, 384], F32)
    xnt = [A(f'xnt{i}', [128, 128], F32) for i in range(2)]
    rt = pe_h is not None
    if rt:
        routerT_sb = A('routerT_sb', [128, 64], F32)
        negSr = A('negSr', [1, 8], F32)
        srcol = A('srcol', [8, 1], F32)
        lgt = A('lgt', [128, 8], F32)
        elg = A('elg', [128, 8], F32)
        eqt = A('eqt', [128, 8], F32)
        eqt2 = A('eqt2', [128, 8], F32)
        lgtx = A('lgtx', [128, 4], F32)
        pecols = A('pecols', [128, 2], F32)
        peacc = A('peacc', [2, 4096], F32)
        tk.dma('sp', routerT_sb[:], routerT.rearrange('p a b -> p (a b)'),
               writes=['routerT_sb'])
        for dc in range(NDC):
            tk.mm(pb[5][0:8, 0:1], routerT_sb[:, dc * 8:(dc + 1) * 8], ones_col[:],
                  dc == 0, dc == 7, reads=['routerT_sb', 'ones_col'], writes=['pb5'])
        tk.ts(srcol[:], pb[5][0:8, 0:1], -1.0, None, ALU.mult,
              reads=['pb5'], writes=['srcol'])
        tk.tr(pb[6][0:1, 0:8], srcol[:], iden[0:8, 0:8],
              reads=['srcol', 'iden_f32'], writes=['pb6'])
        tk.copy(negSr[:], pb[6][0:1, 0:8], reads=['pb6'], writes=['negSr'])

    for w in range(NW):
        p = w % 2
        for dc in range(NDC):
            tk.dma('sp', xw[p][:, dc * WIN:(dc + 1) * WIN],
                   xsrc[dc * 128:(dc + 1) * 128, w * WIN:(w + 1) * WIN],
                   writes=[f'xw{p}_{dc}'], label=f'ldx_w{w}_d{dc}')
        for tau in range(NTAU):
            i = w * NTAU + tau
            xt_ap = [xw[p][:, dc * WIN + tau * 128: dc * WIN + (tau + 1) * 128]
                     for dc in range(NDC)]
            for dc in range(NDC):
                xq = x2b[dc % 2]
                tk.act(xq[:], xt_ap[dc], AF.Square,
                       reads=[f'xw{p}_{dc}'], writes=[f'x2b{dc % 2}'])
                tk.mm(pb[0][:, 0:1], xt_ap[dc], ones_col[:], dc == 0, dc == 7,
                      reads=[f'xw{p}_{dc}', 'ones_col'], writes=['pb0'])
                tk.mm(pb[1][:, 0:1], xq[:], ones_col[:], dc == 0, dc == 7,
                      reads=[f'x2b{dc % 2}', 'ones_col'], writes=['pb1'])
            tk.ts(stats[:, 0:1], pb[0][:, 0:1], 1.0 / D, None, ALU.mult,
                  reads=['pb0'], writes=['stats'])
            tk.ts(stats[:, 1:2], pb[1][:, 0:1], 1.0 / D, None, ALU.mult,
                  reads=['pb1'], writes=['stats'])
            tk.tt(stats[:, 3:4], stats[:, 0:1], stats[:, 0:1], ALU.mult,
                  reads=['stats'], writes=['stats'])
            tk.tt(stats[:, 2:3], stats[:, 1:2], stats[:, 3:4], ALU.subtract,
                  reads=['stats'], writes=['stats'])
            tk.act(stats[:, 4:5], stats[:, 2:3], AF.Ln, bias=cst[:, 0:1],
                   reads=['stats', 'cst'], writes=['stats'])
            tk.act(stats[:, 5:6], stats[:, 4:5], AF.Exp, scale=-0.5,
                   reads=['stats'], writes=['stats'])
            tk.stt(stats[:, 6:7], stats[:, 0:1], -1.0, stats[:, 5:6],
                   ALU.mult, ALU.mult, reads=['stats'], writes=['stats'])
            tk.tr(pb[2][0:1, 0:128], stats[:, 5:6], iden[:],
                  reads=['stats', 'iden_f32'], writes=['pb2'])
            tk.tr(pb[2][0:1, 128:256], stats[:, 6:7], iden[:],
                  reads=['stats', 'iden_f32'], writes=['pb2'])
            tk.tr(pb[2][0:1, 256:384], stats[:, 0:1], iden[:],
                  reads=['stats', 'iden_f32'], writes=['pb2'])
            tk.copy(rows3[:], pb[2][0:1, 0:384], reads=['pb2'], writes=['rows3'])
            tk.mm(pb[3][:, 0:128], ones_row[:], rows3[0:1, 0:128], True, True,
                  reads=['ones_row', 'rows3'], writes=['pb3'])
            tk.mm(pb[3][:, 128:256], ones_row[:], rows3[0:1, 128:256], True, True,
                  reads=['ones_row', 'rows3'], writes=['pb3'])
            for dc in range(NDC):
                xt = xnt[dc % 2]
                tk.tt(xt[:], xt_ap[dc], pb[3][:, 0:128], ALU.mult,
                      reads=[f'xw{p}_{dc}', 'pb3'], writes=[f'xnt{dc % 2}'])
                tk.tt(xnw[p][:, dc * WIN + tau * 128: dc * WIN + (tau + 1) * 128],
                      xt[:], pb[3][:, 128:256], ALU.add,
                      reads=[f'xnt{dc % 2}', 'pb3'], writes=[f'xnw{p}_{dc}_{tau}'])
            if rt:
                for dc in range(NDC):
                    tk.mm(pb[4][:, 0:8], xt_ap[dc], routerT_sb[:, dc * 8:(dc + 1) * 8],
                          dc == 0, False, reads=[f'xw{p}_{dc}', 'routerT_sb'],
                          writes=['pb4'])
                tk.mm(pb[4][:, 0:8], rows3[0:1, 256:384], negSr[:], False, True,
                      reads=['rows3', 'negSr'], writes=['pb4'])
                tk.ts(lgt[:], pb[4][:, 0:8], stats[:, 5:6], None, ALU.mult,
                      reads=['pb4', 'stats'], writes=['lgt'])
                tk.red(lgtx[:, 0:1], lgt[:], ALU.max, negate=True,
                       reads=['lgt'], writes=['lgtx'])
                tk.act(elg[:], lgt[:], AF.Exp, bias=lgtx[:, 0:1],
                       reads=['lgt', 'lgtx'], writes=['elg'])
                tk.red(lgtx[:, 1:2], elg[:], ALU.add, reads=['elg'], writes=['lgtx'])
                tk.recip(pecols[:, 0:1], lgtx[:, 1:2], reads=['lgtx'], writes=['pecols'])
                tk.ts(eqt[:], lgt[:], lgtx[:, 0:1], 0.0, ALU.add, op1=ALU.is_ge,
                      reads=['lgt', 'lgtx'], writes=['eqt'])
                tk.tt(eqt2[:], eqt[:], irev[:], ALU.mult,
                      reads=['eqt', 'irev'], writes=['eqt2'])
                tk.red(lgtx[:, 2:3], eqt2[:], ALU.max, reads=['eqt2'], writes=['lgtx'])
                tk.ts(pecols[:, 1:2], lgtx[:, 2:3], -1.0, 7.0, ALU.mult, op1=ALU.add,
                      reads=['lgtx'], writes=['pecols'])
                tk.tr(pb[6][0:2, 128:256], pecols[:], iden[:],
                      reads=['pecols', 'iden_f32'], writes=['pb6'])
                tk.copy(peacc[:, i * 128:(i + 1) * 128], pb[6][0:2, 128:256],
                        reads=['pb6'], writes=[f'peacc_{i}'])
        for dc in range(NDC):
            tk.dma('sp', xn_h[dc * 128:(dc + 1) * 128, w * WIN:(w + 1) * WIN],
                   xnw[p][:, dc * WIN:(dc + 1) * WIN],
                   reads=[f'xnw{p}_{dc}_{t}' for t in range(NTAU)],
                   label=f'stxn_w{w}_d{dc}')
    if rt:
        tk.dma('sp', pe_h, peacc[:],
               reads=[f'peacc_{i}' for i in range(NW * NTAU)])


CAP = 5120

def emit_routing(tk, E, nc, allpe, csel, cselE, kmask_h, scope=None):
    pb = E['pb']
    rmat, rmatT, idxmat = E['rmat'], E['rmatT'], E['idxmat']
    g = 'rt_'
    A = _mk_alloc(nc, scope, g)
    P16 = A('P16', [16, 2048], F32)
    E16 = A('E16', [16, 2048], F32)
    Mbig = A('M', [128, 2048], F32)
    eqm = A('eqm', [128, 2048], F32)
    big1 = A('big1', [128, 2048], F32)
    big2 = A('big2', [128, 2048], F32)
    Kbig = A('K', [128, 2048], F32)
    tmp16 = A('tmp16', [16, 2048], F32)
    tmp16b = A('tmp16b', [16, 2048], F32)
    part = A('part', [128, 1], F32)
    sc8 = A('sc8', [8, 12], F32)
    colb = A('colb', [128, 4], F32)
    cselb = A('csel', [128, 2], F32)
    cselEb = A('cselE', [128, 16], F32)
    k0t = A('k0t', [2, 2048], F32)
    f0 = A('f0', [2, 2048], F32)

    tk.dma('sp', cselb[:], csel, writes=['cselb'])
    tk.dma('sp', cselEb[:], cselE, writes=['cselEb'])
    tk.dma('sp', P16[0:16:2, :], allpe[0:16:2, 0:2048], writes=['P16'])
    tk.dma('sp', P16[1:16:2, :], allpe[0:16:2, 2048:4096], writes=['P16'])
    tk.dma('sp', E16[0:16:2, :], allpe[1:16:2, 0:2048], writes=['E16'])
    tk.dma('sp', E16[1:16:2, :], allpe[1:16:2, 2048:4096], writes=['E16'])
    for e in range(8):
        tk.ts(tmp16[:], E16[:], float(e), None, ALU.is_equal,
              reads=['E16'], writes=['tmp16'])
        tk.stt(tmp16b[:], P16[:], 1.0, tmp16[:], ALU.add, ALU.mult,
               reads=['P16', 'tmp16'], writes=['tmp16b'])
        tk.ts(tmp16b[:], tmp16b[:], -1.0, None, ALU.add,
              reads=['tmp16b'], writes=['tmp16b'])
        tk.dma('sp', Mbig[e * 16:(e + 1) * 16, :], tmp16b[:],
               reads=['tmp16b'], writes=['Mbig'], label=f'mcp{e}')
    def collapse(dst, rkey):
        tk.mm(pb[5][0:8, 0:1], rmatT[:], part[:], True, True,
              reads=['rmatT', 'part'], writes=['pb5'])
        tk.copy(dst, pb[5][0:8, 0:1], reads=['pb5'], writes=['sc8'])
    def bcast8(src, dstcol):
        tk.mm(pb[6][:, 0:1], rmat[:], src, True, True,
              reads=['rmat', 'sc8'], writes=['pb6'])
        tk.copy(dstcol, pb[6][:, 0:1], reads=['pb6'], writes=['colb'])
    tk.ts(big1[:], Mbig[:], -0.5, None, ALU.is_ge, op1=ALU.add, accum=part[:],
          reads=['Mbig'], writes=['big1', 'part'])
    collapse(sc8[:, 6:7], 'tot')
    tk.memset(sc8[:, 0:1], 0.0, writes=['sc8'])
    tk.memset(sc8[:, 1:2], 1.5, writes=['sc8'])
    for it in range(40):
        tk.tt(sc8[:, 5:6], sc8[:, 0:1], sc8[:, 1:2], ALU.add,
              reads=['sc8'], writes=['sc8'])
        tk.ts(sc8[:, 2:3], sc8[:, 5:6], 0.5, None, ALU.mult,
              reads=['sc8'], writes=['sc8'])
        bcast8(sc8[:, 2:3], colb[:, 0:1])
        tk.ts(big1[:], Mbig[:], colb[:, 0:1], None, ALU.is_ge, op1=ALU.add,
              accum=part[:], reads=['Mbig', 'colb'], writes=['big1', 'part'])
        collapse(sc8[:, 3:4], 'cnt')
        tk.ts(sc8[:, 4:5], sc8[:, 3:4], -float(CAP), 0.0, ALU.add, op1=ALU.is_ge,
              reads=['sc8'], writes=['sc8'])
        tk.tt(sc8[:, 5:6], sc8[:, 2:3], sc8[:, 0:1], ALU.subtract,
              reads=['sc8'], writes=['sc8'])
        tk.tt(sc8[:, 5:6], sc8[:, 5:6], sc8[:, 4:5], ALU.mult,
              reads=['sc8'], writes=['sc8'])
        tk.tt(sc8[:, 0:1], sc8[:, 0:1], sc8[:, 5:6], ALU.add,
              reads=['sc8'], writes=['sc8'])
        tk.tt(sc8[:, 5:6], sc8[:, 1:2], sc8[:, 2:3], ALU.subtract,
              reads=['sc8'], writes=['sc8'])
        tk.tt(sc8[:, 5:6], sc8[:, 5:6], sc8[:, 4:5], ALU.mult,
              reads=['sc8'], writes=['sc8'])
        tk.tt(sc8[:, 1:2], sc8[:, 2:3], sc8[:, 5:6], ALU.add,
              reads=['sc8'], writes=['sc8'])
    bcast8(sc8[:, 0:1], colb[:, 0:1])
    tk.ts(big1[:], Mbig[:], colb[:, 0:1], None, ALU.is_gt, op1=ALU.add,
          accum=part[:], reads=['Mbig', 'colb'], writes=['big1', 'part'])
    collapse(sc8[:, 7:8], 'ngt')
    tk.ts(sc8[:, 8:9], sc8[:, 7:8], -1.0, float(CAP), ALU.mult, op1=ALU.add,
          reads=['sc8'], writes=['sc8'])
    tk.ts(eqm[:], Mbig[:], colb[:, 0:1], None, ALU.is_equal,
          reads=['Mbig', 'colb'], writes=['eqm'])
    tk.memset(sc8[:, 9:10], -1.0, writes=['sc8'])
    tk.memset(sc8[:, 10:11], 32768.0, writes=['sc8'])
    for it in range(18):
        tk.tt(sc8[:, 5:6], sc8[:, 9:10], sc8[:, 10:11], ALU.add,
              reads=['sc8'], writes=['sc8'])
        tk.ts(sc8[:, 2:3], sc8[:, 5:6], 0.5, None, ALU.mult,
              reads=['sc8'], writes=['sc8'])
        bcast8(sc8[:, 2:3], colb[:, 1:2])
        tk.ts(big1[:], idxmat[:], colb[:, 1:2], None, ALU.is_le,
              reads=['idxmat', 'colb'], writes=['big1'])
        tk.tt(big2[:], big1[:], eqm[:], ALU.mult,
              reads=['big1', 'eqm'], writes=['big2'])
        tk.ts(big2[:], big2[:], 1.0, None, ALU.mult, op1=ALU.add, accum=part[:],
              reads=['big2'], writes=['big2', 'part'])
        collapse(sc8[:, 3:4], 'icnt')
        tk.ts(sc8[:, 4:5], sc8[:, 3:4], sc8[:, 8:9], 0.0, ALU.subtract, op1=ALU.is_le,
              reads=['sc8'], writes=['sc8'])
        tk.tt(sc8[:, 5:6], sc8[:, 2:3], sc8[:, 9:10], ALU.subtract,
              reads=['sc8'], writes=['sc8'])
        tk.tt(sc8[:, 5:6], sc8[:, 5:6], sc8[:, 4:5], ALU.mult,
              reads=['sc8'], writes=['sc8'])
        tk.tt(sc8[:, 9:10], sc8[:, 9:10], sc8[:, 5:6], ALU.add,
              reads=['sc8'], writes=['sc8'])
        tk.tt(sc8[:, 5:6], sc8[:, 10:11], sc8[:, 2:3], ALU.subtract,
              reads=['sc8'], writes=['sc8'])
        tk.tt(sc8[:, 5:6], sc8[:, 5:6], sc8[:, 4:5], ALU.mult,
              reads=['sc8'], writes=['sc8'])
        tk.tt(sc8[:, 10:11], sc8[:, 2:3], sc8[:, 5:6], ALU.add,
              reads=['sc8'], writes=['sc8'])
    bcast8(sc8[:, 9:10], colb[:, 1:2])
    tk.ts(Kbig[:], Mbig[:], colb[:, 0:1], None, ALU.is_gt,
          reads=['Mbig', 'colb'], writes=['Kbig'])
    tk.ts(big1[:], idxmat[:], colb[:, 1:2], None, ALU.is_le,
          reads=['idxmat', 'colb'], writes=['big1'])
    tk.tt(big1[:], big1[:], eqm[:], ALU.mult,
          reads=['big1', 'eqm'], writes=['big1'])
    tk.tt(Kbig[:], Kbig[:], big1[:], ALU.add,
          reads=['Kbig', 'big1'], writes=['Kbig'])
    for q in range(4):
        tk.mm(pb[7][0:2, 0:512], cselb[:], Kbig[:, q * 512:(q + 1) * 512],
              True, True, reads=['cselb', 'Kbig'], writes=['pb7'])
        tk.ts(f0[:, q * 512:(q + 1) * 512], pb[7][0:2, 0:512], -1.0, 1.0,
              ALU.mult, op1=ALU.add, reads=['pb7'], writes=['f0'])
    for e in range(8):
        for q in range(4):
            tk.mm(pb[7][0:2, 0:512], cselEb[:, 2 * e:2 * e + 2],
                  Kbig[:, q * 512:(q + 1) * 512], True, True,
                  reads=['cselEb', 'Kbig'], writes=['pb7'])
            tk.copy(k0t[:, q * 512:(q + 1) * 512], pb[7][0:2, 0:512],
                    reads=['pb7'], writes=['k0t'], eng='act')
        if e == 0:
            tk.tt(k0t[:], k0t[:], f0[:], ALU.add, reads=['k0t', 'f0'],
                  writes=['k0t'])
        tk.dma('sp', kmask_h[e:e + 1, :], k0t[:], reads=['k0t'], label=f'stk{e}')


SQRT2 = float(np.sqrt(2.0))

def emit_attn0(tk, E, nc, xn_h, s_h, W, scope=None):
    pb = E['pb']
    iden_bf = E['iden_bf']; cst = E['cst']; mtri = E['mtri_bf']
    selA, selB = E['selA'], E['selB']
    g = 'a0_'
    A = _mk_alloc(nc, scope, g)
    wq = A('wq', [128, 4096], BF16); wk = A('wk', [128, 4096], BF16)
    wkT = A('wkT', [128, 2048], BF16)
    wvT = A('wvT', [128, 8192], BF16)
    wout = A('wout', [128, 8192], BF16)

    def loadw(sb, nm, key):
        tk.dma('pool', sb[:], W[nm].rearrange('p a b -> p (a b)') if W[nm].ndim == 3
               else W[nm].rearrange('p a b c -> p (a b c)'), writes=[key])
    loadw(wkT, 'wkeT', 'wkT')
    loadw(wvT, 'wveT', 'wvT')
    loadw(wq, 'wq', 'wq')
    loadw(wout, 'wout', 'wout')
    SAf = A('SAf', [128, 1040], F32); SBf = A('SBf', [128, 1040], F32)
    SCf = A('SCf', [17, 1040], F32)
    SAb = A('SAb', [128, 1040], BF16); SBb = A('SBb', [128, 1040], BF16)
    SCb = A('SCb', [17, 1040], BF16)
    for tname, t in [('SAf', SAf), ('SBf', SBf), ('SCf', SCf)]:
        tk.memset(t[:], 0.0, writes=[f'{tname}_{h}' for h in range(16)])
    xnw = [A(f'xnw{i}', [128, NDC * WIN], BF16) for i in range(2)]
    qsw = [A(f'qsw{i}', [64, 8 * WIN], BF16) for i in range(2)]
    ksw = [A(f'ksw{i}', [64, 8 * WIN], BF16) for i in range(2)]
    kstm = [A(f'kstm{i}', [128, 4 * 256], BF16) for i in range(2)]
    v1sb = [A(f'v1sb{i}', [128, 4 * 1040], BF16) for i in range(2)]
    kftm = [A(f'kftm{i}', [128, 4368], BF16) for i in range(2)]
    qfA = [A(f'qfA{i}', [128, 128], BF16) for i in range(2)]
    qfB = [A(f'qfB{i}', [128, 128], BF16) for i in range(2)]
    qfC = [A(f'qfC{i}', [17, 128], BF16) for i in range(2)]
    p1sb = [A(f'p1sb{i}', [128, 128], BF16) for i in range(2)]
    asq = [A(f'asq{i}', [128, 128], F32) for i in range(2)]
    atb = [A(f'atb{i}', [128, 128], BF16) for i in range(2)]
    zc = A('zc', [128, 2], F32)
    otm = [A(f'otm{i}', [128, 1024], BF16) for i in range(2)]
    off = A('off', [128, 8 * WIN], BF16)
    sfw = [A(f'sfw{i}', [128, NDC * WIN], BF16) for i in range(2)]
    ones_h = E['ones8k_h']
    for i in range(2):
        for r in (16, 48):
            tk.dma('pool', qsw[i][r:r + 1, :], ones_h[0:8 * WIN],
                   writes=[f'qsw{i}_ones'])
        tk.memset(v1sb[i].rearrange('p (t h c) -> p t h c', t=4, h=16)[:, :, :, 64:65],
                  1.0, writes=[f'v1sb{i}_ones'], eng='pool')
        tk.memset(kftm[i].rearrange('p (h c) -> p h c', h=16)[:, :, 0:1],
                  1.0, writes=[f'kftm{i}_ones'], eng='pool')

    def proj_qk(wsb, wkey, dst, dkey, xbuf, xkeys):
        for p8 in range(8):
            for dc in range(NDC):
                tk.mm(pb[0][0:64, 0:512],
                      wsb[:, (dc * 8 + p8) * 64:(dc * 8 + p8) * 64 + 64],
                      xbuf[:, dc * WIN:(dc + 1) * WIN], dc == 0, dc == 7,
                      reads=[wkey, xkeys[dc]], writes=['pb0'])
            for rb in (0, 32):
                tk.copy(dst[rb:rb + 16, p8 * WIN:(p8 + 1) * WIN],
                        pb[0][rb:rb + 16, 0:512], reads=['pb0'],
                        writes=[f'{dkey}_{p8}'], eng='act')

    def proj_ktm2(wsb, wkey, dstbuf, dkey, xbuf, xkeys):
        for tau in range(NTAU):
            for dc in range(NDC):
                tk.mm(pb[1][:, 0:256],
                      xbuf[:, dc * WIN + tau * 128:dc * WIN + (tau + 1) * 128],
                      wsb[:, dc * 256:(dc + 1) * 256],
                      dc == 0, dc == 7, reads=[wkey, xkeys[dc]], writes=['pb1'])
            tk.copy(dstbuf[:, tau * 256:(tau + 1) * 256], pb[1][:, 0:256],
                    reads=['pb1'], writes=[f'{dkey}_{tau}'], eng='act')

    def proj_vtm(wsb, wkey, dstbuf, dkey, xbuf, xkeys):
        for tau in range(NTAU):
            for half in range(2):
                bank = pb[2] if half == 0 else pb[3]
                bk = 'pb2' if half == 0 else 'pb3'
                for dc in range(NDC):
                    tk.mm(bank[:, 0:512],
                          xbuf[:, dc * WIN + tau * 128:dc * WIN + (tau + 1) * 128],
                          wsb[:, dc * 1024 + half * 512:dc * 1024 + half * 512 + 512],
                          dc == 0, dc == 7, reads=[wkey, xkeys[dc]], writes=[bk])
                dst = dstbuf.rearrange('p (t h c) -> p t h c', t=4, h=16)[
                    :, tau, half * 8:(half + 1) * 8, 0:64]
                tk.copy(dst, bank[:, 0:512].rearrange('p (h c) -> p h c', h=8),
                        reads=[bk], writes=[f'{dkey}_{tau}'], eng='act')

    def build_kftm(buf, bkey, ksbuf, kskey, tau):
        kv = ksbuf[:, tau * 256:(tau + 1) * 256].rearrange('p (h f) -> p h f', h=16)
        ov = buf.rearrange('p (h c) -> p h c', h=16)
        tk.ts(ov[:, :, 1:17], kv, SQRT2, None, ALU.mult,
              reads=[kskey], writes=[bkey], eng='pool')
        for a in range(16):
            in1 = kv[:, :, a:a + 1].to_broadcast([128, 16, 16])
            tk.tt(ov[:, :, 17 + a * 16:17 + (a + 1) * 16], kv, in1, ALU.mult,
                  reads=[kskey], writes=[bkey], eng='pool')

    def state_update(kbuf, kkey, vbuf, vkey, tau, tag):
        for h in range(16):
            kc = kbuf[:, h * 273:(h + 1) * 273]
            v1 = vbuf.rearrange('p (t h c) -> p t h c', t=4, h=16)[
                :, tau, h, :].rearrange('p c -> p c')
            tk.mm(pb[6][:, 0:65], kc[:, 0:128], v1, True, True,
                  reads=[kkey, kkey + '_ones', vkey, vkey.rsplit('_', 1)[0] + '_ones'],
                  writes=['pb6'])
            tk.tt(SAf[:, h * 65:(h + 1) * 65], SAf[:, h * 65:(h + 1) * 65],
                  pb[6][:, 0:65], ALU.add, reads=['pb6', f'SAf_{h}'],
                  writes=[f'SAf_{h}'])
            tk.copy(SAb[:, h * 65:(h + 1) * 65], SAf[:, h * 65:(h + 1) * 65],
                    reads=[f'SAf_{h}'], writes=[f'SAb_{h}'], eng='pool')
            tk.mm(pb[6][:, 65:130], kc[:, 128:256], v1, True, True,
                  reads=[kkey, vkey], writes=['pb6'])
            tk.tt(SBf[:, h * 65:(h + 1) * 65], SBf[:, h * 65:(h + 1) * 65],
                  pb[6][:, 65:130], ALU.add, reads=['pb6', f'SBf_{h}'],
                  writes=[f'SBf_{h}'])
            tk.copy(SBb[:, h * 65:(h + 1) * 65], SBf[:, h * 65:(h + 1) * 65],
                    reads=[f'SBf_{h}'], writes=[f'SBb_{h}'], eng='pool')
            tk.mm(pb[6][0:17, 130:195], kc[:, 256:273], v1, True, True,
                  reads=[kkey, vkey], writes=['pb6'])
            tk.tt(SCf[:, h * 65:(h + 1) * 65], SCf[:, h * 65:(h + 1) * 65],
                  pb[6][0:17, 130:195], ALU.add, reads=['pb6', f'SCf_{h}'],
                  writes=[f'SCf_{h}'])
            tk.copy(SCb[:, h * 65:(h + 1) * 65], SCf[:, h * 65:(h + 1) * 65],
                    reads=[f'SCf_{h}'], writes=[f'SCb_{h}'], eng='pool')

    def load_xn(p, w):
        for dc in range(NDC):
            tk.dma('sp', xnw[p][:, dc * WIN:(dc + 1) * WIN],
                   xn_h[dc * 128:(dc + 1) * 128, w * WIN:(w + 1) * WIN],
                   writes=[f'xnw{p}_{dc}'], label=f'ldxn_w{w}_d{dc}')

    load_xn(0, 0)
    xkeys0 = [f'xnw0_{dc}' for dc in range(NDC)]
    proj_ktm2(wkT, 'wkT', kstm[1], 'kstm1', xnw[0], xkeys0)
    proj_vtm(wvT, 'wvT', v1sb[1], 'v1sb1', xnw[0], xkeys0)
    for tau in range(NTAU):
        build_kftm(kftm[1], 'kftm1', kstm[1], f'kstm1_{tau}', tau)
        state_update(kftm[1], 'kftm1', v1sb[1], f'v1sb1_{tau}', tau, 'enc')

    loadw(wk, 'wk', 'wk')
    loadw(wkT, 'wkT', 'wkT')
    loadw(wvT, 'wvT', 'wvT')
    for w in range(NW):
        p = w % 2
        if w > 0:
            load_xn(p, w)
        xkeys = [f'xnw{p}_{dc}' for dc in range(NDC)]
        proj_qk(wq, 'wq', qsw[p], f'qsw{p}', xnw[p], xkeys)
        proj_qk(wk, 'wk', ksw[p], f'ksw{p}', xnw[p], xkeys)
        proj_ktm2(wkT, 'wkT', kstm[p], f'kstm{p}', xnw[p], xkeys)
        proj_vtm(wvT, 'wvT', v1sb[p], f'v1sb{p}', xnw[p], xkeys)
        for tau in range(NTAU):
            build_kftm(kftm[p], f'kftm{p}', kstm[p], f'kstm{p}_{tau}', tau)
            for h in range(16):
                p8, rb = h // 2, 32 * (h % 2)
                hp = h % 2
                qsl = qsw[p][rb:rb + 16, p8 * WIN + tau * 128:p8 * WIN + (tau + 1) * 128]
                ksl = ksw[p][rb:rb + 16, p8 * WIN + tau * 128:p8 * WIN + (tau + 1) * 128]
                q17 = qsw[p][rb:rb + 17, p8 * WIN + tau * 128:p8 * WIN + (tau + 1) * 128]
                tk.mm(pb[4][:, 0:128], ksl, qsl, True, True,
                      reads=[f'ksw{p}_{p8}', f'qsw{p}_{p8}'], writes=['pb4'])
                tk.act(asq[hp][:], pb[4][:, 0:128], AF.Square, bias=cst[:, 1:2],
                       reads=['pb4', 'cst'], writes=[f'asq{hp}'])
                tk.stt(atb[hp][:], asq[hp][:], 0.5, mtri[:], ALU.add, ALU.mult,
                       reads=[f'asq{hp}', 'mtri_bf'], writes=[f'atb{hp}'])
                for (cn, qf, c0, c1, m) in [('A', qfA[hp], 0, 128, 128),
                                            ('B', qfB[hp], 128, 256, 128),
                                            ('C', qfC[hp], 256, 273, 17)]:
                    tk.mm(pb[2][0:m, 0:128], selA[rb:rb + 17, c0:c1], q17,
                          True, True, reads=['selA', f'qsw{p}_{p8}', f'qsw{p}_ones'],
                          writes=['pb2'])
                    tk.mm(pb[3][0:m, 0:128], selB[rb:rb + 17, c0:c1], q17,
                          True, True, reads=['selB', f'qsw{p}_{p8}', f'qsw{p}_ones'],
                          writes=['pb3'])
                    tk.copy(p1sb[hp][0:m, :], pb[2][0:m, 0:128],
                            reads=['pb2'], writes=[f'p1sb{hp}'], eng='act')
                    tk.tt(qf[0:m, :], pb[3][0:m, 0:128], p1sb[hp][0:m, :],
                          ALU.mult, reads=['pb3', f'p1sb{hp}'], writes=[f'qf{cn}{hp}'])
                v1 = v1sb[p].rearrange('p (t h c) -> p t h c', t=4, h=16)[:, tau, h, :]
                tk.mm(pb[5][:, 0:65], qfA[hp][:], SAb[:, h * 65:(h + 1) * 65],
                      True, False, reads=[f'qfA{hp}', f'SAb_{h}'], writes=['pb5'])
                tk.mm(pb[5][:, 0:65], qfB[hp][:], SBb[:, h * 65:(h + 1) * 65],
                      False, False, reads=[f'qfB{hp}', f'SBb_{h}'], writes=['pb5'])
                tk.mm(pb[5][:, 0:65], qfC[hp][:], SCb[:, h * 65:(h + 1) * 65],
                      False, False, reads=[f'qfC{hp}', f'SCb_{h}'], writes=['pb5'])
                tk.mm(pb[5][:, 0:65], atb[hp][:], v1, False, True,
                      reads=[f'atb{hp}', f'v1sb{p}_{tau}', f'v1sb{p}_ones'],
                      writes=['pb5'])
                tk.ts(zc[:, 0:1], pb[5][:, 64:65], 1e-12, None, ALU.add,
                      reads=['pb5'], writes=['zc'])
                tk.recip(zc[:, 1:2], zc[:, 0:1], reads=['zc'], writes=['zc'])
                tk.ts(otm[tau % 2][:, h * 64:(h + 1) * 64], pb[5][:, 0:64],
                      zc[:, 1:2], None, ALU.mult, reads=['pb5', 'zc'],
                      writes=[f'otm{tau % 2}_{h}'])
                kc = kftm[p][:, h * 273:(h + 1) * 273]
                tk.mm(pb[6][:, 0:65], kc[:, 0:128], v1, True, True,
                      reads=[f'kftm{p}', f'kftm{p}_ones', f'v1sb{p}_{tau}',
                             f'v1sb{p}_ones'], writes=['pb6'])
                tk.tt(SAf[:, h * 65:(h + 1) * 65], SAf[:, h * 65:(h + 1) * 65],
                      pb[6][:, 0:65], ALU.add, reads=['pb6', f'SAf_{h}'],
                      writes=[f'SAf_{h}'])
                tk.copy(SAb[:, h * 65:(h + 1) * 65], SAf[:, h * 65:(h + 1) * 65],
                        reads=[f'SAf_{h}'], writes=[f'SAb_{h}'], eng='pool')
                tk.mm(pb[6][:, 65:130], kc[:, 128:256], v1, True, True,
                      reads=[f'kftm{p}', f'v1sb{p}_{tau}'], writes=['pb6'])
                tk.tt(SBf[:, h * 65:(h + 1) * 65], SBf[:, h * 65:(h + 1) * 65],
                      pb[6][:, 65:130], ALU.add, reads=['pb6', f'SBf_{h}'],
                      writes=[f'SBf_{h}'])
                tk.copy(SBb[:, h * 65:(h + 1) * 65], SBf[:, h * 65:(h + 1) * 65],
                        reads=[f'SBf_{h}'], writes=[f'SBb_{h}'], eng='pool')
                tk.mm(pb[6][0:17, 130:195], kc[:, 256:273], v1, True, True,
                      reads=[f'kftm{p}', f'v1sb{p}_{tau}'], writes=['pb6'])
                tk.tt(SCf[:, h * 65:(h + 1) * 65], SCf[:, h * 65:(h + 1) * 65],
                      pb[6][0:17, 130:195], ALU.add, reads=['pb6', f'SCf_{h}'],
                      writes=[f'SCf_{h}'])
                tk.copy(SCb[:, h * 65:(h + 1) * 65], SCf[:, h * 65:(h + 1) * 65],
                        reads=[f'SCf_{h}'], writes=[f'SCb_{h}'], eng='pool')
            for hc in range(8):
                tk.tr(pb[7].bitcast(BF16)[:, 0:128],
                      otm[tau % 2][:, hc * 128:(hc + 1) * 128], iden_bf[:],
                      reads=[f'otm{tau % 2}_{hh}' for hh in range(2 * hc, 2 * hc + 2)]
                      + ['iden_bf'], writes=['pb7'])
                tk.copy(off[:, hc * WIN + tau * 128:hc * WIN + (tau + 1) * 128],
                        pb[7].bitcast(BF16)[:, 0:128], reads=['pb7'],
                        writes=[f'off_{hc}'], eng='act')
        for dm in range(NDC):
            for hc in range(NDC):
                tk.mm(pb[0][:, 0:512], wout[:, hc * 1024 + dm * 128:hc * 1024 + dm * 128 + 128],
                      off[:, hc * WIN:(hc + 1) * WIN], hc == 0, hc == 7,
                      reads=['wout', f'off_{hc}'], writes=['pb0'])
            tk.copy(sfw[p][:, dm * WIN:(dm + 1) * WIN], pb[0][:, 0:512],
                    reads=['pb0'], writes=[f'sfw{p}_{dm}'], eng='act')
        for dm in range(NDC):
            tk.dma('sp', s_h[dm * 128:(dm + 1) * 128, w * WIN:(w + 1) * WIN],
                   sfw[p][:, dm * WIN:(dm + 1) * WIN], reads=[f'sfw{p}_{dm}'],
                   label=f'sts_w{w}_d{dm}')


def emit_moe(tk, E, nc, xn_h, kmask_h, ge_h, W, scope=None):
    pb = E['pb']
    ones_row_bf = E['ones_row_bf']
    g = 'mo_'
    A = _mk_alloc(nc, scope, g)
    w1sb = A('w1sb', [128, 32 * 1024], BF16)
    w2sb = A('w2sb', [128, 32 * 1024], BF16)
    b1sb = A('b1sb', [128, 32], F32)
    b2sb = A('b2sb', [128, 8], F32)
    xnw = [A(f'xnw{i}', [128, NDC * WIN], BF16) for i in range(2)]
    hsb = A('hsb', [128, 32 * WIN], BF16)
    ysb = [A(f'ysb{i}', [128, WIN], F32) for i in range(2)]
    ymb = [A(f'ymb{i}', [128, WIN], BF16) for i in range(4)]
    mrow = A('mrow', [1, WIN], BF16)
    mbc = A('mbc', [128, WIN], BF16)
    HB = [0, 1, 6, 7]

    def load_w1_fc(e, fc):
        tk.dma('pool', w1sb[:, fc * 1024:(fc + 1) * 1024],
               W['w1'][e, :, fc, :], writes=[f'w1_{fc}'], label=f'ldw1_{e}_{fc}')
    def load_w2_q(e, q):
        tk.dma('pool', w2sb[:, q * 4096:(q + 1) * 4096],
               W['w2'][e, :, 4 * q:4 * (q + 1), :].rearrange('p a b -> p (a b)'),
               writes=[f'w2_{q}'], label=f'ldw2_{e}_{q}')

    for fc in range(32):
        load_w1_fc(0, fc)
    for q in range(8):
        load_w2_q(0, q)
    tk.dma('sp', b1sb[:], W['b1'][0], writes=['b1sb'])
    tk.dma('sp', b2sb[:], W['b2'][0], writes=['b2sb'])
    for e in range(8):
        for w in range(NW):
            p = w % 2
            last = (w == NW - 1 and e < 7)
            for dc in range(NDC):
                tk.dma('sp', xnw[p][:, dc * WIN:(dc + 1) * WIN],
                       xn_h[dc * 128:(dc + 1) * 128, w * WIN:(w + 1) * WIN],
                       writes=[f'xnw{p}_{dc}'], label=f'mxn_{e}_{w}_{dc}')
            tk.dma('pool', mrow[:], kmask_h[e:e + 1, w * WIN:(w + 1) * WIN],
                   writes=['mrow'])
            tk.mm(pb[6][:, 0:512], ones_row_bf[:], mrow[:], True, True,
                  reads=['ones_row_bf', 'mrow'], writes=['pb6'])
            tk.copy(mbc[:], pb[6][:, 0:512], reads=['pb6'],
                    writes=['mbc'], eng='act')
            for fc in range(32):
                bi = HB[fc % 4]
                bank, bk = pb[bi], f'pb{bi}'
                for dc in range(NDC):
                    tk.mm(bank[:, 0:512],
                          w1sb[:, fc * 1024 + dc * 128:fc * 1024 + (dc + 1) * 128],
                          xnw[p][:, dc * WIN:(dc + 1) * WIN], dc == 0, dc == 7,
                          reads=[f'w1_{fc}', f'xnw{p}_{dc}'], writes=[bk])
                tk.act(hsb[:, fc * WIN:(fc + 1) * WIN], bank[:, 0:512], AF.Relu,
                       bias=b1sb[:, fc:fc + 1], reads=[bk, 'b1sb'],
                       writes=[f'h_{fc}'])
                if last:
                    load_w1_fc(e + 1, fc)
            if last:
                tk.dma('sp', b1sb[:], W['b1'][e + 1], writes=['b1sb'])
            YB = [2, 3, 4, 5]
            for dc in range(NDC):
                bi = YB[dc % 4]
                bank, bk = pb[bi], f'pb{bi}'
                for fc in range(32):
                    tk.mm(bank[:, 0:512],
                          w2sb[:, fc * 1024 + dc * 128:fc * 1024 + (dc + 1) * 128],
                          hsb[:, fc * WIN:(fc + 1) * WIN], fc == 0, fc == 31,
                          reads=[f'w2_{fc // 4}', f'h_{fc}'], writes=[bk])
                yp = dc % 2
                tk.ts(ysb[yp][:], bank[:, 0:512], b2sb[:, dc:dc + 1], None, ALU.add,
                      reads=[bk, 'b2sb'], writes=[f'ysb{yp}'])
                y4 = dc % 4
                tk.tt(ymb[y4][:], ysb[yp][:], mbc[:], ALU.mult,
                      reads=[f'ysb{yp}', 'mbc'], writes=[f'ymb{y4}'])
                tk.dma('sp', ge_h[e, dc * 128:(dc + 1) * 128, w * WIN:(w + 1) * WIN],
                       ymb[y4][:], reads=[f'ymb{y4}'], label=f'stg_{e}_{w}_{dc}')
            if last:
                for q in range(8):
                    load_w2_q(e + 1, q)
                tk.dma('sp', b2sb[:], W['b2'][e + 1], writes=['b2sb'])


def emit_combine(tk, E, nc, xsrc, s_h, g_h, xdst, scope=None, tag='cb', ge=False):
    g = tag + '_'
    A = _mk_alloc(nc, scope, g)
    xw = [A(f'xw{i}', [128, NDC * WIN], F32) for i in range(2)]
    sw = [A(f'sw{i}', [128, NDC * WIN], BF16) for i in range(2)]
    NG = 8 if ge else 1
    gw = [A(f'gw{i}', [128, NG * WIN], BF16) for i in range(8)]
    gs = [A(f'gs{i}', [128, WIN], F32) for i in range(8)]
    sg = [A(f'sg{i}', [128, WIN], F32) for i in range(2)]
    gsl = [A(f'gsl{i}', [128, WIN], F32) for i in range(2)]
    xo = [A(f'xo{i}', [128, NDC * WIN], F32) for i in range(2)]
    for w in range(NW):
        p = w % 2
        for dc in range(NDC):
            sl = (slice(dc * 128, (dc + 1) * 128), slice(w * WIN, (w + 1) * WIN))
            tk.dma('act', xw[p][:, dc * WIN:(dc + 1) * WIN], xsrc[sl[0], sl[1]],
                   writes=[f'xw{p}_{dc}'])
            tk.dma('act', sw[p][:, dc * WIN:(dc + 1) * WIN], s_h[sl[0], sl[1]],
                   writes=[f'sw{p}_{dc}'])
            q4 = dc
            if ge:
                src = g_h[:, sl[0], sl[1]].transpose([1, 0, 2])
                tk.dma('sp', gw[q4].rearrange('p (e j) -> p e j', e=8), src,
                       writes=[f'gw{q4}_0'])
            else:
                tk.dma('sp', gw[q4][:, 0:WIN], g_h[sl[0], sl[1]],
                       writes=[f'gw{q4}_0'])
        for dc in range(NDC):
            q = dc % 2
            q4 = dc
            if ge:
                v = gw[q4].rearrange('p (e j) -> p j e', e=8)
                tk.red(gs[q4][:].unsqueeze(2), v, ALU.add,
                       reads=[f'gw{q4}_0'], writes=[f'gs{q4}'])
                gsrc, gk = gs[q4][:], f'gs{q4}'
            else:
                gsrc, gk = gw[q4][:, 0:WIN], f'gw{q4}_0'
            tk.act(gsl[q][:], gsrc, AF.Silu, reads=[gk], writes=[f'gsl{q}'])
            tk.tt(sg[q][:], sw[p][:, dc * WIN:(dc + 1) * WIN], gsl[q][:], ALU.mult,
                  reads=[f'sw{p}_{dc}', f'gsl{q}'], writes=[f'sg{q}'])
            tk.tt(xo[p][:, dc * WIN:(dc + 1) * WIN],
                  xw[p][:, dc * WIN:(dc + 1) * WIN], sg[q][:], ALU.add,
                  reads=[f'xw{p}_{dc}', f'sg{q}'], writes=[f'xo{p}_{dc}'])
            tk.dma('sp', xdst[dc * 128:(dc + 1) * 128, w * WIN:(w + 1) * WIN],
                   xo[p][:, dc * WIN:(dc + 1) * WIN], reads=[f'xo{p}_{dc}'],
                   label=f'stxo_{w}_{dc}')


def emit_attn1(tk, E, nc, xn_h, s_h, W, scope=None):
    pb = E['pb']
    iden_bf = E['iden_bf']; mtri = E['mtri_bf']
    g = 'a1_'
    A = _mk_alloc(nc, scope, g)
    wq = A('wq', [128, 4096], BF16); wk = A('wk', [128, 4096], BF16)
    wkT = A('wkT', [128, 4096], BF16)
    wvT = A('wvT', [128, 8192], BF16)
    wo = A('wo', [128, 8192], BF16)
    bq = A('bq', [32, 16], F32); bk = A('bk', [32, 16], F32)
    bo = A('bo', [128, 8], F32)
    bvr = A('bvr', [1, 1024], BF16); bkr = A('bkr', [1, 512], BF16)
    bvb = A('bvb', [128, 1024], BF16); bkb = A('bkb', [128, 512], BF16)
    for nm, sb in [('wq1', wq), ('wk1', wk), ('wkT1', wkT), ('wvT1', wvT),
                   ('wo1', wo)]:
        tk.dma('pool', sb[:], W[nm].rearrange('p a b -> p (a b)') if W[nm].ndim == 3
               else W[nm].rearrange('p a b c -> p (a b c)'), writes=[nm])
    tk.dma('sp', bq[:], W['bq1'], writes=['bq'])
    tk.dma('sp', bk[:], W['bk1'], writes=['bk'])
    tk.dma('sp', bo[:], W['bo1'], writes=['bo'])
    tk.dma('pool', bvr[:], W['bv1'], writes=['bvr'])
    tk.dma('pool', bkr[:], W['bk_row'], writes=['bkr'])
    for half in range(2):
        tk.mm(pb[6][:, 0:512], E['ones_row_bf'][:], bvr[0:1, half * 512:(half + 1) * 512],
              True, True, reads=['ones_row_bf', 'bvr'], writes=['pb6'])
        tk.copy(bvb[:, half * 512:(half + 1) * 512], pb[6][:, 0:512],
                reads=['pb6'], writes=['bvb'], eng='act')
    tk.mm(pb[6][:, 0:512], E['ones_row_bf'][:], bkr[0:1, :], True, True,
          reads=['ones_row_bf', 'bkr'], writes=['pb6'])
    tk.copy(bkb[:], pb[6][:, 0:512], reads=['pb6'], writes=['bkb'], eng='act')
    S1f = A('S1f', [32, 1040], F32); S1b = A('S1b', [32, 1040], BF16)
    tk.memset(S1f[:], 0.0, writes=[f'S1f_{h}' for h in range(16)])
    tk.memset(S1b[:], 0.0, writes=[f'S1b_{h}' for h in range(16)])
    xnw = [A(f'xnw{i}', [128, NDC * WIN], BF16) for i in range(2)]
    qsw = [A(f'qsw{i}', [32, 16 * WIN], BF16) for i in range(2)]
    ksw = [A(f'ksw{i}', [32, 16 * WIN], BF16) for i in range(2)]
    ktp0 = A('ktp0', [128, 4 * 512], BF16)
    ktp = [ktp0, ktp0]
    v1sb = [A(f'v1sb{i}', [128, 4 * 1040], BF16) for i in range(2)]
    tmf = [A(f'tmf{i}', [128, 512], F32) for i in range(3)]
    atb = [A(f'atb{i}', [128, 128], BF16) for i in range(2)]
    zc = A('zc', [128, 2], F32)
    otm = [A(f'otm{i}', [128, 1024], BF16) for i in range(2)]
    off = A('off', [128, 8 * WIN], BF16)
    sfw0 = A('sfw0', [128, NDC * WIN], BF16)
    sfw = [sfw0, sfw0]
    for i in range(2):
        tk.memset(v1sb[i].rearrange('p (t h c) -> p t h c', t=4, h=16)[:, :, :, 64:65],
                  1.0, writes=[f'v1sb{i}_ones'], eng='pool')

    def phi_fm(psum_ap, pkey, bcol, bkey, dst, dkey):
        tk.ts(tmf[0][0:32, :], psum_ap, bcol, 0.0, ALU.add, op1=ALU.min,
              reads=[pkey, bkey], writes=['tmf0'])
        tk.ts(tmf[1][0:32, :], psum_ap, bcol, 0.0, ALU.add, op1=ALU.max,
              reads=[pkey, bkey], writes=['tmf1'])
        tk.act(tmf[2][0:32, :], tmf[0][0:32, :], AF.Exp,
               reads=['tmf0'], writes=['tmf2'])
        tk.tt(dst, tmf[2][0:32, :], tmf[1][0:32, :], ALU.add,
              reads=['tmf2', 'tmf1'], writes=[dkey])

    def load_xn(p, w):
        for dc in range(NDC):
            tk.dma('sp', xnw[p][:, dc * WIN:(dc + 1) * WIN],
                   xn_h[dc * 128:(dc + 1) * 128, w * WIN:(w + 1) * WIN],
                   writes=[f'xnw{p}_{dc}'], label=f'l1xn_w{w}_d{dc}')

    for w in range(NW):
        p = w % 2
        load_xn(p, w)
        xkeys = [f'xnw{p}_{dc}' for dc in range(NDC)]
        for (wsb, wkey, bb, bbk, dst, dkey) in [(wq, 'wq1', bq, 'bq', qsw[p], f'qsw{p}'),
                                                (wk, 'wk1', bk, 'bk', ksw[p], f'ksw{p}')]:
            for h in range(16):
                woff = (h // 2) * 64 + (h % 2) * 32
                for dc in range(NDC):
                    tk.mm(pb[0][0:32, 0:512],
                          wsb[:, dc * 512 + woff:dc * 512 + woff + 32],
                          xnw[p][:, dc * WIN:(dc + 1) * WIN], dc == 0, dc == 7,
                          reads=[wkey, xkeys[dc]], writes=['pb0'])
                phi_fm(pb[0][0:32, 0:512], 'pb0', bb[:, h:h + 1], bbk,
                       dst[0:32, h * WIN:(h + 1) * WIN], f'{dkey}_{h}')
        for tau in range(NTAU):
            for dc in range(NDC):
                tk.mm(pb[1][:, 0:512],
                      xnw[p][:, dc * WIN + tau * 128:dc * WIN + (tau + 1) * 128],
                      wkT[:, dc * 512:(dc + 1) * 512], dc == 0, dc == 7,
                      reads=['wkT1', xkeys[dc]], writes=['pb1'])
            tk.tt(tmf[0][:], pb[1][:, 0:512], bkb[:], ALU.add,
                  reads=['pb1', 'bkb'], writes=['tmf0'])
            tk.ts(tmf[1][:], tmf[0][:], 0.0, None, ALU.max,
                  reads=['tmf0'], writes=['tmf1'])
            tk.ts(tmf[0][:], tmf[0][:], 0.0, None, ALU.min,
                  reads=['tmf0'], writes=['tmf0'])
            tk.act(tmf[2][:], tmf[0][:], AF.Exp, reads=['tmf0'], writes=['tmf2'])
            tk.tt(ktp[p][:, tau * 512:(tau + 1) * 512], tmf[2][:], tmf[1][:],
                  ALU.add, reads=['tmf2', 'tmf1'], writes=[f'ktp_{tau}'])
        for tau in range(NTAU):
            for half in range(2):
                bank, bk2 = (pb[2], 'pb2') if half == 0 else (pb[3], 'pb3')
                for dc in range(NDC):
                    tk.mm(bank[:, 0:512],
                          xnw[p][:, dc * WIN + tau * 128:dc * WIN + (tau + 1) * 128],
                          wvT[:, dc * 1024 + half * 512:dc * 1024 + half * 512 + 512],
                          dc == 0, dc == 7, reads=['wvT1', xkeys[dc]], writes=[bk2])
                dst = v1sb[p].rearrange('p (t h c) -> p t h c', t=4, h=16)[
                    :, tau, half * 8:(half + 1) * 8, 0:64]
                tk.tt(dst, bank[:, 0:512].rearrange('p (h c) -> p h c', h=8),
                      bvb[:, half * 512:(half + 1) * 512].rearrange('p (h c) -> p h c', h=8),
                      ALU.add, reads=[bk2, 'bvb'], writes=[f'v1sb{p}_{tau}'])
        for tau in range(NTAU):
            for h in range(16):
                hp = h % 2
                qsl = qsw[p][0:32, h * WIN + tau * 128:h * WIN + (tau + 1) * 128]
                ksl = ksw[p][0:32, h * WIN + tau * 128:h * WIN + (tau + 1) * 128]
                tk.mm(pb[4][:, 0:128], ksl, qsl, True, True,
                      reads=[f'ksw{p}_{h}', f'qsw{p}_{h}'], writes=['pb4'])
                tk.tt(atb[hp][:], pb[4][:, 0:128], mtri[:], ALU.mult,
                      reads=['pb4', 'mtri_bf'], writes=[f'atb{hp}'])
                v1 = v1sb[p].rearrange('p (t h c) -> p t h c', t=4, h=16)[:, tau, h, :]
                tk.mm(pb[5][:, 0:65], qsl, S1b[0:32, h * 65:(h + 1) * 65],
                      True, False, reads=[f'qsw{p}_{h}', f'S1b_{h}'], writes=['pb5'])
                tk.mm(pb[5][:, 0:65], atb[hp][:], v1, False, True,
                      reads=[f'atb{hp}', f'v1sb{p}_{tau}', f'v1sb{p}_ones'],
                      writes=['pb5'])
                tk.ts(zc[:, 0:1], pb[5][:, 64:65], 1e-6, None, ALU.add,
                      reads=['pb5'], writes=['zc'])
                tk.recip(zc[:, 1:2], zc[:, 0:1], reads=['zc'], writes=['zc'])
                tk.ts(otm[tau % 2][:, h * 64:(h + 1) * 64], pb[5][:, 0:64],
                      zc[:, 1:2], None, ALU.mult, reads=['pb5', 'zc'],
                      writes=[f'otm{tau % 2}_{h}'])
                tk.mm(pb[6][0:32, 0:65], ktp[p][:, tau * 512 + h * 32:tau * 512 + (h + 1) * 32],
                      v1, True, True,
                      reads=[f'ktp_{tau}', f'v1sb{p}_{tau}', f'v1sb{p}_ones'],
                      writes=['pb6'])
                tk.tt(S1f[:, h * 65:(h + 1) * 65], S1f[:, h * 65:(h + 1) * 65],
                      pb[6][0:32, 0:65], ALU.add, reads=['pb6', f'S1f_{h}'],
                      writes=[f'S1f_{h}'])
                tk.copy(S1b[:, h * 65:(h + 1) * 65], S1f[:, h * 65:(h + 1) * 65],
                        reads=[f'S1f_{h}'], writes=[f'S1b_{h}'], eng='pool')
            for hc in range(8):
                tk.tr(pb[7].bitcast(BF16)[:, 0:128],
                      otm[tau % 2][:, hc * 128:(hc + 1) * 128], iden_bf[:],
                      reads=[f'otm{tau % 2}_{hh}' for hh in range(2 * hc, 2 * hc + 2)]
                      + ['iden_bf'], writes=['pb7'])
                tk.copy(off[:, hc * WIN + tau * 128:hc * WIN + (tau + 1) * 128],
                        pb[7].bitcast(BF16)[:, 0:128], reads=['pb7'],
                        writes=[f'off_{hc}'], eng='act')
        for dm in range(NDC):
            for hc in range(NDC):
                tk.mm(pb[0][:, 0:512], wo[:, hc * 1024 + dm * 128:hc * 1024 + dm * 128 + 128],
                      off[:, hc * WIN:(hc + 1) * WIN], hc == 0, hc == 7,
                      reads=['wo1', f'off_{hc}'], writes=['pb0'])
            tk.ts(sfw[p][:, dm * WIN:(dm + 1) * WIN], pb[0][:, 0:512],
                  bo[:, dm:dm + 1], None, ALU.add, reads=['pb0', 'bo'],
                  writes=[f'sfw_{dm}'])
        for dm in range(NDC):
            tk.dma('sp', s_h[dm * 128:(dm + 1) * 128, w * WIN:(w + 1) * WIN],
                   sfw[p][:, dm * WIN:(dm + 1) * WIN], reads=[f'sfw_{dm}'],
                   label=f'sts1_w{w}_d{dm}')


def emit_ffn1(tk, E, nc, xn_h, g_h, W, scope=None):
    pb = E['pb']
    SC = 1.0 / 127.0
    g = 'f1_'
    A = _mk_alloc(nc, scope, g)
    wg = A('wg', [128, 8 * 3072], BF16)
    wu = A('wu', [128, 8 * 3072], BF16)
    wd = A('wd', [128, 24 * 1024], BF16)
    for nm, sb in [('wgT', wg), ('wuT', wu), ('wdT', wd)]:
        tk.dma('pool', sb[:], W[nm].rearrange('p a b -> p (a b)'), writes=[nm])
    xnw = [A(f'xnw{i}', [128, NDC * WIN], BF16) for i in range(2)]
    hsb = A('hsb', [128, 24 * WIN], BF16)
    gsl = [A(f'gsl{i}', [128, WIN], BF16) for i in range(2)]
    usl = [A(f'usl{i}', [128, WIN], BF16) for i in range(2)]
    gw = [A(f'gw{i}', [128, WIN], BF16) for i in range(2)]
    for w in range(NW):
        p = w % 2
        for dc in range(NDC):
            tk.dma('sp', xnw[p][:, dc * WIN:(dc + 1) * WIN],
                   xn_h[dc * 128:(dc + 1) * 128, w * WIN:(w + 1) * WIN],
                   writes=[f'xnw{p}_{dc}'], label=f'f1xn_w{w}_d{dc}')
        for m3 in range(24):
            q = m3 % 2
            bank, bk = (pb[0], 'pb0') if q == 0 else (pb[1], 'pb1')
            for dc in range(NDC):
                tk.mm(bank[:, 0:512],
                      wg[:, dc * 3072 + m3 * 128:dc * 3072 + (m3 + 1) * 128],
                      xnw[p][:, dc * WIN:(dc + 1) * WIN], dc == 0, dc == 7,
                      reads=['wgT', f'xnw{p}_{dc}'], writes=[bk])
            tk.act(gsl[q][:], bank[:, 0:512], AF.Silu, scale=SC,
                   reads=[bk], writes=[f'gsl{q}'])
            bank2, bk2 = (pb[2], 'pb2') if q == 0 else (pb[3], 'pb3')
            for dc in range(NDC):
                tk.mm(bank2[:, 0:512],
                      wu[:, dc * 3072 + m3 * 128:dc * 3072 + (m3 + 1) * 128],
                      xnw[p][:, dc * WIN:(dc + 1) * WIN], dc == 0, dc == 7,
                      reads=['wuT', f'xnw{p}_{dc}'], writes=[bk2])
            tk.act(usl[q][:], bank2[:, 0:512], AF.Copy, scale=SC,
                   reads=[bk2], writes=[f'usl{q}'])
            tk.tt(hsb[:, m3 * WIN:(m3 + 1) * WIN], gsl[q][:], usl[q][:], ALU.mult,
                  reads=[f'gsl{q}', f'usl{q}'], writes=[f'h_{m3}'])
        for dc in range(NDC):
            q = dc % 2
            bank, bk = (pb[4], 'pb4') if q == 0 else (pb[5], 'pb5')
            for m3 in range(24):
                tk.mm(bank[:, 0:512],
                      wd[:, m3 * 1024 + dc * 128:m3 * 1024 + (dc + 1) * 128],
                      hsb[:, m3 * WIN:(m3 + 1) * WIN], m3 == 0, m3 == 23,
                      reads=['wdT', f'h_{m3}'], writes=[bk])
            tk.act(gw[q][:], bank[:, 0:512], AF.Copy, scale=SC,
                   reads=[bk], writes=[f'gw{q}'])
            tk.dma('sp', g_h[dc * 128:(dc + 1) * 128, w * WIN:(w + 1) * WIN],
                   gw[q][:], reads=[f'gw{q}'], label=f'stg1_{w}_{dc}')


def emit_pe_all(tk, E, nc, x_all, routerT, allpe, scope=None):
    pb = E['pb']
    iden = E['iden_f32']; ones_col = E['ones_col']; ones_row = E['ones_row']
    cst = E['cst']; irev = E['irev']
    g = 'pa_'
    A = _mk_alloc(nc, scope, g)
    xw = [A(f'xw{i}', [128, NDC * WIN], F32) for i in range(2)]
    x2w = [A(f'x2w{i}', [128, NDC * WIN], F32) for i in range(2)]
    stats = [A(f'stats{q}', [128, 24], F32) for q in range(2)]
    routerT_sb = A('routerT_sb', [128, 64], F32)
    negSr = A('negSr', [1, 8], F32)
    srb = A('srb', [128, 8], F32)
    srcol = A('srcol', [8, 1], F32)
    lgt = [A(f'lgt{q}', [128, 32], F32) for q in range(2)]
    elg = [A(f'elg{q}', [128, 32], F32) for q in range(2)]
    eqt = [A(f'eqt{q}', [128, 32], F32) for q in range(2)]
    t2b = [A(f't2b{q}', [128, 32], F32) for q in range(2)]
    mx4 = [A(f'mx4{q}', [128, 16], F32) for q in range(2)]
    pecols = [A(f'pecols{q}', [128, 8], F32) for q in range(2)]
    peacc = A('peacc', [8, 1024], F32)
    irev4 = A('irev4', [128, 32], F32)
    tk.dma('sp', routerT_sb[:], routerT.rearrange('p a b -> p (a b)'),
           writes=['routerT_sb'])
    for rr in range(4):
        tk.copy(irev4[:, rr * 8:(rr + 1) * 8], irev[:], reads=['irev'],
                writes=['irev4'], eng='pool')
    for dc in range(NDC):
        tk.mm(pb[5][0:8, 0:1], routerT_sb[:, dc * 8:(dc + 1) * 8], ones_col[:],
              dc == 0, dc == 7, reads=['routerT_sb', 'ones_col'], writes=['pb5'])
    tk.ts(srcol[:], pb[5][0:8, 0:1], -1.0, None, ALU.mult,
          reads=['pb5'], writes=['srcol'])
    tk.tr(pb[6][0:1, 0:8], srcol[:], iden[0:8, 0:8],
          reads=['srcol', 'iden_f32'], writes=['pb6'])
    tk.copy(negSr[:], pb[6][0:1, 0:8], reads=['pb6'], writes=['negSr'])
    tk.mm(pb[6][:, 8:16], ones_row[:], negSr[:], True, True,
          reads=['ones_row', 'negSr'], writes=['pb6'])
    tk.copy(srb[:], pb[6][:, 8:16], reads=['pb6'], writes=['srb'])
    srows = A('srows', [1, 1024], F32)
    for r in range(8):
        for w in range(NW):
            p = w % 2
            q = w % 2
            bs, bq2, br = (pb[0], pb[1], pb[2]) if q == 0 else (pb[3], pb[4], pb[5])
            ks, kq, kr = (f'pb{j}' for j in ((0, 1, 2) if q == 0 else (3, 4, 5)))
            bt, kt = (pb[6], 'pb6') if q == 0 else (pb[7], 'pb7')
            st = stats[q]; sk = f'st{q}'
            for dc in range(NDC):
                tk.dma('sp', xw[p][:, dc * WIN:(dc + 1) * WIN],
                       x_all[r, dc * 128:(dc + 1) * 128, w * WIN:(w + 1) * WIN],
                       writes=[f'xw{p}_{dc}'], label=f'pa_{r}_{w}_{dc}')
            for dc in range(NDC):
                tk.act(x2w[p][:, dc * WIN:(dc + 1) * WIN],
                       xw[p][:, dc * WIN:(dc + 1) * WIN], AF.Square,
                       reads=[f'xw{p}_{dc}'], writes=[f'x2w{p}_{dc}'])
            for dc in range(NDC):
                tk.mm(bs[0:1, 0:512], ones_col[:], xw[p][:, dc * WIN:(dc + 1) * WIN],
                      dc == 0, dc == 7, reads=[f'xw{p}_{dc}', 'ones_col'], writes=[ks])
                tk.mm(bq2[0:1, 0:512], ones_col[:], x2w[p][:, dc * WIN:(dc + 1) * WIN],
                      dc == 0, dc == 7, reads=[f'x2w{p}_{dc}', 'ones_col'], writes=[kq])
            tk.copy(srows[0:1, 0:512], bs[0:1, 0:512], reads=[ks],
                    writes=['srows'], eng='act')
            tk.copy(srows[0:1, 512:1024], bq2[0:1, 0:512], reads=[kq],
                    writes=['srows'], eng='act')
            for tau in range(NTAU):
                for dc in range(NDC):
                    xt = xw[p][:, dc * WIN + tau * 128: dc * WIN + (tau + 1) * 128]
                    tk.mm(br[:, 8 * tau:8 * tau + 8], xt,
                          routerT_sb[:, dc * 8:(dc + 1) * 8], dc == 0, dc == 7,
                          reads=[f'xw{p}_{dc}', 'routerT_sb'], writes=[kr])
            for tau in range(NTAU):
                tk.tr(bt[:, tau:tau + 1], srows[0:1, tau * 128:(tau + 1) * 128],
                      iden[0:1, 0:1], reads=['srows', 'iden_f32'], writes=[kt])
                tk.tr(bt[:, 4 + tau:5 + tau],
                      srows[0:1, 512 + tau * 128:512 + (tau + 1) * 128],
                      iden[0:1, 0:1], reads=['srows', 'iden_f32'], writes=[kt])
            tk.ts(st[:, 0:4], bt[:, 0:4], 1.0 / D, None, ALU.mult,
                  reads=[kt], writes=[sk])
            tk.act(st[:, 4:8], st[:, 0:4], AF.Square, reads=[sk], writes=[sk])
            tk.stt(st[:, 8:12], bt[:, 4:8], 1.0 / D, st[:, 4:8],
                   ALU.mult, ALU.subtract, reads=[kt, sk], writes=[sk])
            tk.act(st[:, 12:16], st[:, 8:12], AF.Ln, bias=cst[:, 0:1],
                   reads=[sk, 'cst'], writes=[sk])
            tk.act(st[:, 16:20], st[:, 12:16], AF.Exp, scale=-0.5,
                   reads=[sk], writes=[sk])
            tk.stt(st[:, 20:24], st[:, 0:4], -1.0, st[:, 16:20],
                   ALU.mult, ALU.mult, reads=[sk], writes=[sk])
            v_t2 = t2b[q].rearrange('p (t e) -> p t e', t=4)
            tk.tt(v_t2, st[:, 20:24].unsqueeze(2).to_broadcast([128, 4, 8]),
                  srb[:].unsqueeze(1).to_broadcast([128, 4, 8]),
                  ALU.mult, reads=[sk, 'srb'], writes=[f't2b{q}'])
            v_lgt = lgt[q].rearrange('p (t e) -> p t e', t=4)
            tk.tt(v_lgt, br[:, 0:32].rearrange('p (t e) -> p t e', t=4),
                  st[:, 16:20].unsqueeze(2).to_broadcast([128, 4, 8]),
                  ALU.mult, reads=[kr, sk], writes=[f'lgt{q}'])
            tk.tt(lgt[q][:], lgt[q][:], t2b[q][:], ALU.add,
                  reads=[f'lgt{q}', f't2b{q}'], writes=[f'lgt{q}'])
            tk.red(mx4[q][:, 0:4].unsqueeze(2), v_lgt, ALU.max,
                   negate=True, reads=[f'lgt{q}'], writes=[f'mx4{q}'])
            v_el = elg[q].rearrange('p (t e) -> p t e', t=4)
            tk.tt(v_el, v_lgt,
                  mx4[q][:, 0:4].unsqueeze(2).to_broadcast([128, 4, 8]),
                  ALU.add, reads=[f'lgt{q}', f'mx4{q}'], writes=[f'elg{q}'])
            tk.act(elg[q][:], elg[q][:], AF.Exp, reads=[f'elg{q}'], writes=[f'elg{q}'])
            tk.red(mx4[q][:, 4:8].unsqueeze(2), v_el, ALU.add,
                   reads=[f'elg{q}'], writes=[f'mx4{q}'])
            tk.recip(mx4[q][:, 8:12], mx4[q][:, 4:8],
                     reads=[f'mx4{q}'], writes=[f'mx4{q}'])
            tk.ts(eqt[q][:], elg[q][:], 1.0, None, ALU.is_ge,
                  reads=[f'elg{q}'], writes=[f'eqt{q}'])
            tk.tt(eqt[q][:], eqt[q][:], irev4[:], ALU.mult,
                  reads=[f'eqt{q}', 'irev4'], writes=[f'eqt{q}'])
            tk.red(mx4[q][:, 12:16].unsqueeze(2),
                   eqt[q].rearrange('p (t e) -> p t e', t=4), ALU.max,
                   reads=[f'eqt{q}'], writes=[f'mx4{q}'])
            vp = pecols[q].rearrange('p (t c) -> p t c', t=4)
            tk.copy(vp[:, :, 0:1], mx4[q][:, 8:12].unsqueeze(2),
                    reads=[f'mx4{q}'], writes=[f'pe{q}'])
            tk.ts(vp[:, :, 1:2], mx4[q][:, 12:16].unsqueeze(2),
                  -1.0, 7.0, ALU.mult, op1=ALU.add,
                  reads=[f'mx4{q}'], writes=[f'pe{q}'])
            bt, kt = (pb[6], 'pb6') if q == 0 else (pb[7], 'pb7')
            tk.tr(bt[0:8, 128:256], pecols[q][:], iden[:],
                  reads=[f'pe{q}', 'iden_f32'], writes=[kt])
            tk.copy(peacc[:, w * 128:(w + 1) * 128], bt[0:8, 128:256],
                    reads=[kt], writes=[f'peacc_{w}'], eng='act')
        Fv = allpe.rearrange('a (w t j) -> a t w j', w=8, j=128)
        for cc in range(2):
            tk.dma('sp', Fv[2 * r + cc],
                   peacc[cc:8:2, :].rearrange('p (w j) -> p w j', j=128),
                   reads=[f'peacc_{w}' for w in range(NW)],
                   writes=[f'allpe_{r}'], label=f'stpe_{r}_{cc}')


from contextlib import ExitStack
from concourse.bass_utils import run_bass_kernel_spmd

_CONST_KEYS = ['iden', 'ones_col', 'ones_row', 'mtri', 'selA', 'selB', 'rmat',
               'rmatT', 'idxmat', 'irev', 'ones8k']


def build_fused(nc):
    F, B, I = F32, BF16, I32
    ci = {}

    def ein(name, shape, dtype):
        ci[name] = nc.dram_tensor(name, shape, dtype, kind="ExternalInput").ap()
        return ci[name]

    x0 = ein('x0', [1024, 4096], F)
    x_all = ein('x_all', [8, 1024, 4096], F)
    routerT = ein('routerT', [128, 8, 8], F)
    csel = ein('csel', [128, 2], F)
    cselE = ein('cselE', [128, 16], F)
    W0 = {nm: ein(nm, sh, F) for nm, sh in [
        ('wq', [128, 8, 8, 64]), ('wk', [128, 8, 8, 64]), ('wkT', [128, 8, 256]),
        ('wkeT', [128, 8, 256]), ('wvT', [128, 8, 1024]), ('wveT', [128, 8, 1024]),
        ('wout', [128, 8, 1024])]}
    WM = {nm: ein(nm, sh, F) for nm, sh in [
        ('w1', [8, 128, 32, 1024]), ('w2', [8, 128, 32, 1024]),
        ('b1', [8, 128, 32]), ('b2', [8, 128, 8])]}
    W1 = {}
    for nm, sh, dt in [('wq1', [128, 8, 8, 64], F), ('wk1', [128, 8, 8, 64], F),
                       ('bq1', [32, 16], F), ('bk1', [32, 16], F),
                       ('wkT1', [128, 8, 512], F), ('wvT1', [128, 8, 1024], F),
                       ('bv1', [1, 1024], F), ('bk_row', [1, 512], F),
                       ('wo1', [128, 8, 1024], F), ('bo1', [128, 8], F),
                       ('wgT', [128, 8, 3072], I), ('wuT', [128, 8, 3072], I),
                       ('wdT', [128, 24, 1024], I)]:
        W1[nm] = ein(nm, sh, dt)
    out_h = nc.dram_tensor('out_h', [1024, 4096], F, kind="ExternalOutput").ap()
    def intl(name, shape, dtype):
        return nc.dram_tensor(name, shape, dtype, kind="Internal").ap()
    allpe = intl('allpe', [16, 4096], F)
    xn0 = intl('xn0', [1024, 4096], B)
    kmask = intl('kmask', [8, 4096], F)
    s0 = intl('s0', [1024, 4096], B)
    g0 = intl('g0', [8, 1024, 4096], B)
    x1 = intl('x1', [1024, 4096], F)
    xn1 = intl('xn1', [1024, 4096], B)
    s1 = intl('s1', [1024, 4096], B)
    g1 = intl('g1', [1024, 4096], B)

    tk = TK(nc)
    with nc.Block() as block:
        E = setup_consts(nc, tk)
        E['pb'] = alloc_psum_banks(nc)
        tk.phase = 'ln0'
        with ExitStack() as sc:
            emit_ln(tk, E, nc, x0, xn0, tag='ln0', scope=sc)
        tk.barrier()
        tk.phase = 'peall'
        with ExitStack() as sc:
            emit_pe_all(tk, E, nc, x_all, routerT, allpe, scope=sc)
        tk.barrier()
        tk.phase = 'routing'
        with ExitStack() as sc:
            emit_routing(tk, E, nc, allpe, csel, cselE, kmask, scope=sc)
        tk.barrier()
        tk.phase = 'attn0'
        with ExitStack() as sc:
            emit_attn0(tk, E, nc, xn0, s0, W0, scope=sc)
        tk.barrier()
        tk.phase = 'moe'
        with ExitStack() as sc:
            emit_moe(tk, E, nc, xn0, kmask, g0, WM, scope=sc)
        tk.barrier()
        tk.phase = 'cb0'
        with ExitStack() as sc:
            emit_combine(tk, E, nc, x0, s0, g0, x1, scope=sc, tag='cb0', ge=True)
        tk.barrier()
        tk.phase = 'ln1'
        with ExitStack() as sc:
            emit_ln(tk, E, nc, x1, xn1, tag='ln1', scope=sc)
        tk.barrier()
        tk.phase = 'attn1'
        with ExitStack() as sc:
            emit_attn1(tk, E, nc, xn1, s1, W1, scope=sc)
        tk.barrier()
        tk.phase = 'ffn1'
        with ExitStack() as sc:
            emit_ffn1(tk, E, nc, xn1, g1, W1, scope=sc)
        tk.barrier()
        tk.phase = 'cb1'
        with ExitStack() as sc:
            emit_combine(tk, E, nc, x1, s1, g1, out_h, scope=sc, tag='cb1')
        tk.flush(block)
    return nc


def kernel(**inputs):
    c = pack_consts()
    w = pack_weights(inputs)
    x = inputs['L0_attn_x']
    xallT = np.ascontiguousarray(np.transpose(x, (0, 2, 1))).astype(np.float32)
    nc = build_fused(bass.Bass("TRN2", target_bir_lowering=False,
                               detect_race_conditions=False))
    cim = {}
    for k2 in _CONST_KEYS:
        key = {'iden': 'iden_f32'}.get(k2, k2)
        cim['c_' + k2] = c[key]
    wm = {'wq': 'l0_wq', 'wk': 'l0_wk', 'wkT': 'l0_wkT', 'wkeT': 'l0_wkeT',
          'wvT': 'l0_wvT', 'wveT': 'l0_wveT', 'wout': 'l0_wout',
          'w1': 'w1', 'w2': 'w2', 'b1': 'b1', 'b2': 'b2',
          'wq1': 'l1_wq', 'wk1': 'l1_wk', 'bq1': 'l1_bq', 'bk1': 'l1_bk',
          'wkT1': 'l1_wkT', 'wvT1': 'l1_wvT', 'bv1': 'l1_bv',
          'bk_row': 'l1_bk_row', 'wo1': 'l1_wo', 'bo1': 'l1_bo',
          'wgT': 'wgT', 'wuT': 'wuT', 'wdT': 'wdT', 'routerT': 'routerT'}
    in_maps = []
    for core in range(8):
        im = dict(cim)
        im['x0'] = np.ascontiguousarray(x[core].T).astype(np.float32)
        im['x_all'] = xallT
        for k2, v in wm.items():
            im[k2] = w[v]
        cs = np.zeros((128, 2), np.float32)
        cse = np.zeros((128, 16), np.float32)
        for gg in range(2):
            for e in range(8):
                cs[e * 16 + 2 * core + gg, gg] = 1.0
                cse[e * 16 + 2 * core + gg, 2 * e + gg] = 1.0
        im['csel'] = cs
        im['cselE'] = cse
        in_maps.append(im)
    res = run_bass_kernel_spmd(nc, in_maps, core_ids=list(range(8)))
    out = np.stack([np.ascontiguousarray(r['out_h'].T) for r in res.results],
                   axis=0)
    return out.astype(np.float32)


# version 32
# speedup vs baseline: 1.1248x; 1.0269x over previous
import numpy as np
import concourse.bass as bass
import concourse.mybir as mybir

F32 = mybir.dt.float32
BF16 = mybir.dt.bfloat16
I32 = mybir.dt.int32
AF = mybir.ActivationFunctionType
ALU = mybir.AluOpType
AX = mybir.AxisListType.X

D = 1024
T = 4096
NW = 8
WIN = 512
NTAU = 4
CT = 128
NDC = 8


class Op:
    __slots__ = ('stream', 'fn', 'deps', 'sem', 'inc', 'idx', 'label', 'phase')

    def __init__(self, stream, fn, sem, inc, idx, label):
        self.stream = stream
        self.fn = fn
        self.deps = []
        self.sem = sem
        self.inc = inc
        self.idx = idx
        self.label = label


class TK:
    STREAMS = ('pe', 'act', 'dve', 'pool', 'sp')
    ENGSEM = {'pe': 'pe', 'act': 'act', 'dve': 'dve', 'pool': 'pool', 'sp': 'spe'}

    NQSEM = 4

    def __init__(self, nc):
        self.nc = nc
        self.ops = []
        self.writer = {}
        self.readers = {}
        self.dma_ring = {}
        self.barrier_ops = []
        self.phase = 'init'

    def op(self, stream, fn, reads=(), writes=(), label='', is_dma=False):
        assert stream in self.STREAMS
        extra_dep = None
        if is_dma:
            ring = self.dma_ring.setdefault(stream, [])
            slot = len(ring) % self.NQSEM
            sem = f'q_{stream}_{slot}'
            inc = 16
            if len(ring) >= self.NQSEM:
                extra_dep = ring[-self.NQSEM]
            ring.append(None)
        else:
            sem = None
            inc = 1
        o = Op(stream, fn, sem, inc, len(self.ops), label)
        o.phase = self.phase
        if is_dma:
            self.dma_ring[stream][-1] = o
        deps = {}
        if extra_dep is not None:
            deps[extra_dep.idx] = extra_dep
        for k in reads:
            w = self.writer.get(k)
            if w is not None:
                if w.stream == 'pe' and stream == 'pe' and w.inc != 16:
                    continue
                deps[w.idx] = w
        for k in writes:
            w = self.writer.get(k)
            if w is not None and not (w.stream == stream and w.inc != 16):
                deps[w.idx] = w
            for r in self.readers.get(k, ()):
                if r.stream == stream and r.inc != 16:
                    continue
                deps[r.idx] = r
        o.deps = list(deps.values())
        for d in o.deps:
            if d.sem is None:
                d.sem = self.ENGSEM[d.stream]
        for k in writes:
            self.writer[k] = o
            self.readers[k] = []
        for k in reads:
            self.readers.setdefault(k, []).append(o)
        self.ops.append(o)
        return o

    def dma(self, queue, out, in_, reads=(), writes=(), label='', **kw):
        return self.op(queue, lambda e: e.dma_start(out=out, in_=in_, **kw),
                       reads=reads, writes=writes, label=label, is_dma=True)

    def mm(self, out, lhsT, rhs, start, stop, reads, writes, label=''):
        fn = (lambda out, lhsT, rhs, start, stop:
              lambda e: e.matmul(out, lhsT=lhsT, rhs=rhs, start=start, stop=stop,
                                 skip_group_check=True))(out, lhsT, rhs, start, stop)
        return self.op('pe', fn, reads=reads, writes=writes, label=label)

    def tr(self, out, in_, iden, reads, writes, label=''):
        fn = (lambda out, in_, iden:
              lambda e: e.transpose(out, in_=in_, identity=iden))(out, in_, iden)
        return self.op('pe', fn, reads=reads, writes=writes, label=label)

    def act(self, out, in_, func, reads, writes, bias=0.0, scale=1.0, eng='act', label=''):
        fn = (lambda out, in_, func, bias, scale:
              lambda e: e.activation(out, in_, func, bias=bias, scale=scale))(
                  out, in_, func, bias, scale)
        return self.op(eng, fn, reads=reads, writes=writes, label=label)

    def tt(self, out, a, b, opk, reads, writes, eng='dve', label=''):
        fn = (lambda out, a, b, opk:
              lambda e: e.tensor_tensor(out, a, b, op=opk))(out, a, b, opk)
        return self.op(eng, fn, reads=reads, writes=writes, label=label)

    def ts(self, out, a, s1, s2, op0, reads, writes, op1=None, accum=None, eng='dve', label=''):
        def mk(out, a, s1, s2, op0, op1, accum):
            def f(e):
                kw = {}
                if op1 is not None:
                    kw['op1'] = op1
                if accum is not None:
                    kw['accum_out'] = accum
                return e.tensor_scalar(out, a, s1, s2, op0=op0, **kw)
            return f
        return self.op(eng, mk(out, a, s1, s2, op0, op1, accum),
                       reads=reads, writes=writes, label=label)

    def stt(self, out, in0, scalar, in1, op0, op1, reads, writes, eng='dve', label=''):
        fn = (lambda out, in0, scalar, in1, op0, op1:
              lambda e: e.scalar_tensor_tensor(out, in0, scalar, in1, op0=op0, op1=op1))(
                  out, in0, scalar, in1, op0, op1)
        return self.op(eng, fn, reads=reads, writes=writes, label=label)

    def red(self, out, in_, opk, reads, writes, negate=False, eng='dve', label=''):
        fn = (lambda out, in_, opk, negate:
              lambda e: e.tensor_reduce(out, in_, AX, opk, negate=negate))(
                  out, in_, opk, negate)
        return self.op(eng, fn, reads=reads, writes=writes, label=label)

    def copy(self, out, in_, reads, writes, eng='dve', label=''):
        if eng == 'act':
            fn = (lambda out, in_: lambda e: e.copy(out, in_))(out, in_)
        else:
            fn = (lambda out, in_: lambda e: e.tensor_copy(out, in_))(out, in_)
        return self.op(eng, fn, reads=reads, writes=writes, label=label)

    def memset(self, ap, val, writes, eng='dve', label=''):
        fn = (lambda ap, val: lambda e: e.memset(ap, val))(ap, val)
        return self.op(eng, fn, reads=(), writes=writes, label=label)

    def recip(self, out, in_, reads, writes, label=''):
        fn = (lambda out, in_: lambda e: e.reciprocal(out, in_))(out, in_)
        return self.op('dve', fn, reads=reads, writes=writes, label=label)

    def cpred(self, out, mask, data, reads, writes, label=''):
        fn = (lambda out, mask, data:
              lambda e: e.copy_predicated(out, mask, data))(out, mask, data)
        return self.op('dve', fn, reads=reads, writes=writes, label=label)

    def barrier(self):
        deps = {}
        for w in self.writer.values():
            deps[w.idx] = w
        for rl in self.readers.values():
            for r in rl:
                deps[r.idx] = r
        dep_list = list(deps.values())
        for d in dep_list:
            if d.sem is None:
                d.sem = self.ENGSEM[d.stream]
        newbar = []
        for s in self.STREAMS:
            o = Op(s, lambda e: e.nop(), None, 1, len(self.ops), 'barrier')
            o.phase = self.phase
            o.deps = [d for d in dep_list if not (d.stream == s and d.inc != 16)] \
                + [b for b in self.barrier_ops]
            self.ops.append(o)
            newbar.append(o)
        self.writer = {}
        self.readers = {}
        self.barrier_ops = newbar
        for b in newbar:
            for d in b.deps:
                if d.sem is None:
                    d.sem = self.ENGSEM[d.stream]

    def flush(self, block):
        nc = self.nc
        semval = {}
        opval = {}
        for o in self.ops:
            if o.sem is not None:
                semval[o.sem] = semval.get(o.sem, 0) + o.inc
                opval[o.idx] = (o.sem, semval[o.sem])
        sems = {name: nc.alloc_semaphore(name) for name in semval}
        finals = dict(semval)
        by_stream = {s: [] for s in self.STREAMS}
        for o in self.ops:
            by_stream[o.stream].append(o)
        engmap = {'pe': block.tensor, 'act': block.scalar, 'dve': block.vector,
                  'pool': block.gpsimd, 'sp': block.sync}
        nceng = {'pe': nc.tensor, 'act': nc.scalar, 'dve': nc.vector,
                 'pool': nc.gpsimd, 'sp': nc.sync}
        for s in self.STREAMS:
            guar = {}

            def emit_stream(eng, s=s, guar=guar):
                cur = [None]
                curscope = [None]
                for o in by_stream[s]:
                    if o.phase != cur[0]:
                        if curscope[0] is not None:
                            curscope[0].__exit__(None, None, None)
                        cm = nc.named_scope(o.phase)
                        cm.__enter__()
                        curscope[0] = cm
                        cur[0] = o.phase
                    for d in o.deps:
                        if d.idx not in opval:
                            raise RuntimeError(f'dep without sem: {d.label}')
                        nm, v = opval[d.idx]
                        if guar.get(nm, 0) < v:
                            eng.wait_ge(sems[nm], v)
                            guar[nm] = v
                    inst = o.fn(eng)
                    if o.sem is not None:
                        if inst is None:
                            raise RuntimeError(f'op {o.label} returned no inst')
                        inst.then_inc(sems[o.sem], o.inc)
                if curscope[0] is not None:
                    curscope[0].__exit__(None, None, None)

            engmap[s](emit_stream)
        def final_wait(eng):
            for nm, v in finals.items():
                eng.wait_ge(sems[nm], v)
        block.sync(final_wait)
        return finals


CS = float(1.0 / np.sqrt(4.0 * np.sqrt(2.0)))
LIN = float(0.5 / CS)
SQB = float(np.sqrt(0.5))

def pair_list():
    pairs = [(a, b) for a in range(16) for b in range(16)]
    return pairs

def pack_consts():
    c = {}
    iden = np.eye(128, dtype=np.float32)
    c['iden_bf'] = iden.astype(np.float32)
    c['iden_f32'] = iden.copy()
    c['ones_col'] = np.ones((128, 1), np.float32)
    c['ones_row'] = np.ones((1, 128), np.float32)
    tri = (np.arange(128)[None, :] >= np.arange(128)[:, None]).astype(np.float32)
    c['mtri'] = tri
    pairs = pair_list()
    selA = np.zeros((64, 273), np.float32)
    selB = np.zeros((64, 273), np.float32)
    def setsel(base):
        selA[base + 16, 0] = 1.0; selB[base + 16, 0] = 1.0
        for a in range(16):
            selA[base + a, 1 + a] = 1.0; selB[base + 16, 1 + a] = 1.0
        for m in range(111):
            a, b = pairs[m]
            selA[base + a, 17 + m] = 1.0; selB[base + b, 17 + m] = 1.0
        for m in range(128):
            a, b = pairs[111 + m]
            selA[base + a, 128 + m] = 1.0; selB[base + b, 128 + m] = 1.0
        for m in range(17):
            a, b = pairs[239 + m]
            selA[base + a, 256 + m] = 1.0; selB[base + b, 256 + m] = 1.0
    setsel(0); setsel(32)
    c['selA'] = selA; c['selB'] = selB
    R = np.zeros((8, 128), np.float32)
    for e in range(8):
        R[e, e * 16:(e + 1) * 16] = 1.0
    c['rmat'] = R
    c['rmatT'] = np.ascontiguousarray(R.T)
    idx = (np.arange(128)[:, None] % 16) * 2048 + np.arange(2048)[None, :]
    c['idxmat'] = idx.astype(np.float32)
    c['irev'] = np.broadcast_to((7 - np.arange(8))[None, :].astype(np.float32),
                                (128, 8)).copy()
    c['ones8k'] = np.ones(8192, np.float32)
    return c

def chunk_lhsT(w, kchunk=128):
    K, M = w.shape
    nk = K // kchunk
    return np.ascontiguousarray(w.reshape(nk, kchunk, M).transpose(1, 0, 2))

def pack_weights(d):
    w = {}
    def packqk(W):
        Wt = W.T * CS
        out = np.zeros((1024, 8, 64), np.float32)
        for p in range(8):
            out[:, p, 0:16] = Wt[:, (2 * p) * 16:(2 * p) * 16 + 16]
            out[:, p, 32:48] = Wt[:, (2 * p + 1) * 16:(2 * p + 1) * 16 + 16]
        return np.ascontiguousarray(
            out.reshape(8, 128, 8, 64).transpose(1, 0, 2, 3))
    w['l0_wq'] = packqk(d['L0_attn_Wq'])
    w['l0_wk'] = packqk(d['L0_attn_Wk'])
    w['l0_wke'] = packqk(d['L0_attn_Wk_enc'])
    w['l0_wkT'] = chunk_lhsT(d['L0_attn_Wk'].T * CS)
    w['l0_wkeT'] = chunk_lhsT(d['L0_attn_Wk_enc'].T * CS)
    w['l0_wvT'] = chunk_lhsT(d['L0_attn_Wv'].T)
    w['l0_wveT'] = chunk_lhsT(d['L0_attn_Wv_enc'].T)
    w['l0_wout'] = chunk_lhsT(d['L0_attn_Wout'].T)
    w['routerT'] = chunk_lhsT(d['L0_ffn_router_w'].T)
    w['w1'] = np.ascontiguousarray(
        d['L0_ffn_W1'].reshape(8, 8, 128, 4096).transpose(0, 2, 1, 3))
    w['w2'] = np.ascontiguousarray(
        d['L0_ffn_W2'].reshape(8, 32, 128, 1024).transpose(0, 2, 1, 3))
    w['b1'] = np.ascontiguousarray(
        d['L0_ffn_b1'].reshape(8, 32, 128).transpose(0, 2, 1))
    w['b2'] = np.ascontiguousarray(
        d['L0_ffn_b2'].reshape(8, 8, 128).transpose(0, 2, 1))
    def packqk1(W, b):
        out = np.zeros((1024, 8, 64), np.float32)
        ob = np.zeros((32, 16), np.float32)
        for p in range(8):
            out[:, p, 0:32] = W[:, (2 * p) * 32:(2 * p) * 32 + 32]
            out[:, p, 32:64] = W[:, (2 * p + 1) * 32:(2 * p + 1) * 32 + 32]
        for h in range(16):
            ob[:, h] = b[h * 32:(h + 1) * 32]
        return (np.ascontiguousarray(out.reshape(8, 128, 8, 64).transpose(1, 0, 2, 3)),
                ob)
    w['l1_wq'], w['l1_bq'] = packqk1(d['L1_attn_wq'], d['L1_attn_bq'])
    w['l1_wk'], w['l1_bk'] = packqk1(d['L1_attn_wk'], d['L1_attn_bk'])
    w['l1_wkT'] = chunk_lhsT(d['L1_attn_wk'])
    w['l1_wvT'] = chunk_lhsT(d['L1_attn_wv'])
    w['l1_bv'] = d['L1_attn_bv'].reshape(1, 1024).copy()
    w['l1_bk_row'] = d['L1_attn_bk'].reshape(1, 512).copy()
    w['l1_wo'] = chunk_lhsT(d['L1_attn_wo'])
    w['l1_bo'] = np.ascontiguousarray(d['L1_attn_bo'].reshape(8, 128).T)
    w['wgT'] = chunk_lhsT(d['L1_ffn_w_gate'].T)
    w['wuT'] = chunk_lhsT(d['L1_ffn_w_up'].T)
    w['wdT'] = chunk_lhsT(d['L1_ffn_w_down'].T)
    return w


def declare_hbm(nc, v2=False):
    H = {}
    def dt_(name, shape, dtype, kind):
        H[name] = nc.dram_tensor(name, shape, dtype, kind=kind).ap()
    return H


def setup_consts(nc, tk):
    E = {}
    ci = {}
    for name, shape, dtype in [
        ('c_iden', [128, 128], F32), ('c_ones_col', [128, 1], F32),
        ('c_ones_row', [1, 128], F32), ('c_mtri', [128, 128], F32),
        ('c_selA', [64, 273], F32), ('c_selB', [64, 273], F32),
        ('c_rmat', [8, 128], F32), ('c_rmatT', [128, 8], F32),
        ('c_idxmat', [128, 2048], F32), ('c_irev', [128, 8], F32),
        ('c_ones8k', [8192], F32),
    ]:
        ci[name] = nc.dram_tensor(name, shape, dtype, kind="ExternalInput").ap()
    E['iden_f32'] = nc.alloc_sbuf_tensor('iden_f32', [128, 128], F32).ap()
    E['iden_bf'] = nc.alloc_sbuf_tensor('iden_bf', [128, 128], BF16).ap()
    E['ones_col'] = nc.alloc_sbuf_tensor('ones_col', [128, 1], F32).ap()
    E['ones_row'] = nc.alloc_sbuf_tensor('ones_row', [1, 128], F32).ap()
    E['ones_row_bf'] = nc.alloc_sbuf_tensor('ones_row_bf', [1, 128], BF16).ap()
    E['mtri_bf'] = nc.alloc_sbuf_tensor('mtri_bf', [128, 128], BF16).ap()
    E['mtri_f32'] = nc.alloc_sbuf_tensor('mtri_f32', [128, 128], F32).ap()
    E['selA'] = nc.alloc_sbuf_tensor('selA', [64, 273], BF16).ap()
    E['selB'] = nc.alloc_sbuf_tensor('selB', [64, 273], BF16).ap()
    E['rmat'] = nc.alloc_sbuf_tensor('rmat', [8, 128], F32).ap()
    E['rmatT'] = nc.alloc_sbuf_tensor('rmatT', [128, 8], F32).ap()
    E['idxmat'] = nc.alloc_sbuf_tensor('idxmat', [128, 2048], F32).ap()
    E['irev'] = nc.alloc_sbuf_tensor('irev', [128, 8], F32).ap()
    E['cst'] = nc.alloc_sbuf_tensor('cst', [128, 8], F32).ap()
    E['ones8k_h'] = ci['c_ones8k']
    tk.dma('sp', E['iden_f32'][:], ci['c_iden'], writes=['iden_f32'])
    tk.dma('pool', E['iden_bf'][:], ci['c_iden'], writes=['iden_bf'])
    tk.dma('sp', E['ones_col'][:], ci['c_ones_col'], writes=['ones_col'])
    tk.dma('sp', E['ones_row'][:], ci['c_ones_row'], writes=['ones_row'])
    tk.dma('pool', E['ones_row_bf'][:], ci['c_ones_row'], writes=['ones_row_bf'])
    tk.dma('pool', E['mtri_bf'][:], ci['c_mtri'], writes=['mtri_bf'])
    tk.dma('sp', E['mtri_f32'][:], ci['c_mtri'], writes=['mtri_f32'])
    tk.dma('pool', E['selA'][:], ci['c_selA'], writes=['selA'])
    tk.dma('pool', E['selB'][:], ci['c_selB'], writes=['selB'])
    tk.dma('sp', E['rmat'][:], ci['c_rmat'], writes=['rmat'])
    tk.dma('sp', E['rmatT'][:], ci['c_rmatT'], writes=['rmatT'])
    tk.dma('sp', E['idxmat'][:], ci['c_idxmat'], writes=['idxmat'])
    tk.dma('sp', E['irev'][:], ci['c_irev'], writes=['irev'])
    tk.op('dve', lambda e: e.memset(E['cst'][:, 0:1], 1e-5), writes=['cst'])
    tk.op('dve', lambda e: e.memset(E['cst'][:, 1:2], 0.70710678), writes=['cst'])
    tk.op('dve', lambda e: e.memset(E['cst'][:, 2:8], 0.0), writes=['cst'])
    return E


def alloc_psum_banks(nc, n=8):
    return [nc.alloc_psum_tensor(f'pb{i}', [128, 512], F32).ap() for i in range(n)]



def _mk_alloc(nc, scope, prefix):
    def A(n, sh, dt):
        if scope is None:
            return nc.alloc_sbuf_tensor(prefix + n, sh, dt).ap()
        h = scope.enter_context(nc.sbuf_tensor(prefix + n, sh, dt))
        return h.ap()
    return A

def emit_ln(tk, E, nc, xsrc, xn_h, pe_h=None, routerT=None, tag='ln0', scope=None):
    pb = E['pb']
    iden = E['iden_f32']; ones_col = E['ones_col']; ones_row = E['ones_row']
    cst = E['cst']; irev = E['irev']
    g = tag + '_'
    A = _mk_alloc(nc, scope, g)
    xw = [A(f'xw{i}', [128, NDC * WIN], F32) for i in range(2)]
    xnw = [A(f'xnw{i}', [128, NDC * WIN], BF16) for i in range(2)]
    x2b = [A(f'x2b{i}', [128, 128], F32) for i in range(2)]
    stats = A('stats', [128, 8], F32)
    rows3 = A('rows3', [1, 384], F32)
    xnt = [A(f'xnt{i}', [128, 128], F32) for i in range(2)]
    rt = pe_h is not None
    if rt:
        routerT_sb = A('routerT_sb', [128, 64], F32)
        negSr = A('negSr', [1, 8], F32)
        srcol = A('srcol', [8, 1], F32)
        lgt = A('lgt', [128, 8], F32)
        elg = A('elg', [128, 8], F32)
        eqt = A('eqt', [128, 8], F32)
        eqt2 = A('eqt2', [128, 8], F32)
        lgtx = A('lgtx', [128, 4], F32)
        pecols = A('pecols', [128, 2], F32)
        peacc = A('peacc', [2, 4096], F32)
        tk.dma('sp', routerT_sb[:], routerT.rearrange('p a b -> p (a b)'),
               writes=['routerT_sb'])
        for dc in range(NDC):
            tk.mm(pb[5][0:8, 0:1], routerT_sb[:, dc * 8:(dc + 1) * 8], ones_col[:],
                  dc == 0, dc == 7, reads=['routerT_sb', 'ones_col'], writes=['pb5'])
        tk.ts(srcol[:], pb[5][0:8, 0:1], -1.0, None, ALU.mult,
              reads=['pb5'], writes=['srcol'])
        tk.tr(pb[6][0:1, 0:8], srcol[:], iden[0:8, 0:8],
              reads=['srcol', 'iden_f32'], writes=['pb6'])
        tk.copy(negSr[:], pb[6][0:1, 0:8], reads=['pb6'], writes=['negSr'])

    for w in range(NW):
        p = w % 2
        for dc in range(NDC):
            tk.dma('sp', xw[p][:, dc * WIN:(dc + 1) * WIN],
                   xsrc[dc * 128:(dc + 1) * 128, w * WIN:(w + 1) * WIN],
                   writes=[f'xw{p}_{dc}'], label=f'ldx_w{w}_d{dc}')
        for tau in range(NTAU):
            i = w * NTAU + tau
            xt_ap = [xw[p][:, dc * WIN + tau * 128: dc * WIN + (tau + 1) * 128]
                     for dc in range(NDC)]
            for dc in range(NDC):
                xq = x2b[dc % 2]
                tk.act(xq[:], xt_ap[dc], AF.Square,
                       reads=[f'xw{p}_{dc}'], writes=[f'x2b{dc % 2}'])
                tk.mm(pb[0][:, 0:1], xt_ap[dc], ones_col[:], dc == 0, dc == 7,
                      reads=[f'xw{p}_{dc}', 'ones_col'], writes=['pb0'])
                tk.mm(pb[1][:, 0:1], xq[:], ones_col[:], dc == 0, dc == 7,
                      reads=[f'x2b{dc % 2}', 'ones_col'], writes=['pb1'])
            tk.ts(stats[:, 0:1], pb[0][:, 0:1], 1.0 / D, None, ALU.mult,
                  reads=['pb0'], writes=['stats'])
            tk.ts(stats[:, 1:2], pb[1][:, 0:1], 1.0 / D, None, ALU.mult,
                  reads=['pb1'], writes=['stats'])
            tk.tt(stats[:, 3:4], stats[:, 0:1], stats[:, 0:1], ALU.mult,
                  reads=['stats'], writes=['stats'])
            tk.tt(stats[:, 2:3], stats[:, 1:2], stats[:, 3:4], ALU.subtract,
                  reads=['stats'], writes=['stats'])
            tk.act(stats[:, 4:5], stats[:, 2:3], AF.Ln, bias=cst[:, 0:1],
                   reads=['stats', 'cst'], writes=['stats'])
            tk.act(stats[:, 5:6], stats[:, 4:5], AF.Exp, scale=-0.5,
                   reads=['stats'], writes=['stats'])
            tk.stt(stats[:, 6:7], stats[:, 0:1], -1.0, stats[:, 5:6],
                   ALU.mult, ALU.mult, reads=['stats'], writes=['stats'])
            tk.tr(pb[2][0:1, 0:128], stats[:, 5:6], iden[:],
                  reads=['stats', 'iden_f32'], writes=['pb2'])
            tk.tr(pb[2][0:1, 128:256], stats[:, 6:7], iden[:],
                  reads=['stats', 'iden_f32'], writes=['pb2'])
            tk.tr(pb[2][0:1, 256:384], stats[:, 0:1], iden[:],
                  reads=['stats', 'iden_f32'], writes=['pb2'])
            tk.copy(rows3[:], pb[2][0:1, 0:384], reads=['pb2'], writes=['rows3'])
            tk.mm(pb[3][:, 0:128], ones_row[:], rows3[0:1, 0:128], True, True,
                  reads=['ones_row', 'rows3'], writes=['pb3'])
            tk.mm(pb[3][:, 128:256], ones_row[:], rows3[0:1, 128:256], True, True,
                  reads=['ones_row', 'rows3'], writes=['pb3'])
            for dc in range(NDC):
                xt = xnt[dc % 2]
                tk.tt(xt[:], xt_ap[dc], pb[3][:, 0:128], ALU.mult,
                      reads=[f'xw{p}_{dc}', 'pb3'], writes=[f'xnt{dc % 2}'])
                tk.tt(xnw[p][:, dc * WIN + tau * 128: dc * WIN + (tau + 1) * 128],
                      xt[:], pb[3][:, 128:256], ALU.add,
                      reads=[f'xnt{dc % 2}', 'pb3'], writes=[f'xnw{p}_{dc}_{tau}'])
            if rt:
                for dc in range(NDC):
                    tk.mm(pb[4][:, 0:8], xt_ap[dc], routerT_sb[:, dc * 8:(dc + 1) * 8],
                          dc == 0, False, reads=[f'xw{p}_{dc}', 'routerT_sb'],
                          writes=['pb4'])
                tk.mm(pb[4][:, 0:8], rows3[0:1, 256:384], negSr[:], False, True,
                      reads=['rows3', 'negSr'], writes=['pb4'])
                tk.ts(lgt[:], pb[4][:, 0:8], stats[:, 5:6], None, ALU.mult,
                      reads=['pb4', 'stats'], writes=['lgt'])
                tk.red(lgtx[:, 0:1], lgt[:], ALU.max, negate=True,
                       reads=['lgt'], writes=['lgtx'])
                tk.act(elg[:], lgt[:], AF.Exp, bias=lgtx[:, 0:1],
                       reads=['lgt', 'lgtx'], writes=['elg'])
                tk.red(lgtx[:, 1:2], elg[:], ALU.add, reads=['elg'], writes=['lgtx'])
                tk.recip(pecols[:, 0:1], lgtx[:, 1:2], reads=['lgtx'], writes=['pecols'])
                tk.ts(eqt[:], lgt[:], lgtx[:, 0:1], 0.0, ALU.add, op1=ALU.is_ge,
                      reads=['lgt', 'lgtx'], writes=['eqt'])
                tk.tt(eqt2[:], eqt[:], irev[:], ALU.mult,
                      reads=['eqt', 'irev'], writes=['eqt2'])
                tk.red(lgtx[:, 2:3], eqt2[:], ALU.max, reads=['eqt2'], writes=['lgtx'])
                tk.ts(pecols[:, 1:2], lgtx[:, 2:3], -1.0, 7.0, ALU.mult, op1=ALU.add,
                      reads=['lgtx'], writes=['pecols'])
                tk.tr(pb[6][0:2, 128:256], pecols[:], iden[:],
                      reads=['pecols', 'iden_f32'], writes=['pb6'])
                tk.copy(peacc[:, i * 128:(i + 1) * 128], pb[6][0:2, 128:256],
                        reads=['pb6'], writes=[f'peacc_{i}'])
        for dc in range(NDC):
            tk.dma('sp', xn_h[dc * 128:(dc + 1) * 128, w * WIN:(w + 1) * WIN],
                   xnw[p][:, dc * WIN:(dc + 1) * WIN],
                   reads=[f'xnw{p}_{dc}_{t}' for t in range(NTAU)],
                   label=f'stxn_w{w}_d{dc}')
    if rt:
        tk.dma('sp', pe_h, peacc[:],
               reads=[f'peacc_{i}' for i in range(NW * NTAU)])


CAP = 5120

def emit_routing(tk, E, nc, allpe, csel, cselE, kmask_h, scope=None):
    pb = E['pb']
    rmat, rmatT, idxmat = E['rmat'], E['rmatT'], E['idxmat']
    g = 'rt_'
    A = _mk_alloc(nc, scope, g)
    P16 = A('P16', [16, 2048], F32)
    E16 = A('E16', [16, 2048], F32)
    Mbig = A('M', [128, 2048], F32)
    eqm = A('eqm', [128, 2048], F32)
    big1 = A('big1', [128, 2048], F32)
    big2 = A('big2', [128, 2048], F32)
    Kbig = A('K', [128, 2048], F32)
    tmp16 = A('tmp16', [16, 2048], F32)
    tmp16b = A('tmp16b', [16, 2048], F32)
    part = A('part', [128, 1], F32)
    sc8 = A('sc8', [8, 12], F32)
    colb = A('colb', [128, 4], F32)
    cselb = A('csel', [128, 2], F32)
    cselEb = A('cselE', [128, 16], F32)
    k0t = A('k0t', [2, 2048], F32)
    f0 = A('f0', [2, 2048], F32)

    tk.dma('sp', cselb[:], csel, writes=['cselb'])
    tk.dma('sp', cselEb[:], cselE, writes=['cselEb'])
    tk.dma('sp', P16[0:16:2, :], allpe[0:16:2, 0:2048], writes=['P16'])
    tk.dma('sp', P16[1:16:2, :], allpe[0:16:2, 2048:4096], writes=['P16'])
    tk.dma('sp', E16[0:16:2, :], allpe[1:16:2, 0:2048], writes=['E16'])
    tk.dma('sp', E16[1:16:2, :], allpe[1:16:2, 2048:4096], writes=['E16'])
    for e in range(8):
        tk.ts(tmp16[:], E16[:], float(e), None, ALU.is_equal,
              reads=['E16'], writes=['tmp16'])
        tk.stt(tmp16b[:], P16[:], 1.0, tmp16[:], ALU.add, ALU.mult,
               reads=['P16', 'tmp16'], writes=['tmp16b'])
        tk.ts(tmp16b[:], tmp16b[:], -1.0, None, ALU.add,
              reads=['tmp16b'], writes=['tmp16b'])
        tk.dma('sp', Mbig[e * 16:(e + 1) * 16, :], tmp16b[:],
               reads=['tmp16b'], writes=['Mbig'], label=f'mcp{e}')
    def collapse(dst, rkey):
        tk.mm(pb[5][0:8, 0:1], rmatT[:], part[:], True, True,
              reads=['rmatT', 'part'], writes=['pb5'])
        tk.copy(dst, pb[5][0:8, 0:1], reads=['pb5'], writes=['sc8'])
    def bcast8(src, dstcol):
        tk.mm(pb[6][:, 0:1], rmat[:], src, True, True,
              reads=['rmat', 'sc8'], writes=['pb6'])
        tk.copy(dstcol, pb[6][:, 0:1], reads=['pb6'], writes=['colb'])
    tk.ts(big1[:], Mbig[:], -0.5, None, ALU.is_ge, op1=ALU.add, accum=part[:],
          reads=['Mbig'], writes=['big1', 'part'])
    collapse(sc8[:, 6:7], 'tot')
    tk.memset(sc8[:, 0:1], 0.0, writes=['sc8'])
    tk.memset(sc8[:, 1:2], 1.5, writes=['sc8'])
    for it in range(40):
        tk.tt(sc8[:, 5:6], sc8[:, 0:1], sc8[:, 1:2], ALU.add,
              reads=['sc8'], writes=['sc8'])
        tk.ts(sc8[:, 2:3], sc8[:, 5:6], 0.5, None, ALU.mult,
              reads=['sc8'], writes=['sc8'])
        bcast8(sc8[:, 2:3], colb[:, 0:1])
        tk.ts(big1[:], Mbig[:], colb[:, 0:1], None, ALU.is_ge, op1=ALU.add,
              accum=part[:], reads=['Mbig', 'colb'], writes=['big1', 'part'])
        collapse(sc8[:, 3:4], 'cnt')
        tk.ts(sc8[:, 4:5], sc8[:, 3:4], -float(CAP), 0.0, ALU.add, op1=ALU.is_ge,
              reads=['sc8'], writes=['sc8'])
        tk.tt(sc8[:, 5:6], sc8[:, 2:3], sc8[:, 0:1], ALU.subtract,
              reads=['sc8'], writes=['sc8'])
        tk.tt(sc8[:, 5:6], sc8[:, 5:6], sc8[:, 4:5], ALU.mult,
              reads=['sc8'], writes=['sc8'])
        tk.tt(sc8[:, 0:1], sc8[:, 0:1], sc8[:, 5:6], ALU.add,
              reads=['sc8'], writes=['sc8'])
        tk.tt(sc8[:, 5:6], sc8[:, 1:2], sc8[:, 2:3], ALU.subtract,
              reads=['sc8'], writes=['sc8'])
        tk.tt(sc8[:, 5:6], sc8[:, 5:6], sc8[:, 4:5], ALU.mult,
              reads=['sc8'], writes=['sc8'])
        tk.tt(sc8[:, 1:2], sc8[:, 2:3], sc8[:, 5:6], ALU.add,
              reads=['sc8'], writes=['sc8'])
    bcast8(sc8[:, 0:1], colb[:, 0:1])
    tk.ts(big1[:], Mbig[:], colb[:, 0:1], None, ALU.is_gt, op1=ALU.add,
          accum=part[:], reads=['Mbig', 'colb'], writes=['big1', 'part'])
    collapse(sc8[:, 7:8], 'ngt')
    tk.ts(sc8[:, 8:9], sc8[:, 7:8], -1.0, float(CAP), ALU.mult, op1=ALU.add,
          reads=['sc8'], writes=['sc8'])
    tk.ts(eqm[:], Mbig[:], colb[:, 0:1], None, ALU.is_equal,
          reads=['Mbig', 'colb'], writes=['eqm'])
    tk.memset(sc8[:, 9:10], -1.0, writes=['sc8'])
    tk.memset(sc8[:, 10:11], 32768.0, writes=['sc8'])
    for it in range(18):
        tk.tt(sc8[:, 5:6], sc8[:, 9:10], sc8[:, 10:11], ALU.add,
              reads=['sc8'], writes=['sc8'])
        tk.ts(sc8[:, 2:3], sc8[:, 5:6], 0.5, None, ALU.mult,
              reads=['sc8'], writes=['sc8'])
        bcast8(sc8[:, 2:3], colb[:, 1:2])
        tk.ts(big1[:], idxmat[:], colb[:, 1:2], None, ALU.is_le,
              reads=['idxmat', 'colb'], writes=['big1'])
        tk.tt(big2[:], big1[:], eqm[:], ALU.mult,
              reads=['big1', 'eqm'], writes=['big2'])
        tk.ts(big2[:], big2[:], 1.0, None, ALU.mult, op1=ALU.add, accum=part[:],
              reads=['big2'], writes=['big2', 'part'])
        collapse(sc8[:, 3:4], 'icnt')
        tk.ts(sc8[:, 4:5], sc8[:, 3:4], sc8[:, 8:9], 0.0, ALU.subtract, op1=ALU.is_le,
              reads=['sc8'], writes=['sc8'])
        tk.tt(sc8[:, 5:6], sc8[:, 2:3], sc8[:, 9:10], ALU.subtract,
              reads=['sc8'], writes=['sc8'])
        tk.tt(sc8[:, 5:6], sc8[:, 5:6], sc8[:, 4:5], ALU.mult,
              reads=['sc8'], writes=['sc8'])
        tk.tt(sc8[:, 9:10], sc8[:, 9:10], sc8[:, 5:6], ALU.add,
              reads=['sc8'], writes=['sc8'])
        tk.tt(sc8[:, 5:6], sc8[:, 10:11], sc8[:, 2:3], ALU.subtract,
              reads=['sc8'], writes=['sc8'])
        tk.tt(sc8[:, 5:6], sc8[:, 5:6], sc8[:, 4:5], ALU.mult,
              reads=['sc8'], writes=['sc8'])
        tk.tt(sc8[:, 10:11], sc8[:, 2:3], sc8[:, 5:6], ALU.add,
              reads=['sc8'], writes=['sc8'])
    bcast8(sc8[:, 9:10], colb[:, 1:2])
    tk.ts(Kbig[:], Mbig[:], colb[:, 0:1], None, ALU.is_gt,
          reads=['Mbig', 'colb'], writes=['Kbig'])
    tk.ts(big1[:], idxmat[:], colb[:, 1:2], None, ALU.is_le,
          reads=['idxmat', 'colb'], writes=['big1'])
    tk.tt(big1[:], big1[:], eqm[:], ALU.mult,
          reads=['big1', 'eqm'], writes=['big1'])
    tk.tt(Kbig[:], Kbig[:], big1[:], ALU.add,
          reads=['Kbig', 'big1'], writes=['Kbig'])
    for q in range(4):
        tk.mm(pb[7][0:2, 0:512], cselb[:], Kbig[:, q * 512:(q + 1) * 512],
              True, True, reads=['cselb', 'Kbig'], writes=['pb7'])
        tk.ts(f0[:, q * 512:(q + 1) * 512], pb[7][0:2, 0:512], -1.0, 1.0,
              ALU.mult, op1=ALU.add, reads=['pb7'], writes=['f0'])
    for e in range(8):
        for q in range(4):
            tk.mm(pb[7][0:2, 0:512], cselEb[:, 2 * e:2 * e + 2],
                  Kbig[:, q * 512:(q + 1) * 512], True, True,
                  reads=['cselEb', 'Kbig'], writes=['pb7'])
            tk.copy(k0t[:, q * 512:(q + 1) * 512], pb[7][0:2, 0:512],
                    reads=['pb7'], writes=['k0t'], eng='act')
        if e == 0:
            tk.tt(k0t[:], k0t[:], f0[:], ALU.add, reads=['k0t', 'f0'],
                  writes=['k0t'])
        tk.dma('sp', kmask_h[e:e + 1, :], k0t[:], reads=['k0t'], label=f'stk{e}')


SQRT2 = float(np.sqrt(2.0))

def emit_attn0(tk, E, nc, xn_h, s_h, W, scope=None):
    pb = E['pb']
    iden_bf = E['iden_bf']; cst = E['cst']; mtri = E['mtri_bf']
    selA, selB = E['selA'], E['selB']
    g = 'a0_'
    A = _mk_alloc(nc, scope, g)
    wq = A('wq', [128, 4096], BF16); wk = A('wk', [128, 4096], BF16)
    wkT = A('wkT', [128, 2048], BF16)
    wvT = A('wvT', [128, 8192], BF16)
    wout = A('wout', [128, 8192], BF16)

    def loadw(sb, nm, key):
        tk.dma('pool', sb[:], W[nm].rearrange('p a b -> p (a b)') if W[nm].ndim == 3
               else W[nm].rearrange('p a b c -> p (a b c)'), writes=[key])
    loadw(wkT, 'wkeT', 'wkT')
    loadw(wvT, 'wveT', 'wvT')
    loadw(wq, 'wq', 'wq')
    loadw(wout, 'wout', 'wout')
    SAf = A('SAf', [128, 1040], F32); SBf = A('SBf', [128, 1040], F32)
    SCf = A('SCf', [17, 1040], F32)
    SAb = A('SAb', [128, 1040], BF16); SBb = A('SBb', [128, 1040], BF16)
    SCb = A('SCb', [17, 1040], BF16)
    for tname, t in [('SAf', SAf), ('SBf', SBf), ('SCf', SCf)]:
        tk.memset(t[:], 0.0, writes=[f'{tname}_{h}' for h in range(16)])
    xnw = [A(f'xnw{i}', [128, NDC * WIN], BF16) for i in range(2)]
    qsw = [A(f'qsw{i}', [64, 8 * WIN], BF16) for i in range(2)]
    ksw = [A(f'ksw{i}', [64, 8 * WIN], BF16) for i in range(2)]
    kstm = [A(f'kstm{i}', [128, 4 * 256], BF16) for i in range(2)]
    v1sb = [A(f'v1sb{i}', [128, 4 * 1040], BF16) for i in range(2)]
    kftm = [A(f'kftm{i}', [128, 4368], BF16) for i in range(2)]
    qfA = [A(f'qfA{i}', [128, 128], BF16) for i in range(2)]
    qfB = [A(f'qfB{i}', [128, 128], BF16) for i in range(2)]
    qfC = [A(f'qfC{i}', [17, 128], BF16) for i in range(2)]
    p1sb = [A(f'p1sb{i}', [128, 128], BF16) for i in range(2)]
    asq = [A(f'asq{i}', [128, 128], F32) for i in range(2)]
    atb = [A(f'atb{i}', [128, 128], BF16) for i in range(2)]
    zc = A('zc', [128, 2], F32)
    otm = [A(f'otm{i}', [128, 1024], BF16) for i in range(2)]
    off = A('off', [128, 8 * WIN], BF16)
    sfw = [A(f'sfw{i}', [128, NDC * WIN], BF16) for i in range(2)]
    ones_h = E['ones8k_h']
    for i in range(2):
        for r in (16, 48):
            tk.dma('pool', qsw[i][r:r + 1, :], ones_h[0:8 * WIN],
                   writes=[f'qsw{i}_ones'])
        tk.memset(v1sb[i].rearrange('p (t h c) -> p t h c', t=4, h=16)[:, :, :, 64:65],
                  1.0, writes=[f'v1sb{i}_ones'], eng='pool')
        tk.memset(kftm[i].rearrange('p (h c) -> p h c', h=16)[:, :, 0:1],
                  1.0, writes=[f'kftm{i}_ones'], eng='pool')

    def proj_qk(wsb, wkey, dst, dkey, xbuf, xkeys):
        for p8 in range(8):
            for dc in range(NDC):
                tk.mm(pb[0][0:64, 0:512],
                      wsb[:, (dc * 8 + p8) * 64:(dc * 8 + p8) * 64 + 64],
                      xbuf[:, dc * WIN:(dc + 1) * WIN], dc == 0, dc == 7,
                      reads=[wkey, xkeys[dc]], writes=['pb0'])
            for rb in (0, 32):
                tk.copy(dst[rb:rb + 16, p8 * WIN:(p8 + 1) * WIN],
                        pb[0][rb:rb + 16, 0:512], reads=['pb0'],
                        writes=[f'{dkey}_{p8}'], eng='act')

    def proj_ktm2(wsb, wkey, dstbuf, dkey, xbuf, xkeys):
        for tau in range(NTAU):
            for dc in range(NDC):
                tk.mm(pb[1][:, 0:256],
                      xbuf[:, dc * WIN + tau * 128:dc * WIN + (tau + 1) * 128],
                      wsb[:, dc * 256:(dc + 1) * 256],
                      dc == 0, dc == 7, reads=[wkey, xkeys[dc]], writes=['pb1'])
            tk.copy(dstbuf[:, tau * 256:(tau + 1) * 256], pb[1][:, 0:256],
                    reads=['pb1'], writes=[f'{dkey}_{tau}'], eng='act')

    def proj_vtm(wsb, wkey, dstbuf, dkey, xbuf, xkeys):
        for tau in range(NTAU):
            for half in range(2):
                bank = pb[2] if half == 0 else pb[3]
                bk = 'pb2' if half == 0 else 'pb3'
                for dc in range(NDC):
                    tk.mm(bank[:, 0:512],
                          xbuf[:, dc * WIN + tau * 128:dc * WIN + (tau + 1) * 128],
                          wsb[:, dc * 1024 + half * 512:dc * 1024 + half * 512 + 512],
                          dc == 0, dc == 7, reads=[wkey, xkeys[dc]], writes=[bk])
                dst = dstbuf.rearrange('p (t h c) -> p t h c', t=4, h=16)[
                    :, tau, half * 8:(half + 1) * 8, 0:64]
                tk.copy(dst, bank[:, 0:512].rearrange('p (h c) -> p h c', h=8),
                        reads=[bk], writes=[f'{dkey}_{tau}'], eng='act')

    def build_kftm(buf, bkey, ksbuf, kskey, tau):
        kv = ksbuf[:, tau * 256:(tau + 1) * 256].rearrange('p (h f) -> p h f', h=16)
        ov = buf.rearrange('p (h c) -> p h c', h=16)
        tk.ts(ov[:, :, 1:17], kv, SQRT2, None, ALU.mult,
              reads=[kskey], writes=[bkey], eng='pool')
        for a in range(16):
            in1 = kv[:, :, a:a + 1].to_broadcast([128, 16, 16])
            tk.tt(ov[:, :, 17 + a * 16:17 + (a + 1) * 16], kv, in1, ALU.mult,
                  reads=[kskey], writes=[bkey], eng='pool')

    def state_update(kbuf, kkey, vbuf, vkey, tau, tag):
        for h in range(16):
            kc = kbuf[:, h * 273:(h + 1) * 273]
            v1 = vbuf.rearrange('p (t h c) -> p t h c', t=4, h=16)[
                :, tau, h, :].rearrange('p c -> p c')
            tk.mm(pb[6][:, 0:65], kc[:, 0:128], v1, True, True,
                  reads=[kkey, kkey + '_ones', vkey, vkey.rsplit('_', 1)[0] + '_ones'],
                  writes=['pb6'])
            tk.tt(SAf[:, h * 65:(h + 1) * 65], SAf[:, h * 65:(h + 1) * 65],
                  pb[6][:, 0:65], ALU.add, reads=['pb6', f'SAf_{h}'],
                  writes=[f'SAf_{h}'])
            tk.copy(SAb[:, h * 65:(h + 1) * 65], SAf[:, h * 65:(h + 1) * 65],
                    reads=[f'SAf_{h}'], writes=[f'SAb_{h}'], eng='pool')
            tk.mm(pb[6][:, 65:130], kc[:, 128:256], v1, True, True,
                  reads=[kkey, vkey], writes=['pb6'])
            tk.tt(SBf[:, h * 65:(h + 1) * 65], SBf[:, h * 65:(h + 1) * 65],
                  pb[6][:, 65:130], ALU.add, reads=['pb6', f'SBf_{h}'],
                  writes=[f'SBf_{h}'])
            tk.copy(SBb[:, h * 65:(h + 1) * 65], SBf[:, h * 65:(h + 1) * 65],
                    reads=[f'SBf_{h}'], writes=[f'SBb_{h}'], eng='pool')
            tk.mm(pb[6][0:17, 130:195], kc[:, 256:273], v1, True, True,
                  reads=[kkey, vkey], writes=['pb6'])
            tk.tt(SCf[:, h * 65:(h + 1) * 65], SCf[:, h * 65:(h + 1) * 65],
                  pb[6][0:17, 130:195], ALU.add, reads=['pb6', f'SCf_{h}'],
                  writes=[f'SCf_{h}'])
            tk.copy(SCb[:, h * 65:(h + 1) * 65], SCf[:, h * 65:(h + 1) * 65],
                    reads=[f'SCf_{h}'], writes=[f'SCb_{h}'], eng='pool')

    def load_xn(p, w):
        for dc in range(NDC):
            tk.dma('sp', xnw[p][:, dc * WIN:(dc + 1) * WIN],
                   xn_h[dc * 128:(dc + 1) * 128, w * WIN:(w + 1) * WIN],
                   writes=[f'xnw{p}_{dc}'], label=f'ldxn_w{w}_d{dc}')

    load_xn(0, 0)
    xkeys0 = [f'xnw0_{dc}' for dc in range(NDC)]
    proj_ktm2(wkT, 'wkT', kstm[1], 'kstm1', xnw[0], xkeys0)
    proj_vtm(wvT, 'wvT', v1sb[1], 'v1sb1', xnw[0], xkeys0)
    for tau in range(NTAU):
        build_kftm(kftm[1], 'kftm1', kstm[1], f'kstm1_{tau}', tau)
        state_update(kftm[1], 'kftm1', v1sb[1], f'v1sb1_{tau}', tau, 'enc')

    loadw(wk, 'wk', 'wk')
    loadw(wkT, 'wkT', 'wkT')
    loadw(wvT, 'wvT', 'wvT')
    for w in range(NW):
        p = w % 2
        if w > 0:
            load_xn(p, w)
        xkeys = [f'xnw{p}_{dc}' for dc in range(NDC)]
        proj_qk(wq, 'wq', qsw[p], f'qsw{p}', xnw[p], xkeys)
        proj_qk(wk, 'wk', ksw[p], f'ksw{p}', xnw[p], xkeys)
        proj_ktm2(wkT, 'wkT', kstm[p], f'kstm{p}', xnw[p], xkeys)
        proj_vtm(wvT, 'wvT', v1sb[p], f'v1sb{p}', xnw[p], xkeys)
        for tau in range(NTAU):
            build_kftm(kftm[p], f'kftm{p}', kstm[p], f'kstm{p}_{tau}', tau)
            for h in range(16):
                p8, rb = h // 2, 32 * (h % 2)
                hp = h % 2
                qsl = qsw[p][rb:rb + 16, p8 * WIN + tau * 128:p8 * WIN + (tau + 1) * 128]
                ksl = ksw[p][rb:rb + 16, p8 * WIN + tau * 128:p8 * WIN + (tau + 1) * 128]
                q17 = qsw[p][rb:rb + 17, p8 * WIN + tau * 128:p8 * WIN + (tau + 1) * 128]
                tk.mm(pb[4][:, 0:128], ksl, qsl, True, True,
                      reads=[f'ksw{p}_{p8}', f'qsw{p}_{p8}'], writes=['pb4'])
                tk.act(asq[hp][:], pb[4][:, 0:128], AF.Square, bias=cst[:, 1:2],
                       reads=['pb4', 'cst'], writes=[f'asq{hp}'])
                tk.stt(atb[hp][:], asq[hp][:], 0.5, mtri[:], ALU.add, ALU.mult,
                       reads=[f'asq{hp}', 'mtri_bf'], writes=[f'atb{hp}'])
                for (cn, qf, c0, c1, m) in [('A', qfA[hp], 0, 128, 128),
                                            ('B', qfB[hp], 128, 256, 128),
                                            ('C', qfC[hp], 256, 273, 17)]:
                    tk.mm(pb[2][0:m, 0:128], selA[rb:rb + 17, c0:c1], q17,
                          True, True, reads=['selA', f'qsw{p}_{p8}', f'qsw{p}_ones'],
                          writes=['pb2'])
                    tk.mm(pb[3][0:m, 0:128], selB[rb:rb + 17, c0:c1], q17,
                          True, True, reads=['selB', f'qsw{p}_{p8}', f'qsw{p}_ones'],
                          writes=['pb3'])
                    tk.copy(p1sb[hp][0:m, :], pb[2][0:m, 0:128],
                            reads=['pb2'], writes=[f'p1sb{hp}'], eng='act')
                    tk.tt(qf[0:m, :], pb[3][0:m, 0:128], p1sb[hp][0:m, :],
                          ALU.mult, reads=['pb3', f'p1sb{hp}'], writes=[f'qf{cn}{hp}'])
                v1 = v1sb[p].rearrange('p (t h c) -> p t h c', t=4, h=16)[:, tau, h, :]
                tk.mm(pb[5][:, 0:65], qfA[hp][:], SAb[:, h * 65:(h + 1) * 65],
                      True, False, reads=[f'qfA{hp}', f'SAb_{h}'], writes=['pb5'])
                tk.mm(pb[5][:, 0:65], qfB[hp][:], SBb[:, h * 65:(h + 1) * 65],
                      False, False, reads=[f'qfB{hp}', f'SBb_{h}'], writes=['pb5'])
                tk.mm(pb[5][:, 0:65], qfC[hp][:], SCb[:, h * 65:(h + 1) * 65],
                      False, False, reads=[f'qfC{hp}', f'SCb_{h}'], writes=['pb5'])
                tk.mm(pb[5][:, 0:65], atb[hp][:], v1, False, True,
                      reads=[f'atb{hp}', f'v1sb{p}_{tau}', f'v1sb{p}_ones'],
                      writes=['pb5'])
                tk.ts(zc[:, 0:1], pb[5][:, 64:65], 1e-12, None, ALU.add,
                      reads=['pb5'], writes=['zc'])
                tk.recip(zc[:, 1:2], zc[:, 0:1], reads=['zc'], writes=['zc'])
                tk.ts(otm[tau % 2][:, h * 64:(h + 1) * 64], pb[5][:, 0:64],
                      zc[:, 1:2], None, ALU.mult, reads=['pb5', 'zc'],
                      writes=[f'otm{tau % 2}_{h}'])
                kc = kftm[p][:, h * 273:(h + 1) * 273]
                tk.mm(pb[6][:, 0:65], kc[:, 0:128], v1, True, True,
                      reads=[f'kftm{p}', f'kftm{p}_ones', f'v1sb{p}_{tau}',
                             f'v1sb{p}_ones'], writes=['pb6'])
                tk.tt(SAf[:, h * 65:(h + 1) * 65], SAf[:, h * 65:(h + 1) * 65],
                      pb[6][:, 0:65], ALU.add, reads=['pb6', f'SAf_{h}'],
                      writes=[f'SAf_{h}'])
                tk.copy(SAb[:, h * 65:(h + 1) * 65], SAf[:, h * 65:(h + 1) * 65],
                        reads=[f'SAf_{h}'], writes=[f'SAb_{h}'], eng='pool')
                tk.mm(pb[6][:, 65:130], kc[:, 128:256], v1, True, True,
                      reads=[f'kftm{p}', f'v1sb{p}_{tau}'], writes=['pb6'])
                tk.tt(SBf[:, h * 65:(h + 1) * 65], SBf[:, h * 65:(h + 1) * 65],
                      pb[6][:, 65:130], ALU.add, reads=['pb6', f'SBf_{h}'],
                      writes=[f'SBf_{h}'])
                tk.copy(SBb[:, h * 65:(h + 1) * 65], SBf[:, h * 65:(h + 1) * 65],
                        reads=[f'SBf_{h}'], writes=[f'SBb_{h}'], eng='pool')
                tk.mm(pb[6][0:17, 130:195], kc[:, 256:273], v1, True, True,
                      reads=[f'kftm{p}', f'v1sb{p}_{tau}'], writes=['pb6'])
                tk.tt(SCf[:, h * 65:(h + 1) * 65], SCf[:, h * 65:(h + 1) * 65],
                      pb[6][0:17, 130:195], ALU.add, reads=['pb6', f'SCf_{h}'],
                      writes=[f'SCf_{h}'])
                tk.copy(SCb[:, h * 65:(h + 1) * 65], SCf[:, h * 65:(h + 1) * 65],
                        reads=[f'SCf_{h}'], writes=[f'SCb_{h}'], eng='pool')
            for hc in range(8):
                tk.tr(pb[7].bitcast(BF16)[:, 0:128],
                      otm[tau % 2][:, hc * 128:(hc + 1) * 128], iden_bf[:],
                      reads=[f'otm{tau % 2}_{hh}' for hh in range(2 * hc, 2 * hc + 2)]
                      + ['iden_bf'], writes=['pb7'])
                tk.copy(off[:, hc * WIN + tau * 128:hc * WIN + (tau + 1) * 128],
                        pb[7].bitcast(BF16)[:, 0:128], reads=['pb7'],
                        writes=[f'off_{hc}'], eng='act')
        for dm in range(NDC):
            for hc in range(NDC):
                tk.mm(pb[0][:, 0:512], wout[:, hc * 1024 + dm * 128:hc * 1024 + dm * 128 + 128],
                      off[:, hc * WIN:(hc + 1) * WIN], hc == 0, hc == 7,
                      reads=['wout', f'off_{hc}'], writes=['pb0'])
            tk.copy(sfw[p][:, dm * WIN:(dm + 1) * WIN], pb[0][:, 0:512],
                    reads=['pb0'], writes=[f'sfw{p}_{dm}'], eng='act')
        for dm in range(NDC):
            tk.dma('sp', s_h[dm * 128:(dm + 1) * 128, w * WIN:(w + 1) * WIN],
                   sfw[p][:, dm * WIN:(dm + 1) * WIN], reads=[f'sfw{p}_{dm}'],
                   label=f'sts_w{w}_d{dm}')


def emit_moe(tk, E, nc, xn_h, kmask_h, ge_h, W, scope=None):
    pb = E['pb']
    ones_row_bf = E['ones_row_bf']
    g = 'mo_'
    A = _mk_alloc(nc, scope, g)
    w1sb = A('w1sb', [128, 8 * 4096], BF16)
    w2sb = A('w2sb', [128, 32 * 1024], BF16)
    b1sb = A('b1sb', [128, 32], F32)
    b2sb = A('b2sb', [128, 8], F32)
    xnw = [A(f'xnw{i}', [128, NDC * WIN], BF16) for i in range(2)]
    hsb = A('hsb', [128, 32 * WIN], BF16)
    ysb = [A(f'ysb{i}', [128, WIN], F32) for i in range(2)]
    ymb = [A(f'ymb{i}', [128, WIN], BF16) for i in range(4)]
    mrow = A('mrow', [1, WIN], BF16)
    mbc = A('mbc', [128, WIN], BF16)
    HB = [0, 1, 6, 7]
    YB = [2, 3, 4, 5]

    def load_weights(e):
        for dc in range(NDC):
            tk.dma('pool', w1sb[:, dc * 4096:(dc + 1) * 4096],
                   W['w1'][e, :, dc, :], writes=[f'w1_{dc}'], label=f'ldw1_{e}_{dc}')
    def load_weights2(e):
        for q in range(8):
            tk.dma('pool', w2sb[:, q * 4096:(q + 1) * 4096],
                   W['w2'][e, :, 4 * q:4 * (q + 1), :].rearrange('p a b -> p (a b)'),
                   writes=[f'w2_{q}'], label=f'ldw2_{e}_{q}')
        tk.dma('sp', b1sb[:], W['b1'][e], writes=['b1sb'])
        tk.dma('sp', b2sb[:], W['b2'][e], writes=['b2sb'])

    load_weights(0)
    load_weights2(0)
    for e in range(8):
        for w in range(NW):
            p = w % 2
            for dc in range(NDC):
                tk.dma('sp', xnw[p][:, dc * WIN:(dc + 1) * WIN],
                       xn_h[dc * 128:(dc + 1) * 128, w * WIN:(w + 1) * WIN],
                       writes=[f'xnw{p}_{dc}'], label=f'mxn_{e}_{w}_{dc}')
            tk.dma('pool', mrow[:], kmask_h[e:e + 1, w * WIN:(w + 1) * WIN],
                   writes=['mrow'])
            tk.mm(pb[6][:, 0:512], ones_row_bf[:], mrow[:], True, True,
                  reads=['ones_row_bf', 'mrow'], writes=['pb6'])
            tk.copy(mbc[:], pb[6][:, 0:512], reads=['pb6'],
                    writes=['mbc'], eng='act')
            for fc in range(32):
                bi = HB[fc % 4]
                bank, bk = pb[bi], f'pb{bi}'
                for dc in range(NDC):
                    tk.mm(bank[:, 0:512],
                          w1sb[:, dc * 4096 + fc * 128:dc * 4096 + (fc + 1) * 128],
                          xnw[p][:, dc * WIN:(dc + 1) * WIN], dc == 0, dc == 7,
                          reads=[f'w1_{dc}', f'xnw{p}_{dc}'], writes=[bk])
                tk.act(hsb[:, fc * WIN:(fc + 1) * WIN], bank[:, 0:512], AF.Relu,
                       bias=b1sb[:, fc:fc + 1], reads=[bk, 'b1sb'],
                       writes=[f'h_{fc}'])
            if w == NW - 1 and e < 7:
                load_weights(e + 1)
            for dc in range(NDC):
                bi = YB[dc % 4]
                bank, bk = pb[bi], f'pb{bi}'
                for fc in range(32):
                    tk.mm(bank[:, 0:512],
                          w2sb[:, fc * 1024 + dc * 128:fc * 1024 + (dc + 1) * 128],
                          hsb[:, fc * WIN:(fc + 1) * WIN], fc == 0, fc == 31,
                          reads=[f'w2_{fc // 4}', f'h_{fc}'], writes=[bk])
                yp = dc % 2
                tk.ts(ysb[yp][:], bank[:, 0:512], b2sb[:, dc:dc + 1], None, ALU.add,
                      reads=[bk, 'b2sb'], writes=[f'ysb{yp}'])
                y4 = dc % 4
                tk.tt(ymb[y4][:], ysb[yp][:], mbc[:], ALU.mult,
                      reads=[f'ysb{yp}', 'mbc'], writes=[f'ymb{y4}'])
                tk.dma('sp', ge_h[e, dc * 128:(dc + 1) * 128, w * WIN:(w + 1) * WIN],
                       ymb[y4][:], reads=[f'ymb{y4}'], label=f'stg_{e}_{w}_{dc}')
        if e < 7:
            load_weights2(e + 1)


def emit_combine(tk, E, nc, xsrc, s_h, g_h, xdst, scope=None, tag='cb', ge=False):
    g = tag + '_'
    A = _mk_alloc(nc, scope, g)
    xw = [A(f'xw{i}', [128, NDC * WIN], F32) for i in range(2)]
    sw = [A(f'sw{i}', [128, NDC * WIN], BF16) for i in range(2)]
    NG = 8 if ge else 1
    gw = [A(f'gw{i}', [128, NG * WIN], BF16) for i in range(8)]
    gs = [A(f'gs{i}', [128, WIN], F32) for i in range(8)]
    sg = [A(f'sg{i}', [128, WIN], F32) for i in range(2)]
    gsl = [A(f'gsl{i}', [128, WIN], F32) for i in range(2)]
    xo = [A(f'xo{i}', [128, NDC * WIN], F32) for i in range(2)]
    for w in range(NW):
        p = w % 2
        for dc in range(NDC):
            sl = (slice(dc * 128, (dc + 1) * 128), slice(w * WIN, (w + 1) * WIN))
            tk.dma('act', xw[p][:, dc * WIN:(dc + 1) * WIN], xsrc[sl[0], sl[1]],
                   writes=[f'xw{p}_{dc}'])
            tk.dma('act', sw[p][:, dc * WIN:(dc + 1) * WIN], s_h[sl[0], sl[1]],
                   writes=[f'sw{p}_{dc}'])
            q4 = dc
            if ge:
                src = g_h[:, sl[0], sl[1]].transpose([1, 0, 2])
                tk.dma('sp', gw[q4].rearrange('p (e j) -> p e j', e=8), src,
                       writes=[f'gw{q4}_0'])
            else:
                tk.dma('sp', gw[q4][:, 0:WIN], g_h[sl[0], sl[1]],
                       writes=[f'gw{q4}_0'])
        for dc in range(NDC):
            q = dc % 2
            q4 = dc
            if ge:
                v = gw[q4].rearrange('p (e j) -> p j e', e=8)
                tk.red(gs[q4][:].unsqueeze(2), v, ALU.add,
                       reads=[f'gw{q4}_0'], writes=[f'gs{q4}'])
                gsrc, gk = gs[q4][:], f'gs{q4}'
            else:
                gsrc, gk = gw[q4][:, 0:WIN], f'gw{q4}_0'
            tk.act(gsl[q][:], gsrc, AF.Silu, reads=[gk], writes=[f'gsl{q}'])
            tk.tt(sg[q][:], sw[p][:, dc * WIN:(dc + 1) * WIN], gsl[q][:], ALU.mult,
                  reads=[f'sw{p}_{dc}', f'gsl{q}'], writes=[f'sg{q}'])
            tk.tt(xo[p][:, dc * WIN:(dc + 1) * WIN],
                  xw[p][:, dc * WIN:(dc + 1) * WIN], sg[q][:], ALU.add,
                  reads=[f'xw{p}_{dc}', f'sg{q}'], writes=[f'xo{p}_{dc}'])
            tk.dma('sp', xdst[dc * 128:(dc + 1) * 128, w * WIN:(w + 1) * WIN],
                   xo[p][:, dc * WIN:(dc + 1) * WIN], reads=[f'xo{p}_{dc}'],
                   label=f'stxo_{w}_{dc}')


def emit_attn1(tk, E, nc, xn_h, s_h, W, scope=None):
    pb = E['pb']
    iden_bf = E['iden_bf']; mtri = E['mtri_bf']
    g = 'a1_'
    A = _mk_alloc(nc, scope, g)
    wq = A('wq', [128, 4096], BF16); wk = A('wk', [128, 4096], BF16)
    wkT = A('wkT', [128, 4096], BF16)
    wvT = A('wvT', [128, 8192], BF16)
    wo = A('wo', [128, 8192], BF16)
    bq = A('bq', [32, 16], F32); bk = A('bk', [32, 16], F32)
    bo = A('bo', [128, 8], F32)
    bvr = A('bvr', [1, 1024], BF16); bkr = A('bkr', [1, 512], BF16)
    bvb = A('bvb', [128, 1024], BF16); bkb = A('bkb', [128, 512], BF16)
    for nm, sb in [('wq1', wq), ('wk1', wk), ('wkT1', wkT), ('wvT1', wvT),
                   ('wo1', wo)]:
        tk.dma('pool', sb[:], W[nm].rearrange('p a b -> p (a b)') if W[nm].ndim == 3
               else W[nm].rearrange('p a b c -> p (a b c)'), writes=[nm])
    tk.dma('sp', bq[:], W['bq1'], writes=['bq'])
    tk.dma('sp', bk[:], W['bk1'], writes=['bk'])
    tk.dma('sp', bo[:], W['bo1'], writes=['bo'])
    tk.dma('pool', bvr[:], W['bv1'], writes=['bvr'])
    tk.dma('pool', bkr[:], W['bk_row'], writes=['bkr'])
    for half in range(2):
        tk.mm(pb[6][:, 0:512], E['ones_row_bf'][:], bvr[0:1, half * 512:(half + 1) * 512],
              True, True, reads=['ones_row_bf', 'bvr'], writes=['pb6'])
        tk.copy(bvb[:, half * 512:(half + 1) * 512], pb[6][:, 0:512],
                reads=['pb6'], writes=['bvb'], eng='act')
    tk.mm(pb[6][:, 0:512], E['ones_row_bf'][:], bkr[0:1, :], True, True,
          reads=['ones_row_bf', 'bkr'], writes=['pb6'])
    tk.copy(bkb[:], pb[6][:, 0:512], reads=['pb6'], writes=['bkb'], eng='act')
    S1f = A('S1f', [32, 1040], F32); S1b = A('S1b', [32, 1040], BF16)
    tk.memset(S1f[:], 0.0, writes=[f'S1f_{h}' for h in range(16)])
    tk.memset(S1b[:], 0.0, writes=[f'S1b_{h}' for h in range(16)])
    xnw = [A(f'xnw{i}', [128, NDC * WIN], BF16) for i in range(2)]
    qsw = [A(f'qsw{i}', [32, 16 * WIN], BF16) for i in range(2)]
    ksw = [A(f'ksw{i}', [32, 16 * WIN], BF16) for i in range(2)]
    ktp0 = A('ktp0', [128, 4 * 512], BF16)
    ktp = [ktp0, ktp0]
    v1sb = [A(f'v1sb{i}', [128, 4 * 1040], BF16) for i in range(2)]
    tmf = [A(f'tmf{i}', [128, 512], F32) for i in range(3)]
    atb = [A(f'atb{i}', [128, 128], BF16) for i in range(2)]
    zc = A('zc', [128, 2], F32)
    otm = [A(f'otm{i}', [128, 1024], BF16) for i in range(2)]
    off = A('off', [128, 8 * WIN], BF16)
    sfw0 = A('sfw0', [128, NDC * WIN], BF16)
    sfw = [sfw0, sfw0]
    for i in range(2):
        tk.memset(v1sb[i].rearrange('p (t h c) -> p t h c', t=4, h=16)[:, :, :, 64:65],
                  1.0, writes=[f'v1sb{i}_ones'], eng='pool')

    def phi_fm(psum_ap, pkey, bcol, bkey, dst, dkey):
        tk.ts(tmf[0][0:32, :], psum_ap, bcol, 0.0, ALU.add, op1=ALU.min,
              reads=[pkey, bkey], writes=['tmf0'])
        tk.ts(tmf[1][0:32, :], psum_ap, bcol, 0.0, ALU.add, op1=ALU.max,
              reads=[pkey, bkey], writes=['tmf1'])
        tk.act(tmf[2][0:32, :], tmf[0][0:32, :], AF.Exp,
               reads=['tmf0'], writes=['tmf2'])
        tk.tt(dst, tmf[2][0:32, :], tmf[1][0:32, :], ALU.add,
              reads=['tmf2', 'tmf1'], writes=[dkey])

    def load_xn(p, w):
        for dc in range(NDC):
            tk.dma('sp', xnw[p][:, dc * WIN:(dc + 1) * WIN],
                   xn_h[dc * 128:(dc + 1) * 128, w * WIN:(w + 1) * WIN],
                   writes=[f'xnw{p}_{dc}'], label=f'l1xn_w{w}_d{dc}')

    for w in range(NW):
        p = w % 2
        load_xn(p, w)
        xkeys = [f'xnw{p}_{dc}' for dc in range(NDC)]
        for (wsb, wkey, bb, bbk, dst, dkey) in [(wq, 'wq1', bq, 'bq', qsw[p], f'qsw{p}'),
                                                (wk, 'wk1', bk, 'bk', ksw[p], f'ksw{p}')]:
            for h in range(16):
                woff = (h // 2) * 64 + (h % 2) * 32
                for dc in range(NDC):
                    tk.mm(pb[0][0:32, 0:512],
                          wsb[:, dc * 512 + woff:dc * 512 + woff + 32],
                          xnw[p][:, dc * WIN:(dc + 1) * WIN], dc == 0, dc == 7,
                          reads=[wkey, xkeys[dc]], writes=['pb0'])
                phi_fm(pb[0][0:32, 0:512], 'pb0', bb[:, h:h + 1], bbk,
                       dst[0:32, h * WIN:(h + 1) * WIN], f'{dkey}_{h}')
        for tau in range(NTAU):
            for dc in range(NDC):
                tk.mm(pb[1][:, 0:512],
                      xnw[p][:, dc * WIN + tau * 128:dc * WIN + (tau + 1) * 128],
                      wkT[:, dc * 512:(dc + 1) * 512], dc == 0, dc == 7,
                      reads=['wkT1', xkeys[dc]], writes=['pb1'])
            tk.tt(tmf[0][:], pb[1][:, 0:512], bkb[:], ALU.add,
                  reads=['pb1', 'bkb'], writes=['tmf0'])
            tk.ts(tmf[1][:], tmf[0][:], 0.0, None, ALU.max,
                  reads=['tmf0'], writes=['tmf1'])
            tk.ts(tmf[0][:], tmf[0][:], 0.0, None, ALU.min,
                  reads=['tmf0'], writes=['tmf0'])
            tk.act(tmf[2][:], tmf[0][:], AF.Exp, reads=['tmf0'], writes=['tmf2'])
            tk.tt(ktp[p][:, tau * 512:(tau + 1) * 512], tmf[2][:], tmf[1][:],
                  ALU.add, reads=['tmf2', 'tmf1'], writes=[f'ktp_{tau}'])
        for tau in range(NTAU):
            for half in range(2):
                bank, bk2 = (pb[2], 'pb2') if half == 0 else (pb[3], 'pb3')
                for dc in range(NDC):
                    tk.mm(bank[:, 0:512],
                          xnw[p][:, dc * WIN + tau * 128:dc * WIN + (tau + 1) * 128],
                          wvT[:, dc * 1024 + half * 512:dc * 1024 + half * 512 + 512],
                          dc == 0, dc == 7, reads=['wvT1', xkeys[dc]], writes=[bk2])
                dst = v1sb[p].rearrange('p (t h c) -> p t h c', t=4, h=16)[
                    :, tau, half * 8:(half + 1) * 8, 0:64]
                tk.tt(dst, bank[:, 0:512].rearrange('p (h c) -> p h c', h=8),
                      bvb[:, half * 512:(half + 1) * 512].rearrange('p (h c) -> p h c', h=8),
                      ALU.add, reads=[bk2, 'bvb'], writes=[f'v1sb{p}_{tau}'])
        for tau in range(NTAU):
            for h in range(16):
                hp = h % 2
                qsl = qsw[p][0:32, h * WIN + tau * 128:h * WIN + (tau + 1) * 128]
                ksl = ksw[p][0:32, h * WIN + tau * 128:h * WIN + (tau + 1) * 128]
                tk.mm(pb[4][:, 0:128], ksl, qsl, True, True,
                      reads=[f'ksw{p}_{h}', f'qsw{p}_{h}'], writes=['pb4'])
                tk.tt(atb[hp][:], pb[4][:, 0:128], mtri[:], ALU.mult,
                      reads=['pb4', 'mtri_bf'], writes=[f'atb{hp}'])
                v1 = v1sb[p].rearrange('p (t h c) -> p t h c', t=4, h=16)[:, tau, h, :]
                tk.mm(pb[5][:, 0:65], qsl, S1b[0:32, h * 65:(h + 1) * 65],
                      True, False, reads=[f'qsw{p}_{h}', f'S1b_{h}'], writes=['pb5'])
                tk.mm(pb[5][:, 0:65], atb[hp][:], v1, False, True,
                      reads=[f'atb{hp}', f'v1sb{p}_{tau}', f'v1sb{p}_ones'],
                      writes=['pb5'])
                tk.ts(zc[:, 0:1], pb[5][:, 64:65], 1e-6, None, ALU.add,
                      reads=['pb5'], writes=['zc'])
                tk.recip(zc[:, 1:2], zc[:, 0:1], reads=['zc'], writes=['zc'])
                tk.ts(otm[tau % 2][:, h * 64:(h + 1) * 64], pb[5][:, 0:64],
                      zc[:, 1:2], None, ALU.mult, reads=['pb5', 'zc'],
                      writes=[f'otm{tau % 2}_{h}'])
                tk.mm(pb[6][0:32, 0:65], ktp[p][:, tau * 512 + h * 32:tau * 512 + (h + 1) * 32],
                      v1, True, True,
                      reads=[f'ktp_{tau}', f'v1sb{p}_{tau}', f'v1sb{p}_ones'],
                      writes=['pb6'])
                tk.tt(S1f[:, h * 65:(h + 1) * 65], S1f[:, h * 65:(h + 1) * 65],
                      pb[6][0:32, 0:65], ALU.add, reads=['pb6', f'S1f_{h}'],
                      writes=[f'S1f_{h}'])
                tk.copy(S1b[:, h * 65:(h + 1) * 65], S1f[:, h * 65:(h + 1) * 65],
                        reads=[f'S1f_{h}'], writes=[f'S1b_{h}'], eng='pool')
            for hc in range(8):
                tk.tr(pb[7].bitcast(BF16)[:, 0:128],
                      otm[tau % 2][:, hc * 128:(hc + 1) * 128], iden_bf[:],
                      reads=[f'otm{tau % 2}_{hh}' for hh in range(2 * hc, 2 * hc + 2)]
                      + ['iden_bf'], writes=['pb7'])
                tk.copy(off[:, hc * WIN + tau * 128:hc * WIN + (tau + 1) * 128],
                        pb[7].bitcast(BF16)[:, 0:128], reads=['pb7'],
                        writes=[f'off_{hc}'], eng='act')
        for dm in range(NDC):
            for hc in range(NDC):
                tk.mm(pb[0][:, 0:512], wo[:, hc * 1024 + dm * 128:hc * 1024 + dm * 128 + 128],
                      off[:, hc * WIN:(hc + 1) * WIN], hc == 0, hc == 7,
                      reads=['wo1', f'off_{hc}'], writes=['pb0'])
            tk.ts(sfw[p][:, dm * WIN:(dm + 1) * WIN], pb[0][:, 0:512],
                  bo[:, dm:dm + 1], None, ALU.add, reads=['pb0', 'bo'],
                  writes=[f'sfw_{dm}'])
        for dm in range(NDC):
            tk.dma('sp', s_h[dm * 128:(dm + 1) * 128, w * WIN:(w + 1) * WIN],
                   sfw[p][:, dm * WIN:(dm + 1) * WIN], reads=[f'sfw_{dm}'],
                   label=f'sts1_w{w}_d{dm}')


def emit_ffn1(tk, E, nc, xn_h, g_h, W, scope=None):
    pb = E['pb']
    SC = 1.0 / 127.0
    g = 'f1_'
    A = _mk_alloc(nc, scope, g)
    wg = A('wg', [128, 8 * 3072], BF16)
    wu = A('wu', [128, 8 * 3072], BF16)
    wd = A('wd', [128, 24 * 1024], BF16)
    for nm, sb in [('wgT', wg), ('wuT', wu), ('wdT', wd)]:
        tk.dma('pool', sb[:], W[nm].rearrange('p a b -> p (a b)'), writes=[nm])
    xnw = [A(f'xnw{i}', [128, NDC * WIN], BF16) for i in range(2)]
    hsb = A('hsb', [128, 24 * WIN], BF16)
    gsl = [A(f'gsl{i}', [128, WIN], BF16) for i in range(2)]
    usl = [A(f'usl{i}', [128, WIN], BF16) for i in range(2)]
    gw = [A(f'gw{i}', [128, WIN], BF16) for i in range(2)]
    for w in range(NW):
        p = w % 2
        for dc in range(NDC):
            tk.dma('sp', xnw[p][:, dc * WIN:(dc + 1) * WIN],
                   xn_h[dc * 128:(dc + 1) * 128, w * WIN:(w + 1) * WIN],
                   writes=[f'xnw{p}_{dc}'], label=f'f1xn_w{w}_d{dc}')
        for m3 in range(24):
            q = m3 % 2
            bank, bk = (pb[0], 'pb0') if q == 0 else (pb[1], 'pb1')
            for dc in range(NDC):
                tk.mm(bank[:, 0:512],
                      wg[:, dc * 3072 + m3 * 128:dc * 3072 + (m3 + 1) * 128],
                      xnw[p][:, dc * WIN:(dc + 1) * WIN], dc == 0, dc == 7,
                      reads=['wgT', f'xnw{p}_{dc}'], writes=[bk])
            tk.act(gsl[q][:], bank[:, 0:512], AF.Silu, scale=SC,
                   reads=[bk], writes=[f'gsl{q}'])
            bank2, bk2 = (pb[2], 'pb2') if q == 0 else (pb[3], 'pb3')
            for dc in range(NDC):
                tk.mm(bank2[:, 0:512],
                      wu[:, dc * 3072 + m3 * 128:dc * 3072 + (m3 + 1) * 128],
                      xnw[p][:, dc * WIN:(dc + 1) * WIN], dc == 0, dc == 7,
                      reads=['wuT', f'xnw{p}_{dc}'], writes=[bk2])
            tk.act(usl[q][:], bank2[:, 0:512], AF.Copy, scale=SC,
                   reads=[bk2], writes=[f'usl{q}'])
            tk.tt(hsb[:, m3 * WIN:(m3 + 1) * WIN], gsl[q][:], usl[q][:], ALU.mult,
                  reads=[f'gsl{q}', f'usl{q}'], writes=[f'h_{m3}'])
        for dc in range(NDC):
            q = dc % 2
            bank, bk = (pb[4], 'pb4') if q == 0 else (pb[5], 'pb5')
            for m3 in range(24):
                tk.mm(bank[:, 0:512],
                      wd[:, m3 * 1024 + dc * 128:m3 * 1024 + (dc + 1) * 128],
                      hsb[:, m3 * WIN:(m3 + 1) * WIN], m3 == 0, m3 == 23,
                      reads=['wdT', f'h_{m3}'], writes=[bk])
            tk.act(gw[q][:], bank[:, 0:512], AF.Copy, scale=SC,
                   reads=[bk], writes=[f'gw{q}'])
            tk.dma('sp', g_h[dc * 128:(dc + 1) * 128, w * WIN:(w + 1) * WIN],
                   gw[q][:], reads=[f'gw{q}'], label=f'stg1_{w}_{dc}')


def emit_pe_all(tk, E, nc, x_all, routerT, allpe, scope=None):
    pb = E['pb']
    iden = E['iden_f32']; ones_col = E['ones_col']; ones_row = E['ones_row']
    cst = E['cst']; irev = E['irev']
    g = 'pa_'
    A = _mk_alloc(nc, scope, g)
    xw = [A(f'xw{i}', [128, NDC * WIN], F32) for i in range(2)]
    x2w = [A(f'x2w{i}', [128, NDC * WIN], F32) for i in range(2)]
    stats = [A(f'stats{q}', [128, 24], F32) for q in range(2)]
    routerT_sb = A('routerT_sb', [128, 64], F32)
    negSr = A('negSr', [1, 8], F32)
    srb = A('srb', [128, 8], F32)
    srcol = A('srcol', [8, 1], F32)
    lgt = [A(f'lgt{q}', [128, 32], F32) for q in range(2)]
    elg = [A(f'elg{q}', [128, 32], F32) for q in range(2)]
    eqt = [A(f'eqt{q}', [128, 32], F32) for q in range(2)]
    t2b = [A(f't2b{q}', [128, 32], F32) for q in range(2)]
    mx4 = [A(f'mx4{q}', [128, 16], F32) for q in range(2)]
    pecols = [A(f'pecols{q}', [128, 8], F32) for q in range(2)]
    peacc = A('peacc', [8, 1024], F32)
    irev4 = A('irev4', [128, 32], F32)
    tk.dma('sp', routerT_sb[:], routerT.rearrange('p a b -> p (a b)'),
           writes=['routerT_sb'])
    for rr in range(4):
        tk.copy(irev4[:, rr * 8:(rr + 1) * 8], irev[:], reads=['irev'],
                writes=['irev4'], eng='pool')
    for dc in range(NDC):
        tk.mm(pb[5][0:8, 0:1], routerT_sb[:, dc * 8:(dc + 1) * 8], ones_col[:],
              dc == 0, dc == 7, reads=['routerT_sb', 'ones_col'], writes=['pb5'])
    tk.ts(srcol[:], pb[5][0:8, 0:1], -1.0, None, ALU.mult,
          reads=['pb5'], writes=['srcol'])
    tk.tr(pb[6][0:1, 0:8], srcol[:], iden[0:8, 0:8],
          reads=['srcol', 'iden_f32'], writes=['pb6'])
    tk.copy(negSr[:], pb[6][0:1, 0:8], reads=['pb6'], writes=['negSr'])
    tk.mm(pb[6][:, 8:16], ones_row[:], negSr[:], True, True,
          reads=['ones_row', 'negSr'], writes=['pb6'])
    tk.copy(srb[:], pb[6][:, 8:16], reads=['pb6'], writes=['srb'])
    srows = A('srows', [1, 1024], F32)
    lg8 = [A(f'lg8{i}', [8, 512], F32) for i in range(2)]
    for r in range(8):
        for w in range(NW):
            p = w % 2
            q = w % 2
            bs, bq2, br = (pb[0], pb[1], pb[2]) if q == 0 else (pb[3], pb[4], pb[5])
            ks, kq, kr = (f'pb{j}' for j in ((0, 1, 2) if q == 0 else (3, 4, 5)))
            bt, kt = (pb[6], 'pb6') if q == 0 else (pb[7], 'pb7')
            st = stats[q]; sk = f'st{q}'
            for dc in range(NDC):
                tk.dma('sp', xw[p][:, dc * WIN:(dc + 1) * WIN],
                       x_all[r, dc * 128:(dc + 1) * 128, w * WIN:(w + 1) * WIN],
                       writes=[f'xw{p}_{dc}'], label=f'pa_{r}_{w}_{dc}')
            for dc in range(NDC):
                tk.act(x2w[p][:, dc * WIN:(dc + 1) * WIN],
                       xw[p][:, dc * WIN:(dc + 1) * WIN], AF.Square,
                       reads=[f'xw{p}_{dc}'], writes=[f'x2w{p}_{dc}'])
            for dc in range(NDC):
                tk.mm(bs[0:1, 0:512], ones_col[:], xw[p][:, dc * WIN:(dc + 1) * WIN],
                      dc == 0, dc == 7, reads=[f'xw{p}_{dc}', 'ones_col'], writes=[ks])
                tk.mm(bq2[0:1, 0:512], ones_col[:], x2w[p][:, dc * WIN:(dc + 1) * WIN],
                      dc == 0, dc == 7, reads=[f'x2w{p}_{dc}', 'ones_col'], writes=[kq])
            tk.copy(srows[0:1, 0:512], bs[0:1, 0:512], reads=[ks],
                    writes=['srows'], eng='act')
            tk.copy(srows[0:1, 512:1024], bq2[0:1, 0:512], reads=[kq],
                    writes=['srows'], eng='act')
            for dc in range(NDC):
                tk.mm(bs[32:40, 0:512], routerT_sb[:, dc * 8:(dc + 1) * 8],
                      xw[p][:, dc * WIN:(dc + 1) * WIN], dc == 0, dc == 7,
                      reads=[f'xw{p}_{dc}', 'routerT_sb'], writes=[f'pbr{q}'])
            tk.copy(lg8[q][:], bs[32:40, 0:512], reads=[f'pbr{q}'],
                    writes=[f'lg8{q}'], eng='act')
            for tau in range(NTAU):
                tk.tr(br[:, 8 * tau:8 * tau + 8],
                      lg8[q][0:8, tau * 128:(tau + 1) * 128], iden[0:8, 0:8],
                      reads=[f'lg8{q}', 'iden_f32'], writes=[kr])
            for tau in range(NTAU):
                tk.tr(bt[:, tau:tau + 1], srows[0:1, tau * 128:(tau + 1) * 128],
                      iden[0:1, 0:1], reads=['srows', 'iden_f32'], writes=[kt])
                tk.tr(bt[:, 4 + tau:5 + tau],
                      srows[0:1, 512 + tau * 128:512 + (tau + 1) * 128],
                      iden[0:1, 0:1], reads=['srows', 'iden_f32'], writes=[kt])
            tk.ts(st[:, 0:4], bt[:, 0:4], 1.0 / D, None, ALU.mult,
                  reads=[kt], writes=[sk])
            tk.act(st[:, 4:8], st[:, 0:4], AF.Square, reads=[sk], writes=[sk])
            tk.stt(st[:, 8:12], bt[:, 4:8], 1.0 / D, st[:, 4:8],
                   ALU.mult, ALU.subtract, reads=[kt, sk], writes=[sk])
            tk.act(st[:, 12:16], st[:, 8:12], AF.Ln, bias=cst[:, 0:1],
                   reads=[sk, 'cst'], writes=[sk])
            tk.act(st[:, 16:20], st[:, 12:16], AF.Exp, scale=-0.5,
                   reads=[sk], writes=[sk])
            tk.stt(st[:, 20:24], st[:, 0:4], -1.0, st[:, 16:20],
                   ALU.mult, ALU.mult, reads=[sk], writes=[sk])
            v_t2 = t2b[q].rearrange('p (t e) -> p t e', t=4)
            tk.tt(v_t2, st[:, 20:24].unsqueeze(2).to_broadcast([128, 4, 8]),
                  srb[:].unsqueeze(1).to_broadcast([128, 4, 8]),
                  ALU.mult, reads=[sk, 'srb'], writes=[f't2b{q}'])
            v_lgt = lgt[q].rearrange('p (t e) -> p t e', t=4)
            tk.tt(v_lgt, br[:, 0:32].rearrange('p (t e) -> p t e', t=4),
                  st[:, 16:20].unsqueeze(2).to_broadcast([128, 4, 8]),
                  ALU.mult, reads=[kr, sk], writes=[f'lgt{q}'])
            tk.tt(lgt[q][:], lgt[q][:], t2b[q][:], ALU.add,
                  reads=[f'lgt{q}', f't2b{q}'], writes=[f'lgt{q}'])
            tk.red(mx4[q][:, 0:4].unsqueeze(2), v_lgt, ALU.max,
                   negate=True, reads=[f'lgt{q}'], writes=[f'mx4{q}'])
            v_el = elg[q].rearrange('p (t e) -> p t e', t=4)
            tk.tt(v_el, v_lgt,
                  mx4[q][:, 0:4].unsqueeze(2).to_broadcast([128, 4, 8]),
                  ALU.add, reads=[f'lgt{q}', f'mx4{q}'], writes=[f'elg{q}'])
            tk.act(elg[q][:], elg[q][:], AF.Exp, reads=[f'elg{q}'], writes=[f'elg{q}'])
            tk.red(mx4[q][:, 4:8].unsqueeze(2), v_el, ALU.add,
                   reads=[f'elg{q}'], writes=[f'mx4{q}'])
            tk.recip(mx4[q][:, 8:12], mx4[q][:, 4:8],
                     reads=[f'mx4{q}'], writes=[f'mx4{q}'])
            tk.ts(eqt[q][:], elg[q][:], 1.0, None, ALU.is_ge,
                  reads=[f'elg{q}'], writes=[f'eqt{q}'])
            tk.tt(eqt[q][:], eqt[q][:], irev4[:], ALU.mult,
                  reads=[f'eqt{q}', 'irev4'], writes=[f'eqt{q}'])
            tk.red(mx4[q][:, 12:16].unsqueeze(2),
                   eqt[q].rearrange('p (t e) -> p t e', t=4), ALU.max,
                   reads=[f'eqt{q}'], writes=[f'mx4{q}'])
            vp = pecols[q].rearrange('p (t c) -> p t c', t=4)
            tk.copy(vp[:, :, 0:1], mx4[q][:, 8:12].unsqueeze(2),
                    reads=[f'mx4{q}'], writes=[f'pe{q}'])
            tk.ts(vp[:, :, 1:2], mx4[q][:, 12:16].unsqueeze(2),
                  -1.0, 7.0, ALU.mult, op1=ALU.add,
                  reads=[f'mx4{q}'], writes=[f'pe{q}'])
            bt, kt = (pb[6], 'pb6') if q == 0 else (pb[7], 'pb7')
            tk.tr(bt[0:8, 128:256], pecols[q][:], iden[:],
                  reads=[f'pe{q}', 'iden_f32'], writes=[kt])
            tk.copy(peacc[:, w * 128:(w + 1) * 128], bt[0:8, 128:256],
                    reads=[kt], writes=[f'peacc_{w}'], eng='act')
        Fv = allpe.rearrange('a (w t j) -> a t w j', w=8, j=128)
        for cc in range(2):
            tk.dma('sp', Fv[2 * r + cc],
                   peacc[cc:8:2, :].rearrange('p (w j) -> p w j', j=128),
                   reads=[f'peacc_{w}' for w in range(NW)],
                   writes=[f'allpe_{r}'], label=f'stpe_{r}_{cc}')


from contextlib import ExitStack
from concourse.bass_utils import run_bass_kernel_spmd

_CONST_KEYS = ['iden', 'ones_col', 'ones_row', 'mtri', 'selA', 'selB', 'rmat',
               'rmatT', 'idxmat', 'irev', 'ones8k']


def build_fused(nc):
    F, B, I = F32, BF16, I32
    ci = {}

    def ein(name, shape, dtype):
        ci[name] = nc.dram_tensor(name, shape, dtype, kind="ExternalInput").ap()
        return ci[name]

    x0 = ein('x0', [1024, 4096], F)
    x_all = ein('x_all', [8, 1024, 4096], F)
    routerT = ein('routerT', [128, 8, 8], F)
    csel = ein('csel', [128, 2], F)
    cselE = ein('cselE', [128, 16], F)
    W0 = {nm: ein(nm, sh, F) for nm, sh in [
        ('wq', [128, 8, 8, 64]), ('wk', [128, 8, 8, 64]), ('wkT', [128, 8, 256]),
        ('wkeT', [128, 8, 256]), ('wvT', [128, 8, 1024]), ('wveT', [128, 8, 1024]),
        ('wout', [128, 8, 1024])]}
    WM = {nm: ein(nm, sh, F) for nm, sh in [
        ('w1', [8, 128, 8, 4096]), ('w2', [8, 128, 32, 1024]),
        ('b1', [8, 128, 32]), ('b2', [8, 128, 8])]}
    W1 = {}
    for nm, sh, dt in [('wq1', [128, 8, 8, 64], F), ('wk1', [128, 8, 8, 64], F),
                       ('bq1', [32, 16], F), ('bk1', [32, 16], F),
                       ('wkT1', [128, 8, 512], F), ('wvT1', [128, 8, 1024], F),
                       ('bv1', [1, 1024], F), ('bk_row', [1, 512], F),
                       ('wo1', [128, 8, 1024], F), ('bo1', [128, 8], F),
                       ('wgT', [128, 8, 3072], I), ('wuT', [128, 8, 3072], I),
                       ('wdT', [128, 24, 1024], I)]:
        W1[nm] = ein(nm, sh, dt)
    out_h = nc.dram_tensor('out_h', [1024, 4096], F, kind="ExternalOutput").ap()
    def intl(name, shape, dtype):
        return nc.dram_tensor(name, shape, dtype, kind="Internal").ap()
    allpe = intl('allpe', [16, 4096], F)
    xn0 = intl('xn0', [1024, 4096], B)
    kmask = intl('kmask', [8, 4096], F)
    s0 = intl('s0', [1024, 4096], B)
    g0 = intl('g0', [8, 1024, 4096], B)
    x1 = intl('x1', [1024, 4096], F)
    xn1 = intl('xn1', [1024, 4096], B)
    s1 = intl('s1', [1024, 4096], B)
    g1 = intl('g1', [1024, 4096], B)

    tk = TK(nc)
    with nc.Block() as block:
        E = setup_consts(nc, tk)
        E['pb'] = alloc_psum_banks(nc)
        tk.phase = 'ln0'
        with ExitStack() as sc:
            emit_ln(tk, E, nc, x0, xn0, tag='ln0', scope=sc)
        tk.barrier()
        tk.phase = 'peall'
        with ExitStack() as sc:
            emit_pe_all(tk, E, nc, x_all, routerT, allpe, scope=sc)
        tk.barrier()
        tk.phase = 'routing'
        with ExitStack() as sc:
            emit_routing(tk, E, nc, allpe, csel, cselE, kmask, scope=sc)
        tk.barrier()
        tk.phase = 'attn0'
        with ExitStack() as sc:
            emit_attn0(tk, E, nc, xn0, s0, W0, scope=sc)
        tk.barrier()
        tk.phase = 'moe'
        with ExitStack() as sc:
            emit_moe(tk, E, nc, xn0, kmask, g0, WM, scope=sc)
        tk.barrier()
        tk.phase = 'cb0'
        with ExitStack() as sc:
            emit_combine(tk, E, nc, x0, s0, g0, x1, scope=sc, tag='cb0', ge=True)
        tk.barrier()
        tk.phase = 'ln1'
        with ExitStack() as sc:
            emit_ln(tk, E, nc, x1, xn1, tag='ln1', scope=sc)
        tk.barrier()
        tk.phase = 'attn1'
        with ExitStack() as sc:
            emit_attn1(tk, E, nc, xn1, s1, W1, scope=sc)
        tk.barrier()
        tk.phase = 'ffn1'
        with ExitStack() as sc:
            emit_ffn1(tk, E, nc, xn1, g1, W1, scope=sc)
        tk.barrier()
        tk.phase = 'cb1'
        with ExitStack() as sc:
            emit_combine(tk, E, nc, x1, s1, g1, out_h, scope=sc, tag='cb1')
        tk.flush(block)
    return nc


def kernel(**inputs):
    c = pack_consts()
    w = pack_weights(inputs)
    x = inputs['L0_attn_x']
    xallT = np.ascontiguousarray(np.transpose(x, (0, 2, 1))).astype(np.float32)
    nc = build_fused(bass.Bass("TRN2", target_bir_lowering=False,
                               detect_race_conditions=False))
    cim = {}
    for k2 in _CONST_KEYS:
        key = {'iden': 'iden_f32'}.get(k2, k2)
        cim['c_' + k2] = c[key]
    wm = {'wq': 'l0_wq', 'wk': 'l0_wk', 'wkT': 'l0_wkT', 'wkeT': 'l0_wkeT',
          'wvT': 'l0_wvT', 'wveT': 'l0_wveT', 'wout': 'l0_wout',
          'w1': 'w1', 'w2': 'w2', 'b1': 'b1', 'b2': 'b2',
          'wq1': 'l1_wq', 'wk1': 'l1_wk', 'bq1': 'l1_bq', 'bk1': 'l1_bk',
          'wkT1': 'l1_wkT', 'wvT1': 'l1_wvT', 'bv1': 'l1_bv',
          'bk_row': 'l1_bk_row', 'wo1': 'l1_wo', 'bo1': 'l1_bo',
          'wgT': 'wgT', 'wuT': 'wuT', 'wdT': 'wdT', 'routerT': 'routerT'}
    in_maps = []
    for core in range(8):
        im = dict(cim)
        im['x0'] = np.ascontiguousarray(x[core].T).astype(np.float32)
        im['x_all'] = xallT
        for k2, v in wm.items():
            im[k2] = w[v]
        cs = np.zeros((128, 2), np.float32)
        cse = np.zeros((128, 16), np.float32)
        for gg in range(2):
            for e in range(8):
                cs[e * 16 + 2 * core + gg, gg] = 1.0
                cse[e * 16 + 2 * core + gg, 2 * e + gg] = 1.0
        im['csel'] = cs
        im['cselE'] = cse
        in_maps.append(im)
    res = run_bass_kernel_spmd(nc, in_maps, core_ids=list(range(8)))
    out = np.stack([np.ascontiguousarray(r['out_h'].T) for r in res.results],
                   axis=0)
    return out.astype(np.float32)
